# version 02
# baseline (speedup 1.0000x reference)
import contextlib
import numpy as np
import concourse.bass as bass
import concourse.mybir as mybir

F32 = mybir.dt.float32
BF16 = mybir.dt.bfloat16
AF = mybir.ActivationFunctionType
ALU = mybir.AluOpType
AX = mybir.AxisListType


class CFG:
    def __init__(self, D=2048, T=2048, FF=8192, LR=32):
        self.D = D; self.T = T; self.FF = FF; self.LR = LR
        self.R = 4; self.DH = 128
        self.NH = D // 128
        self.NT = T // 128
        self.NG = T // 512
        self.DC = D // 128
        self.FC = FF // 128
        self.S = 4 * T
        self.EPS = 1e-5
        self.maxslots = 4 * (4 * (self.NG - 1) + 4)


def cdiv(a, b):
    return (a + b - 1) // b


class Sem:
    def __init__(self, nc, es, name):
        self.h = es.enter_context(nc.semaphore(name)); self.n = 0
    def inc(self, inst, k=1):
        inst.then_inc(self.h, k); self.n += k
    def wait(self, eng, val=None):
        eng.wait_ge(self.h, self.n if val is None else val)


class B:
    def __init__(self, cfg: CFG):
        self.cfg = cfg
        self.nc = bass.Bass("TRN2", target_bir_lowering=False, num_devices=8)
        self.es = contextlib.ExitStack()
        self.t = {}
        self.dram = {}
        self.sems = {}
        self.blk = None

    def sem(self, name):
        if name not in self.sems:
            self.sems[name] = Sem(self.nc, self.es, name)
        return self.sems[name]

    def sb(self, es, name, shape, dtype):
        self.t[name] = es.enter_context(self.nc.sbuf_tensor(name, shape, dtype))
        return self.t[name]

    def ps(self, es, name, shape, dtype=F32):
        self.t[name] = es.enter_context(self.nc.psum_tensor(name, shape, dtype))
        return self.t[name]

    def dr(self, name, shape, dtype, kind="Internal"):
        self.dram[name] = self.nc.dram_tensor(name, shape, dtype, kind=kind).ap()
        return self.dram[name]

    def on(self, engname, f):
        getattr(self.blk, engname)(f)



def host_masks(j: int):
    import ml_dtypes
    out = np.zeros((4, 4, 128, 512), dtype=np.float32)
    k = np.arange(128)[:, None]; q = np.arange(128)[None, :]
    for d in range(4):
        for r in range(4):
            tri = (k < q) | ((k == q) & (r <= j))
            for m in range(4):
                if d < m:
                    out[d, r, :, 128 * m:128 * (m + 1)] = 1.0
                elif d == m:
                    out[d, r, :, 128 * m:128 * (m + 1)] = tri
    return np.ascontiguousarray(out.astype(ml_dtypes.bfloat16))


def shard_x(x):
    return [np.ascontiguousarray(x[c // 4, (c % 4)::4, :]) for c in range(8)]


def unshard_y(cfg, ys):
    out = np.empty((2, cfg.S, cfg.D), dtype=np.float32)
    for c in range(8):
        out[c // 4, (c % 4)::4, :] = ys[c]
    return out



def ln_stats(b, src, n, stat, scratch, s_sc, s_v):
    cfg = b.cfg
    def fsc(sc):
        s_v.wait(sc)
        s_sc.inc(sc.activation(out=scratch[:, 0:n], in_=src[:, 0:n], func=AF.Square,
                               accum_out=stat[:, 0:1]), 1)
    b.on("scalar", fsc)
    def fv(v):
        s_v.wait(v)
        s_v.inc(v.reduce_sum(out=stat[:, 1:2], in_=src[:, 0:n], axis=AX.X), 1)
        s_v.wait(v)
        s_v.inc(v.tensor_scalar(out=stat[:, 2:3], in0=stat[:, 1:2], scalar1=1.0 / n,
                                scalar2=None, op0=ALU.mult), 1)
        s_sc.wait(v)
        s_v.inc(v.tensor_scalar(out=stat[:, 3:4], in0=stat[:, 0:1], scalar1=1.0 / n,
                                scalar2=None, op0=ALU.mult), 1)
        s_v.wait(v)
        s_v.inc(v.tensor_tensor(out=stat[:, 4:5], in0=stat[:, 2:3], in1=stat[:, 2:3],
                                op=ALU.mult), 1)
        s_v.wait(v)
        s_v.inc(v.tensor_tensor(out=stat[:, 5:6], in0=stat[:, 3:4], in1=stat[:, 4:5],
                                op=ALU.subtract), 1)
        s_v.wait(v)
        s_v.inc(v.tensor_scalar(out=stat[:, 5:6], in0=stat[:, 5:6], scalar1=float(cfg.EPS),
                                scalar2=None, op0=ALU.add), 1)
    b.on("vector", fv)
    def fsc2(sc):
        s_v.wait(sc)
        s_sc.inc(sc.activation(out=stat[:, 6:7], in_=stat[:, 5:6], func=AF.Ln), 1)
        s_sc.wait(sc)
        s_sc.inc(sc.activation(out=stat[:, 7:8], in_=stat[:, 6:7], func=AF.Exp,
                               scale=-0.5), 1)
    b.on("scalar", fsc2)


def ln_apply(b, src, dst, n, stat, s_sc, s_v, gbc=None, bbc=None):
    def fv(v):
        s_sc.wait(v)
        s_v.inc(v.tensor_scalar(out=dst[:, 0:n], in0=src[:, 0:n],
                                scalar1=stat[:, 2:3], scalar2=stat[:, 7:8],
                                op0=ALU.subtract, op1=ALU.mult), 1)
        if gbc is not None:
            s_v.wait(v)
            s_v.inc(v.tensor_tensor(out=dst[:, 0:n], in0=dst[:, 0:n], in1=gbc[:, 0:n],
                                    op=ALU.mult), 1)
            s_v.wait(v)
            s_v.inc(v.tensor_tensor(out=dst[:, 0:n], in0=dst[:, 0:n], in1=bbc[:, 0:n],
                                    op=ALU.add), 1)
    b.on("vector", fv)


def bcast_rows(b, row_sb, out_sb, n, s_t, s_v, bcps):
    for i in range(cdiv(n, 512)):
        c0, c1 = i * 512, min(n, (i + 1) * 512)
        def ft(t, c0=c0, c1=c1):
            s_v.wait(t)
            s_t.inc(t.matmul(bcps[:, 0:c1 - c0], b.t["ONES_row_f32"][:, :],
                             row_sb[:, c0:c1], start=True, stop=True), 1)
        b.on("tensor", ft)
        def fv(v, c0=c0, c1=c1):
            s_t.wait(v)
            s_v.inc(v.tensor_copy(out=out_sb[:, c0:c1], in_=bcps[:, 0:c1 - c0]), 1)
        b.on("vector", fv)


def load_fm(b, dst_sb, src_dram, nchunks, tcols, s_d, wait_sems=()):
    def fs(s):
        for sm, val in wait_sems:
            sm.wait(s, val)
        for fc in range(nchunks):
            s_d.inc(s.dma_start(out=dst_sb[:, fc * tcols:(fc + 1) * tcols],
                                in_=src_dram[:, fc * 128:(fc + 1) * 128], transpose=True), 16)
    b.on("sync", fs)



def proj_fm_out(b, wsrc_dram, wcol0, ncols, dst_dram, drow0, s_g, s_t, s_v, s_d,
                xnfm, prj, stg, wslots, wload_snap, scale_cols=None, bias_pp=None):
    raise NotImplementedError



def build(cfg: CFG, upto="D"):
    b = B(cfg)
    nc = b.nc
    import ml_dtypes
    D, T, NT, DC, NG, NH = cfg.D, cfg.T, cfg.NT, cfg.DC, cfg.NG, cfg.NH
    FF, FC, LR = cfg.FF, cfg.FC, cfg.LR

    x_in = b.dr("x", [T, D], F32, kind="ExternalInput")
    msk_in = b.dr("masks", [4, 4, 128, 512], BF16, kind="ExternalInput")
    wnames = [
        ("L0_attn_qkv", [D, 3 * D]), ("L0_attn_o_proj", [D, D]),
        ("L0_attn_gamma", [D]), ("L0_attn_beta", [D]),
        ("L0_ffn_simple_bias", [D]), ("L0_ffn_w_down", [LR, D]), ("L0_ffn_w_up", [D, LR]),
        ("L0_ffn_w_router", [3, D]), ("L0_ffn_ln1_scale", [D]), ("L0_ffn_ln1_bias", [D]),
        ("L0_ffn_wq", [D, D]), ("L0_ffn_wk", [D, D]), ("L0_ffn_wv", [D, D]),
        ("L0_ffn_wo", [D, D]), ("L0_ffn_ln2_scale", [D]), ("L0_ffn_ln2_bias", [D]),
        ("L0_ffn_w_ff1", [FF, D]), ("L0_ffn_w_ff2", [D, FF]),
        ("L1_attn_w_qkv", [D, 3 * D]), ("L1_attn_b_qkv", [3 * D]),
        ("L1_attn_w_out", [D, D]), ("L1_attn_b_out", [D]),
        ("L1_ffn_w_gate", [D, FF]), ("L1_ffn_w_up", [D, FF]), ("L1_ffn_w_down", [FF, D]),
    ]
    w = {}
    for name, shape in wnames:
        w[name] = b.dr(name, shape, F32, kind="ExternalInput")
    y_out = b.dr("y", [T, D], F32, kind="ExternalOutput")

    X = b.dr("X", [T, D], F32)
    XNBF = b.dr("XNBF", [T, D], BF16)
    QF = b.dr("QF", [D, T], BF16)
    KLOC = b.dr("KLOC", [D, T], BF16)
    VLOC = b.dr("VLOC", [T, D], BF16)
    KALL = b.dr("KALL", [4 * D, T], BF16)
    VALL = b.dr("VALL", [4 * T, D], BF16)
    CTXBF = b.dr("CTXBF", [T, D], BF16)
    KN2L = b.dr("KN2L", [1, 128], F32)
    KN2G = b.dr("KN2G", [1, 128], F32)
    HS = b.dr("HS", [T, D], F32)
    PART1 = b.dr("PART1", [T, D], F32)
    XPRIME = b.dr("XPRIME", [T, D], F32)
    XN2BF = b.dr("XN2BF", [T, D], BF16)
    HBF = b.dr("HBF", [T, FF], BF16)
    WQB = b.dr("WQB", [D, D], BF16)
    WKB = b.dr("WKB", [D, D], BF16)
    WVB = b.dr("WVB", [D, D], BF16)
    WOB = b.dr("WOB", [D, D], BF16)
    F1B = b.dr("F1B", [FF, D], BF16)
    F2B = b.dr("F2B", [D, FF], BF16)
    WDB = b.dr("WDB", [LR, D], BF16)
    WUB = b.dr("WUB", [D, LR], BF16)

    idbf = nc.inline_tensor(np.eye(128, dtype=ml_dtypes.bfloat16), "c_idbf").ap()
    idf32 = nc.inline_tensor(np.eye(128, dtype=np.float32), "c_idf32").ap()
    b.sb(b.es, "ID_BF", [128, 128], BF16)
    b.sb(b.es, "ID_F32", [128, 128], F32)
    b.sb(b.es, "ONES_col_bf", [128, 1], BF16)
    b.sb(b.es, "ONES_row_f32", [1, 128], F32)
    b.sb(b.es, "MSK", [128, 16 * 512], BF16)
    b.sb(b.es, "ANMAX", [1, 2], F32)
    b.sb(b.es, "BNMAX", [1, 2 * NH], F32)
    b.sb(b.es, "CNMAX", [1, 2 * NH], F32)
    b.sb(b.es, "RM0", [128, NT], F32)
    b.sb(b.es, "RM1", [128, NT], F32)
    b.sb(b.es, "RM2", [128, NT], F32)

    s_g = b.sem("s_g"); s_d = b.sem("s_d"); s_e = b.sem("s_e")
    s_t = b.sem("s_t"); s_v = b.sem("s_v"); s_sc = b.sem("s_sc")
    s_p = b.sem("s_p"); s_cc = b.sem("s_cc"); s_o = b.sem("s_o")
    s_pc = b.sem("B_pc")
    s_y = b.sem("s_y")
    S = (s_g, s_d, s_e, s_t, s_v, s_sc, s_p, s_o)

    def fin_copy(xsem_names):
        seg = contextlib.ExitStack()
        blk = seg.enter_context(nc.Block("fin"))
        b.blk = blk
        emit_seg_fence(b)
        fin = b.sb(seg, "FIN", [128, D], F32)
        def f(s):
            for nm in xsem_names:
                if nm in b.sems:
                    b.sems[nm].wait(s)
            for i in range(NT):
                s_d.inc(s.dma_start(out=fin[:, :], in_=X[i * 128:(i + 1) * 128, :]), 16)
                s_d.wait(s)
                s_y.inc(s.dma_start(out=y_out[i * 128:(i + 1) * 128, :], in_=fin[:, :]), 16)
                s_y.wait(s)
        b.on("sync", f)
        seg.close()

    with b.es:
        seg = contextlib.ExitStack()
        blk = seg.enter_context(nc.Block("seg1"))
        b.blk = blk
        emit_seg_fence(b)
        def f(s):
            s_d.inc(s.dma_start(out=b.t["ID_BF"][:, :], in_=idbf[:, :]), 16)
            s_d.inc(s.dma_start(out=b.t["ID_F32"][:, :], in_=idf32[:, :]), 16)
            s_d.inc(s.dma_start(
                out=b.t["MSK"][:, :].rearrange("p (s n) -> p s n", n=512),
                in_=msk_in.rearrange("d r p n -> p (d r) n")), 16)
        b.on("sync", f)
        def f(g):
            g.memset(b.t["ONES_col_bf"][:, :], 1.0)
            s_p.inc(g.memset(b.t["ONES_row_f32"][:, :], 1.0), 1)
        b.on("gpsimd", f)
        emit_attnA(b, seg, x_in, w, X, XNBF, QF, KLOC, VLOC, KN2L, s_list=S)
        if upto != "A":
            def f(g):
                for src_n, dst in (("L0_ffn_wq", WQB), ("L0_ffn_wk", WKB),
                                   ("L0_ffn_wv", WVB), ("L0_ffn_wo", WOB),
                                   ("L0_ffn_w_ff1", F1B), ("L0_ffn_w_ff2", F2B),
                                   ("L0_ffn_w_down", WDB), ("L0_ffn_w_up", WUB)):
                    s_pc.inc(g.dma_start(out=dst[:, :], in_=b.dram[src_n][:, :]), 16)
            b.on("gpsimd", f)
        seg.close()

        emit_gather3(b, KLOC, VLOC, KN2L, KALL, VALL, KN2G,
                     wait_names=["Aqk_stg", "Aqk_stgv", "s_d"])

        seg = contextlib.ExitStack()
        blk = seg.enter_context(nc.Block("seg2"))
        b.blk = blk
        emit_seg_fence(b)
        sub = contextlib.ExitStack()
        emit_flash_L0(b, sub, QF, KALL, VALL, KN2G, CTXBF, s_list=S)
        sub.close()
        emit_oproj_resid(b, seg, CTXBF, w["L0_attn_o_proj"], X, s_list=S,
                         tag="A5", prev_x_sem="A_xst", src_store_sem="fl_cts",
                         dst=X, dst_sem_name="op_xw")
        seg.close()

        if upto == "A":
            fin_copy(["op_xw"])
            return b

        seg = contextlib.ExitStack()
        blk = seg.enter_context(nc.Block("seg3"))
        b.blk = blk
        emit_seg_fence(b)
        emit_B1(b, seg, X, w, HS, PART1, XNBF, s_list=S)
        seg.close()

        seg = contextlib.ExitStack()
        blk = seg.enter_context(nc.Block("seg4"))
        b.blk = blk
        emit_seg_fence(b)
        emit_qkv_proj(b, seg, XNBF, None, None, QF, KLOC, VLOC, s_list=S,
                      interleaved=False, tag="B", nb_sem="B1_nb",
                      wsrcT={"q": WQB, "k": WKB, "v": WVB})
        emit_norms(b, seg, QF, KLOC, KN2L, s_list=S, per_head=True, tag="B")
        seg.close()

        emit_gather3(b, KLOC, VLOC, KN2L, KALL, VALL, KN2G,
                     wait_names=["Bqk_stg", "Bqk_stgv", "s_d"])

        seg = contextlib.ExitStack()
        blk = seg.enter_context(nc.Block("seg5"))
        b.blk = blk
        emit_seg_fence(b)
        ctxfm = b.sb(seg, "BCTXFM", [128, NH * T], BF16)
        sub = contextlib.ExitStack()
        emit_flash_MHA(b, sub, QF, KALL, VALL, KN2G, ctxfm, "BNMAX", s_list=S, tag="BF")
        sub.close()
        emit_oproj_ln2(b, seg, ctxfm, WOB, HS, XPRIME, XN2BF, w, s_list=S)
        seg.close()

        seg = contextlib.ExitStack()
        blk = seg.enter_context(nc.Block("seg6"))
        b.blk = blk
        emit_seg_fence(b)
        sub = contextlib.ExitStack()
        b.sems["B6_nbsrc"] = b.sems["B5_nb"]
        def w1load_B(s, ip, dst, sem, which):
            if ip == 0:
                b.sems["B_pc"].wait(s)
            for fc in range(DC):
                sem.inc(s.dma_start(out=dst[:, fc * 512:(fc + 1) * 512],
                                    in_=F1B[ip * 512:(ip + 1) * 512, fc * 128:(fc + 1) * 128],
                                    transpose=True), 16)
        emit_ffn_up(b, sub, XN2BF, HBF, s_list=S, tag="B6", mode="gelu",
                    wload_up=w1load_B)
        sub.close()
        emit_local_fence(b)
        xpb = [b.sb(seg, "B6XPa", [128, 512], F32), b.sb(seg, "B6XPb", [128, 512], F32)]
        xcb = [b.sb(seg, "B6XCa", [128, 512], F32), b.sb(seg, "B6XCb", [128, 512], F32)]
        ptb = [b.sb(seg, "B6PTa", [128, 512], F32), b.sb(seg, "B6PTb", [128, 512], F32)]
        tb1 = b.sb(seg, "B6T1", [128, 512], F32)
        tb2 = b.sb(seg, "B6T2", [128, 512], F32)
        stb = b.sb(seg, "B6ST", [128, 512], F32)
        s_x3 = [b.sem("B6_x3a"), b.sem("B6_x3b")]
        s_xw3 = b.sem("B6_xw")
        ecnt = [0]
        def evacB(tt, oc, psum):
            par = ecnt[0] % 2
            ecnt[0] += 1
            def f(s, tt=tt, oc=oc, par=par):
                s_v.wait(s)
                if tt == 0 and oc == 0:
                    b.sems["B5_xp"].wait(s)
                    b.sems["B1_p1"].wait(s)
                s_x3[par].inc(s.dma_start(
                    out=xpb[par][:, :],
                    in_=XPRIME[tt * 128:(tt + 1) * 128, oc * 512:(oc + 1) * 512]), 16)
                s_x3[par].inc(s.dma_start(
                    out=xcb[par][:, :],
                    in_=X[tt * 128:(tt + 1) * 128, oc * 512:(oc + 1) * 512]), 16)
                s_x3[par].inc(s.dma_start(
                    out=ptb[par][:, :],
                    in_=PART1[tt * 128:(tt + 1) * 128, oc * 512:(oc + 1) * 512]), 16)
            b.on("sync", f)
            xl = s_x3[par].n
            def f(v, tt=tt, oc=oc, par=par, psum=psum, xl=xl):
                s_t.wait(v)
                s_x3[par].wait(v, xl)
                s_xw3.wait(v)
                s_v.inc(v.tensor_tensor(out=tb1[:, :], in0=psum[:, :],
                                        in1=xpb[par][:, :], op=ALU.add), 1)
                s_v.inc(v.tensor_tensor(out=tb2[:, :], in0=xcb[par][:, :],
                                        in1=ptb[par][:, :], op=ALU.add), 1)
                s_v.wait(v)
                s_v.inc(v.scalar_tensor_tensor(
                    out=stb[:, :], in0=tb1[:, :], scalar=b.t["RM2"][:, tt:tt + 1],
                    in1=tb2[:, :], op0=ALU.mult, op1=ALU.add), 1)
            b.on("vector", f)
            def f(s, tt=tt, oc=oc):
                s_v.wait(s)
                s_xw3.inc(s.dma_start(
                    out=X[tt * 128:(tt + 1) * 128, oc * 512:(oc + 1) * 512],
                    in_=stb[:, :]), 16)
            b.on("sync", f)
        def w2load_B2(s, oc, w2p, sem):
            for ic in range(FC):
                sem.inc(s.dma_start(out=w2p[:, ic * 512:(ic + 1) * 512],
                                    in_=F2B[oc * 512:(oc + 1) * 512, ic * 128:(ic + 1) * 128],
                                    transpose=True), 16)
        emit_ffn_down(b, seg, HBF, w2load_B2, evacB, s_list=S, tag="B6d",
                      loader_eng="sync", h_store_sem="B6_hst")
        seg.close()

        if upto == "B":
            fin_copy(["B6_xw"])
            return b

        seg = contextlib.ExitStack()
        blk = seg.enter_context(nc.Block("seg7"))
        b.blk = blk
        emit_seg_fence(b)
        emit_ln_simple(b, seg, X, XNBF, s_list=S, tag="C1", prev_x_sem="B6_xw")
        vb = b.sb(seg, "CVBC", [128, D], F32)
        vrow = b.sb(seg, "CVROW", [1, D], F32)
        s_vb = b.sem("C_vb")
        with nc.psum_tensor("CVBPS", [128, 512], F32) as bcps:
            def f(s):
                for h in range(NH):
                    s_vb.inc(s.dma_start(out=vrow[0:1, h * 128:(h + 1) * 128],
                                         in_=w["L1_attn_b_qkv"][None, 384 * h + 256:384 * h + 384]), 16)
            b.on("sync", f)
            def f(t):
                s_vb.wait(t)
            b.on("tensor", f)
            bcast_rows(b, vrow[0:1, :], vb, D, s_t, s_v, bcps)
        emit_qkv_proj(b, seg, XNBF, w["L1_attn_w_qkv"], w["L1_attn_b_qkv"],
                      QF, KLOC, VLOC, s_list=S, interleaved=True, tag="C",
                      nb_sem="C1_nb", vbias_bc=vb)
        emit_norms(b, seg, QF, KLOC, KN2L, s_list=S, per_head=True, tag="C")
        seg.close()

        emit_gather3(b, KLOC, VLOC, KN2L, KALL, VALL, KN2G,
                     wait_names=["Cqk_stg", "Cqk_stgv", "s_d"])

        seg = contextlib.ExitStack()
        blk = seg.enter_context(nc.Block("seg8"))
        b.blk = blk
        emit_seg_fence(b)
        ctxfm2 = b.sb(seg, "CCTXFM", [128, NH * T], BF16)
        sub = contextlib.ExitStack()
        emit_flash_MHA(b, sub, QF, KALL, VALL, KN2G, ctxfm2, "CNMAX", s_list=S, tag="CF")
        sub.close()
        emit_cproj(b, seg, ctxfm2, w["L1_attn_w_out"], w["L1_attn_b_out"], X, s_list=S)
        seg.close()

        if upto == "C":
            fin_copy(["C5_xw"])
            return b

        seg = contextlib.ExitStack()
        blk = seg.enter_context(nc.Block("seg9"))
        b.blk = blk
        emit_seg_fence(b)
        emit_ln_simple(b, seg, X, XNBF, s_list=S, tag="D1", prev_x_sem="C5_xw")
        sub = contextlib.ExitStack()
        b.sems["D2_nbsrc"] = b.sems["D1_nb"]
        def wgu_load(g, ip, dst, sem, which):
            wsrc = w["L1_ffn_w_gate"] if which == "gate" else w["L1_ffn_w_up"]
            sem.inc(g.dma_start(
                out=dst[:, :].rearrange("p (c n) -> p c n", n=512),
                in_=wsrc[:, ip * 512:(ip + 1) * 512].rearrange("(c p) n -> p c n", p=128)), 16)
        emit_ffn_up(b, sub, XNBF, HBF, s_list=S, tag="D2", mode="gateup24",
                    wload_up=wgu_load)
        sub.close()
        emit_local_fence(b)
        xcd = [b.sb(seg, "DXCa", [128, 512], F32), b.sb(seg, "DXCb", [128, 512], F32)]
        yod = b.sb(seg, "DYO", [128, 512], F32)
        s_x4 = [b.sem("D_x4a"), b.sem("D_x4b")]
        ecnt2 = [0]
        def w2load_D(g, oc, w2p, sem):
            sem.inc(g.dma_start(
                out=w2p[:, :].rearrange("p (c n) -> p c n", n=512),
                in_=w["L1_ffn_w_down"][:, oc * 512:(oc + 1) * 512]
                    .rearrange("(c p) n -> p c n", p=128)), 16)
        def evacD(tt, oc, psum):
            par = ecnt2[0] % 2
            ecnt2[0] += 1
            def f(s, tt=tt, oc=oc, par=par):
                s_v.wait(s)
                s_x4[par].inc(s.dma_start(
                    out=xcd[par][:, :],
                    in_=X[tt * 128:(tt + 1) * 128, oc * 512:(oc + 1) * 512]), 16)
            b.on("sync", f)
            xl = s_x4[par].n
            def f(v, tt=tt, oc=oc, par=par, psum=psum, xl=xl):
                s_t.wait(v)
                s_x4[par].wait(v, xl)
                s_y.wait(v)
                s_v.inc(v.tensor_tensor(out=yod[:, :], in0=psum[:, :],
                                        in1=xcd[par][:, :], op=ALU.add), 1)
            b.on("vector", f)
            def f(s, tt=tt, oc=oc):
                s_v.wait(s)
                s_y.inc(s.dma_start(
                    out=y_out[tt * 128:(tt + 1) * 128, oc * 512:(oc + 1) * 512],
                    in_=yod[:, :]), 16)
            b.on("sync", f)
        emit_ffn_down(b, seg, HBF, w2load_D, evacD, s_list=S, tag="D3",
                      loader_eng="gpsimd", h_store_sem="D2_hst")
        def f(s):
            s_y.wait(s)
        b.on("sync", f)
        seg.close()
    return b


def emit_local_fence(b, engs=("sync", "gpsimd", "scalar", "vector", "tensor")):
    snap = [(s.h, s.n) for s in b.sems.values() if s.n > 0]
    for eng in engs:
        def f(e, snap=snap):
            for h, n in snap:
                e.wait_ge(h, n)
        b.on(eng, f)


def emit_seg_fence(b):
    snap = [(s.h, s.n) for s in b.sems.values() if s.n > 0]
    for eng in ("sync", "gpsimd", "scalar", "vector", "tensor"):
        def f(e, snap=snap):
            for h, n in snap:
                e.wait_ge(h, n)
        b.on(eng, f)


def emit_ln_simple(b, seg, X, XNBF, s_list, tag, prev_x_sem):
    (s_g, s_d, s_e, s_t, s_v, s_sc, s_p, s_o) = s_list
    nc, cfg = b.nc, b.cfg
    D, NT = b.cfg.D, b.cfg.NT
    xt = b.sb(seg, tag + "XT", [128, D], F32)
    scr = b.sb(seg, tag + "SCR", [128, D], F32)
    stat = b.sb(seg, tag + "STAT", [128, 8], F32)
    xnb = b.sb(seg, tag + "XNB", [128, D], BF16)
    s_xt = b.sem(tag + "_xt")
    s_nb = b.sem(tag + "_nb")
    for i in range(NT):
        def f(s, i=i):
            s_v.wait(s); s_sc.wait(s)
            if i == 0 and prev_x_sem in b.sems:
                b.sems[prev_x_sem].wait(s)
            s_xt.inc(s.dma_start(out=xt[:, :], in_=X[i * 128:(i + 1) * 128, :]), 16)
        b.on("sync", f)
        def f(v):
            s_xt.wait(v)
        b.on("vector", f)
        def f(sc):
            s_xt.wait(sc)
        b.on("scalar", f)
        ln_stats(b, xt, D, stat, scr, s_sc, s_v)
        ln_apply(b, xt, scr, D, stat, s_sc, s_v)
        def f(v):
            s_v.wait(v)
            s_nb.wait(v)
            s_v.inc(v.tensor_copy(out=xnb[:, :], in_=scr[:, :]), 1)
        b.on("vector", f)
        def f(s, i=i):
            s_v.wait(s)
            s_nb.inc(s.dma_start(out=XNBF[i * 128:(i + 1) * 128, :], in_=xnb[:, :]), 16)
        b.on("sync", f)


def emit_cproj(b, seg, ctxfm, w_out, b_out, X, s_list):
    (s_g, s_d, s_e, s_t, s_v, s_sc, s_p, s_o) = s_list
    nc, cfg = b.nc, b.cfg
    D, T, NT, DC, NH = cfg.D, cfg.T, cfg.NT, cfg.DC, cfg.NH

    emit_local_fence(b)
    bo = b.sb(seg, "CBOBC", [128, D], F32)
    rowt = b.sb(seg, "C5ROW", [1, D], F32)
    s_br = b.sem("C5_br")
    with nc.psum_tensor("C5BCPS", [128, 512], F32) as bcps:
        def f(s):
            s_br.inc(s.dma_start(out=rowt[0:1, :], in_=b_out[None, :]), 16)
        b.on("sync", f)
        def f(t):
            s_br.wait(t)
        b.on("tensor", f)
        bcast_rows(b, rowt[0:1, :], bo, D, s_t, s_v, bcps)

    NPAN = D // 512
    owp = b.sb(seg, "C5WP", [128, NPAN * DC * 512], BF16)
    prj = [b.ps(seg, "C5P0", [128, 512]), b.ps(seg, "C5P1", [128, 512])]
    xrt = b.sb(seg, "C5XR", [128, D], F32)
    xwt = b.sb(seg, "C5XW", [128, D], F32)
    s_ow = b.sem("C5_ow")
    s_xr = b.sem("C5_xr")
    s_xw = b.sem("C5_xw")
    for p in range(NPAN):
        def f(g, p=p):
            s_ow.inc(g.dma_start(
                out=owp[:, p * DC * 512:(p + 1) * DC * 512]
                    .rearrange("p (c n) -> p c n", n=512),
                in_=w_out[:, p * 512:(p + 1) * 512].rearrange("(c p) n -> p c n", p=128)), 16)
        b.on("gpsimd", f)
    owdone = s_ow.n
    prjfree = [0, 0]
    for tt in range(NT):
        def f(s, tt=tt):
            s_v.wait(s)
            if tt == 0:
                b.sems["B6_xw"].wait(s)
            s_xr.inc(s.dma_start(out=xrt[:, :], in_=X[tt * 128:(tt + 1) * 128, :]), 16)
        b.on("sync", f)
        xload = s_xr.n
        for p in range(NPAN):
            bank = p % 2
            def f(t, tt=tt, p=p, bank=bank):
                if tt == 0 and p == 0:
                    s_ow.wait(t, owdone)
                    s_v.wait(t)
                s_v.wait(t, prjfree[bank])
                for h in range(NH):
                    s_t.inc(t.matmul(
                        prj[bank][:, :],
                        ctxfm[:, h * T + tt * 128: h * T + (tt + 1) * 128],
                        owp[:, p * DC * 512 + h * 512: p * DC * 512 + (h + 1) * 512],
                        start=(h == 0), stop=(h == NH - 1)), 1)
            b.on("tensor", f)
            mmdone = s_t.n
            def f(v, p=p, bank=bank, xload=xload, tt=tt):
                s_t.wait(v, mmdone)
                s_xr.wait(v, xload)
                if p == 0:
                    s_xw.wait(v)
                s_v.inc(v.tensor_tensor(out=xwt[:, p * 512:(p + 1) * 512],
                                        in0=prj[bank][:, :],
                                        in1=bo[:, p * 512:(p + 1) * 512], op=ALU.add), 1)
                s_v.wait(v)
                s_v.inc(v.tensor_tensor(out=xwt[:, p * 512:(p + 1) * 512],
                                        in0=xwt[:, p * 512:(p + 1) * 512],
                                        in1=xrt[:, p * 512:(p + 1) * 512], op=ALU.add), 1)
            b.on("vector", f)
            prjfree[bank] = s_v.n
        def f(s, tt=tt):
            s_v.wait(s)
            s_xw.inc(s.dma_start(out=X[tt * 128:(tt + 1) * 128, :], in_=xwt[:, :]), 16)
        b.on("sync", f)


GROUPS = [[0, 1, 2, 3], [4, 5, 6, 7]]


def emit_attnA(b, seg, x_in, w, X, XNBF, QF, KLOC, VLOC, KN2L, s_list):
    (s_g, s_d, s_e, s_t, s_v, s_sc, s_p, s_o) = s_list
    nc, cfg = b.nc, b.cfg
    D, T, NT, DC = b.cfg.D, b.cfg.T, b.cfg.NT, b.cfg.DC

    rowa = b.sb(seg, "ROWA", [1, D], F32)
    rowb = b.sb(seg, "ROWB", [1, D], F32)
    g0 = b.sb(seg, "G0BC", [128, D], F32)
    b0 = b.sb(seg, "B0BC", [128, D], F32)
    bcps = b.ps(seg, "BCPS", [128, 512])
    def f(s):
        s_d.inc(s.dma_start(out=rowa[0:1, :], in_=w["L0_attn_gamma"][None, :]), 16)
        s_d.inc(s.dma_start(out=rowb[0:1, :], in_=w["L0_attn_beta"][None, :]), 16)
    b.on("sync", f)
    def f(t):
        s_d.wait(t); s_p.wait(t)
    b.on("tensor", f)
    bcast_rows(b, rowa[0:1, :], g0, D, s_t, s_v, bcps)
    bcast_rows(b, rowb[0:1, :], b0, D, s_t, s_v, bcps)

    s_xt = b.sem("A_xt")
    s_xst = b.sem("A_xst")
    s_nbst = b.sem("A_nbst")
    xt = b.sb(seg, "XT", [128, D], F32)
    scr = b.sb(seg, "SCR", [128, D], F32)
    stat = b.sb(seg, "STAT", [128, 8], F32)
    xn1 = b.sb(seg, "XN1", [128, D], F32)
    xnb = b.sb(seg, "XNB", [128, D], BF16)
    for i in range(NT):
        def f(s, i=i):
            s_v.wait(s)
            s_sc.wait(s)
            s_xt.inc(s.dma_start(out=xt[:, :], in_=x_in[i * 128:(i + 1) * 128, :]), 16)
        b.on("sync", f)
        def f(v):
            s_xt.wait(v)
        b.on("vector", f)
        def f(sc):
            s_xt.wait(sc)
        b.on("scalar", f)
        ln_stats(b, xt, D, stat, scr, s_sc, s_v)
        ln_apply(b, xt, xn1, D, stat, s_sc, s_v)
        def f(v):
            s_v.wait(v)
            s_v.inc(v.tensor_tensor(out=scr[:, :], in0=xt[:, :], in1=xn1[:, :],
                                    op=ALU.add), 1)
        b.on("vector", f)
        def f(s, i=i):
            s_v.wait(s)
            s_xst.inc(s.dma_start(out=X[i * 128:(i + 1) * 128, :], in_=scr[:, :]), 16)
        b.on("sync", f)
        def f(sc):
            s_xst.wait(sc)
        b.on("scalar", f)
        ln_stats(b, xn1, D, stat, scr, s_sc, s_v)
        ln_apply(b, xn1, xn1, D, stat, s_sc, s_v, gbc=b.t["G0BC"], bbc=b.t["B0BC"])
        def f(v):
            s_v.wait(v)
            s_nbst.wait(v)
            s_v.inc(v.tensor_copy(out=xnb[:, :], in_=xn1[:, :]), 1)
        b.on("vector", f)
        def f(s, i=i):
            s_v.wait(s)
            s_nbst.inc(s.dma_start(out=XNBF[i * 128:(i + 1) * 128, :], in_=xnb[:, :]), 16)
        b.on("sync", f)

    emit_qkv_proj(b, seg, XNBF, w["L0_attn_qkv"], None, QF, KLOC, VLOC, s_list,
                  interleaved=False, tag="A", nb_sem="A_nbst")

    emit_norms(b, seg, QF, KLOC, KN2L, s_list, per_head=False, tag="A")


def emit_qkv_proj(b, seg, XNBF, wqkv, bqkv, QF, KLOC, VLOC, s_list, interleaved,
                  tag="A", nb_sem="A_nbst", wsrcT=None, vbias_bc=None):
    (s_g, s_d, s_e, s_t, s_v, s_sc, s_p, s_o) = s_list
    nc, cfg = b.nc, b.cfg
    D, T, NT, DC, NH = cfg.D, cfg.T, cfg.NT, cfg.DC, cfg.NH

    s_fm = b.sem(tag + "qk_fm")
    s_wp = [b.sem(tag + "qk_wp0"), b.sem(tag + "qk_wp1")]
    s_stg = b.sem(tag + "qk_stg")
    s_stgv = b.sem(tag + "qk_stgv")
    xnfm = b.sb(seg, tag + "XN_FM", [128, DC * T], BF16)
    nbst = b.sems.get(nb_sem)
    waits = [(nbst, nbst.n)] if nbst is not None else []
    load_fm(b, xnfm, XNBF, DC, T, s_fm, wait_sems=waits)
    xnfm_ready_fm = s_fm.n

    wp = [b.sb(seg, tag + "WP0", [128, DC * 512], BF16),
          b.sb(seg, tag + "WP1", [128, DC * 512], BF16)]
    prj = [b.ps(seg, tag + "PRJ0", [128, 512]), b.ps(seg, tag + "PRJ1", [128, 512])]
    stg = b.sb(seg, tag + "STG", [128, T], BF16)
    stgv = b.sb(seg, tag + "STGV", [128, 512], BF16)
    bq = None
    if bqkv is not None:
        bq = b.sb(seg, tag + "BQPP", [128, 3 * D // 128], F32)
        emit_bias_cols(b, seg, bqkv, bq, s_list, tag)

    NPAN = 3 * D // 512
    panload = [None] * NPAN
    panfree = [0, 0]
    prjfree = [0, 0]
    stg_rd = [0]
    xnfm_ready = xnfm_ready_fm

    for p in range(NPAN):
        slot = p % 2
        if wsrcT is None:
            def f(g, p=p, slot=slot):
                s_t.wait(g, panfree[slot])
                s_wp[slot].inc(g.dma_start(
                    out=wp[slot][:, :].rearrange("p (c n) -> p c n", n=512),
                    in_=wqkv[:, p * 512:(p + 1) * 512].rearrange("(c p) n -> p c n", p=128)), 16)
            b.on("gpsimd", f)
        else:
            c0 = p * 512
            role = "q" if c0 < D else ("k" if c0 < 2 * D else "v")
            r0 = c0 - (0 if role == "q" else (D if role == "k" else 2 * D))
            def f(s, p=p, slot=slot, role=role, r0=r0):
                s_t.wait(s, panfree[slot])
                if p == 0:
                    b.sems["B_pc"].wait(s)
                for fc in range(DC):
                    s_wp[slot].inc(s.dma_start(
                        out=wp[slot][:, fc * 512:(fc + 1) * 512],
                        in_=wsrcT[role][r0:r0 + 512, fc * 128:(fc + 1) * 128],
                        transpose=True), 16)
            b.on("sync", f)
        panload[p] = s_wp[slot].n

        for cc in range(4):
            col0 = p * 512 + cc * 128
            if not interleaved:
                if col0 < D:
                    role, drow = "q", col0
                elif col0 < 2 * D:
                    role, drow = "k", col0 - D
                else:
                    role, drow = "v", col0 - 2 * D
            else:
                h, off = divmod(col0, 384)
                if off < 128:
                    role, drow = "q", h * 128 + off
                elif off < 256:
                    role, drow = "k", h * 128 + (off - 128)
                else:
                    role, drow = "v", h * 128 + (off - 256)
            if role in ("q", "k"):
                dst = QF if role == "q" else KLOC
                for tg in range(T // 512):
                    bank = tg % 2
                    def f(t, p=p, cc=cc, tg=tg, bank=bank, slot=slot):
                        s_wp[slot].wait(t, panload[p])
                        if p == 0 and cc == 0 and tg == 0:
                            s_fm.wait(t, xnfm_ready)
                            s_v.wait(t)
                        s_v.wait(t, prjfree[bank])
                        for fc in range(DC):
                            s_t.inc(t.matmul(
                                prj[bank][:, :],
                                wp[slot][:, fc * 512 + cc * 128: fc * 512 + (cc + 1) * 128],
                                xnfm[:, fc * T + tg * 512: fc * T + (tg + 1) * 512],
                                start=(fc == 0), stop=(fc == DC - 1)), 1)
                    b.on("tensor", f)
                    mmdone = s_t.n
                    def f(v, tg=tg, bank=bank, drow=drow):
                        s_t.wait(v, mmdone)
                        if tg == 0:
                            s_stg.wait(v, stg_rd[0])
                        if bq is not None:
                            s_v.inc(v.tensor_scalar(
                                out=stg[:, tg * 512:(tg + 1) * 512], in0=prj[bank][:, :],
                                scalar1=bq[:, drow // 128: drow // 128 + 1], scalar2=None,
                                op0=ALU.add), 1)
                        else:
                            s_v.inc(v.tensor_copy(out=stg[:, tg * 512:(tg + 1) * 512],
                                                  in_=prj[bank][:, :]), 1)
                    b.on("vector", f)
                    prjfree[bank] = s_v.n
                def f(s, dst=dst, drow=drow):
                    s_v.wait(s)
                    s_stg.inc(s.dma_start(out=dst[drow:drow + 128, :], in_=stg[:, :]), 16)
                b.on("sync", f)
                stg_rd[0] = s_stg.n
            else:
                pass
        if (not interleaved) and p * 512 >= 2 * D:
            col0 = p * 512 - 2 * D
            for tt in range(NT):
                bank = tt % 2
                def f(t, p=p, tt=tt, bank=bank, slot=slot):
                    s_wp[slot].wait(t, panload[p])
                    s_v.wait(t, prjfree[bank])
                    for fc in range(DC):
                        s_t.inc(t.matmul(
                            prj[bank][:, :],
                            xnfm[:, fc * T + tt * 128: fc * T + (tt + 1) * 128],
                            wp[slot][:, fc * 512:(fc + 1) * 512],
                            start=(fc == 0), stop=(fc == DC - 1)), 1)
                b.on("tensor", f)
                mmdone = s_t.n
                def f(v, bank=bank, col0=col0):
                    s_t.wait(v, mmdone)
                    s_stgv.wait(v)
                    if vbias_bc is not None:
                        s_v.inc(v.tensor_tensor(out=stgv[:, :], in0=prj[bank][:, :],
                                                in1=vbias_bc[:, col0:col0 + 512],
                                                op=ALU.add), 1)
                    else:
                        s_v.inc(v.tensor_copy(out=stgv[:, :], in_=prj[bank][:, :]), 1)
                b.on("vector", f)
                prjfree[bank] = s_v.n
                def f(s, tt=tt, col0=col0):
                    s_v.wait(s)
                    s_stgv.inc(s.dma_start(out=VLOC[tt * 128:(tt + 1) * 128, col0:col0 + 512],
                                           in_=stgv[:, :]), 16)
                b.on("scalar", f)
        if interleaved:
            for cc in range(4):
                col0 = p * 512 + cc * 128
                h, off = divmod(col0, 384)
                if off < 256:
                    continue
                vcol = h * 128 + (off - 256)
                for tt in range(NT):
                    bank = tt % 2
                    def f(t, p=p, cc=cc, tt=tt, bank=bank, slot=slot):
                        s_wp[slot].wait(t, panload[p])
                        s_v.wait(t, prjfree[bank])
                        for fc in range(DC):
                            s_t.inc(t.matmul(
                                prj[bank][:, 0:128],
                                xnfm[:, fc * T + tt * 128: fc * T + (tt + 1) * 128],
                                wp[slot][:, fc * 512 + cc * 128: fc * 512 + (cc + 1) * 128],
                                start=(fc == 0), stop=(fc == DC - 1)), 1)
                    b.on("tensor", f)
                    mmdone = s_t.n
                    def f(v, bank=bank, vcol=vcol):
                        s_t.wait(v, mmdone)
                        s_stgv.wait(v)
                        if vbias_bc is not None:
                            s_v.inc(v.tensor_tensor(out=stgv[:, 0:128],
                                                    in0=prj[bank][:, 0:128],
                                                    in1=vbias_bc[:, vcol:vcol + 128],
                                                    op=ALU.add), 1)
                        else:
                            s_v.inc(v.tensor_copy(out=stgv[:, 0:128],
                                                  in_=prj[bank][:, 0:128]), 1)
                    b.on("vector", f)
                    prjfree[bank] = s_v.n
                    def f(s, tt=tt, vcol=vcol):
                        s_v.wait(s)
                        s_stgv.inc(s.dma_start(out=VLOC[tt * 128:(tt + 1) * 128, vcol:vcol + 128],
                                               in_=stgv[:, 0:128]), 16)
                    b.on("scalar", f)
        panfree[slot] = s_t.n


def emit_bias_cols(b, seg, bqkv, bq, s_list, tag=""):
    (s_g, s_d, s_e, s_t, s_v, s_sc, s_p, s_o) = s_list
    nc, cfg = b.nc, b.cfg
    nch = 3 * cfg.D // 128
    s_bq = b.sem(tag + "_bq")
    def f(g):
        s_bq.inc(g.dma_start(out=bq[:, :],
                             in_=bqkv.rearrange("(i p) -> p i", p=128),
                             allow_slow_non_contiguous=True), 16)
    b.on("gpsimd", f)
    def f(t):
        s_bq.wait(t)
    b.on("tensor", f)
    def f(v):
        s_bq.wait(v)
    b.on("vector", f)


def emit_norms(b, seg, QF, KLOC, KN2L, s_list, per_head, tag=""):
    (s_g, s_d, s_e, s_t, s_v, s_sc, s_p, s_o) = s_list
    nc, cfg = b.nc, b.cfg
    D, T, DC = cfg.D, cfg.T, cfg.DC
    ntg = T // 512

    s_nr = b.sem(tag + "_nr")
    nrm = b.sb(seg, tag + "NRMT", [128, T], BF16)
    nsq = b.sb(seg, tag + "NSQ", [128, T], BF16)
    nps = b.ps(seg, tag + "NPS", [1, (T // 512) * 512])
    nacc = b.sb(seg, tag + "NACC", [1, 2 * DC * ntg], F32)
    nmax = b.t[tag + "NMAX"]
    krow = b.sb(seg, tag + "KROWS", [1, 128], F32)

    stg_done = b.sems.get(tag[0] + "qk_stg") if tag else None
    stgv_done = b.sems.get(tag[0] + "qk_stgv") if tag else None
    first_load = [True]
    for wi, src in ((0, QF), (1, KLOC)):
        for fc in range(DC):
            def f(s, fc=fc, src=src):
                if first_load[0]:
                    if stg_done is not None:
                        stg_done.wait(s)
                    if stgv_done is not None:
                        stgv_done.wait(s)
                    first_load[0] = False
                s_sc.wait(s)
                s_nr.inc(s.dma_start(out=nrm[:, :], in_=src[fc * 128:(fc + 1) * 128, :]), 16)
            b.on("sync", f)
            def f(sc):
                s_nr.wait(sc)
                s_t.wait(sc)
                s_sc.inc(sc.activation(out=nsq[:, :], in_=nrm[:, :], func=AF.Square), 1)
            b.on("scalar", f)
            for tg in range(ntg):
                first = (fc == 0) or per_head
                last = (fc == DC - 1) or per_head
                def f(t, fc=fc, tg=tg, first=first, last=last):
                    s_sc.wait(t)
                    if first:
                        s_v.wait(t)
                    s_t.inc(t.matmul(nps[:, tg * 512:(tg + 1) * 512],
                                     b.t["ONES_col_bf"][:, :],
                                     nsq[:, tg * 512:(tg + 1) * 512],
                                     start=first, stop=last), 1)
                b.on("tensor", f)
                if last:
                    idx = (wi * DC * ntg) + fc * ntg + tg if per_head else \
                          wi * DC * ntg + tg
                    def f(v, idx=idx, tg=tg):
                        s_t.wait(v)
                        s_v.inc(v.reduce_max(out=nacc[0:1, idx:idx + 1],
                                             in_=nps[0:1, tg * 512:(tg + 1) * 512],
                                             axis=AX.X), 1)
                    b.on("vector", f)
    def f(v):
        s_v.wait(v)
        if per_head:
            for h in range(cfg.NH):
                s_v.inc(v.reduce_max(out=nmax[0:1, h:h + 1],
                                     in_=nacc[0:1, h * ntg:(h + 1) * ntg], axis=AX.X), 1)
                s_v.inc(v.reduce_max(
                    out=nmax[0:1, cfg.NH + h:cfg.NH + h + 1],
                    in_=nacc[0:1, (cfg.DC * ntg) + h * ntg:(cfg.DC * ntg) + (h + 1) * ntg],
                    axis=AX.X), 1)
        else:
            s_v.inc(v.reduce_max(out=nmax[0:1, 0:1], in_=nacc[0:1, 0:ntg], axis=AX.X), 1)
            s_v.inc(v.reduce_max(out=nmax[0:1, 1:2],
                                 in_=nacc[0:1, cfg.DC * ntg:cfg.DC * ntg + ntg],
                                 axis=AX.X), 1)
    b.on("vector", f)
    emit_norms_small(b, nmax, krow, KN2L, per_head, s_list)


def emit_flash_L0(b, seg, QF, KALL, VALL, KN2G, CTXBF, s_list, nmax_name="ANMAX"):
    (s_g, s_d, s_e, s_t, s_v, s_sc, s_p, s_o) = s_list
    nc, cfg = b.nc, b.cfg
    D, T, DC, NG = cfg.D, cfg.T, cfg.DC, cfg.NG
    c_soft = 1.0 / float(np.sqrt(np.float32(D)))

    negc = b.sb(seg, "NEGC", [128, 1], F32)
    kng = b.sb(seg, "KNGS", [1, 1], F32)
    nsc = b.sb(seg, "NSC", [1, 4], F32)
    with nc.psum_tensor("NEGCPS", [128, 1], F32) as negcps:
        s_kng = b.sem("fl_kng")
        def f(s):
            b.sems["s_cc"].wait(s)
            s_kng.inc(s.dma_start(out=kng[0:1, :], in_=KN2G[0:1, 0:1]), 16)
        b.on("sync", f)
        def f(v):
            s_kng.wait(v)
            s_v.inc(v.tensor_tensor(out=nsc[0:1, 0:1], in0=b.t[nmax_name][0:1, 0:1],
                                    in1=kng[0:1, 0:1], op=ALU.mult), 1)
        b.on("vector", f)
        def f(sc):
            s_v.wait(sc)
            s_sc.inc(sc.activation(out=nsc[0:1, 1:2], in_=nsc[0:1, 0:1], func=AF.Ln), 1)
            s_sc.wait(sc)
            s_sc.inc(sc.activation(out=nsc[0:1, 2:3], in_=nsc[0:1, 1:2], func=AF.Exp,
                                   scale=0.5), 1)
        b.on("scalar", f)
        def f(v):
            s_sc.wait(v)
            s_v.inc(v.tensor_scalar(out=nsc[0:1, 3:4], in0=nsc[0:1, 2:3],
                                    scalar1=-c_soft, scalar2=None, op0=ALU.mult), 1)
        b.on("vector", f)
        def f(t):
            s_v.wait(t)
            s_t.inc(t.matmul(negcps[:, :], b.t["ONES_row_f32"][:, :], nsc[0:1, 3:4],
                             start=True, stop=True), 1)
        b.on("tensor", f)
        def f(v):
            s_t.wait(v)
            s_v.inc(v.tensor_copy(out=negc[:, :], in_=negcps[:, :]), 1)
        b.on("vector", f)

    NR = 4
    kch = [b.sb(seg, f"KCH{i}", [128, DC * 128], BF16) for i in range(NR)]
    vch = [b.sb(seg, f"VCH{i}", [128, D], BF16) for i in range(NR)]
    qg = b.sb(seg, "QG", [128, DC * 512], BF16)
    ptar = b.sb(seg, "PTAR", [128, cfg.maxslots * 512], BF16)
    ctstg = b.sb(seg, "CTSTG", [128, D], BF16)
    lic = b.sb(seg, "LIC", [128, 4], F32)
    scps = [b.ps(seg, "SCPS0", [128, 512]), b.ps(seg, "SCPS1", [128, 512])]
    lcps = b.ps(seg, "LCPS", [128, 4])
    ctxps = b.ps(seg, "CTXPS", [128, D])
    s_kc = [b.sem(f"fl_kc{i}") for i in range(NR)]
    s_vc = [b.sem(f"fl_vc{i}") for i in range(NR)]
    s_qg = b.sem("fl_qg")
    s_cts = b.sem("fl_cts")
    kfree = [0] * NR
    scfree = [0, 0]
    vfree = [0] * NR

    for g in range(NG):
        chunks = [(r, c) for r in range(4) for c in range(4 * g + 4)]
        nch = len(chunks)
        for ci, (r, c) in enumerate(chunks):
            par = ci % NR
            def f(s, g=g, r=r, c=c, par=par, ci=ci):
                s_t.wait(s, kfree[par])
                s_kc[par].inc(s.dma_start(
                    out=kch[par][:, :].rearrange("p (f k) -> p f k", k=128),
                    in_=KALL[r * D:(r + 1) * D, c * 128:(c + 1) * 128]
                        .rearrange("(f p) k -> p f k", p=128)), 16)
            b.on("sync", f)
            kload = s_kc[par].n
            if ci == 0:
                def f(s, g=g):
                    s_t.wait(s)
                    for fc in range(DC):
                        s_qg.inc(s.dma_start(
                            out=qg[:, fc * 512:(fc + 1) * 512],
                            in_=QF[fc * 128:(fc + 1) * 128, g * 512:(g + 1) * 512]), 16)
                b.on("scalar", f)
            qload = s_qg.n
            spar = ci % 2
            def f(t, ci=ci, par=par, spar=spar, kload=kload, qload=qload, g=g):
                if ci == 0 and g == 0:
                    s_v.wait(t)
                s_kc[par].wait(t, kload)
                if ci == 0:
                    s_qg.wait(t, qload)
                s_sc.wait(t, scfree[spar])
                for fc in range(DC):
                    s_t.inc(t.matmul(scps[spar][:, :],
                                     kch[par][:, fc * 128:(fc + 1) * 128],
                                     qg[:, fc * 512:(fc + 1) * 512],
                                     start=(fc == 0), stop=(fc == DC - 1)), 1)
            b.on("tensor", f)
            scdone = s_t.n
            kfree[par] = scdone
            def f(sc, spar=spar, slot=ci, scdone=scdone, ci=ci):
                s_t.wait(sc, scdone)
                if ci == 0:
                    s_v.wait(sc)
                s_sc.inc(sc.activation(out=ptar[:, slot * 512:(slot + 1) * 512],
                                       in_=scps[spar][:, :], func=AF.Exp,
                                       bias=negc[:, 0:1], scale=c_soft), 1)
            b.on("scalar", f)
            scfree[spar] = s_sc.n
            d = c - 4 * g
            if d >= 0:
                def f(v, slot=ci, d=d, r=r):
                    s_sc.wait(v)
                    s_v.inc(v.tensor_tensor(
                        out=ptar[:, slot * 512:(slot + 1) * 512],
                        in0=ptar[:, slot * 512:(slot + 1) * 512],
                        in1=b.t["MSK"][:, (d * 4 + r) * 512:(d * 4 + r + 1) * 512],
                        op=ALU.mult), 1)
                b.on("vector", f)
        def f(t, chunks=chunks, nch=nch):
            s_sc.wait(t)
            s_v.wait(t)
            for ci in range(nch):
                for m in range(4):
                    s_t.inc(t.matmul(lcps[:, m:m + 1],
                                     ptar[:, ci * 512 + m * 128: ci * 512 + (m + 1) * 128],
                                     b.t["ONES_col_bf"][:, :],
                                     start=(ci == 0 and m == 0),
                                     stop=(ci == nch - 1 and m == 3)), 1)
        b.on("tensor", f)
        ldone = s_t.n
        def f(v, ldone=ldone):
            s_t.wait(v, ldone)
            s_v.inc(v.reciprocal(out=lic[:, :], in_=lcps[:, :]), 1)
        b.on("vector", f)
        for m in range(4):
            sub = [(r, c) for (r, c) in chunks if c <= 4 * g + m]
            for si, (r, c) in enumerate(sub):
                par = si % NR
                def f(s, r=r, c=c, par=par):
                    s_t.wait(s, vfree[par])
                    s_vc[par].inc(s.dma_start(out=vch[par][:, :],
                                              in_=VALL[r * T + c * 128: r * T + (c + 1) * 128, :]), 16)
                b.on("sync", f)
                vload = s_vc[par].n
                slot = chunks.index((r, c))
                def f(t, si=si, par=par, slot=slot, m=m, vload=vload, nsub=len(sub)):
                    s_vc[par].wait(t, vload)
                    if si == 0:
                        s_v.wait(t)
                    for dv in range(D // 512):
                        s_t.inc(t.matmul(
                            ctxps[:, dv * 512:(dv + 1) * 512],
                            ptar[:, slot * 512 + m * 128: slot * 512 + (m + 1) * 128],
                            vch[par][:, dv * 512:(dv + 1) * 512],
                            start=(si == 0), stop=(si == nsub - 1)), 1)
                b.on("tensor", f)
                vfree[par] = s_t.n
            def f(v, m=m, g=g):
                s_t.wait(v)
                s_cts.wait(v)
                s_v.inc(v.tensor_scalar(out=ctstg[:, :], in0=ctxps[:, :],
                                        scalar1=lic[:, m:m + 1], scalar2=None,
                                        op0=ALU.mult), 1)
            b.on("vector", f)
            def f(s, m=m, g=g):
                s_v.wait(s)
                s_cts.inc(s.dma_start(
                    out=CTXBF[g * 512 + m * 128: g * 512 + (m + 1) * 128, :],
                    in_=ctstg[:, :]), 16)
            b.on("sync", f)


def emit_oproj_resid(b, seg, CTXBF, wo, X, s_list, tag="A5", prev_x_sem="A_xst",
                     src_store_sem="fl_cts", dst=None, dst_sem_name="op_xw"):
    (s_g, s_d, s_e, s_t, s_v, s_sc, s_p, s_o) = s_list
    nc, cfg = b.nc, b.cfg
    D, T, NT, DC = cfg.D, cfg.T, cfg.NT, cfg.DC
    if dst is None:
        dst = X

    s_fm2 = b.sem(tag + "op_fm")
    s_ow = b.sem(tag + "op_w")
    s_xr = b.sem(tag + "op_xr")
    s_xw = b.sem(dst_sem_name)
    cts = b.sems.get(src_store_sem)
    cfm = b.sb(seg, tag + "CFM", [128, DC * T], BF16)
    emit_local_fence(b)
    load_fm(b, cfm, CTXBF, DC, T, s_fm2, wait_sems=[(cts, cts.n)])
    cfm_ready = s_fm2.n
    NPAN = D // 512
    owp = b.sb(seg, tag + "OWP", [128, NPAN * DC * 512], BF16)
    prj = [b.ps(seg, tag + "OPRJ0", [128, 512]), b.ps(seg, tag + "OPRJ1", [128, 512])]
    xrt = b.sb(seg, tag + "XRT", [128, D], F32)
    xwt = b.sb(seg, tag + "XWT", [128, D], F32)

    panload = [None] * NPAN
    prjfree = [0, 0]
    for p in range(NPAN):
        def f(g, p=p):
            s_ow.inc(g.dma_start(
                out=owp[:, p * DC * 512:(p + 1) * DC * 512]
                    .rearrange("p (c n) -> p c n", n=512),
                in_=wo[:, p * 512:(p + 1) * 512].rearrange("(c p) n -> p c n", p=128)), 16)
        b.on("gpsimd", f)
        panload[p] = s_ow.n
    for tt in range(NT):
        def f(s, tt=tt):
            s_v.wait(s)
            if tt == 0 and prev_x_sem in b.sems:
                b.sems[prev_x_sem].wait(s)
            s_xr.inc(s.dma_start(out=xrt[:, :], in_=X[tt * 128:(tt + 1) * 128, :]), 16)
        b.on("sync", f)
        xload = s_xr.n
        for p in range(NPAN):
            bank = p % 2
            def f(t, tt=tt, p=p, bank=bank):
                if tt == 0:
                    s_ow.wait(t, panload[NPAN - 1])
                if tt == 0 and p == 0:
                    s_fm2.wait(t, cfm_ready)
                s_v.wait(t, prjfree[bank])
                for dc in range(DC):
                    s_t.inc(t.matmul(
                        prj[bank][:, :],
                        cfm[:, dc * T + tt * 128: dc * T + (tt + 1) * 128],
                        owp[:, p * DC * 512 + dc * 512: p * DC * 512 + (dc + 1) * 512],
                        start=(dc == 0), stop=(dc == DC - 1)), 1)
            b.on("tensor", f)
            mmdone = s_t.n
            def f(v, p=p, bank=bank, xload=xload):
                s_t.wait(v, mmdone)
                s_xr.wait(v, xload)
                if p == 0:
                    s_xw.wait(v)
                s_v.inc(v.tensor_tensor(out=xwt[:, p * 512:(p + 1) * 512],
                                        in0=prj[bank][:, :],
                                        in1=xrt[:, p * 512:(p + 1) * 512], op=ALU.add), 1)
            b.on("vector", f)
            prjfree[bank] = s_v.n
        def f(s, tt=tt):
            s_v.wait(s)
            s_xw.inc(s.dma_start(out=X[tt * 128:(tt + 1) * 128, :], in_=xwt[:, :]), 16)
        b.on("sync", f)


def emit_norms_small(b, nmax, krow, KN2L, per_head, s_list):
    (s_g, s_d, s_e, s_t, s_v, s_sc, s_p, s_o) = s_list
    cfg = b.cfg
    nh = cfg.NH if per_head else 1
    def f(g):
        s_d.wait(g)
        s_p.inc(g.memset(krow[0:1, :], 0.0), 1)
    b.on("gpsimd", f)
    def f(v):
        s_p.wait(v)
        s_v.wait(v)
        s_v.inc(v.tensor_copy(out=krow[0:1, 0:nh], in_=nmax[0:1, nh:2 * nh]), 1)
    b.on("vector", f)
    def f(s):
        s_v.wait(s)
        s_d.inc(s.dma_start(out=KN2L[0:1, :], in_=krow[0:1, :]), 16)
    b.on("sync", f)


def emit_negc_mha(b, seg, KN2G, nmax_name, negch, c_soft, s_list, tag):
    (s_g, s_d, s_e, s_t, s_v, s_sc, s_p, s_o) = s_list
    nc, cfg = b.nc, b.cfg
    NH = cfg.NH
    kng = b.sb(seg, tag + "KNGH", [1, NH], F32)
    nrow = b.sb(seg, tag + "NROW", [1, NH], F32)
    s_kg = b.sem(tag + "_kng")
    with nc.psum_tensor(tag + "NGPS", [128, NH], F32) as ngps:
        def f(s):
            b.sems["s_cc"].wait(s)
            s_kg.inc(s.dma_start(out=kng[0:1, :], in_=KN2G[0:1, 0:NH]), 16)
        b.on("sync", f)
        def f(v):
            s_kg.wait(v)
            s_v.inc(v.tensor_tensor(out=nrow[0:1, :], in0=b.t[nmax_name][0:1, 0:NH],
                                    in1=kng[0:1, :], op=ALU.mult), 1)
        b.on("vector", f)
        def f(sc):
            s_v.wait(sc)
            s_sc.inc(sc.activation(out=nrow[0:1, :], in_=nrow[0:1, :], func=AF.Ln), 1)
            s_sc.wait(sc)
            s_sc.inc(sc.activation(out=nrow[0:1, :], in_=nrow[0:1, :], func=AF.Exp,
                                   scale=0.5), 1)
        b.on("scalar", f)
        def f(v):
            s_sc.wait(v)
            s_v.inc(v.tensor_scalar(out=nrow[0:1, :], in0=nrow[0:1, :],
                                    scalar1=-c_soft, scalar2=None, op0=ALU.mult), 1)
        b.on("vector", f)
        def f(t):
            s_v.wait(t)
            for h in range(NH):
                s_t.inc(t.matmul(ngps[:, h:h + 1], b.t["ONES_row_f32"][:, :],
                                 nrow[0:1, h:h + 1], start=(h == 0), stop=(h == NH - 1)), 1)
        b.on("tensor", f)
        def f(v):
            s_t.wait(v)
            s_v.inc(v.tensor_copy(out=negch[:, :], in_=ngps[:, :]), 1)
        b.on("vector", f)


def emit_flash_MHA(b, seg, QF, KALL, VALL, KN2G, ctxfm, nmax_name, s_list, tag):
    (s_g, s_d, s_e, s_t, s_v, s_sc, s_p, s_o) = s_list
    nc, cfg = b.nc, b.cfg
    D, T, DC, NG, NH = cfg.D, cfg.T, cfg.DC, cfg.NG, cfg.NH
    c_soft = 1.0 / float(np.sqrt(np.float32(cfg.DH)))
    EB = 3

    negch = b.sb(seg, tag + "NEGCH", [128, NH], F32)
    emit_negc_mha(b, seg, KN2G, nmax_name, negch, c_soft, s_list, tag)

    kh = b.sb(seg, tag + "KH", [128, 4 * T], BF16)
    vh = b.sb(seg, tag + "VH", [128, 4 * T], BF16)
    qh = b.sb(seg, tag + "QH", [128, T], BF16)
    ptar = b.sb(seg, tag + "PTA", [128, cfg.maxslots * 512], BF16)
    linv = b.sb(seg, tag + "LINV", [1, 512], F32)
    lbcs = b.sb(seg, tag + "LBCS", [128, 512], F32)
    scps = [b.ps(seg, tag + "SC0", [128, EB * 512]), b.ps(seg, tag + "SC1", [128, EB * 512])]
    lrow = b.ps(seg, tag + "LROW", [128, 512])
    cfps = b.ps(seg, tag + "CFPS", [128, 512])
    s_kh = b.sem(tag + "_kh")
    s_vh = b.sem(tag + "_vh")
    s_qh = b.sem(tag + "_qh")
    scfree = [0, 0]

    for h in range(NH):
        def f(s, h=h):
            s_t.wait(s)
            for r in range(4):
                s_kh.inc(s.dma_start(
                    out=kh[:, r * T:(r + 1) * T],
                    in_=KALL[r * D + h * 128: r * D + (h + 1) * 128, :]), 16)
            s_vh.inc(s.dma_start(
                out=vh[:, :].rearrange("p (n d) -> p n d", d=128),
                in_=VALL[:, h * 128:(h + 1) * 128].rearrange("(n p) d -> p n d", p=128)), 16)
            s_qh.inc(s.dma_start(out=qh[:, :], in_=QF[h * 128:(h + 1) * 128, :]), 16)
        b.on("sync", f)
        khload, vhload, qhload = s_kh.n, s_vh.n, s_qh.n
        for g in range(NG):
            chunks = [(r, c) for r in range(4) for c in range(4 * g + 4)]
            nch = len(chunks)
            blocks = [chunks[i:i + EB] for i in range(0, nch, EB)]
            ci0 = 0
            for bi, blk in enumerate(blocks):
                par = bi % 2
                def f(t, h=h, g=g, bi=bi, par=par, blk=blk, ci0=ci0,
                      khload=khload, qhload=qhload):
                    if bi == 0 and g == 0:
                        s_kh.wait(t, khload)
                        s_qh.wait(t, qhload)
                        s_v.wait(t)
                    s_sc.wait(t, scfree[par])
                    for j, (r, c) in enumerate(blk):
                        s_t.inc(t.matmul(scps[par][:, j * 512:(j + 1) * 512],
                                         kh[:, r * T + c * 128: r * T + (c + 1) * 128],
                                         qh[:, g * 512:(g + 1) * 512],
                                         start=True, stop=True), 1)
                b.on("tensor", f)
                scdone = s_t.n
                def f(sc, h=h, par=par, blk=blk, ci0=ci0, scdone=scdone, bi=bi, g=g):
                    s_t.wait(sc, scdone)
                    if bi == 0:
                        s_v.wait(sc)
                    s_sc.inc(sc.activation(
                        out=ptar[:, ci0 * 512:(ci0 + len(blk)) * 512],
                        in_=scps[par][:, 0:len(blk) * 512], func=AF.Exp,
                        bias=negch[:, h:h + 1], scale=c_soft), 1)
                b.on("scalar", f)
                scfree[par] = s_sc.n
                dm = [(j, r, c) for j, (r, c) in enumerate(blk) if c - 4 * g >= 0]
                if dm:
                    def f(v, dm=dm, ci0=ci0, g=g):
                        s_sc.wait(v)
                        for j, r, c in dm:
                            d = c - 4 * g
                            slot = ci0 + j
                            s_v.inc(v.tensor_tensor(
                                out=ptar[:, slot * 512:(slot + 1) * 512],
                                in0=ptar[:, slot * 512:(slot + 1) * 512],
                                in1=b.t["MSK"][:, (d * 4 + r) * 512:(d * 4 + r + 1) * 512],
                                op=ALU.mult), 1)
                    b.on("vector", f)
                ci0 += len(blk)
            def f(t, h=h, g=g, chunks=chunks, nch=nch, vhload=vhload):
                s_sc.wait(t)
                s_v.wait(t)
                s_vh.wait(t, vhload)
                for ci, (r, c) in enumerate(chunks):
                    n = r * (T // 128) + c
                    s_t.inc(t.matmul(cfps[:, :],
                                     vh[:, n * 128:(n + 1) * 128],
                                     ptar[:, ci * 512:(ci + 1) * 512],
                                     start=(ci == 0), stop=(ci == nch - 1)), 1)
                for ci in range(nch):
                    s_t.inc(t.matmul(lrow[0:1, :], b.t["ONES_col_bf"][:, :],
                                     ptar[:, ci * 512:(ci + 1) * 512],
                                     start=(ci == 0), stop=(ci == nch - 1)), 1)
            b.on("tensor", f)
            pvdone = s_t.n
            def f(v, pvdone=pvdone):
                s_t.wait(v, pvdone)
                s_v.inc(v.reciprocal(out=linv[0:1, :], in_=lrow[0:1, :]), 1)
            b.on("vector", f)
            def f(t):
                s_v.wait(t)
                s_t.inc(t.matmul(lrow[:, :], b.t["ONES_row_f32"][:, :],
                                 linv[0:1, :], start=True, stop=True), 1)
            b.on("tensor", f)
            def f(v, h=h, g=g):
                s_t.wait(v)
                s_v.inc(v.tensor_copy(out=lbcs[:, :], in_=lrow[:, :]), 1)
                s_v.wait(v)
                s_v.inc(v.tensor_tensor(out=ctxfm[:, h * T + g * 512: h * T + (g + 1) * 512],
                                        in0=cfps[:, :], in1=lbcs[:, :], op=ALU.mult), 1)
            b.on("vector", f)


def emit_ffn_down(b, seg, HBF, w2_load, evac, s_list, tag, loader_eng="gpsimd",
                  h_store_sem=None):
    (s_g, s_d, s_e, s_t, s_v, s_sc, s_p, s_o) = s_list
    nc, cfg = b.nc, b.cfg
    D, T, FF, FC = cfg.D, cfg.T, cfg.FF, cfg.FC

    hfmg = b.sb(seg, tag + "HFMG", [128, FC * 512], BF16)
    w2p = b.sb(seg, tag + "W2P", [128, FC * 512], BF16)
    dps = [b.ps(seg, tag + "DPS0", [128, 512]), b.ps(seg, tag + "DPS1", [128, 512])]
    s_hf = b.sem(tag + "_hf")
    s_w2 = b.sem(tag + "_w2")
    dpsfree = [0, 0]
    hs_sem = b.sems.get(h_store_sem) if h_store_sem else None
    hs_n = hs_sem.n if hs_sem else 0
    for tg in range(T // 512):
        def f(s, tg=tg):
            if tg == 0 and hs_sem is not None:
                hs_sem.wait(s, hs_n)
            s_t.wait(s)
            for ic in range(FC):
                s_hf.inc(s.dma_start(
                    out=hfmg[:, ic * 512:(ic + 1) * 512],
                    in_=HBF[tg * 512:(tg + 1) * 512, ic * 128:(ic + 1) * 128],
                    transpose=True), 16)
        b.on("sync", f)
        hload = s_hf.n
        for oc in range(D // 512):
            def f(g, tg=tg, oc=oc):
                s_t.wait(g)
                w2_load(g, oc, w2p, s_w2)
            b.on(loader_eng, f)
            w2load = s_w2.n
            for tt in range(4):
                bank = tt % 2
                def f(t, tg=tg, oc=oc, tt=tt, bank=bank, hload=hload, w2load=w2load):
                    s_hf.wait(t, hload)
                    s_w2.wait(t, w2load)
                    s_v.wait(t, dpsfree[bank])
                    for ic in range(FC):
                        s_t.inc(t.matmul(
                            dps[bank][:, :],
                            hfmg[:, ic * 512 + tt * 128: ic * 512 + (tt + 1) * 128],
                            w2p[:, ic * 512:(ic + 1) * 512],
                            start=(ic == 0), stop=(ic == FC - 1)), 1)
                b.on("tensor", f)
                evac(tg * 4 + tt, oc, dps[bank])
                dpsfree[bank] = s_v.n


def emit_gather3(b, KLOC, VLOC, KN2L, KALL, VALL, KN2G, wait_names):
    nc = b.nc
    s_cc = b.sems["s_cc"]
    for nm in wait_names:
        if nm in b.sems:
            nc.gpsimd.wait_ge(b.sems[nm].h, b.sems[nm].n)
    s_cc.inc(nc.gpsimd.collective_compute(
        "AllGather", ALU.bypass, replica_groups=GROUPS,
        ins=[KLOC.opt()], outs=[KALL.opt()]), 1)
    s_cc.inc(nc.gpsimd.collective_compute(
        "AllGather", ALU.bypass, replica_groups=GROUPS,
        ins=[VLOC.opt()], outs=[VALL.opt()]), 1)
    s_cc.inc(nc.gpsimd.collective_compute(
        "AllReduce", ALU.max, replica_groups=GROUPS,
        ins=[KN2L.opt()], outs=[KN2G.opt()]), 1)
    nc.gpsimd.wait_ge(s_cc.h, s_cc.n)


def emit_B1(b, seg, X, w, HS, PART1, XNBF, s_list):
    (s_g, s_d, s_e, s_t, s_v, s_sc, s_p, s_o) = s_list
    nc, cfg = b.nc, b.cfg
    D, T, NT, DC, LR = cfg.D, cfg.T, cfg.NT, cfg.DC, cfg.LR
    import ml_dtypes

    g1 = b.sb(seg, "G1BC", [128, D], F32)
    b1 = b.sb(seg, "B1BC", [128, D], F32)
    bs = b.sb(seg, "BSBC", [128, D], F32)
    rowt = b.sb(seg, "B1ROW", [1, D], F32)
    s_br = b.sem("B1_br")
    with nc.psum_tensor("B1BCPS", [128, 512], F32) as bcps:
        for wname, dstbc in (("L0_ffn_ln1_scale", g1), ("L0_ffn_ln1_bias", b1),
                             ("L0_ffn_simple_bias", bs)):
            def f(s, wname=wname):
                s_v.wait(s)
                s_br.inc(s.dma_start(out=rowt[0:1, :], in_=b.dram[wname][None, :]), 16)
            b.on("sync", f)
            def f(t):
                s_br.wait(t)
            b.on("tensor", f)
            bcast_rows(b, rowt[0:1, :], dstbc, D, s_t, s_v, bcps)

    wrt = b.sb(seg, "WRT", [128, DC * 3], F32)
    wrow = b.sb(seg, "WRROW", [3, D], F32)
    id3 = nc.inline_tensor(np.eye(3, dtype=np.float32), "c_id3f").ap()
    id3s = b.sb(seg, "ID3F", [3, 3], F32)
    s_wr = b.sem("B1_wr")
    with nc.psum_tensor("B1WRPS", [128, 3], F32) as wrps:
        def f(s):
            s_wr.inc(s.dma_start(out=wrow[0:3, :], in_=b.dram["L0_ffn_w_router"][:, :]), 16)
            s_wr.inc(s.dma_start(out=id3s[0:3, :], in_=id3[:, :]), 16)
        b.on("sync", f)
        for fc in range(DC):
            def f(t, fc=fc):
                s_wr.wait(t)
                s_v.wait(t)
                s_t.inc(t.transpose(wrps[:, :], wrow[0:3, fc * 128:(fc + 1) * 128],
                                    id3s[0:3, 0:3]), 1)
            b.on("tensor", f)
            def f(v, fc=fc):
                s_t.wait(v)
                s_v.inc(v.tensor_copy(out=wrt[:, fc * 3:(fc + 1) * 3], in_=wrps[:, :]), 1)
            b.on("vector", f)

    wdt = b.sb(seg, "WDT", [128, DC * LR], BF16)
    wut = b.sb(seg, "WUT", [32, D], BF16)
    s_lw = b.sem("B1_lw")
    def f(s):
        b.sems["B_pc"].wait(s)
        for fc in range(DC):
            s_lw.inc(s.dma_start(out=wdt[:, fc * LR:(fc + 1) * LR],
                                 in_=b.dram["WDB"][:, fc * 128:(fc + 1) * 128],
                                 transpose=True), 16)
    b.on("sync", f)
    wub_sb = b.sb(seg, "WUBSB", [128, LR], BF16)
    with nc.psum_tensor("B1WUPS", [32, 128], BF16) as wups:
        for fc in range(DC):
            def f(s, fc=fc):
                s_t.wait(s)
                s_lw.inc(s.dma_start(out=wub_sb[:, :],
                                     in_=b.dram["WUB"][fc * 128:(fc + 1) * 128, :]), 16)
            b.on("sync", f)
            lw = s_lw.n
            def f(t, fc=fc, lw=lw):
                s_lw.wait(t, lw)
                s_v.wait(t)
                s_t.inc(t.transpose(wups[:, :], wub_sb[:, 0:LR], b.t["ID_BF"][:, :]), 1)
            b.on("tensor", f)
            def f(v, fc=fc):
                s_t.wait(v)
                s_v.inc(v.tensor_copy(out=wut[0:32, fc * 128:(fc + 1) * 128],
                                      in_=wups[0:32, :]), 1)
            b.on("vector", f)

    trps = [b.ps(seg, "B1TR0", [128, 128]), b.ps(seg, "B1TR1", [128, 128])]
    psml = b.ps(seg, "B1SML", [128, 512])
    y1t_ps = b.ps(seg, "B1Y1T", [32, 128], BF16)
    delps = b.ps(seg, "B1DEL", [128, 2048] if D >= 2048 else [128, D])

    xt = b.sb(seg, "BXT", [128, D], F32)
    hst = b.sb(seg, "BHST", [128, D], F32)
    scr = b.sb(seg, "BSCR", [128, D], F32)
    stat = b.sb(seg, "BSTAT", [128, 8], F32)
    pa = b.sb(seg, "BPA", [128, D], F32)
    pb = b.sb(seg, "BPB", [128, D], F32)
    xnb = b.sb(seg, "BXNB", [128, D], BF16)
    hfm32 = b.sb(seg, "HSFM32", [128, DC * 128], F32)
    hfmbf = b.sb(seg, "HSFMBF", [128, DC * 128], BF16)
    y1b = b.sb(seg, "BY1B", [128, LR], BF16)
    y1t = b.sb(seg, "BY1T", [32, 128], BF16)
    s_xt = b.sem("B1_xt")
    s_hs = b.sem("B1_hs")
    s_p1 = b.sem("B1_p1")
    s_nb = b.sem("B1_nb")

    for i in range(NT):
        def f(s, i=i):
            s_v.wait(s); s_sc.wait(s)
            if i == 0:
                b.sems["op_xw"].wait(s)
            s_xt.inc(s.dma_start(out=xt[:, :], in_=X[i * 128:(i + 1) * 128, :]), 16)
        b.on("sync", f)
        def f(v):
            s_xt.wait(v)
        b.on("vector", f)
        def f(sc):
            s_xt.wait(sc)
        b.on("scalar", f)
        ln_stats(b, xt, D, stat, scr, s_sc, s_v)
        def f(v):
            s_hs.wait(v)
        b.on("vector", f)
        ln_apply(b, xt, hst, D, stat, s_sc, s_v)
        def f(s, i=i):
            s_v.wait(s)
            s_hs.inc(s.dma_start(out=HS[i * 128:(i + 1) * 128, :], in_=hst[:, :]), 16)
        b.on("sync", f)
        for fc in range(DC):
            par = fc % 2
            def f(t, fc=fc, par=par):
                s_v.wait(t)
                s_t.inc(t.transpose(trps[par][:, :], xt_dummy_src(b, hst, fc),
                                    b.t["ID_F32"][:, :]), 1)
            b.on("tensor", f)
            def f(v, fc=fc, par=par):
                s_t.wait(v)
                s_v.inc(v.tensor_copy(out=hfm32[:, fc * 128:(fc + 1) * 128],
                                      in_=trps[par][:, :]), 1)
            b.on("vector", f)
        def f(v):
            s_v.wait(v)
            s_v.inc(v.tensor_copy(out=hfmbf[:, :], in_=hfm32[:, :]), 1)
        b.on("vector", f)
        hfm_done = s_v.n
        def f(t, hfm_done=hfm_done):
            s_v.wait(t, hfm_done)
            for fc in range(DC):
                s_t.inc(t.matmul(psml[:, 0:3], hfm32[:, fc * 128:(fc + 1) * 128],
                                 wrt[:, fc * 3:(fc + 1) * 3],
                                 start=(fc == 0), stop=(fc == DC - 1)), 1)
        b.on("tensor", f)
        rt_done = s_t.n
        def f(v, i=i, rt_done=rt_done):
            s_t.wait(v, rt_done)
            s_v.inc(v.tensor_copy(out=stat[:, 5:8], in_=psml[:, 0:3]), 1)
            s_v.wait(v)
            l0, l1, l2 = stat[:, 5:6], stat[:, 6:7], stat[:, 7:8]
            s_v.inc(v.tensor_tensor(out=stat[:, 0:1], in0=l2, in1=l0, op=ALU.is_gt), 1)
            s_v.inc(v.tensor_tensor(out=stat[:, 1:2], in0=l2, in1=l1, op=ALU.is_gt), 1)
            s_v.inc(v.tensor_tensor(out=stat[:, 2:3], in0=l1, in1=l0, op=ALU.is_gt), 1)
            s_v.inc(v.tensor_tensor(out=stat[:, 3:4], in0=l1, in1=l2, op=ALU.is_ge), 1)
            s_v.wait(v)
            s_v.inc(v.tensor_tensor(out=b.t["RM2"][:, i:i + 1], in0=stat[:, 0:1],
                                    in1=stat[:, 1:2], op=ALU.mult), 1)
            s_v.inc(v.tensor_tensor(out=stat[:, 4:5], in0=stat[:, 2:3],
                                    in1=stat[:, 3:4], op=ALU.mult), 1)
            s_v.wait(v)
            s_v.inc(v.tensor_scalar(out=stat[:, 0:1], in0=b.t["RM2"][:, i:i + 1],
                                    scalar1=-1.0, scalar2=1.0, op0=ALU.mult,
                                    op1=ALU.add), 1)
            s_v.wait(v)
            s_v.inc(v.tensor_tensor(out=b.t["RM1"][:, i:i + 1], in0=stat[:, 4:5],
                                    in1=stat[:, 0:1], op=ALU.mult), 1)
            s_v.wait(v)
            s_v.inc(v.tensor_tensor(out=b.t["RM0"][:, i:i + 1], in0=stat[:, 0:1],
                                    in1=b.t["RM1"][:, i:i + 1], op=ALU.subtract), 1)
        b.on("vector", f)
        masks_done = s_v.n
        def f(t, masks_done=masks_done):
            s_v.wait(t, masks_done)
            for fc in range(DC):
                s_t.inc(t.matmul(psml[:, 32:32 + LR], hfmbf[:, fc * 128:(fc + 1) * 128],
                                 wdt[:, fc * LR:(fc + 1) * LR],
                                 start=(fc == 0), stop=(fc == DC - 1)), 1)
        b.on("tensor", f)
        def f(v):
            s_t.wait(v)
            s_v.inc(v.tensor_copy(out=y1b[:, :], in_=psml[:, 32:32 + LR]), 1)
        b.on("vector", f)
        def f(t):
            s_v.wait(t)
            s_t.inc(t.transpose(y1t_ps[:, :], y1b[:, 0:LR], b.t["ID_BF"][:, :]), 1)
        b.on("tensor", f)
        def f(v):
            s_t.wait(v)
            s_v.inc(v.tensor_copy(out=y1t[0:32, :], in_=y1t_ps[0:32, :]), 1)
        b.on("vector", f)
        y1t_done = s_v.n
        def f(t, y1t_done=y1t_done):
            s_v.wait(t, y1t_done)
            for oc in range(D // 512):
                s_t.inc(t.matmul(delps[:, oc * 512:(oc + 1) * 512], y1t[0:32, :],
                                 wut[0:32, oc * 512:(oc + 1) * 512],
                                 start=True, stop=True), 1)
        b.on("tensor", f)
        del_done = s_t.n
        def f(v, i=i, del_done=del_done):
            s_t.wait(v, del_done)
            s_p1.wait(v)
            s_v.inc(v.tensor_scalar(out=pa[:, :], in0=hst[:, :],
                                    scalar1=stat[:, 0:1], scalar2=None,
                                    op0=ALU.mult), 1)
            s_v.wait(v)
            s_v.inc(v.scalar_tensor_tensor(out=pb[:, :], in0=bs[:, :],
                                           scalar=b.t["RM0"][:, i:i + 1], in1=pa[:, :],
                                           op0=ALU.mult, op1=ALU.add), 1)
            s_v.wait(v)
            for oc in range(D // 512):
                s_v.inc(v.scalar_tensor_tensor(
                    out=pa[:, oc * 512:(oc + 1) * 512],
                    in0=delps[:, oc * 512:(oc + 1) * 512],
                    scalar=b.t["RM1"][:, i:i + 1], in1=pb[:, oc * 512:(oc + 1) * 512],
                    op0=ALU.mult, op1=ALU.add), 1)
        b.on("vector", f)
        def f(s, i=i):
            s_v.wait(s)
            s_p1.inc(s.dma_start(out=PART1[i * 128:(i + 1) * 128, :], in_=pa[:, :]), 16)
        b.on("sync", f)
        ln_stats(b, hst, D, stat, scr, s_sc, s_v)
        ln_apply(b, hst, pb, D, stat, s_sc, s_v, gbc=g1, bbc=b1)
        def f(v):
            s_v.wait(v)
            s_nb.wait(v)
            s_v.inc(v.tensor_copy(out=xnb[:, :], in_=pb[:, :]), 1)
        b.on("vector", f)
        def f(s, i=i):
            s_v.wait(s)
            s_nb.inc(s.dma_start(out=XNBF[i * 128:(i + 1) * 128, :], in_=xnb[:, :]), 16)
        b.on("sync", f)


def xt_dummy_src(b, hst, fc):
    return hst[:, fc * 128:(fc + 1) * 128]


def emit_oproj_ln2(b, seg, ctxfm, WOB, HS, XPRIME, XN2BF, w, s_list):
    (s_g, s_d, s_e, s_t, s_v, s_sc, s_p, s_o) = s_list
    nc, cfg = b.nc, b.cfg
    D, T, NT, DC, NH = cfg.D, cfg.T, cfg.NT, cfg.DC, cfg.NH

    emit_local_fence(b)
    g2 = b.sb(seg, "G2BC", [128, D], F32)
    b2 = b.sb(seg, "B2BC", [128, D], F32)
    rowt = b.sb(seg, "B5ROW", [1, D], F32)
    s_br = b.sem("B5_br")
    with nc.psum_tensor("B5BCPS", [128, 512], F32) as bcps:
        for wname, dstbc in (("L0_ffn_ln2_scale", g2), ("L0_ffn_ln2_bias", b2)):
            def f(s, wname=wname):
                s_v.wait(s)
                s_br.inc(s.dma_start(out=rowt[0:1, :], in_=b.dram[wname][None, :]), 16)
            b.on("sync", f)
            def f(t):
                s_br.wait(t)
            b.on("tensor", f)
            bcast_rows(b, rowt[0:1, :], dstbc, D, s_t, s_v, bcps)

    wot = b.sb(seg, "WOT", [128, DC * D], BF16)
    s_wo = b.sem("B5_wo")
    def f(s):
        b.sems["B_pc"].wait(s)
        for hc in range(DC):
            s_wo.inc(s.dma_start(out=wot[:, hc * D:(hc + 1) * D],
                                 in_=WOB[:, hc * 128:(hc + 1) * 128], transpose=True), 16)
    b.on("sync", f)
    wo_done = s_wo.n

    ops = [b.ps(seg, "B5OP0", [128, 512]), b.ps(seg, "B5OP1", [128, 512])]
    hst = b.sb(seg, "B5HS", [128, D], F32)
    xp = b.sb(seg, "B5XP", [128, D], F32)
    scr = b.sb(seg, "B5SCR", [128, D], F32)
    stat = b.sb(seg, "B5STAT", [128, 8], F32)
    xnb = b.sb(seg, "B5XNB", [128, D], BF16)
    s_hs2 = b.sem("B5_hs")
    s_xp = b.sem("B5_xp")
    s_nb2 = b.sem("B5_nb")
    opfree = [0, 0]
    for tt in range(NT):
        def f(s, tt=tt):
            s_v.wait(s)
            b.sems["B1_hs"].wait(s)
            s_hs2.inc(s.dma_start(out=hst[:, :], in_=HS[tt * 128:(tt + 1) * 128, :]), 16)
        b.on("sync", f)
        hsl = s_hs2.n
        for oc in range(D // 512):
            bank = oc % 2
            def f(t, tt=tt, oc=oc, bank=bank, wo_done=wo_done):
                if tt == 0 and oc == 0:
                    s_wo.wait(t, wo_done)
                    s_v.wait(t)
                s_v.wait(t, opfree[bank])
                for h in range(NH):
                    s_t.inc(t.matmul(
                        ops[bank][:, :],
                        ctxfm[:, h * T + tt * 128: h * T + (tt + 1) * 128],
                        wot[:, h * D + oc * 512: h * D + (oc + 1) * 512],
                        start=(h == 0), stop=(h == NH - 1)), 1)
            b.on("tensor", f)
            mmdone = s_t.n
            def f(v, oc=oc, bank=bank, hsl=hsl, mmdone=mmdone):
                s_t.wait(v, mmdone)
                s_hs2.wait(v, hsl)
                if oc == 0:
                    s_xp.wait(v)
                    s_sc.wait(v)
                s_v.inc(v.tensor_tensor(out=xp[:, oc * 512:(oc + 1) * 512],
                                        in0=ops[bank][:, :],
                                        in1=hst[:, oc * 512:(oc + 1) * 512], op=ALU.add), 1)
            b.on("vector", f)
            opfree[bank] = s_v.n
        def f(s, tt=tt):
            s_v.wait(s)
            s_xp.inc(s.dma_start(out=XPRIME[tt * 128:(tt + 1) * 128, :], in_=xp[:, :]), 16)
        b.on("sync", f)
        def f(sc):
            s_xp.wait(sc)
        b.on("scalar", f)
        ln_stats(b, xp, D, stat, scr, s_sc, s_v)
        ln_apply(b, xp, scr, D, stat, s_sc, s_v, gbc=g2, bbc=b2)
        def f(v):
            s_v.wait(v)
            s_nb2.wait(v)
            s_v.inc(v.tensor_copy(out=xnb[:, :], in_=scr[:, :]), 1)
        b.on("vector", f)
        def f(s, tt=tt):
            s_v.wait(s)
            s_nb2.inc(s.dma_start(out=XN2BF[tt * 128:(tt + 1) * 128, :], in_=xnb[:, :]), 16)
        b.on("sync", f)


def emit_ffn_up(b, seg, XNSRC, HBF, s_list, tag, mode, wload_up):
    (s_g, s_d, s_e, s_t, s_v, s_sc, s_p, s_o) = s_list
    nc, cfg = b.nc, b.cfg
    D, T, NT, DC, FF = cfg.D, cfg.T, cfg.NT, cfg.DC, cfg.FF

    s_fmu = b.sem(tag + "_fmu")
    xnfm = b.sb(seg, tag + "XNFM", [128, DC * T], BF16)
    nbs = b.sems.get(tag + "_nbsrc")
    load_fm(b, xnfm, XNSRC, DC, T, s_fmu,
            wait_sems=[(nbs, nbs.n)] if nbs else [])
    xn_ready = s_fmu.n

    nsl = 2 if mode == "gelu" else 4
    wps = [b.sb(seg, tag + f"W1P{i}", [128, DC * 512], BF16) for i in range(nsl)]
    s_w1 = [b.sem(tag + f"_w1{i}") for i in range(nsl)]
    fps = [b.ps(seg, tag + "FPS0", [128, 512]), b.ps(seg, tag + "FPS1", [128, 512])]
    gpsfree = [0, 0]
    if mode == "gateup24":
        gps = [b.ps(seg, tag + "GPS0", [128, 512]), b.ps(seg, tag + "GPS1", [128, 512])]
        sg = b.sb(seg, tag + "SG", [128, 512], F32)
        hh = b.sb(seg, tag + "HH", [128, 512], F32)
        aa = b.sb(seg, tag + "AA", [128, 512], F32)
        nn = b.sb(seg, tag + "NN", [128, 512], F32)
        mk_ = b.sb(seg, tag + "MK", [128, 512], F32)
        hsp = b.sb(seg, tag + "HSP", [128, 512], F32)
    stg = b.sb(seg, tag + "FSTG", [128, 512], BF16)
    s_hst = b.sem(tag + "_hst")
    panfree = {}
    fpsfree = [0, 0]
    NP = FF // 512
    for ip in range(NP):
        slot = ip % 2
        if mode == "gelu":
            def f(s, ip=ip, slot=slot):
                s_t.wait(s, panfree.get(("w", slot), 0))
                wload_up(s, ip, wps[slot], s_w1[slot], "w")
            b.on("sync", f)
            wl = s_w1[slot].n
        else:
            def f(g, ip=ip, slot=slot):
                s_t.wait(g, panfree.get(("g", slot), 0))
                wload_up(g, ip, wps[slot], s_w1[slot], "gate")
            b.on("gpsimd", f)
            def f(g, ip=ip, slot=slot):
                s_t.wait(g, panfree.get(("u", slot), 0))
                wload_up(g, ip, wps[2 + slot], s_w1[2 + slot], "up")
            b.on("gpsimd", f)
            wl = s_w1[slot].n
            wl2 = s_w1[2 + slot].n
        for tt in range(NT):
            bank = tt % 2
            def f(t, ip=ip, tt=tt, bank=bank, slot=slot, wl=wl):
                s_w1[slot].wait(t, wl)
                if ip == 0 and tt == 0:
                    s_fmu.wait(t, xn_ready)
                if mode == "gelu":
                    s_sc.wait(t, fpsfree[bank])
                else:
                    s_sc.wait(t, fpsfree[bank])
                for fc in range(DC):
                    s_t.inc(t.matmul(
                        fps[bank][:, :],
                        xnfm[:, fc * T + tt * 128: fc * T + (tt + 1) * 128],
                        wps[slot][:, fc * 512:(fc + 1) * 512],
                        start=(fc == 0), stop=(fc == DC - 1)), 1)
            b.on("tensor", f)
            mmdone = s_t.n
            if mode == "gelu":
                def f(sc, bank=bank, mmdone=mmdone):
                    s_t.wait(sc, mmdone)
                    s_hst.wait(sc)
                    s_sc.inc(sc.activation(out=stg[:, :], in_=fps[bank][:, :],
                                           func=AF.Gelu_apprx_tanh), 1)
                b.on("scalar", f)
                fpsfree[bank] = s_sc.n
                def f(s, ip=ip, tt=tt):
                    s_sc.wait(s)
                    s_hst.inc(s.dma_start(
                        out=HBF[tt * 128:(tt + 1) * 128, ip * 512:(ip + 1) * 512],
                        in_=stg[:, :]), 16)
                b.on("sync", f)
            else:
                def f(t, ip=ip, tt=tt, bank=bank, slot=slot, wl2=wl2):
                    s_w1[2 + slot].wait(t, wl2)
                    s_v.wait(t, gpsfree[bank])
                    for fc in range(DC):
                        s_t.inc(t.matmul(
                            gps[bank][:, :],
                            xnfm[:, fc * T + tt * 128: fc * T + (tt + 1) * 128],
                            wps[2 + slot][:, fc * 512:(fc + 1) * 512],
                            start=(fc == 0), stop=(fc == DC - 1)), 1)
                b.on("tensor", f)
                mm2 = s_t.n
                def f(sc, bank=bank, mmdone=mmdone):
                    s_t.wait(sc, mmdone)
                    s_v.wait(sc)
                    s_sc.inc(sc.activation(out=sg[:, :], in_=fps[bank][:, :],
                                           func=AF.Silu), 1)
                b.on("scalar", f)
                fpsfree[bank] = s_sc.n
                def f(v, bank=bank, mm2=mm2):
                    s_t.wait(v, mm2)
                    s_sc.wait(v)
                    s_v.wait(v)
                    s_v.inc(v.tensor_tensor(out=hh[:, :], in0=sg[:, :],
                                            in1=gps[bank][:, :], op=ALU.mult), 1)
                b.on("vector", f)
                gpsfree[bank] = s_v.n
                def f(sc):
                    s_v.wait(sc)
                    s_sc.inc(sc.activation(out=aa[:, :], in_=hh[:, :], func=AF.Abs), 1)
                b.on("scalar", f)
                def f(v):
                    s_sc.wait(v)
                    a = [aa[:, k::4] for k in range(4)]
                    g01, g02, g03 = nn[:, 0::4], nn[:, 1::4], nn[:, 2::4]
                    g12 = nn[:, 3::4]
                    s_v.inc(v.tensor_tensor(out=g01, in0=a[1], in1=a[0], op=ALU.is_gt), 1)
                    s_v.inc(v.tensor_tensor(out=g02, in0=a[2], in1=a[0], op=ALU.is_gt), 1)
                    s_v.inc(v.tensor_tensor(out=g03, in0=a[3], in1=a[0], op=ALU.is_gt), 1)
                    s_v.inc(v.tensor_tensor(out=g12, in0=a[2], in1=a[1], op=ALU.is_gt), 1)
                    g13 = mk_[:, 0::4]; g23 = mk_[:, 1::4]
                    s_v.inc(v.tensor_tensor(out=g13, in0=a[3], in1=a[1], op=ALU.is_gt), 1)
                    s_v.inc(v.tensor_tensor(out=g23, in0=a[3], in1=a[2], op=ALU.is_gt), 1)
                    s_v.wait(v)
                    n0 = aa[:, 0::4]; n1 = aa[:, 1::4]; n2 = aa[:, 2::4]; n3 = aa[:, 3::4]
                    s_v.inc(v.tensor_tensor(out=n0, in0=g01, in1=g02, op=ALU.add), 1)
                    s_v.inc(v.tensor_tensor(out=n1, in0=g12, in1=g13, op=ALU.add), 1)
                    s_v.inc(v.tensor_tensor(out=n2, in0=g23, in1=g02, op=ALU.subtract), 1)
                    s_v.inc(v.tensor_tensor(out=n3, in0=g03, in1=g13, op=ALU.add), 1)
                    s_v.wait(v)
                    s_v.inc(v.tensor_tensor(out=n0, in0=n0, in1=g03, op=ALU.add), 1)
                    s_v.inc(v.tensor_tensor(out=n1, in0=n1, in1=g01, op=ALU.subtract), 1)
                    s_v.inc(v.tensor_tensor(out=n2, in0=n2, in1=g12, op=ALU.subtract), 1)
                    s_v.inc(v.tensor_tensor(out=n3, in0=n3, in1=g23, op=ALU.add), 1)
                    s_v.wait(v)
                    s_v.inc(v.tensor_scalar(out=n1, in0=n1, scalar1=1.0, scalar2=None,
                                            op0=ALU.add), 1)
                    s_v.inc(v.tensor_scalar(out=n2, in0=n2, scalar1=2.0, scalar2=None,
                                            op0=ALU.add), 1)
                    s_v.inc(v.tensor_scalar(out=n3, in0=n3, scalar1=-1.0, scalar2=3.0,
                                            op0=ALU.mult, op1=ALU.add), 1)
                    s_v.wait(v)
                    for k in range(4):
                        s_v.inc(v.tensor_scalar(out=mk_[:, k::4], in0=aa[:, k::4],
                                                scalar1=1.5, scalar2=None,
                                                op0=ALU.is_le), 1)
                    s_v.wait(v)
                    s_v.inc(v.tensor_tensor(out=hsp[:, :], in0=hh[:, :], in1=mk_[:, :],
                                            op=ALU.mult), 1)
                    s_v.wait(v)
                    s_hst.wait(v)
                    s_v.inc(v.tensor_copy(out=stg[:, :], in_=hsp[:, :]), 1)
                b.on("vector", f)
                def f(s, ip=ip, tt=tt):
                    s_v.wait(s)
                    s_hst.inc(s.dma_start(
                        out=HBF[tt * 128:(tt + 1) * 128, ip * 512:(ip + 1) * 512],
                        in_=stg[:, :]), 16)
                b.on("sync", f)
        if mode == "gelu":
            panfree[("w", slot)] = s_t.n
        else:
            panfree[("g", slot)] = s_t.n
            panfree[("u", slot)] = s_t.n



_W_NAMES = [
    "L0_attn_qkv", "L0_attn_o_proj", "L0_attn_gamma", "L0_attn_beta",
    "L0_ffn_simple_bias", "L0_ffn_w_down", "L0_ffn_w_up", "L0_ffn_w_router",
    "L0_ffn_ln1_scale", "L0_ffn_ln1_bias", "L0_ffn_wq", "L0_ffn_wk",
    "L0_ffn_wv", "L0_ffn_wo", "L0_ffn_ln2_scale", "L0_ffn_ln2_bias",
    "L0_ffn_w_ff1", "L0_ffn_w_ff2", "L1_attn_w_qkv", "L1_attn_b_qkv",
    "L1_attn_w_out", "L1_attn_b_out", "L1_ffn_w_gate", "L1_ffn_w_up",
    "L1_ffn_w_down",
]


def kernel(**inputs):
    from concourse.bass_utils import run_bass_kernel_spmd
    cfg = CFG(D=2048, T=2048, FF=8192, LR=32)
    x = np.ascontiguousarray(inputs["L0_attn_x"], dtype=np.float32)
    shards = shard_x(x)
    w = {n: np.ascontiguousarray(inputs[n], dtype=np.float32) for n in _W_NAMES}
    b = build(cfg, upto="D")
    in_maps = []
    for c in range(8):
        m = {"x": shards[c], "masks": host_masks(c % 4)}
        m.update(w)
        in_maps.append(m)
    res = run_bass_kernel_spmd(b.nc, in_maps, core_ids=list(range(8)))
    ys = [res.results[c]["y"] for c in range(8)]
    return unshard_y(cfg, ys)


# version 04
# speedup vs baseline: 1.0393x; 1.0393x over previous
import contextlib
import numpy as np
import concourse.bass as bass
import concourse.mybir as mybir

F32 = mybir.dt.float32
BF16 = mybir.dt.bfloat16
AF = mybir.ActivationFunctionType
ALU = mybir.AluOpType
AX = mybir.AxisListType


class CFG:
    def __init__(self, D=2048, T=2048, FF=8192, LR=32):
        self.D = D; self.T = T; self.FF = FF; self.LR = LR
        self.R = 4; self.DH = 128
        self.NH = D // 128
        self.NT = T // 128
        self.NG = T // 512
        self.DC = D // 128
        self.FC = FF // 128
        self.S = 4 * T
        self.EPS = 1e-5
        self.maxslots = 4 * (4 * (self.NG - 1) + 4)


def cdiv(a, b):
    return (a + b - 1) // b


class Sem:
    def __init__(self, nc, es, name):
        self.h = es.enter_context(nc.semaphore(name)); self.n = 0
    def inc(self, inst, k=1):
        inst.then_inc(self.h, k); self.n += k
    def wait(self, eng, val=None):
        eng.wait_ge(self.h, self.n if val is None else val)


class B:
    def __init__(self, cfg: CFG):
        self.cfg = cfg
        self.nc = bass.Bass("TRN2", target_bir_lowering=False, num_devices=8)
        self.es = contextlib.ExitStack()
        self.t = {}
        self.dram = {}
        self.sems = {}
        self.blk = None

    def sem(self, name):
        if name not in self.sems:
            self.sems[name] = Sem(self.nc, self.es, name)
        return self.sems[name]

    def sb(self, es, name, shape, dtype):
        self.t[name] = es.enter_context(self.nc.sbuf_tensor(name, shape, dtype))
        return self.t[name]

    def ps(self, es, name, shape, dtype=F32):
        self.t[name] = es.enter_context(self.nc.psum_tensor(name, shape, dtype))
        return self.t[name]

    def dr(self, name, shape, dtype, kind="Internal"):
        self.dram[name] = self.nc.dram_tensor(name, shape, dtype, kind=kind).ap()
        return self.dram[name]

    def on(self, engname, f):
        getattr(self.blk, engname)(f)



def host_masks(j: int):
    import ml_dtypes
    out = np.zeros((4, 4, 128, 512), dtype=np.float32)
    k = np.arange(128)[:, None]; q = np.arange(128)[None, :]
    for d in range(4):
        for r in range(4):
            tri = (k < q) | ((k == q) & (r <= j))
            for m in range(4):
                if d < m:
                    out[d, r, :, 128 * m:128 * (m + 1)] = 1.0
                elif d == m:
                    out[d, r, :, 128 * m:128 * (m + 1)] = tri
    return np.ascontiguousarray(out.astype(ml_dtypes.bfloat16))


def shard_x(x):
    return [np.ascontiguousarray(x[c // 4, (c % 4)::4, :]) for c in range(8)]


def unshard_y(cfg, ys):
    out = np.empty((2, cfg.S, cfg.D), dtype=np.float32)
    for c in range(8):
        out[c // 4, (c % 4)::4, :] = ys[c]
    return out



def ln_stats(b, src, n, stat, scratch, s_sc, s_v):
    cfg = b.cfg
    def fsc(sc):
        s_v.wait(sc)
        s_sc.inc(sc.activation(out=scratch[:, 0:n], in_=src[:, 0:n], func=AF.Square,
                               accum_out=stat[:, 0:1]), 1)
    b.on("scalar", fsc)
    def fv(v):
        s_v.wait(v)
        s_v.inc(v.reduce_sum(out=stat[:, 1:2], in_=src[:, 0:n], axis=AX.X), 1)
        s_v.wait(v)
        s_v.inc(v.tensor_scalar(out=stat[:, 2:3], in0=stat[:, 1:2], scalar1=1.0 / n,
                                scalar2=None, op0=ALU.mult), 1)
        s_sc.wait(v)
        s_v.inc(v.tensor_scalar(out=stat[:, 3:4], in0=stat[:, 0:1], scalar1=1.0 / n,
                                scalar2=None, op0=ALU.mult), 1)
        s_v.wait(v)
        s_v.inc(v.tensor_tensor(out=stat[:, 4:5], in0=stat[:, 2:3], in1=stat[:, 2:3],
                                op=ALU.mult), 1)
        s_v.wait(v)
        s_v.inc(v.tensor_tensor(out=stat[:, 5:6], in0=stat[:, 3:4], in1=stat[:, 4:5],
                                op=ALU.subtract), 1)
        s_v.wait(v)
        s_v.inc(v.tensor_scalar(out=stat[:, 5:6], in0=stat[:, 5:6], scalar1=float(cfg.EPS),
                                scalar2=None, op0=ALU.add), 1)
    b.on("vector", fv)
    def fsc2(sc):
        s_v.wait(sc)
        s_sc.inc(sc.activation(out=stat[:, 6:7], in_=stat[:, 5:6], func=AF.Ln), 1)
        s_sc.wait(sc)
        s_sc.inc(sc.activation(out=stat[:, 7:8], in_=stat[:, 6:7], func=AF.Exp,
                               scale=-0.5), 1)
    b.on("scalar", fsc2)


def ln_apply(b, src, dst, n, stat, s_sc, s_v, gbc=None, bbc=None):
    def fv(v):
        s_sc.wait(v)
        s_v.inc(v.tensor_scalar(out=dst[:, 0:n], in0=src[:, 0:n],
                                scalar1=stat[:, 2:3], scalar2=stat[:, 7:8],
                                op0=ALU.subtract, op1=ALU.mult), 1)
        if gbc is not None:
            s_v.wait(v)
            s_v.inc(v.tensor_tensor(out=dst[:, 0:n], in0=dst[:, 0:n], in1=gbc[:, 0:n],
                                    op=ALU.mult), 1)
            s_v.wait(v)
            s_v.inc(v.tensor_tensor(out=dst[:, 0:n], in0=dst[:, 0:n], in1=bbc[:, 0:n],
                                    op=ALU.add), 1)
    b.on("vector", fv)


def bcast_rows(b, row_sb, out_sb, n, s_t, s_v, bcps):
    for i in range(cdiv(n, 512)):
        c0, c1 = i * 512, min(n, (i + 1) * 512)
        def ft(t, c0=c0, c1=c1):
            s_v.wait(t)
            s_t.inc(t.matmul(bcps[:, 0:c1 - c0], b.t["ONES_row_f32"][:, :],
                             row_sb[:, c0:c1], start=True, stop=True), 1)
        b.on("tensor", ft)
        def fv(v, c0=c0, c1=c1):
            s_t.wait(v)
            s_v.inc(v.tensor_copy(out=out_sb[:, c0:c1], in_=bcps[:, 0:c1 - c0]), 1)
        b.on("vector", fv)


def load_fm(b, dst_sb, src_dram, nchunks, tcols, s_d, wait_sems=()):
    def fs(s):
        for sm, val in wait_sems:
            sm.wait(s, val)
        for fc in range(nchunks):
            s_d.inc(s.dma_start(out=dst_sb[:, fc * tcols:(fc + 1) * tcols],
                                in_=src_dram[:, fc * 128:(fc + 1) * 128], transpose=True), 16)
    b.on("sync", fs)



def proj_fm_out(b, wsrc_dram, wcol0, ncols, dst_dram, drow0, s_g, s_t, s_v, s_d,
                xnfm, prj, stg, wslots, wload_snap, scale_cols=None, bias_pp=None):
    raise NotImplementedError



def build(cfg: CFG, upto="D"):
    b = B(cfg)
    nc = b.nc
    import ml_dtypes
    D, T, NT, DC, NG, NH = cfg.D, cfg.T, cfg.NT, cfg.DC, cfg.NG, cfg.NH
    FF, FC, LR = cfg.FF, cfg.FC, cfg.LR

    x_in = b.dr("x", [T, D], F32, kind="ExternalInput")
    msk_in = b.dr("masks", [4, 4, 128, 512], BF16, kind="ExternalInput")
    wnames = [
        ("L0_attn_qkv", [D, 3 * D]), ("L0_attn_o_proj", [D, D]),
        ("L0_attn_gamma", [D]), ("L0_attn_beta", [D]),
        ("L0_ffn_simple_bias", [D]), ("L0_ffn_w_down", [LR, D]), ("L0_ffn_w_up", [D, LR]),
        ("L0_ffn_w_router", [3, D]), ("L0_ffn_ln1_scale", [D]), ("L0_ffn_ln1_bias", [D]),
        ("L0_ffn_wq", [D, D]), ("L0_ffn_wk", [D, D]), ("L0_ffn_wv", [D, D]),
        ("L0_ffn_wo", [D, D]), ("L0_ffn_ln2_scale", [D]), ("L0_ffn_ln2_bias", [D]),
        ("L0_ffn_w_ff1", [FF, D]), ("L0_ffn_w_ff2", [D, FF]),
        ("L1_attn_w_qkv", [D, 3 * D]), ("L1_attn_b_qkv", [3 * D]),
        ("L1_attn_w_out", [D, D]), ("L1_attn_b_out", [D]),
        ("L1_ffn_w_gate", [D, FF]), ("L1_ffn_w_up", [D, FF]), ("L1_ffn_w_down", [FF, D]),
    ]
    w = {}
    for name, shape in wnames:
        w[name] = b.dr(name, shape, F32, kind="ExternalInput")
    y_out = b.dr("y", [T, D], F32, kind="ExternalOutput")

    X = b.dr("X", [T, D], F32)
    XNBF = b.dr("XNBF", [T, D], BF16)
    QF = b.dr("QF", [D, T], BF16)
    KLOC = b.dr("KLOC", [D, T], BF16)
    VLOC = b.dr("VLOC", [T, D], BF16)
    KALL = b.dr("KALL", [4 * D, T], BF16)
    VALL = b.dr("VALL", [4 * T, D], BF16)
    CTXBF = b.dr("CTXBF", [T, D], BF16)
    KN2L = b.dr("KN2L", [1, 128], F32)
    KN2G = b.dr("KN2G", [1, 128], F32)
    HS = b.dr("HS", [T, D], F32)
    PART1 = b.dr("PART1", [T, D], F32)
    XPRIME = b.dr("XPRIME", [T, D], F32)
    XN2BF = b.dr("XN2BF", [T, D], BF16)
    HBF = b.dr("HBF", [T, FF], BF16)
    WQB = b.dr("WQB", [D, D], BF16)
    WKB = b.dr("WKB", [D, D], BF16)
    WVB = b.dr("WVB", [D, D], BF16)
    WOB = b.dr("WOB", [D, D], BF16)
    F1B = b.dr("F1B", [FF, D], BF16)
    F2B = b.dr("F2B", [D, FF], BF16)
    WDB = b.dr("WDB", [LR, D], BF16)
    WUB = b.dr("WUB", [D, LR], BF16)

    idbf = nc.inline_tensor(np.eye(128, dtype=ml_dtypes.bfloat16), "c_idbf").ap()
    idf32 = nc.inline_tensor(np.eye(128, dtype=np.float32), "c_idf32").ap()
    b.sb(b.es, "ID_BF", [128, 128], BF16)
    b.sb(b.es, "ID_F32", [128, 128], F32)
    b.sb(b.es, "ONES_col_bf", [128, 1], BF16)
    b.sb(b.es, "ONES_row_f32", [1, 128], F32)
    b.sb(b.es, "MSK", [128, 16 * 512], BF16)
    b.sb(b.es, "ANMAX", [1, 2], F32)
    b.sb(b.es, "BNMAX", [1, 2 * NH], F32)
    b.sb(b.es, "CNMAX", [1, 2 * NH], F32)
    b.sb(b.es, "RM0", [128, NT], F32)
    b.sb(b.es, "RM1", [128, NT], F32)
    b.sb(b.es, "RM2", [128, NT], F32)

    s_g = b.sem("s_g"); s_d = b.sem("s_d"); s_e = b.sem("s_e")
    s_t = b.sem("s_t"); s_v = b.sem("s_v"); s_sc = b.sem("s_sc")
    s_p = b.sem("s_p"); s_cc = b.sem("s_cc"); s_o = b.sem("s_o")
    s_pc = b.sem("B_pc")
    s_y = b.sem("s_y")
    S = (s_g, s_d, s_e, s_t, s_v, s_sc, s_p, s_o)

    def fin_copy(xsem_names):
        seg = contextlib.ExitStack()
        blk = seg.enter_context(nc.Block("fin"))
        b.blk = blk
        emit_seg_fence(b)
        fin = b.sb(seg, "FIN", [128, D], F32)
        def f(s):
            for nm in xsem_names:
                if nm in b.sems:
                    b.sems[nm].wait(s)
            for i in range(NT):
                s_d.inc(s.dma_start(out=fin[:, :], in_=X[i * 128:(i + 1) * 128, :]), 16)
                s_d.wait(s)
                s_y.inc(s.dma_start(out=y_out[i * 128:(i + 1) * 128, :], in_=fin[:, :]), 16)
                s_y.wait(s)
        b.on("sync", f)
        seg.close()

    with b.es:
        seg = contextlib.ExitStack()
        blk = seg.enter_context(nc.Block("seg1"))
        b.blk = blk
        emit_seg_fence(b)
        def f(s):
            s_d.inc(s.dma_start(out=b.t["ID_BF"][:, :], in_=idbf[:, :]), 16)
            s_d.inc(s.dma_start(out=b.t["ID_F32"][:, :], in_=idf32[:, :]), 16)
            s_d.inc(s.dma_start(
                out=b.t["MSK"][:, :].rearrange("p (s n) -> p s n", n=512),
                in_=msk_in.rearrange("d r p n -> p (d r) n")), 16)
        b.on("sync", f)
        def f(g):
            g.memset(b.t["ONES_col_bf"][:, :], 1.0)
            s_p.inc(g.memset(b.t["ONES_row_f32"][:, :], 1.0), 1)
        b.on("gpsimd", f)
        emit_attnA(b, seg, x_in, w, X, XNBF, QF, KLOC, VLOC, KN2L, s_list=S)
        if upto != "A":
            def f(g):
                for src_n, dst in (("L0_ffn_wq", WQB), ("L0_ffn_wk", WKB),
                                   ("L0_ffn_wv", WVB), ("L0_ffn_wo", WOB),
                                   ("L0_ffn_w_ff1", F1B), ("L0_ffn_w_ff2", F2B),
                                   ("L0_ffn_w_down", WDB), ("L0_ffn_w_up", WUB)):
                    s_pc.inc(g.dma_start(out=dst[:, :], in_=b.dram[src_n][:, :]), 16)
            b.on("gpsimd", f)
        seg.close()

        emit_gather3(b, KLOC, VLOC, KN2L, KALL, VALL, KN2G,
                     wait_names=["Aqk_stg", "Aqk_stgv", "s_d"])

        seg = contextlib.ExitStack()
        blk = seg.enter_context(nc.Block("seg2"))
        b.blk = blk
        emit_seg_fence(b)
        sub = contextlib.ExitStack()
        emit_flash_L0(b, sub, QF, KALL, VALL, KN2G, CTXBF, s_list=S)
        sub.close()
        emit_oproj_resid(b, seg, CTXBF, w["L0_attn_o_proj"], X, s_list=S,
                         tag="A5", prev_x_sem="A_xst", src_store_sem="fl_cts",
                         dst=X, dst_sem_name="op_xw")
        seg.close()

        if upto == "A":
            fin_copy(["op_xw"])
            return b

        seg = contextlib.ExitStack()
        blk = seg.enter_context(nc.Block("seg3"))
        b.blk = blk
        emit_seg_fence(b)
        emit_B1(b, seg, X, w, HS, PART1, XNBF, s_list=S)
        seg.close()

        seg = contextlib.ExitStack()
        blk = seg.enter_context(nc.Block("seg4"))
        b.blk = blk
        emit_seg_fence(b)
        emit_qkv_proj(b, seg, XNBF, None, None, QF, KLOC, VLOC, s_list=S,
                      interleaved=False, tag="B", nb_sem="B1_nb",
                      wsrcT={"q": WQB, "k": WKB, "v": WVB})
        emit_norms(b, seg, QF, KLOC, KN2L, s_list=S, per_head=True, tag="B")
        seg.close()

        emit_gather3(b, KLOC, VLOC, KN2L, KALL, VALL, KN2G,
                     wait_names=["Bqk_stg", "Bqk_stgv", "s_d"])

        seg = contextlib.ExitStack()
        blk = seg.enter_context(nc.Block("seg5"))
        b.blk = blk
        emit_seg_fence(b)
        ctxfm = b.sb(seg, "BCTXFM", [128, NH * T], BF16)
        sub = contextlib.ExitStack()
        emit_flash_MHA(b, sub, QF, KALL, VALL, KN2G, ctxfm, "BNMAX", s_list=S, tag="BF")
        sub.close()
        emit_oproj_ln2(b, seg, ctxfm, WOB, HS, XPRIME, XN2BF, w, s_list=S)
        seg.close()

        seg = contextlib.ExitStack()
        blk = seg.enter_context(nc.Block("seg6"))
        b.blk = blk
        emit_seg_fence(b)
        sub = contextlib.ExitStack()
        b.sems["B6_nbsrc"] = b.sems["B5_nb"]
        def w1load_B(s, ip, dst, sem, which):
            if ip == 0:
                b.sems["B_pc"].wait(s)
            for fc in range(DC):
                sem.inc(s.dma_start(out=dst[:, fc * 512:(fc + 1) * 512],
                                    in_=F1B[ip * 512:(ip + 1) * 512, fc * 128:(fc + 1) * 128],
                                    transpose=True), 16)
        emit_ffn_up(b, sub, XN2BF, HBF, s_list=S, tag="B6", mode="gelu",
                    wload_up=w1load_B)
        sub.close()
        emit_local_fence(b)
        xpb = [b.sb(seg, "B6XPa", [128, 512], F32), b.sb(seg, "B6XPb", [128, 512], F32)]
        xcb = [b.sb(seg, "B6XCa", [128, 512], F32), b.sb(seg, "B6XCb", [128, 512], F32)]
        ptb = [b.sb(seg, "B6PTa", [128, 512], F32), b.sb(seg, "B6PTb", [128, 512], F32)]
        tb1 = b.sb(seg, "B6T1", [128, 512], F32)
        tb2 = b.sb(seg, "B6T2", [128, 512], F32)
        stb = b.sb(seg, "B6ST", [128, 512], F32)
        s_x3 = [b.sem("B6_x3a"), b.sem("B6_x3b")]
        s_xw3 = b.sem("B6_xw")
        ecnt = [0]
        def evacB(tt, oc, psum):
            par = ecnt[0] % 2
            ecnt[0] += 1
            def f(s, tt=tt, oc=oc, par=par):
                s_v.wait(s)
                if tt == 0 and oc == 0:
                    b.sems["B5_xp"].wait(s)
                    b.sems["B1_p1"].wait(s)
                s_x3[par].inc(s.dma_start(
                    out=xpb[par][:, :],
                    in_=XPRIME[tt * 128:(tt + 1) * 128, oc * 512:(oc + 1) * 512]), 16)
                s_x3[par].inc(s.dma_start(
                    out=xcb[par][:, :],
                    in_=X[tt * 128:(tt + 1) * 128, oc * 512:(oc + 1) * 512]), 16)
                s_x3[par].inc(s.dma_start(
                    out=ptb[par][:, :],
                    in_=PART1[tt * 128:(tt + 1) * 128, oc * 512:(oc + 1) * 512]), 16)
            b.on("sync", f)
            xl = s_x3[par].n
            def f(v, tt=tt, oc=oc, par=par, psum=psum, xl=xl):
                s_t.wait(v)
                s_x3[par].wait(v, xl)
                s_xw3.wait(v)
                s_v.inc(v.tensor_tensor(out=tb1[:, :], in0=psum[:, :],
                                        in1=xpb[par][:, :], op=ALU.add), 1)
                s_v.inc(v.tensor_tensor(out=tb2[:, :], in0=xcb[par][:, :],
                                        in1=ptb[par][:, :], op=ALU.add), 1)
                s_v.wait(v)
                s_v.inc(v.scalar_tensor_tensor(
                    out=stb[:, :], in0=tb1[:, :], scalar=b.t["RM2"][:, tt:tt + 1],
                    in1=tb2[:, :], op0=ALU.mult, op1=ALU.add), 1)
            b.on("vector", f)
            def f(s, tt=tt, oc=oc):
                s_v.wait(s)
                s_xw3.inc(s.dma_start(
                    out=X[tt * 128:(tt + 1) * 128, oc * 512:(oc + 1) * 512],
                    in_=stb[:, :]), 16)
            b.on("sync", f)
        def w2load_B2(s, oc, w2p, sem):
            for ic in range(FC):
                sem.inc(s.dma_start(out=w2p[:, ic * 512:(ic + 1) * 512],
                                    in_=F2B[oc * 512:(oc + 1) * 512, ic * 128:(ic + 1) * 128],
                                    transpose=True), 16)
        emit_ffn_down(b, seg, HBF, w2load_B2, evacB, s_list=S, tag="B6d",
                      loader_eng="sync", h_store_sem="B6_hst")
        seg.close()

        if upto == "B":
            fin_copy(["B6_xw"])
            return b

        seg = contextlib.ExitStack()
        blk = seg.enter_context(nc.Block("seg7"))
        b.blk = blk
        emit_seg_fence(b)
        emit_ln_simple(b, seg, X, XNBF, s_list=S, tag="C1", prev_x_sem="B6_xw")
        vb = b.sb(seg, "CVBC", [128, D], F32)
        vrow = b.sb(seg, "CVROW", [1, D], F32)
        s_vb = b.sem("C_vb")
        with nc.psum_tensor("CVBPS", [128, 512], F32) as bcps:
            def f(s):
                for h in range(NH):
                    s_vb.inc(s.dma_start(out=vrow[0:1, h * 128:(h + 1) * 128],
                                         in_=w["L1_attn_b_qkv"][None, 384 * h + 256:384 * h + 384]), 16)
            b.on("sync", f)
            def f(t):
                s_vb.wait(t)
            b.on("tensor", f)
            bcast_rows(b, vrow[0:1, :], vb, D, s_t, s_v, bcps)
        emit_qkv_proj(b, seg, XNBF, w["L1_attn_w_qkv"], w["L1_attn_b_qkv"],
                      QF, KLOC, VLOC, s_list=S, interleaved=True, tag="C",
                      nb_sem="C1_nb", vbias_bc=vb)
        emit_norms(b, seg, QF, KLOC, KN2L, s_list=S, per_head=True, tag="C")
        seg.close()

        emit_gather3(b, KLOC, VLOC, KN2L, KALL, VALL, KN2G,
                     wait_names=["Cqk_stg", "Cqk_stgv", "s_d"])

        seg = contextlib.ExitStack()
        blk = seg.enter_context(nc.Block("seg8"))
        b.blk = blk
        emit_seg_fence(b)
        ctxfm2 = b.sb(seg, "CCTXFM", [128, NH * T], BF16)
        sub = contextlib.ExitStack()
        emit_flash_MHA(b, sub, QF, KALL, VALL, KN2G, ctxfm2, "CNMAX", s_list=S, tag="CF")
        sub.close()
        emit_cproj(b, seg, ctxfm2, w["L1_attn_w_out"], w["L1_attn_b_out"], X, s_list=S)
        seg.close()

        if upto == "C":
            fin_copy(["C5_xw"])
            return b

        seg = contextlib.ExitStack()
        blk = seg.enter_context(nc.Block("seg9"))
        b.blk = blk
        emit_seg_fence(b)
        emit_ln_simple(b, seg, X, XNBF, s_list=S, tag="D1", prev_x_sem="C5_xw")
        sub = contextlib.ExitStack()
        b.sems["D2_nbsrc"] = b.sems["D1_nb"]
        def wgu_load(g, ip, dst, sem, which):
            wsrc = w["L1_ffn_w_gate"] if which == "gate" else w["L1_ffn_w_up"]
            sem.inc(g.dma_start(
                out=dst[:, :].rearrange("p (c n) -> p c n", n=512),
                in_=wsrc[:, ip * 512:(ip + 1) * 512].rearrange("(c p) n -> p c n", p=128)), 16)
        emit_ffn_up(b, sub, XNBF, HBF, s_list=S, tag="D2", mode="gateup24",
                    wload_up=wgu_load)
        sub.close()
        emit_local_fence(b)
        xcd = [b.sb(seg, "DXCa", [128, 512], F32), b.sb(seg, "DXCb", [128, 512], F32)]
        yod = b.sb(seg, "DYO", [128, 512], F32)
        s_x4 = [b.sem("D_x4a"), b.sem("D_x4b")]
        ecnt2 = [0]
        def w2load_D(g, oc, w2p, sem):
            sem.inc(g.dma_start(
                out=w2p[:, :].rearrange("p (c n) -> p c n", n=512),
                in_=w["L1_ffn_w_down"][:, oc * 512:(oc + 1) * 512]
                    .rearrange("(c p) n -> p c n", p=128)), 16)
        def evacD(tt, oc, psum):
            par = ecnt2[0] % 2
            ecnt2[0] += 1
            def f(s, tt=tt, oc=oc, par=par):
                s_v.wait(s)
                s_x4[par].inc(s.dma_start(
                    out=xcd[par][:, :],
                    in_=X[tt * 128:(tt + 1) * 128, oc * 512:(oc + 1) * 512]), 16)
            b.on("sync", f)
            xl = s_x4[par].n
            def f(v, tt=tt, oc=oc, par=par, psum=psum, xl=xl):
                s_t.wait(v)
                s_x4[par].wait(v, xl)
                s_y.wait(v)
                s_v.inc(v.tensor_tensor(out=yod[:, :], in0=psum[:, :],
                                        in1=xcd[par][:, :], op=ALU.add), 1)
            b.on("vector", f)
            def f(s, tt=tt, oc=oc):
                s_v.wait(s)
                s_y.inc(s.dma_start(
                    out=y_out[tt * 128:(tt + 1) * 128, oc * 512:(oc + 1) * 512],
                    in_=yod[:, :]), 16)
            b.on("sync", f)
        emit_ffn_down(b, seg, HBF, w2load_D, evacD, s_list=S, tag="D3",
                      loader_eng="gpsimd", h_store_sem="D2_hst")
        def f(s):
            s_y.wait(s)
        b.on("sync", f)
        seg.close()
    return b


def emit_local_fence(b, engs=("sync", "gpsimd", "scalar", "vector", "tensor")):
    snap = [(s.h, s.n) for s in b.sems.values() if s.n > 0]
    for eng in engs:
        def f(e, snap=snap):
            for h, n in snap:
                e.wait_ge(h, n)
        b.on(eng, f)


def emit_seg_fence(b):
    snap = [(s.h, s.n) for s in b.sems.values() if s.n > 0]
    for eng in ("sync", "gpsimd", "scalar", "vector", "tensor"):
        def f(e, snap=snap):
            for h, n in snap:
                e.wait_ge(h, n)
        b.on(eng, f)


def emit_ln_simple(b, seg, X, XNBF, s_list, tag, prev_x_sem):
    (s_g, s_d, s_e, s_t, s_v, s_sc, s_p, s_o) = s_list
    nc, cfg = b.nc, b.cfg
    D, NT = b.cfg.D, b.cfg.NT
    xt = b.sb(seg, tag + "XT", [128, D], F32)
    scr = b.sb(seg, tag + "SCR", [128, D], F32)
    stat = b.sb(seg, tag + "STAT", [128, 8], F32)
    xnb = b.sb(seg, tag + "XNB", [128, D], BF16)
    s_xt = b.sem(tag + "_xt")
    s_nb = b.sem(tag + "_nb")
    for i in range(NT):
        def f(s, i=i):
            s_v.wait(s); s_sc.wait(s)
            if i == 0 and prev_x_sem in b.sems:
                b.sems[prev_x_sem].wait(s)
            s_xt.inc(s.dma_start(out=xt[:, :], in_=X[i * 128:(i + 1) * 128, :]), 16)
        b.on("sync", f)
        def f(v):
            s_xt.wait(v)
        b.on("vector", f)
        def f(sc):
            s_xt.wait(sc)
        b.on("scalar", f)
        ln_stats(b, xt, D, stat, scr, s_sc, s_v)
        ln_apply(b, xt, scr, D, stat, s_sc, s_v)
        def f(v):
            s_v.wait(v)
            s_nb.wait(v)
            s_v.inc(v.tensor_copy(out=xnb[:, :], in_=scr[:, :]), 1)
        b.on("vector", f)
        def f(s, i=i):
            s_v.wait(s)
            s_nb.inc(s.dma_start(out=XNBF[i * 128:(i + 1) * 128, :], in_=xnb[:, :]), 16)
        b.on("sync", f)


def emit_cproj(b, seg, ctxfm, w_out, b_out, X, s_list):
    (s_g, s_d, s_e, s_t, s_v, s_sc, s_p, s_o) = s_list
    nc, cfg = b.nc, b.cfg
    D, T, NT, DC, NH = cfg.D, cfg.T, cfg.NT, cfg.DC, cfg.NH

    emit_local_fence(b)
    bo = b.sb(seg, "CBOBC", [128, D], F32)
    rowt = b.sb(seg, "C5ROW", [1, D], F32)
    s_br = b.sem("C5_br")
    with nc.psum_tensor("C5BCPS", [128, 512], F32) as bcps:
        def f(s):
            s_br.inc(s.dma_start(out=rowt[0:1, :], in_=b_out[None, :]), 16)
        b.on("sync", f)
        def f(t):
            s_br.wait(t)
        b.on("tensor", f)
        bcast_rows(b, rowt[0:1, :], bo, D, s_t, s_v, bcps)

    NPAN = D // 512
    owp = b.sb(seg, "C5WP", [128, NPAN * DC * 512], BF16)
    prj = [b.ps(seg, "C5P0", [128, 512]), b.ps(seg, "C5P1", [128, 512])]
    xrt = b.sb(seg, "C5XR", [128, D], F32)
    xwt = b.sb(seg, "C5XW", [128, D], F32)
    s_ow = b.sem("C5_ow")
    s_xr = b.sem("C5_xr")
    s_xw = b.sem("C5_xw")
    for p in range(NPAN):
        def f(g, p=p):
            s_ow.inc(g.dma_start(
                out=owp[:, p * DC * 512:(p + 1) * DC * 512]
                    .rearrange("p (c n) -> p c n", n=512),
                in_=w_out[:, p * 512:(p + 1) * 512].rearrange("(c p) n -> p c n", p=128)), 16)
        b.on("gpsimd", f)
    owdone = s_ow.n
    prjfree = [0, 0]
    for tt in range(NT):
        def f(s, tt=tt):
            s_v.wait(s)
            if tt == 0:
                b.sems["B6_xw"].wait(s)
            s_xr.inc(s.dma_start(out=xrt[:, :], in_=X[tt * 128:(tt + 1) * 128, :]), 16)
        b.on("sync", f)
        xload = s_xr.n
        for p in range(NPAN):
            bank = p % 2
            def f(t, tt=tt, p=p, bank=bank):
                if tt == 0 and p == 0:
                    s_ow.wait(t, owdone)
                    s_v.wait(t)
                s_v.wait(t, prjfree[bank])
                for h in range(NH):
                    s_t.inc(t.matmul(
                        prj[bank][:, :],
                        ctxfm[:, h * T + tt * 128: h * T + (tt + 1) * 128],
                        owp[:, p * DC * 512 + h * 512: p * DC * 512 + (h + 1) * 512],
                        start=(h == 0), stop=(h == NH - 1)), 1)
            b.on("tensor", f)
            mmdone = s_t.n
            def f(v, p=p, bank=bank, xload=xload, tt=tt):
                s_t.wait(v, mmdone)
                s_xr.wait(v, xload)
                if p == 0:
                    s_xw.wait(v)
                s_v.inc(v.tensor_tensor(out=xwt[:, p * 512:(p + 1) * 512],
                                        in0=prj[bank][:, :],
                                        in1=bo[:, p * 512:(p + 1) * 512], op=ALU.add), 1)
                s_v.wait(v)
                s_v.inc(v.tensor_tensor(out=xwt[:, p * 512:(p + 1) * 512],
                                        in0=xwt[:, p * 512:(p + 1) * 512],
                                        in1=xrt[:, p * 512:(p + 1) * 512], op=ALU.add), 1)
            b.on("vector", f)
            prjfree[bank] = s_v.n
        def f(s, tt=tt):
            s_v.wait(s)
            s_xw.inc(s.dma_start(out=X[tt * 128:(tt + 1) * 128, :], in_=xwt[:, :]), 16)
        b.on("sync", f)


GROUPS = [[0, 1, 2, 3], [4, 5, 6, 7]]


def emit_attnA(b, seg, x_in, w, X, XNBF, QF, KLOC, VLOC, KN2L, s_list):
    (s_g, s_d, s_e, s_t, s_v, s_sc, s_p, s_o) = s_list
    nc, cfg = b.nc, b.cfg
    D, T, NT, DC = b.cfg.D, b.cfg.T, b.cfg.NT, b.cfg.DC

    rowa = b.sb(seg, "ROWA", [1, D], F32)
    rowb = b.sb(seg, "ROWB", [1, D], F32)
    g0 = b.sb(seg, "G0BC", [128, D], F32)
    b0 = b.sb(seg, "B0BC", [128, D], F32)
    bcps = b.ps(seg, "BCPS", [128, 512])
    def f(s):
        s_d.inc(s.dma_start(out=rowa[0:1, :], in_=w["L0_attn_gamma"][None, :]), 16)
        s_d.inc(s.dma_start(out=rowb[0:1, :], in_=w["L0_attn_beta"][None, :]), 16)
    b.on("sync", f)
    def f(t):
        s_d.wait(t); s_p.wait(t)
    b.on("tensor", f)
    bcast_rows(b, rowa[0:1, :], g0, D, s_t, s_v, bcps)
    bcast_rows(b, rowb[0:1, :], b0, D, s_t, s_v, bcps)

    s_xt = b.sem("A_xt")
    s_xst = b.sem("A_xst")
    s_nbst = b.sem("A_nbst")
    xt = b.sb(seg, "XT", [128, D], F32)
    scr = b.sb(seg, "SCR", [128, D], F32)
    stat = b.sb(seg, "STAT", [128, 8], F32)
    xn1 = b.sb(seg, "XN1", [128, D], F32)
    xnb = b.sb(seg, "XNB", [128, D], BF16)
    for i in range(NT):
        def f(s, i=i):
            s_v.wait(s)
            s_sc.wait(s)
            s_xt.inc(s.dma_start(out=xt[:, :], in_=x_in[i * 128:(i + 1) * 128, :]), 16)
        b.on("sync", f)
        def f(v):
            s_xt.wait(v)
        b.on("vector", f)
        def f(sc):
            s_xt.wait(sc)
        b.on("scalar", f)
        ln_stats(b, xt, D, stat, scr, s_sc, s_v)
        ln_apply(b, xt, xn1, D, stat, s_sc, s_v)
        def f(v):
            s_v.wait(v)
            s_v.inc(v.tensor_tensor(out=scr[:, :], in0=xt[:, :], in1=xn1[:, :],
                                    op=ALU.add), 1)
        b.on("vector", f)
        def f(s, i=i):
            s_v.wait(s)
            s_xst.inc(s.dma_start(out=X[i * 128:(i + 1) * 128, :], in_=scr[:, :]), 16)
        b.on("sync", f)
        def f(sc):
            s_xst.wait(sc)
        b.on("scalar", f)
        ln_stats(b, xn1, D, stat, scr, s_sc, s_v)
        ln_apply(b, xn1, xn1, D, stat, s_sc, s_v, gbc=b.t["G0BC"], bbc=b.t["B0BC"])
        def f(v):
            s_v.wait(v)
            s_nbst.wait(v)
            s_v.inc(v.tensor_copy(out=xnb[:, :], in_=xn1[:, :]), 1)
        b.on("vector", f)
        def f(s, i=i):
            s_v.wait(s)
            s_nbst.inc(s.dma_start(out=XNBF[i * 128:(i + 1) * 128, :], in_=xnb[:, :]), 16)
        b.on("sync", f)

    emit_qkv_proj(b, seg, XNBF, w["L0_attn_qkv"], None, QF, KLOC, VLOC, s_list,
                  interleaved=False, tag="A", nb_sem="A_nbst")

    emit_norms(b, seg, QF, KLOC, KN2L, s_list, per_head=False, tag="A")


def emit_qkv_proj(b, seg, XNBF, wqkv, bqkv, QF, KLOC, VLOC, s_list, interleaved,
                  tag="A", nb_sem="A_nbst", wsrcT=None, vbias_bc=None):
    (s_g, s_d, s_e, s_t, s_v, s_sc, s_p, s_o) = s_list
    nc, cfg = b.nc, b.cfg
    D, T, NT, DC, NH = cfg.D, cfg.T, cfg.NT, cfg.DC, cfg.NH

    s_fm = b.sem(tag + "qk_fm")
    s_wp = [b.sem(tag + "qk_wp0"), b.sem(tag + "qk_wp1")]
    s_stg = b.sem(tag + "qk_stg")
    s_stgv = b.sem(tag + "qk_stgv")
    xnfm = b.sb(seg, tag + "XN_FM", [128, DC * T], BF16)
    nbst = b.sems.get(nb_sem)
    waits = [(nbst, nbst.n)] if nbst is not None else []
    load_fm(b, xnfm, XNBF, DC, T, s_fm, wait_sems=waits)
    xnfm_ready_fm = s_fm.n

    wp = [b.sb(seg, tag + "WP0", [128, DC * 512], BF16),
          b.sb(seg, tag + "WP1", [128, DC * 512], BF16)]
    prj = [b.ps(seg, tag + "PRJ0", [128, 512]), b.ps(seg, tag + "PRJ1", [128, 512])]
    stg = b.sb(seg, tag + "STG", [128, T], BF16)
    stgv = b.sb(seg, tag + "STGV", [128, 512], BF16)
    bq = None
    if bqkv is not None:
        bq = b.sb(seg, tag + "BQPP", [128, 3 * D // 128], F32)
        emit_bias_cols(b, seg, bqkv, bq, s_list, tag)

    NPAN = 3 * D // 512
    panload = [None] * NPAN
    panfree = [0, 0]
    prjfree = [0, 0]
    stg_rd = [0]
    xnfm_ready = xnfm_ready_fm

    for p in range(NPAN):
        slot = p % 2
        if wsrcT is None:
            def f(g, p=p, slot=slot):
                s_t.wait(g, panfree[slot])
                s_wp[slot].inc(g.dma_start(
                    out=wp[slot][:, :].rearrange("p (c n) -> p c n", n=512),
                    in_=wqkv[:, p * 512:(p + 1) * 512].rearrange("(c p) n -> p c n", p=128)), 16)
            b.on("gpsimd", f)
        else:
            c0 = p * 512
            role = "q" if c0 < D else ("k" if c0 < 2 * D else "v")
            r0 = c0 - (0 if role == "q" else (D if role == "k" else 2 * D))
            def f(s, p=p, slot=slot, role=role, r0=r0):
                s_t.wait(s, panfree[slot])
                if p == 0:
                    b.sems["B_pc"].wait(s)
                for fc in range(DC):
                    s_wp[slot].inc(s.dma_start(
                        out=wp[slot][:, fc * 512:(fc + 1) * 512],
                        in_=wsrcT[role][r0:r0 + 512, fc * 128:(fc + 1) * 128],
                        transpose=True), 16)
            b.on("sync", f)
        panload[p] = s_wp[slot].n

        for cc in range(4):
            col0 = p * 512 + cc * 128
            if not interleaved:
                if col0 < D:
                    role, drow = "q", col0
                elif col0 < 2 * D:
                    role, drow = "k", col0 - D
                else:
                    role, drow = "v", col0 - 2 * D
            else:
                h, off = divmod(col0, 384)
                if off < 128:
                    role, drow = "q", h * 128 + off
                elif off < 256:
                    role, drow = "k", h * 128 + (off - 128)
                else:
                    role, drow = "v", h * 128 + (off - 256)
            if role in ("q", "k"):
                dst = QF if role == "q" else KLOC
                for tg in range(T // 512):
                    bank = tg % 2
                    def f(t, p=p, cc=cc, tg=tg, bank=bank, slot=slot):
                        s_wp[slot].wait(t, panload[p])
                        if p == 0 and cc == 0 and tg == 0:
                            s_fm.wait(t, xnfm_ready)
                            s_v.wait(t)
                        s_v.wait(t, prjfree[bank])
                        for fc in range(DC):
                            s_t.inc(t.matmul(
                                prj[bank][:, :],
                                wp[slot][:, fc * 512 + cc * 128: fc * 512 + (cc + 1) * 128],
                                xnfm[:, fc * T + tg * 512: fc * T + (tg + 1) * 512],
                                start=(fc == 0), stop=(fc == DC - 1)), 1)
                    b.on("tensor", f)
                    mmdone = s_t.n
                    def f(v, tg=tg, bank=bank, drow=drow):
                        s_t.wait(v, mmdone)
                        if tg == 0:
                            s_stg.wait(v, stg_rd[0])
                        if bq is not None:
                            s_v.inc(v.tensor_scalar(
                                out=stg[:, tg * 512:(tg + 1) * 512], in0=prj[bank][:, :],
                                scalar1=bq[:, drow // 128: drow // 128 + 1], scalar2=None,
                                op0=ALU.add), 1)
                        else:
                            s_v.inc(v.tensor_copy(out=stg[:, tg * 512:(tg + 1) * 512],
                                                  in_=prj[bank][:, :]), 1)
                    b.on("vector", f)
                    prjfree[bank] = s_v.n
                def f(s, dst=dst, drow=drow):
                    s_v.wait(s)
                    s_stg.inc(s.dma_start(out=dst[drow:drow + 128, :], in_=stg[:, :]), 16)
                b.on("sync", f)
                stg_rd[0] = s_stg.n
            else:
                pass
        if (not interleaved) and p * 512 >= 2 * D:
            col0 = p * 512 - 2 * D
            for tt in range(NT):
                bank = tt % 2
                def f(t, p=p, tt=tt, bank=bank, slot=slot):
                    s_wp[slot].wait(t, panload[p])
                    s_v.wait(t, prjfree[bank])
                    for fc in range(DC):
                        s_t.inc(t.matmul(
                            prj[bank][:, :],
                            xnfm[:, fc * T + tt * 128: fc * T + (tt + 1) * 128],
                            wp[slot][:, fc * 512:(fc + 1) * 512],
                            start=(fc == 0), stop=(fc == DC - 1)), 1)
                b.on("tensor", f)
                mmdone = s_t.n
                def f(v, bank=bank, col0=col0):
                    s_t.wait(v, mmdone)
                    s_stgv.wait(v)
                    if vbias_bc is not None:
                        s_v.inc(v.tensor_tensor(out=stgv[:, :], in0=prj[bank][:, :],
                                                in1=vbias_bc[:, col0:col0 + 512],
                                                op=ALU.add), 1)
                    else:
                        s_v.inc(v.tensor_copy(out=stgv[:, :], in_=prj[bank][:, :]), 1)
                b.on("vector", f)
                prjfree[bank] = s_v.n
                def f(s, tt=tt, col0=col0):
                    s_v.wait(s)
                    s_stgv.inc(s.dma_start(out=VLOC[tt * 128:(tt + 1) * 128, col0:col0 + 512],
                                           in_=stgv[:, :]), 16)
                b.on("scalar", f)
        if interleaved:
            for cc in range(4):
                col0 = p * 512 + cc * 128
                h, off = divmod(col0, 384)
                if off < 256:
                    continue
                vcol = h * 128 + (off - 256)
                for tt in range(NT):
                    bank = tt % 2
                    def f(t, p=p, cc=cc, tt=tt, bank=bank, slot=slot):
                        s_wp[slot].wait(t, panload[p])
                        s_v.wait(t, prjfree[bank])
                        for fc in range(DC):
                            s_t.inc(t.matmul(
                                prj[bank][:, 0:128],
                                xnfm[:, fc * T + tt * 128: fc * T + (tt + 1) * 128],
                                wp[slot][:, fc * 512 + cc * 128: fc * 512 + (cc + 1) * 128],
                                start=(fc == 0), stop=(fc == DC - 1)), 1)
                    b.on("tensor", f)
                    mmdone = s_t.n
                    def f(v, bank=bank, vcol=vcol):
                        s_t.wait(v, mmdone)
                        s_stgv.wait(v)
                        if vbias_bc is not None:
                            s_v.inc(v.tensor_tensor(out=stgv[:, 0:128],
                                                    in0=prj[bank][:, 0:128],
                                                    in1=vbias_bc[:, vcol:vcol + 128],
                                                    op=ALU.add), 1)
                        else:
                            s_v.inc(v.tensor_copy(out=stgv[:, 0:128],
                                                  in_=prj[bank][:, 0:128]), 1)
                    b.on("vector", f)
                    prjfree[bank] = s_v.n
                    def f(s, tt=tt, vcol=vcol):
                        s_v.wait(s)
                        s_stgv.inc(s.dma_start(out=VLOC[tt * 128:(tt + 1) * 128, vcol:vcol + 128],
                                               in_=stgv[:, 0:128]), 16)
                    b.on("scalar", f)
        panfree[slot] = s_t.n


def emit_bias_cols(b, seg, bqkv, bq, s_list, tag=""):
    (s_g, s_d, s_e, s_t, s_v, s_sc, s_p, s_o) = s_list
    nc, cfg = b.nc, b.cfg
    nch = 3 * cfg.D // 128
    s_bq = b.sem(tag + "_bq")
    def f(g):
        s_bq.inc(g.dma_start(out=bq[:, :],
                             in_=bqkv.rearrange("(i p) -> p i", p=128),
                             allow_slow_non_contiguous=True), 16)
    b.on("gpsimd", f)
    def f(t):
        s_bq.wait(t)
    b.on("tensor", f)
    def f(v):
        s_bq.wait(v)
    b.on("vector", f)


def emit_norms(b, seg, QF, KLOC, KN2L, s_list, per_head, tag=""):
    (s_g, s_d, s_e, s_t, s_v, s_sc, s_p, s_o) = s_list
    nc, cfg = b.nc, b.cfg
    D, T, DC = cfg.D, cfg.T, cfg.DC
    ntg = T // 512

    s_nr = b.sem(tag + "_nr")
    nrm = b.sb(seg, tag + "NRMT", [128, T], BF16)
    nsq = b.sb(seg, tag + "NSQ", [128, T], BF16)
    nps = b.ps(seg, tag + "NPS", [1, (T // 512) * 512])
    nacc = b.sb(seg, tag + "NACC", [1, 2 * DC * ntg], F32)
    nmax = b.t[tag + "NMAX"]
    krow = b.sb(seg, tag + "KROWS", [1, 128], F32)

    stg_done = b.sems.get(tag[0] + "qk_stg") if tag else None
    stgv_done = b.sems.get(tag[0] + "qk_stgv") if tag else None
    first_load = [True]
    for wi, src in ((0, QF), (1, KLOC)):
        for fc in range(DC):
            def f(s, fc=fc, src=src):
                if first_load[0]:
                    if stg_done is not None:
                        stg_done.wait(s)
                    if stgv_done is not None:
                        stgv_done.wait(s)
                    first_load[0] = False
                s_sc.wait(s)
                s_nr.inc(s.dma_start(out=nrm[:, :], in_=src[fc * 128:(fc + 1) * 128, :]), 16)
            b.on("sync", f)
            def f(sc):
                s_nr.wait(sc)
                s_t.wait(sc)
                s_sc.inc(sc.activation(out=nsq[:, :], in_=nrm[:, :], func=AF.Square), 1)
            b.on("scalar", f)
            for tg in range(ntg):
                first = (fc == 0) or per_head
                last = (fc == DC - 1) or per_head
                def f(t, fc=fc, tg=tg, first=first, last=last):
                    s_sc.wait(t)
                    if first:
                        s_v.wait(t)
                    s_t.inc(t.matmul(nps[:, tg * 512:(tg + 1) * 512],
                                     b.t["ONES_col_bf"][:, :],
                                     nsq[:, tg * 512:(tg + 1) * 512],
                                     start=first, stop=last), 1)
                b.on("tensor", f)
                if last:
                    idx = (wi * DC * ntg) + fc * ntg + tg if per_head else \
                          wi * DC * ntg + tg
                    def f(v, idx=idx, tg=tg):
                        s_t.wait(v)
                        s_v.inc(v.reduce_max(out=nacc[0:1, idx:idx + 1],
                                             in_=nps[0:1, tg * 512:(tg + 1) * 512],
                                             axis=AX.X), 1)
                    b.on("vector", f)
    def f(v):
        s_v.wait(v)
        if per_head:
            for h in range(cfg.NH):
                s_v.inc(v.reduce_max(out=nmax[0:1, h:h + 1],
                                     in_=nacc[0:1, h * ntg:(h + 1) * ntg], axis=AX.X), 1)
                s_v.inc(v.reduce_max(
                    out=nmax[0:1, cfg.NH + h:cfg.NH + h + 1],
                    in_=nacc[0:1, (cfg.DC * ntg) + h * ntg:(cfg.DC * ntg) + (h + 1) * ntg],
                    axis=AX.X), 1)
        else:
            s_v.inc(v.reduce_max(out=nmax[0:1, 0:1], in_=nacc[0:1, 0:ntg], axis=AX.X), 1)
            s_v.inc(v.reduce_max(out=nmax[0:1, 1:2],
                                 in_=nacc[0:1, cfg.DC * ntg:cfg.DC * ntg + ntg],
                                 axis=AX.X), 1)
    b.on("vector", f)
    emit_norms_small(b, nmax, krow, KN2L, per_head, s_list)


def emit_flash_L0(b, seg, QF, KALL, VALL, KN2G, CTXBF, s_list, nmax_name="ANMAX"):
    (s_g, s_d, s_e, s_t, s_v, s_sc, s_p, s_o) = s_list
    nc, cfg = b.nc, b.cfg
    D, T, DC, NG = cfg.D, cfg.T, cfg.DC, cfg.NG
    c_soft = 1.0 / float(np.sqrt(np.float32(D)))

    negc = b.sb(seg, "NEGC", [128, 1], F32)
    kng = b.sb(seg, "KNGS", [1, 1], F32)
    nsc = b.sb(seg, "NSC", [1, 4], F32)
    with nc.psum_tensor("NEGCPS", [128, 1], F32) as negcps:
        s_kng = b.sem("fl_kng")
        def f(s):
            b.sems["s_cc"].wait(s)
            s_kng.inc(s.dma_start(out=kng[0:1, :], in_=KN2G[0:1, 0:1]), 16)
        b.on("sync", f)
        def f(v):
            s_kng.wait(v)
            s_v.inc(v.tensor_tensor(out=nsc[0:1, 0:1], in0=b.t[nmax_name][0:1, 0:1],
                                    in1=kng[0:1, 0:1], op=ALU.mult), 1)
        b.on("vector", f)
        def f(sc):
            s_v.wait(sc)
            s_sc.inc(sc.activation(out=nsc[0:1, 1:2], in_=nsc[0:1, 0:1], func=AF.Ln), 1)
            s_sc.wait(sc)
            s_sc.inc(sc.activation(out=nsc[0:1, 2:3], in_=nsc[0:1, 1:2], func=AF.Exp,
                                   scale=0.5), 1)
        b.on("scalar", f)
        def f(v):
            s_sc.wait(v)
            s_v.inc(v.tensor_scalar(out=nsc[0:1, 3:4], in0=nsc[0:1, 2:3],
                                    scalar1=-c_soft, scalar2=None, op0=ALU.mult), 1)
        b.on("vector", f)
        def f(t):
            s_v.wait(t)
            s_t.inc(t.matmul(negcps[:, :], b.t["ONES_row_f32"][:, :], nsc[0:1, 3:4],
                             start=True, stop=True), 1)
        b.on("tensor", f)
        def f(v):
            s_t.wait(v)
            s_v.inc(v.tensor_copy(out=negc[:, :], in_=negcps[:, :]), 1)
        b.on("vector", f)

    NR = 4
    kch = [b.sb(seg, f"KCH{i}", [128, DC * 128], BF16) for i in range(NR)]
    vch = [b.sb(seg, f"VCH{i}", [128, D], BF16) for i in range(NR)]
    qg = b.sb(seg, "QG", [128, DC * 512], BF16)
    ptar = b.sb(seg, "PTAR", [128, cfg.maxslots * 512], BF16)
    ctstg = b.sb(seg, "CTSTG", [128, D], BF16)
    lic = b.sb(seg, "LIC", [128, 4], F32)
    scps = [b.ps(seg, "SCPS0", [128, 512]), b.ps(seg, "SCPS1", [128, 512])]
    lcps = b.ps(seg, "LCPS", [128, 4])
    ctxps = b.ps(seg, "CTXPS", [128, D])
    s_kc = [b.sem(f"fl_kc{i}") for i in range(NR)]
    s_vc = [b.sem(f"fl_vc{i}") for i in range(NR)]
    s_qg = b.sem("fl_qg")
    s_cts = b.sem("fl_cts")
    kfree = [0] * NR
    scfree = [0, 0]
    vfree = [0] * NR

    for g in range(NG):
        chunks = [(r, c) for r in range(4) for c in range(4 * g + 4)]
        nch = len(chunks)
        for ci, (r, c) in enumerate(chunks):
            par = ci % NR
            def f(s, g=g, r=r, c=c, par=par, ci=ci):
                s_t.wait(s, kfree[par])
                s_kc[par].inc(s.dma_start(
                    out=kch[par][:, :].rearrange("p (f k) -> p f k", k=128),
                    in_=KALL[r * D:(r + 1) * D, c * 128:(c + 1) * 128]
                        .rearrange("(f p) k -> p f k", p=128)), 16)
            b.on("sync", f)
            kload = s_kc[par].n
            if ci == 0:
                def f(s, g=g):
                    s_t.wait(s)
                    for fc in range(DC):
                        s_qg.inc(s.dma_start(
                            out=qg[:, fc * 512:(fc + 1) * 512],
                            in_=QF[fc * 128:(fc + 1) * 128, g * 512:(g + 1) * 512]), 16)
                b.on("scalar", f)
            qload = s_qg.n
            spar = ci % 2
            def f(t, ci=ci, par=par, spar=spar, kload=kload, qload=qload, g=g):
                if ci == 0 and g == 0:
                    s_v.wait(t)
                s_kc[par].wait(t, kload)
                if ci == 0:
                    s_qg.wait(t, qload)
                s_sc.wait(t, scfree[spar])
                for fc in range(DC):
                    s_t.inc(t.matmul(scps[spar][:, :],
                                     kch[par][:, fc * 128:(fc + 1) * 128],
                                     qg[:, fc * 512:(fc + 1) * 512],
                                     start=(fc == 0), stop=(fc == DC - 1)), 1)
            b.on("tensor", f)
            scdone = s_t.n
            kfree[par] = scdone
            def f(sc, spar=spar, slot=ci, scdone=scdone, ci=ci):
                s_t.wait(sc, scdone)
                if ci == 0:
                    s_v.wait(sc)
                s_sc.inc(sc.activation(out=ptar[:, slot * 512:(slot + 1) * 512],
                                       in_=scps[spar][:, :], func=AF.Exp,
                                       bias=negc[:, 0:1], scale=c_soft), 1)
            b.on("scalar", f)
            scfree[spar] = s_sc.n
            d = c - 4 * g
            if d >= 0:
                def f(v, slot=ci, d=d, r=r):
                    s_sc.wait(v)
                    s_v.inc(v.tensor_tensor(
                        out=ptar[:, slot * 512:(slot + 1) * 512],
                        in0=ptar[:, slot * 512:(slot + 1) * 512],
                        in1=b.t["MSK"][:, (d * 4 + r) * 512:(d * 4 + r + 1) * 512],
                        op=ALU.mult), 1)
                b.on("vector", f)
        def f(t, chunks=chunks, nch=nch):
            s_sc.wait(t)
            s_v.wait(t)
            for ci in range(nch):
                for m in range(4):
                    s_t.inc(t.matmul(lcps[:, m:m + 1],
                                     ptar[:, ci * 512 + m * 128: ci * 512 + (m + 1) * 128],
                                     b.t["ONES_col_bf"][:, :],
                                     start=(ci == 0 and m == 0),
                                     stop=(ci == nch - 1 and m == 3)), 1)
        b.on("tensor", f)
        ldone = s_t.n
        def f(v, ldone=ldone):
            s_t.wait(v, ldone)
            s_v.inc(v.reciprocal(out=lic[:, :], in_=lcps[:, :]), 1)
        b.on("vector", f)
        for m in range(4):
            sub = [(r, c) for (r, c) in chunks if c <= 4 * g + m]
            for si, (r, c) in enumerate(sub):
                par = si % NR
                def f(s, r=r, c=c, par=par):
                    s_t.wait(s, vfree[par])
                    s_vc[par].inc(s.dma_start(out=vch[par][:, :],
                                              in_=VALL[r * T + c * 128: r * T + (c + 1) * 128, :]), 16)
                b.on("sync", f)
                vload = s_vc[par].n
                slot = chunks.index((r, c))
                def f(t, si=si, par=par, slot=slot, m=m, vload=vload, nsub=len(sub)):
                    s_vc[par].wait(t, vload)
                    if si == 0:
                        s_v.wait(t)
                    for dv in range(D // 512):
                        s_t.inc(t.matmul(
                            ctxps[:, dv * 512:(dv + 1) * 512],
                            ptar[:, slot * 512 + m * 128: slot * 512 + (m + 1) * 128],
                            vch[par][:, dv * 512:(dv + 1) * 512],
                            start=(si == 0), stop=(si == nsub - 1)), 1)
                b.on("tensor", f)
                vfree[par] = s_t.n
            def f(v, m=m, g=g):
                s_t.wait(v)
                s_cts.wait(v)
                s_v.inc(v.tensor_scalar(out=ctstg[:, :], in0=ctxps[:, :],
                                        scalar1=lic[:, m:m + 1], scalar2=None,
                                        op0=ALU.mult), 1)
            b.on("vector", f)
            def f(s, m=m, g=g):
                s_v.wait(s)
                s_cts.inc(s.dma_start(
                    out=CTXBF[g * 512 + m * 128: g * 512 + (m + 1) * 128, :],
                    in_=ctstg[:, :]), 16)
            b.on("sync", f)


def emit_oproj_resid(b, seg, CTXBF, wo, X, s_list, tag="A5", prev_x_sem="A_xst",
                     src_store_sem="fl_cts", dst=None, dst_sem_name="op_xw"):
    (s_g, s_d, s_e, s_t, s_v, s_sc, s_p, s_o) = s_list
    nc, cfg = b.nc, b.cfg
    D, T, NT, DC = cfg.D, cfg.T, cfg.NT, cfg.DC
    if dst is None:
        dst = X

    s_fm2 = b.sem(tag + "op_fm")
    s_ow = b.sem(tag + "op_w")
    s_xr = b.sem(tag + "op_xr")
    s_xw = b.sem(dst_sem_name)
    cts = b.sems.get(src_store_sem)
    cfm = b.sb(seg, tag + "CFM", [128, DC * T], BF16)
    emit_local_fence(b)
    load_fm(b, cfm, CTXBF, DC, T, s_fm2, wait_sems=[(cts, cts.n)])
    cfm_ready = s_fm2.n
    NPAN = D // 512
    owp = b.sb(seg, tag + "OWP", [128, NPAN * DC * 512], BF16)
    prj = [b.ps(seg, tag + "OPRJ0", [128, 512]), b.ps(seg, tag + "OPRJ1", [128, 512])]
    xrt = b.sb(seg, tag + "XRT", [128, D], F32)
    xwt = b.sb(seg, tag + "XWT", [128, D], F32)

    panload = [None] * NPAN
    prjfree = [0, 0]
    for p in range(NPAN):
        def f(g, p=p):
            s_ow.inc(g.dma_start(
                out=owp[:, p * DC * 512:(p + 1) * DC * 512]
                    .rearrange("p (c n) -> p c n", n=512),
                in_=wo[:, p * 512:(p + 1) * 512].rearrange("(c p) n -> p c n", p=128)), 16)
        b.on("gpsimd", f)
        panload[p] = s_ow.n
    for tt in range(NT):
        def f(s, tt=tt):
            s_v.wait(s)
            if tt == 0 and prev_x_sem in b.sems:
                b.sems[prev_x_sem].wait(s)
            s_xr.inc(s.dma_start(out=xrt[:, :], in_=X[tt * 128:(tt + 1) * 128, :]), 16)
        b.on("sync", f)
        xload = s_xr.n
        for p in range(NPAN):
            bank = p % 2
            def f(t, tt=tt, p=p, bank=bank):
                if tt == 0:
                    s_ow.wait(t, panload[NPAN - 1])
                if tt == 0 and p == 0:
                    s_fm2.wait(t, cfm_ready)
                s_v.wait(t, prjfree[bank])
                for dc in range(DC):
                    s_t.inc(t.matmul(
                        prj[bank][:, :],
                        cfm[:, dc * T + tt * 128: dc * T + (tt + 1) * 128],
                        owp[:, p * DC * 512 + dc * 512: p * DC * 512 + (dc + 1) * 512],
                        start=(dc == 0), stop=(dc == DC - 1)), 1)
            b.on("tensor", f)
            mmdone = s_t.n
            def f(v, p=p, bank=bank, xload=xload):
                s_t.wait(v, mmdone)
                s_xr.wait(v, xload)
                if p == 0:
                    s_xw.wait(v)
                s_v.inc(v.tensor_tensor(out=xwt[:, p * 512:(p + 1) * 512],
                                        in0=prj[bank][:, :],
                                        in1=xrt[:, p * 512:(p + 1) * 512], op=ALU.add), 1)
            b.on("vector", f)
            prjfree[bank] = s_v.n
        def f(s, tt=tt):
            s_v.wait(s)
            s_xw.inc(s.dma_start(out=X[tt * 128:(tt + 1) * 128, :], in_=xwt[:, :]), 16)
        b.on("sync", f)


def emit_norms_small(b, nmax, krow, KN2L, per_head, s_list):
    (s_g, s_d, s_e, s_t, s_v, s_sc, s_p, s_o) = s_list
    cfg = b.cfg
    nh = cfg.NH if per_head else 1
    def f(g):
        s_d.wait(g)
        s_p.inc(g.memset(krow[0:1, :], 0.0), 1)
    b.on("gpsimd", f)
    def f(v):
        s_p.wait(v)
        s_v.wait(v)
        s_v.inc(v.tensor_copy(out=krow[0:1, 0:nh], in_=nmax[0:1, nh:2 * nh]), 1)
    b.on("vector", f)
    def f(s):
        s_v.wait(s)
        s_d.inc(s.dma_start(out=KN2L[0:1, :], in_=krow[0:1, :]), 16)
    b.on("sync", f)


def emit_negc_mha(b, seg, KN2G, nmax_name, negch, c_soft, s_list, tag):
    (s_g, s_d, s_e, s_t, s_v, s_sc, s_p, s_o) = s_list
    nc, cfg = b.nc, b.cfg
    NH = cfg.NH
    kng = b.sb(seg, tag + "KNGH", [1, NH], F32)
    nrow = b.sb(seg, tag + "NROW", [1, NH], F32)
    s_kg = b.sem(tag + "_kng")
    with nc.psum_tensor(tag + "NGPS", [128, NH], F32) as ngps:
        def f(s):
            b.sems["s_cc"].wait(s)
            s_kg.inc(s.dma_start(out=kng[0:1, :], in_=KN2G[0:1, 0:NH]), 16)
        b.on("sync", f)
        def f(v):
            s_kg.wait(v)
            s_v.inc(v.tensor_tensor(out=nrow[0:1, :], in0=b.t[nmax_name][0:1, 0:NH],
                                    in1=kng[0:1, :], op=ALU.mult), 1)
        b.on("vector", f)
        def f(sc):
            s_v.wait(sc)
            s_sc.inc(sc.activation(out=nrow[0:1, :], in_=nrow[0:1, :], func=AF.Ln), 1)
            s_sc.wait(sc)
            s_sc.inc(sc.activation(out=nrow[0:1, :], in_=nrow[0:1, :], func=AF.Exp,
                                   scale=0.5), 1)
        b.on("scalar", f)
        def f(v):
            s_sc.wait(v)
            s_v.inc(v.tensor_scalar(out=nrow[0:1, :], in0=nrow[0:1, :],
                                    scalar1=-c_soft, scalar2=None, op0=ALU.mult), 1)
        b.on("vector", f)
        def f(t):
            s_v.wait(t)
            for h in range(NH):
                s_t.inc(t.matmul(ngps[:, h:h + 1], b.t["ONES_row_f32"][:, :],
                                 nrow[0:1, h:h + 1], start=(h == 0), stop=(h == NH - 1)), 1)
        b.on("tensor", f)
        def f(v):
            s_t.wait(v)
            s_v.inc(v.tensor_copy(out=negch[:, :], in_=ngps[:, :]), 1)
        b.on("vector", f)


def emit_flash_MHA(b, seg, QF, KALL, VALL, KN2G, ctxfm, nmax_name, s_list, tag):
    (s_g, s_d, s_e, s_t, s_v, s_sc, s_p, s_o) = s_list
    nc, cfg = b.nc, b.cfg
    D, T, DC, NG, NH = cfg.D, cfg.T, cfg.DC, cfg.NG, cfg.NH
    c_soft = 1.0 / float(np.sqrt(np.float32(cfg.DH)))
    EB = 3
    NSLOT = 12

    negch = b.sb(seg, tag + "NEGCH", [128, NH], F32)
    emit_negc_mha(b, seg, KN2G, nmax_name, negch, c_soft, s_list, tag)

    kh = b.sb(seg, tag + "KH", [128, 4 * T], BF16)
    vh = b.sb(seg, tag + "VH", [128, 4 * T], BF16)
    qh = b.sb(seg, tag + "QH", [128, T], BF16)
    ptar = b.sb(seg, tag + "PTA", [128, NSLOT * 512], BF16)
    linv = b.sb(seg, tag + "LINV", [1, 512], F32)
    lbcs = b.sb(seg, tag + "LBCS", [128, 512], F32)
    scps = [b.ps(seg, tag + "SC0", [128, EB * 512]), b.ps(seg, tag + "SC1", [128, EB * 512])]
    lrow = b.ps(seg, tag + "LROW", [128, 512])
    cfps = b.ps(seg, tag + "CFPS", [128, 512])
    s_kh = b.sem(tag + "_kh")
    s_vh = b.sem(tag + "_vh")
    s_qh = b.sem(tag + "_qh")
    scfree = [0, 0]
    slot_rd = {}

    for h in range(NH):
        def f(s, h=h):
            s_t.wait(s)
            for r in range(4):
                s_kh.inc(s.dma_start(
                    out=kh[:, r * T:(r + 1) * T],
                    in_=KALL[r * D + h * 128: r * D + (h + 1) * 128, :]), 16)
            s_vh.inc(s.dma_start(
                out=vh[:, :].rearrange("p (n d) -> p n d", d=128),
                in_=VALL[:, h * 128:(h + 1) * 128].rearrange("(n p) d -> p n d", p=128)), 16)
            s_qh.inc(s.dma_start(out=qh[:, :], in_=QF[h * 128:(h + 1) * 128, :]), 16)
        b.on("sync", f)
        khload, vhload, qhload = s_kh.n, s_vh.n, s_qh.n
        bcounter = [0]
        for g in range(NG):
            chunks = [(r, c) for r in range(4) for c in range(4 * g + 4)]
            nch = len(chunks)
            blocks = [chunks[i:i + EB] for i in range(0, nch, EB)]
            nb = len(blocks)
            expdone = [0] * nb
            maskdone = [0] * nb

            def emit_pv(t, bi, blocks=blocks, nch=nch, g=g):
                blk = blocks[bi]
                ci0 = bi * EB
                for j, (r, c) in enumerate(blk):
                    ci = ci0 + j
                    slot = ci % NSLOT
                    n = r * (T // 128) + c
                    s_t.inc(t.matmul(cfps[:, :],
                                     vh[:, n * 128:(n + 1) * 128],
                                     ptar[:, slot * 512:(slot + 1) * 512],
                                     start=(ci == 0), stop=(ci == nch - 1)), 1)
                    s_t.inc(t.matmul(lrow[0:1, :], b.t["ONES_col_bf"][:, :],
                                     ptar[:, slot * 512:(slot + 1) * 512],
                                     start=(ci == 0), stop=(ci == nch - 1)), 1)

            sc_cell = {}
            for bi, blk in enumerate(blocks):
                par = bi % 2
                ci0 = bi * EB
                def f(t, h=h, g=g, bi=bi, par=par, blk=blk, ci0=ci0,
                      khload=khload, vhload=vhload, qhload=qhload):
                    if bi == 0 and g == 0:
                        s_kh.wait(t, khload)
                        s_qh.wait(t, qhload)
                        s_vh.wait(t, vhload)
                        s_v.wait(t)
                    if bi == 0:
                        s_v.wait(t)
                    s_sc.wait(t, scfree[par])
                    for j, (r, c) in enumerate(blk):
                        s_t.inc(t.matmul(scps[par][:, j * 512:(j + 1) * 512],
                                         kh[:, r * T + c * 128: r * T + (c + 1) * 128],
                                         qh[:, g * 512:(g + 1) * 512],
                                         start=True, stop=True), 1)
                    sc_cell[bi] = s_t.n
                    if bi >= 1:
                        s_sc.wait(t, expdone[bi - 1])
                        if maskdone[bi - 1]:
                            s_v.wait(t, maskdone[bi - 1])
                        emit_pv(t, bi - 1)
                b.on("tensor", f)
                scdone = sc_cell[bi]
                def f(sc, h=h, par=par, blk=blk, ci0=ci0, scdone=scdone, bi=bi, g=g):
                    s_t.wait(sc, scdone)
                    gb = bcounter[0]
                    if gb >= NSLOT // EB:
                        pass
                    slot0 = ci0 % NSLOT
                    s_sc.inc(sc.activation(
                        out=ptar[:, slot0 * 512:(slot0 + len(blk)) * 512],
                        in_=scps[par][:, 0:len(blk) * 512], func=AF.Exp,
                        bias=negch[:, h:h + 1], scale=c_soft), 1)
                b.on("scalar", f)
                expdone[bi] = s_sc.n
                scfree[par] = s_sc.n
                dm = [(j, r, c) for j, (r, c) in enumerate(blk) if c - 4 * g >= 0]
                if dm:
                    def f(v, dm=dm, ci0=ci0, g=g, bi=bi):
                        s_sc.wait(v, expdone[bi])
                        for j, r, c in dm:
                            d = c - 4 * g
                            slot = (ci0 + j) % NSLOT
                            s_v.inc(v.tensor_tensor(
                                out=ptar[:, slot * 512:(slot + 1) * 512],
                                in0=ptar[:, slot * 512:(slot + 1) * 512],
                                in1=b.t["MSK"][:, (d * 4 + r) * 512:(d * 4 + r + 1) * 512],
                                op=ALU.mult), 1)
                    b.on("vector", f)
                    maskdone[bi] = s_v.n
                bcounter[0] += 1
            def f(t, nb=nb):
                s_sc.wait(t, expdone[nb - 1])
                if maskdone[nb - 1]:
                    s_v.wait(t, maskdone[nb - 1])
                emit_pv(t, nb - 1)
            b.on("tensor", f)
            pvdone = s_t.n
            def f(v, pvdone=pvdone):
                s_t.wait(v, pvdone)
                s_v.inc(v.reciprocal(out=linv[0:1, :], in_=lrow[0:1, :]), 1)
            b.on("vector", f)
            def f(t):
                s_v.wait(t)
                s_t.inc(t.matmul(lrow[:, :], b.t["ONES_row_f32"][:, :],
                                 linv[0:1, :], start=True, stop=True), 1)
            b.on("tensor", f)
            def f(v, h=h, g=g):
                s_t.wait(v)
                s_v.inc(v.tensor_copy(out=lbcs[:, :], in_=lrow[:, :]), 1)
                s_v.wait(v)
                s_v.inc(v.tensor_tensor(out=ctxfm[:, h * T + g * 512: h * T + (g + 1) * 512],
                                        in0=cfps[:, :], in1=lbcs[:, :], op=ALU.mult), 1)
            b.on("vector", f)


def emit_ffn_down(b, seg, HBF, w2_load, evac, s_list, tag, loader_eng="gpsimd",
                  h_store_sem=None):
    (s_g, s_d, s_e, s_t, s_v, s_sc, s_p, s_o) = s_list
    nc, cfg = b.nc, b.cfg
    D, T, FF, FC = cfg.D, cfg.T, cfg.FF, cfg.FC

    hfmg = b.sb(seg, tag + "HFMG", [128, FC * 512], BF16)
    w2p = b.sb(seg, tag + "W2P", [128, FC * 512], BF16)
    dps = [b.ps(seg, tag + "DPS0", [128, 512]), b.ps(seg, tag + "DPS1", [128, 512])]
    s_hf = b.sem(tag + "_hf")
    s_w2 = b.sem(tag + "_w2")
    dpsfree = [0, 0]
    hs_sem = b.sems.get(h_store_sem) if h_store_sem else None
    hs_n = hs_sem.n if hs_sem else 0
    for tg in range(T // 512):
        def f(s, tg=tg):
            if tg == 0 and hs_sem is not None:
                hs_sem.wait(s, hs_n)
            s_t.wait(s)
            for ic in range(FC):
                s_hf.inc(s.dma_start(
                    out=hfmg[:, ic * 512:(ic + 1) * 512],
                    in_=HBF[tg * 512:(tg + 1) * 512, ic * 128:(ic + 1) * 128],
                    transpose=True), 16)
        b.on("sync", f)
        hload = s_hf.n
        for oc in range(D // 512):
            def f(g, tg=tg, oc=oc):
                s_t.wait(g)
                w2_load(g, oc, w2p, s_w2)
            b.on(loader_eng, f)
            w2load = s_w2.n
            for tt in range(4):
                bank = tt % 2
                def f(t, tg=tg, oc=oc, tt=tt, bank=bank, hload=hload, w2load=w2load):
                    s_hf.wait(t, hload)
                    s_w2.wait(t, w2load)
                    s_v.wait(t, dpsfree[bank])
                    for ic in range(FC):
                        s_t.inc(t.matmul(
                            dps[bank][:, :],
                            hfmg[:, ic * 512 + tt * 128: ic * 512 + (tt + 1) * 128],
                            w2p[:, ic * 512:(ic + 1) * 512],
                            start=(ic == 0), stop=(ic == FC - 1)), 1)
                b.on("tensor", f)
                evac(tg * 4 + tt, oc, dps[bank])
                dpsfree[bank] = s_v.n


def emit_gather3(b, KLOC, VLOC, KN2L, KALL, VALL, KN2G, wait_names):
    nc = b.nc
    s_cc = b.sems["s_cc"]
    for nm in wait_names:
        if nm in b.sems:
            nc.gpsimd.wait_ge(b.sems[nm].h, b.sems[nm].n)
    s_cc.inc(nc.gpsimd.collective_compute(
        "AllGather", ALU.bypass, replica_groups=GROUPS,
        ins=[KLOC.opt()], outs=[KALL.opt()]), 1)
    s_cc.inc(nc.gpsimd.collective_compute(
        "AllGather", ALU.bypass, replica_groups=GROUPS,
        ins=[VLOC.opt()], outs=[VALL.opt()]), 1)
    s_cc.inc(nc.gpsimd.collective_compute(
        "AllReduce", ALU.max, replica_groups=GROUPS,
        ins=[KN2L.opt()], outs=[KN2G.opt()]), 1)
    nc.gpsimd.wait_ge(s_cc.h, s_cc.n)


def emit_B1(b, seg, X, w, HS, PART1, XNBF, s_list):
    (s_g, s_d, s_e, s_t, s_v, s_sc, s_p, s_o) = s_list
    nc, cfg = b.nc, b.cfg
    D, T, NT, DC, LR = cfg.D, cfg.T, cfg.NT, cfg.DC, cfg.LR
    import ml_dtypes

    g1 = b.sb(seg, "G1BC", [128, D], F32)
    b1 = b.sb(seg, "B1BC", [128, D], F32)
    bs = b.sb(seg, "BSBC", [128, D], F32)
    rowt = b.sb(seg, "B1ROW", [1, D], F32)
    s_br = b.sem("B1_br")
    with nc.psum_tensor("B1BCPS", [128, 512], F32) as bcps:
        for wname, dstbc in (("L0_ffn_ln1_scale", g1), ("L0_ffn_ln1_bias", b1),
                             ("L0_ffn_simple_bias", bs)):
            def f(s, wname=wname):
                s_v.wait(s)
                s_br.inc(s.dma_start(out=rowt[0:1, :], in_=b.dram[wname][None, :]), 16)
            b.on("sync", f)
            def f(t):
                s_br.wait(t)
            b.on("tensor", f)
            bcast_rows(b, rowt[0:1, :], dstbc, D, s_t, s_v, bcps)

    wrt = b.sb(seg, "WRT", [128, DC * 3], F32)
    wrow = b.sb(seg, "WRROW", [3, D], F32)
    id3 = nc.inline_tensor(np.eye(3, dtype=np.float32), "c_id3f").ap()
    id3s = b.sb(seg, "ID3F", [3, 3], F32)
    s_wr = b.sem("B1_wr")
    with nc.psum_tensor("B1WRPS", [128, 3], F32) as wrps:
        def f(s):
            s_wr.inc(s.dma_start(out=wrow[0:3, :], in_=b.dram["L0_ffn_w_router"][:, :]), 16)
            s_wr.inc(s.dma_start(out=id3s[0:3, :], in_=id3[:, :]), 16)
        b.on("sync", f)
        for fc in range(DC):
            def f(t, fc=fc):
                s_wr.wait(t)
                s_v.wait(t)
                s_t.inc(t.transpose(wrps[:, :], wrow[0:3, fc * 128:(fc + 1) * 128],
                                    id3s[0:3, 0:3]), 1)
            b.on("tensor", f)
            def f(v, fc=fc):
                s_t.wait(v)
                s_v.inc(v.tensor_copy(out=wrt[:, fc * 3:(fc + 1) * 3], in_=wrps[:, :]), 1)
            b.on("vector", f)

    wdt = b.sb(seg, "WDT", [128, DC * LR], BF16)
    wut = b.sb(seg, "WUT", [32, D], BF16)
    s_lw = b.sem("B1_lw")
    def f(s):
        b.sems["B_pc"].wait(s)
        for fc in range(DC):
            s_lw.inc(s.dma_start(out=wdt[:, fc * LR:(fc + 1) * LR],
                                 in_=b.dram["WDB"][:, fc * 128:(fc + 1) * 128],
                                 transpose=True), 16)
    b.on("sync", f)
    wub_sb = b.sb(seg, "WUBSB", [128, LR], BF16)
    with nc.psum_tensor("B1WUPS", [32, 128], BF16) as wups:
        for fc in range(DC):
            def f(s, fc=fc):
                s_t.wait(s)
                s_lw.inc(s.dma_start(out=wub_sb[:, :],
                                     in_=b.dram["WUB"][fc * 128:(fc + 1) * 128, :]), 16)
            b.on("sync", f)
            lw = s_lw.n
            def f(t, fc=fc, lw=lw):
                s_lw.wait(t, lw)
                s_v.wait(t)
                s_t.inc(t.transpose(wups[:, :], wub_sb[:, 0:LR], b.t["ID_BF"][:, :]), 1)
            b.on("tensor", f)
            def f(v, fc=fc):
                s_t.wait(v)
                s_v.inc(v.tensor_copy(out=wut[0:32, fc * 128:(fc + 1) * 128],
                                      in_=wups[0:32, :]), 1)
            b.on("vector", f)

    trps = [b.ps(seg, "B1TR0", [128, 128]), b.ps(seg, "B1TR1", [128, 128])]
    psml = b.ps(seg, "B1SML", [128, 512])
    y1t_ps = b.ps(seg, "B1Y1T", [32, 128], BF16)
    delps = b.ps(seg, "B1DEL", [128, 2048] if D >= 2048 else [128, D])

    xt = b.sb(seg, "BXT", [128, D], F32)
    hst = b.sb(seg, "BHST", [128, D], F32)
    scr = b.sb(seg, "BSCR", [128, D], F32)
    stat = b.sb(seg, "BSTAT", [128, 8], F32)
    pa = b.sb(seg, "BPA", [128, D], F32)
    pb = b.sb(seg, "BPB", [128, D], F32)
    xnb = b.sb(seg, "BXNB", [128, D], BF16)
    hfm32 = b.sb(seg, "HSFM32", [128, DC * 128], F32)
    hfmbf = b.sb(seg, "HSFMBF", [128, DC * 128], BF16)
    y1b = b.sb(seg, "BY1B", [128, LR], BF16)
    y1t = b.sb(seg, "BY1T", [32, 128], BF16)
    s_xt = b.sem("B1_xt")
    s_hs = b.sem("B1_hs")
    s_p1 = b.sem("B1_p1")
    s_nb = b.sem("B1_nb")

    for i in range(NT):
        def f(s, i=i):
            s_v.wait(s); s_sc.wait(s)
            if i == 0:
                b.sems["op_xw"].wait(s)
            s_xt.inc(s.dma_start(out=xt[:, :], in_=X[i * 128:(i + 1) * 128, :]), 16)
        b.on("sync", f)
        def f(v):
            s_xt.wait(v)
        b.on("vector", f)
        def f(sc):
            s_xt.wait(sc)
        b.on("scalar", f)
        ln_stats(b, xt, D, stat, scr, s_sc, s_v)
        def f(v):
            s_hs.wait(v)
        b.on("vector", f)
        ln_apply(b, xt, hst, D, stat, s_sc, s_v)
        def f(s, i=i):
            s_v.wait(s)
            s_hs.inc(s.dma_start(out=HS[i * 128:(i + 1) * 128, :], in_=hst[:, :]), 16)
        b.on("sync", f)
        for fc in range(DC):
            par = fc % 2
            def f(t, fc=fc, par=par):
                s_v.wait(t)
                s_t.inc(t.transpose(trps[par][:, :], xt_dummy_src(b, hst, fc),
                                    b.t["ID_F32"][:, :]), 1)
            b.on("tensor", f)
            def f(v, fc=fc, par=par):
                s_t.wait(v)
                s_v.inc(v.tensor_copy(out=hfm32[:, fc * 128:(fc + 1) * 128],
                                      in_=trps[par][:, :]), 1)
            b.on("vector", f)
        def f(v):
            s_v.wait(v)
            s_v.inc(v.tensor_copy(out=hfmbf[:, :], in_=hfm32[:, :]), 1)
        b.on("vector", f)
        hfm_done = s_v.n
        def f(t, hfm_done=hfm_done):
            s_v.wait(t, hfm_done)
            for fc in range(DC):
                s_t.inc(t.matmul(psml[:, 0:3], hfm32[:, fc * 128:(fc + 1) * 128],
                                 wrt[:, fc * 3:(fc + 1) * 3],
                                 start=(fc == 0), stop=(fc == DC - 1)), 1)
        b.on("tensor", f)
        rt_done = s_t.n
        def f(v, i=i, rt_done=rt_done):
            s_t.wait(v, rt_done)
            s_v.inc(v.tensor_copy(out=stat[:, 5:8], in_=psml[:, 0:3]), 1)
            s_v.wait(v)
            l0, l1, l2 = stat[:, 5:6], stat[:, 6:7], stat[:, 7:8]
            s_v.inc(v.tensor_tensor(out=stat[:, 0:1], in0=l2, in1=l0, op=ALU.is_gt), 1)
            s_v.inc(v.tensor_tensor(out=stat[:, 1:2], in0=l2, in1=l1, op=ALU.is_gt), 1)
            s_v.inc(v.tensor_tensor(out=stat[:, 2:3], in0=l1, in1=l0, op=ALU.is_gt), 1)
            s_v.inc(v.tensor_tensor(out=stat[:, 3:4], in0=l1, in1=l2, op=ALU.is_ge), 1)
            s_v.wait(v)
            s_v.inc(v.tensor_tensor(out=b.t["RM2"][:, i:i + 1], in0=stat[:, 0:1],
                                    in1=stat[:, 1:2], op=ALU.mult), 1)
            s_v.inc(v.tensor_tensor(out=stat[:, 4:5], in0=stat[:, 2:3],
                                    in1=stat[:, 3:4], op=ALU.mult), 1)
            s_v.wait(v)
            s_v.inc(v.tensor_scalar(out=stat[:, 0:1], in0=b.t["RM2"][:, i:i + 1],
                                    scalar1=-1.0, scalar2=1.0, op0=ALU.mult,
                                    op1=ALU.add), 1)
            s_v.wait(v)
            s_v.inc(v.tensor_tensor(out=b.t["RM1"][:, i:i + 1], in0=stat[:, 4:5],
                                    in1=stat[:, 0:1], op=ALU.mult), 1)
            s_v.wait(v)
            s_v.inc(v.tensor_tensor(out=b.t["RM0"][:, i:i + 1], in0=stat[:, 0:1],
                                    in1=b.t["RM1"][:, i:i + 1], op=ALU.subtract), 1)
        b.on("vector", f)
        masks_done = s_v.n
        def f(t, masks_done=masks_done):
            s_v.wait(t, masks_done)
            for fc in range(DC):
                s_t.inc(t.matmul(psml[:, 32:32 + LR], hfmbf[:, fc * 128:(fc + 1) * 128],
                                 wdt[:, fc * LR:(fc + 1) * LR],
                                 start=(fc == 0), stop=(fc == DC - 1)), 1)
        b.on("tensor", f)
        def f(v):
            s_t.wait(v)
            s_v.inc(v.tensor_copy(out=y1b[:, :], in_=psml[:, 32:32 + LR]), 1)
        b.on("vector", f)
        def f(t):
            s_v.wait(t)
            s_t.inc(t.transpose(y1t_ps[:, :], y1b[:, 0:LR], b.t["ID_BF"][:, :]), 1)
        b.on("tensor", f)
        def f(v):
            s_t.wait(v)
            s_v.inc(v.tensor_copy(out=y1t[0:32, :], in_=y1t_ps[0:32, :]), 1)
        b.on("vector", f)
        y1t_done = s_v.n
        def f(t, y1t_done=y1t_done):
            s_v.wait(t, y1t_done)
            for oc in range(D // 512):
                s_t.inc(t.matmul(delps[:, oc * 512:(oc + 1) * 512], y1t[0:32, :],
                                 wut[0:32, oc * 512:(oc + 1) * 512],
                                 start=True, stop=True), 1)
        b.on("tensor", f)
        del_done = s_t.n
        def f(v, i=i, del_done=del_done):
            s_t.wait(v, del_done)
            s_p1.wait(v)
            s_v.inc(v.tensor_scalar(out=pa[:, :], in0=hst[:, :],
                                    scalar1=stat[:, 0:1], scalar2=None,
                                    op0=ALU.mult), 1)
            s_v.wait(v)
            s_v.inc(v.scalar_tensor_tensor(out=pb[:, :], in0=bs[:, :],
                                           scalar=b.t["RM0"][:, i:i + 1], in1=pa[:, :],
                                           op0=ALU.mult, op1=ALU.add), 1)
            s_v.wait(v)
            for oc in range(D // 512):
                s_v.inc(v.scalar_tensor_tensor(
                    out=pa[:, oc * 512:(oc + 1) * 512],
                    in0=delps[:, oc * 512:(oc + 1) * 512],
                    scalar=b.t["RM1"][:, i:i + 1], in1=pb[:, oc * 512:(oc + 1) * 512],
                    op0=ALU.mult, op1=ALU.add), 1)
        b.on("vector", f)
        def f(s, i=i):
            s_v.wait(s)
            s_p1.inc(s.dma_start(out=PART1[i * 128:(i + 1) * 128, :], in_=pa[:, :]), 16)
        b.on("sync", f)
        ln_stats(b, hst, D, stat, scr, s_sc, s_v)
        ln_apply(b, hst, pb, D, stat, s_sc, s_v, gbc=g1, bbc=b1)
        def f(v):
            s_v.wait(v)
            s_nb.wait(v)
            s_v.inc(v.tensor_copy(out=xnb[:, :], in_=pb[:, :]), 1)
        b.on("vector", f)
        def f(s, i=i):
            s_v.wait(s)
            s_nb.inc(s.dma_start(out=XNBF[i * 128:(i + 1) * 128, :], in_=xnb[:, :]), 16)
        b.on("sync", f)


def xt_dummy_src(b, hst, fc):
    return hst[:, fc * 128:(fc + 1) * 128]


def emit_oproj_ln2(b, seg, ctxfm, WOB, HS, XPRIME, XN2BF, w, s_list):
    (s_g, s_d, s_e, s_t, s_v, s_sc, s_p, s_o) = s_list
    nc, cfg = b.nc, b.cfg
    D, T, NT, DC, NH = cfg.D, cfg.T, cfg.NT, cfg.DC, cfg.NH

    emit_local_fence(b)
    g2 = b.sb(seg, "G2BC", [128, D], F32)
    b2 = b.sb(seg, "B2BC", [128, D], F32)
    rowt = b.sb(seg, "B5ROW", [1, D], F32)
    s_br = b.sem("B5_br")
    with nc.psum_tensor("B5BCPS", [128, 512], F32) as bcps:
        for wname, dstbc in (("L0_ffn_ln2_scale", g2), ("L0_ffn_ln2_bias", b2)):
            def f(s, wname=wname):
                s_v.wait(s)
                s_br.inc(s.dma_start(out=rowt[0:1, :], in_=b.dram[wname][None, :]), 16)
            b.on("sync", f)
            def f(t):
                s_br.wait(t)
            b.on("tensor", f)
            bcast_rows(b, rowt[0:1, :], dstbc, D, s_t, s_v, bcps)

    wot = b.sb(seg, "WOT", [128, DC * D], BF16)
    s_wo = b.sem("B5_wo")
    def f(s):
        b.sems["B_pc"].wait(s)
        for hc in range(DC):
            s_wo.inc(s.dma_start(out=wot[:, hc * D:(hc + 1) * D],
                                 in_=WOB[:, hc * 128:(hc + 1) * 128], transpose=True), 16)
    b.on("sync", f)
    wo_done = s_wo.n

    ops = [b.ps(seg, "B5OP0", [128, 512]), b.ps(seg, "B5OP1", [128, 512])]
    hst = b.sb(seg, "B5HS", [128, D], F32)
    xp = b.sb(seg, "B5XP", [128, D], F32)
    scr = b.sb(seg, "B5SCR", [128, D], F32)
    stat = b.sb(seg, "B5STAT", [128, 8], F32)
    xnb = b.sb(seg, "B5XNB", [128, D], BF16)
    s_hs2 = b.sem("B5_hs")
    s_xp = b.sem("B5_xp")
    s_nb2 = b.sem("B5_nb")
    opfree = [0, 0]
    for tt in range(NT):
        def f(s, tt=tt):
            s_v.wait(s)
            b.sems["B1_hs"].wait(s)
            s_hs2.inc(s.dma_start(out=hst[:, :], in_=HS[tt * 128:(tt + 1) * 128, :]), 16)
        b.on("sync", f)
        hsl = s_hs2.n
        for oc in range(D // 512):
            bank = oc % 2
            def f(t, tt=tt, oc=oc, bank=bank, wo_done=wo_done):
                if tt == 0 and oc == 0:
                    s_wo.wait(t, wo_done)
                    s_v.wait(t)
                s_v.wait(t, opfree[bank])
                for h in range(NH):
                    s_t.inc(t.matmul(
                        ops[bank][:, :],
                        ctxfm[:, h * T + tt * 128: h * T + (tt + 1) * 128],
                        wot[:, h * D + oc * 512: h * D + (oc + 1) * 512],
                        start=(h == 0), stop=(h == NH - 1)), 1)
            b.on("tensor", f)
            mmdone = s_t.n
            def f(v, oc=oc, bank=bank, hsl=hsl, mmdone=mmdone):
                s_t.wait(v, mmdone)
                s_hs2.wait(v, hsl)
                if oc == 0:
                    s_xp.wait(v)
                    s_sc.wait(v)
                s_v.inc(v.tensor_tensor(out=xp[:, oc * 512:(oc + 1) * 512],
                                        in0=ops[bank][:, :],
                                        in1=hst[:, oc * 512:(oc + 1) * 512], op=ALU.add), 1)
            b.on("vector", f)
            opfree[bank] = s_v.n
        def f(s, tt=tt):
            s_v.wait(s)
            s_xp.inc(s.dma_start(out=XPRIME[tt * 128:(tt + 1) * 128, :], in_=xp[:, :]), 16)
        b.on("sync", f)
        def f(sc):
            s_xp.wait(sc)
        b.on("scalar", f)
        ln_stats(b, xp, D, stat, scr, s_sc, s_v)
        ln_apply(b, xp, scr, D, stat, s_sc, s_v, gbc=g2, bbc=b2)
        def f(v):
            s_v.wait(v)
            s_nb2.wait(v)
            s_v.inc(v.tensor_copy(out=xnb[:, :], in_=scr[:, :]), 1)
        b.on("vector", f)
        def f(s, tt=tt):
            s_v.wait(s)
            s_nb2.inc(s.dma_start(out=XN2BF[tt * 128:(tt + 1) * 128, :], in_=xnb[:, :]), 16)
        b.on("sync", f)


def emit_ffn_up(b, seg, XNSRC, HBF, s_list, tag, mode, wload_up):
    (s_g, s_d, s_e, s_t, s_v, s_sc, s_p, s_o) = s_list
    nc, cfg = b.nc, b.cfg
    D, T, NT, DC, FF = cfg.D, cfg.T, cfg.NT, cfg.DC, cfg.FF

    s_fmu = b.sem(tag + "_fmu")
    xnfm = b.sb(seg, tag + "XNFM", [128, DC * T], BF16)
    nbs = b.sems.get(tag + "_nbsrc")
    load_fm(b, xnfm, XNSRC, DC, T, s_fmu,
            wait_sems=[(nbs, nbs.n)] if nbs else [])
    xn_ready = s_fmu.n

    nsl = 2 if mode == "gelu" else 4
    wps = [b.sb(seg, tag + f"W1P{i}", [128, DC * 512], BF16) for i in range(nsl)]
    s_w1 = [b.sem(tag + f"_w1{i}") for i in range(nsl)]
    fps = [b.ps(seg, tag + "FPS0", [128, 512]), b.ps(seg, tag + "FPS1", [128, 512])]
    gpsfree = [0, 0]
    if mode == "gateup24":
        gps = [b.ps(seg, tag + "GPS0", [128, 512]), b.ps(seg, tag + "GPS1", [128, 512])]
        sg = b.sb(seg, tag + "SG", [128, 512], F32)
        hh = b.sb(seg, tag + "HH", [128, 512], F32)
        aa = b.sb(seg, tag + "AA", [128, 512], F32)
        nn = b.sb(seg, tag + "NN", [128, 512], F32)
        mk_ = b.sb(seg, tag + "MK", [128, 512], F32)
        hsp = b.sb(seg, tag + "HSP", [128, 512], F32)
    stg = b.sb(seg, tag + "FSTG", [128, 512], BF16)
    s_hst = b.sem(tag + "_hst")
    panfree = {}
    fpsfree = [0, 0]
    NP = FF // 512
    for ip in range(NP):
        slot = ip % 2
        if mode == "gelu":
            def f(s, ip=ip, slot=slot):
                s_t.wait(s, panfree.get(("w", slot), 0))
                wload_up(s, ip, wps[slot], s_w1[slot], "w")
            b.on("sync", f)
            wl = s_w1[slot].n
        else:
            def f(g, ip=ip, slot=slot):
                s_t.wait(g, panfree.get(("g", slot), 0))
                wload_up(g, ip, wps[slot], s_w1[slot], "gate")
            b.on("gpsimd", f)
            def f(g, ip=ip, slot=slot):
                s_t.wait(g, panfree.get(("u", slot), 0))
                wload_up(g, ip, wps[2 + slot], s_w1[2 + slot], "up")
            b.on("gpsimd", f)
            wl = s_w1[slot].n
            wl2 = s_w1[2 + slot].n
        for tt in range(NT):
            bank = tt % 2
            def f(t, ip=ip, tt=tt, bank=bank, slot=slot, wl=wl):
                s_w1[slot].wait(t, wl)
                if ip == 0 and tt == 0:
                    s_fmu.wait(t, xn_ready)
                if mode == "gelu":
                    s_sc.wait(t, fpsfree[bank])
                else:
                    s_sc.wait(t, fpsfree[bank])
                for fc in range(DC):
                    s_t.inc(t.matmul(
                        fps[bank][:, :],
                        xnfm[:, fc * T + tt * 128: fc * T + (tt + 1) * 128],
                        wps[slot][:, fc * 512:(fc + 1) * 512],
                        start=(fc == 0), stop=(fc == DC - 1)), 1)
            b.on("tensor", f)
            mmdone = s_t.n
            if mode == "gelu":
                def f(sc, bank=bank, mmdone=mmdone):
                    s_t.wait(sc, mmdone)
                    s_hst.wait(sc)
                    s_sc.inc(sc.activation(out=stg[:, :], in_=fps[bank][:, :],
                                           func=AF.Gelu_apprx_tanh), 1)
                b.on("scalar", f)
                fpsfree[bank] = s_sc.n
                def f(s, ip=ip, tt=tt):
                    s_sc.wait(s)
                    s_hst.inc(s.dma_start(
                        out=HBF[tt * 128:(tt + 1) * 128, ip * 512:(ip + 1) * 512],
                        in_=stg[:, :]), 16)
                b.on("sync", f)
            else:
                def f(t, ip=ip, tt=tt, bank=bank, slot=slot, wl2=wl2):
                    s_w1[2 + slot].wait(t, wl2)
                    s_v.wait(t, gpsfree[bank])
                    for fc in range(DC):
                        s_t.inc(t.matmul(
                            gps[bank][:, :],
                            xnfm[:, fc * T + tt * 128: fc * T + (tt + 1) * 128],
                            wps[2 + slot][:, fc * 512:(fc + 1) * 512],
                            start=(fc == 0), stop=(fc == DC - 1)), 1)
                b.on("tensor", f)
                mm2 = s_t.n
                def f(sc, bank=bank, mmdone=mmdone):
                    s_t.wait(sc, mmdone)
                    s_v.wait(sc)
                    s_sc.inc(sc.activation(out=sg[:, :], in_=fps[bank][:, :],
                                           func=AF.Silu), 1)
                b.on("scalar", f)
                fpsfree[bank] = s_sc.n
                def f(v, bank=bank, mm2=mm2):
                    s_t.wait(v, mm2)
                    s_sc.wait(v)
                    s_v.wait(v)
                    s_v.inc(v.tensor_tensor(out=hh[:, :], in0=sg[:, :],
                                            in1=gps[bank][:, :], op=ALU.mult), 1)
                b.on("vector", f)
                gpsfree[bank] = s_v.n
                def f(sc):
                    s_v.wait(sc)
                    s_sc.inc(sc.activation(out=aa[:, :], in_=hh[:, :], func=AF.Abs), 1)
                b.on("scalar", f)
                def f(v):
                    s_sc.wait(v)
                    a = [aa[:, k::4] for k in range(4)]
                    g01, g02, g03 = nn[:, 0::4], nn[:, 1::4], nn[:, 2::4]
                    g12 = nn[:, 3::4]
                    s_v.inc(v.tensor_tensor(out=g01, in0=a[1], in1=a[0], op=ALU.is_gt), 1)
                    s_v.inc(v.tensor_tensor(out=g02, in0=a[2], in1=a[0], op=ALU.is_gt), 1)
                    s_v.inc(v.tensor_tensor(out=g03, in0=a[3], in1=a[0], op=ALU.is_gt), 1)
                    s_v.inc(v.tensor_tensor(out=g12, in0=a[2], in1=a[1], op=ALU.is_gt), 1)
                    g13 = mk_[:, 0::4]; g23 = mk_[:, 1::4]
                    s_v.inc(v.tensor_tensor(out=g13, in0=a[3], in1=a[1], op=ALU.is_gt), 1)
                    s_v.inc(v.tensor_tensor(out=g23, in0=a[3], in1=a[2], op=ALU.is_gt), 1)
                    s_v.wait(v)
                    n0 = aa[:, 0::4]; n1 = aa[:, 1::4]; n2 = aa[:, 2::4]; n3 = aa[:, 3::4]
                    s_v.inc(v.tensor_tensor(out=n0, in0=g01, in1=g02, op=ALU.add), 1)
                    s_v.inc(v.tensor_tensor(out=n1, in0=g12, in1=g13, op=ALU.add), 1)
                    s_v.inc(v.tensor_tensor(out=n2, in0=g23, in1=g02, op=ALU.subtract), 1)
                    s_v.inc(v.tensor_tensor(out=n3, in0=g03, in1=g13, op=ALU.add), 1)
                    s_v.wait(v)
                    s_v.inc(v.tensor_tensor(out=n0, in0=n0, in1=g03, op=ALU.add), 1)
                    s_v.inc(v.tensor_tensor(out=n1, in0=n1, in1=g01, op=ALU.subtract), 1)
                    s_v.inc(v.tensor_tensor(out=n2, in0=n2, in1=g12, op=ALU.subtract), 1)
                    s_v.inc(v.tensor_tensor(out=n3, in0=n3, in1=g23, op=ALU.add), 1)
                    s_v.wait(v)
                    s_v.inc(v.tensor_scalar(out=n1, in0=n1, scalar1=1.0, scalar2=None,
                                            op0=ALU.add), 1)
                    s_v.inc(v.tensor_scalar(out=n2, in0=n2, scalar1=2.0, scalar2=None,
                                            op0=ALU.add), 1)
                    s_v.inc(v.tensor_scalar(out=n3, in0=n3, scalar1=-1.0, scalar2=3.0,
                                            op0=ALU.mult, op1=ALU.add), 1)
                    s_v.wait(v)
                    for k in range(4):
                        s_v.inc(v.tensor_scalar(out=mk_[:, k::4], in0=aa[:, k::4],
                                                scalar1=1.5, scalar2=None,
                                                op0=ALU.is_le), 1)
                    s_v.wait(v)
                    s_v.inc(v.tensor_tensor(out=hsp[:, :], in0=hh[:, :], in1=mk_[:, :],
                                            op=ALU.mult), 1)
                    s_v.wait(v)
                    s_hst.wait(v)
                    s_v.inc(v.tensor_copy(out=stg[:, :], in_=hsp[:, :]), 1)
                b.on("vector", f)
                def f(s, ip=ip, tt=tt):
                    s_v.wait(s)
                    s_hst.inc(s.dma_start(
                        out=HBF[tt * 128:(tt + 1) * 128, ip * 512:(ip + 1) * 512],
                        in_=stg[:, :]), 16)
                b.on("sync", f)
        if mode == "gelu":
            panfree[("w", slot)] = s_t.n
        else:
            panfree[("g", slot)] = s_t.n
            panfree[("u", slot)] = s_t.n



_W_NAMES = [
    "L0_attn_qkv", "L0_attn_o_proj", "L0_attn_gamma", "L0_attn_beta",
    "L0_ffn_simple_bias", "L0_ffn_w_down", "L0_ffn_w_up", "L0_ffn_w_router",
    "L0_ffn_ln1_scale", "L0_ffn_ln1_bias", "L0_ffn_wq", "L0_ffn_wk",
    "L0_ffn_wv", "L0_ffn_wo", "L0_ffn_ln2_scale", "L0_ffn_ln2_bias",
    "L0_ffn_w_ff1", "L0_ffn_w_ff2", "L1_attn_w_qkv", "L1_attn_b_qkv",
    "L1_attn_w_out", "L1_attn_b_out", "L1_ffn_w_gate", "L1_ffn_w_up",
    "L1_ffn_w_down",
]


def kernel(**inputs):
    from concourse.bass_utils import run_bass_kernel_spmd
    cfg = CFG(D=2048, T=2048, FF=8192, LR=32)
    x = np.ascontiguousarray(inputs["L0_attn_x"], dtype=np.float32)
    shards = shard_x(x)
    w = {n: np.ascontiguousarray(inputs[n], dtype=np.float32) for n in _W_NAMES}
    b = build(cfg, upto="D")
    in_maps = []
    for c in range(8):
        m = {"x": shards[c], "masks": host_masks(c % 4)}
        m.update(w)
        in_maps.append(m)
    res = run_bass_kernel_spmd(b.nc, in_maps, core_ids=list(range(8)))
    ys = [res.results[c]["y"] for c in range(8)]
    return unshard_y(cfg, ys)


# version 05
# speedup vs baseline: 1.0465x; 1.0070x over previous
import contextlib
import numpy as np
import concourse.bass as bass
import concourse.mybir as mybir

F32 = mybir.dt.float32
BF16 = mybir.dt.bfloat16
AF = mybir.ActivationFunctionType
ALU = mybir.AluOpType
AX = mybir.AxisListType


class CFG:
    def __init__(self, D=2048, T=2048, FF=8192, LR=32):
        self.D = D; self.T = T; self.FF = FF; self.LR = LR
        self.R = 4; self.DH = 128
        self.NH = D // 128
        self.NT = T // 128
        self.NG = T // 512
        self.DC = D // 128
        self.FC = FF // 128
        self.S = 4 * T
        self.EPS = 1e-5
        self.maxslots = 4 * (4 * (self.NG - 1) + 4)


def cdiv(a, b):
    return (a + b - 1) // b


class Sem:
    def __init__(self, nc, es, name):
        self.h = es.enter_context(nc.semaphore(name)); self.n = 0
    def inc(self, inst, k=1):
        inst.then_inc(self.h, k); self.n += k
    def wait(self, eng, val=None):
        eng.wait_ge(self.h, self.n if val is None else val)


class B:
    def __init__(self, cfg: CFG):
        self.cfg = cfg
        self.nc = bass.Bass("TRN2", target_bir_lowering=False, num_devices=8)
        self.es = contextlib.ExitStack()
        self.t = {}
        self.dram = {}
        self.sems = {}
        self.blk = None

    def sem(self, name):
        if name not in self.sems:
            self.sems[name] = Sem(self.nc, self.es, name)
        return self.sems[name]

    def sb(self, es, name, shape, dtype):
        self.t[name] = es.enter_context(self.nc.sbuf_tensor(name, shape, dtype))
        return self.t[name]

    def ps(self, es, name, shape, dtype=F32):
        self.t[name] = es.enter_context(self.nc.psum_tensor(name, shape, dtype))
        return self.t[name]

    def dr(self, name, shape, dtype, kind="Internal"):
        self.dram[name] = self.nc.dram_tensor(name, shape, dtype, kind=kind).ap()
        return self.dram[name]

    def on(self, engname, f):
        getattr(self.blk, engname)(f)



def host_masks(j: int):
    import ml_dtypes
    out = np.zeros((4, 4, 128, 512), dtype=np.float32)
    k = np.arange(128)[:, None]; q = np.arange(128)[None, :]
    for d in range(4):
        for r in range(4):
            tri = (k < q) | ((k == q) & (r <= j))
            for m in range(4):
                if d < m:
                    out[d, r, :, 128 * m:128 * (m + 1)] = 1.0
                elif d == m:
                    out[d, r, :, 128 * m:128 * (m + 1)] = tri
    return np.ascontiguousarray(out.astype(ml_dtypes.bfloat16))


def shard_x(x):
    return [np.ascontiguousarray(x[c // 4, (c % 4)::4, :]) for c in range(8)]


def unshard_y(cfg, ys):
    out = np.empty((2, cfg.S, cfg.D), dtype=np.float32)
    for c in range(8):
        out[c // 4, (c % 4)::4, :] = ys[c]
    return out



def ln_stats(b, src, n, stat, scratch, s_sc, s_v):
    cfg = b.cfg
    def fsc(sc):
        s_v.wait(sc)
        s_sc.inc(sc.activation(out=scratch[:, 0:n], in_=src[:, 0:n], func=AF.Square,
                               accum_out=stat[:, 0:1]), 1)
    b.on("scalar", fsc)
    def fv(v):
        s_v.wait(v)
        s_v.inc(v.reduce_sum(out=stat[:, 1:2], in_=src[:, 0:n], axis=AX.X), 1)
        s_v.wait(v)
        s_v.inc(v.tensor_scalar(out=stat[:, 2:3], in0=stat[:, 1:2], scalar1=1.0 / n,
                                scalar2=None, op0=ALU.mult), 1)
        s_sc.wait(v)
        s_v.inc(v.tensor_scalar(out=stat[:, 3:4], in0=stat[:, 0:1], scalar1=1.0 / n,
                                scalar2=None, op0=ALU.mult), 1)
        s_v.wait(v)
        s_v.inc(v.tensor_tensor(out=stat[:, 4:5], in0=stat[:, 2:3], in1=stat[:, 2:3],
                                op=ALU.mult), 1)
        s_v.wait(v)
        s_v.inc(v.tensor_tensor(out=stat[:, 5:6], in0=stat[:, 3:4], in1=stat[:, 4:5],
                                op=ALU.subtract), 1)
        s_v.wait(v)
        s_v.inc(v.tensor_scalar(out=stat[:, 5:6], in0=stat[:, 5:6], scalar1=float(cfg.EPS),
                                scalar2=None, op0=ALU.add), 1)
        v_ = stat[:, 5:6]
        y = stat[:, 7:8]
        t_ = stat[:, 6:7]
        s_v.wait(v)
        s_v.inc(v.tensor_scalar(out=y.bitcast(mybir.dt.uint32), in0=v_.bitcast(mybir.dt.uint32),
                                scalar1=1, scalar2=None, op0=ALU.logical_shift_right), 1)
        s_v.wait(v)
        s_v.inc(v.tensor_tensor(out=y.bitcast(mybir.dt.uint32), in0=b.t["MAGIC"][:, :],
                                in1=y.bitcast(mybir.dt.uint32), op=ALU.subtract), 1)
        for _ in range(3):
            s_v.wait(v)
            s_v.inc(v.tensor_tensor(out=t_, in0=y, in1=y, op=ALU.mult), 1)
            s_v.wait(v)
            s_v.inc(v.tensor_tensor(out=t_, in0=t_, in1=v_, op=ALU.mult), 1)
            s_v.wait(v)
            s_v.inc(v.tensor_scalar(out=t_, in0=t_, scalar1=-0.5, scalar2=1.5,
                                    op0=ALU.mult, op1=ALU.add), 1)
            s_v.wait(v)
            s_v.inc(v.tensor_tensor(out=y, in0=y, in1=t_, op=ALU.mult), 1)
    b.on("vector", fv)


def ln_apply(b, src, dst, n, stat, s_sc, s_v, gbc=None, bbc=None):
    def fv(v):
        s_sc.wait(v)
        s_v.wait(v)
        s_v.inc(v.tensor_scalar(out=dst[:, 0:n], in0=src[:, 0:n],
                                scalar1=stat[:, 2:3], scalar2=stat[:, 7:8],
                                op0=ALU.subtract, op1=ALU.mult), 1)
        if gbc is not None:
            s_v.wait(v)
            s_v.inc(v.tensor_tensor(out=dst[:, 0:n], in0=dst[:, 0:n], in1=gbc[:, 0:n],
                                    op=ALU.mult), 1)
            s_v.wait(v)
            s_v.inc(v.tensor_tensor(out=dst[:, 0:n], in0=dst[:, 0:n], in1=bbc[:, 0:n],
                                    op=ALU.add), 1)
    b.on("vector", fv)


def bcast_rows(b, row_sb, out_sb, n, s_t, s_v, bcps):
    for i in range(cdiv(n, 512)):
        c0, c1 = i * 512, min(n, (i + 1) * 512)
        def ft(t, c0=c0, c1=c1):
            s_v.wait(t)
            s_t.inc(t.matmul(bcps[:, 0:c1 - c0], b.t["ONES_row_f32"][:, :],
                             row_sb[:, c0:c1], start=True, stop=True), 1)
        b.on("tensor", ft)
        def fv(v, c0=c0, c1=c1):
            s_t.wait(v)
            s_v.inc(v.tensor_copy(out=out_sb[:, c0:c1], in_=bcps[:, 0:c1 - c0]), 1)
        b.on("vector", fv)


def load_fm(b, dst_sb, src_dram, nchunks, tcols, s_d, wait_sems=()):
    def fs(s):
        for sm, val in wait_sems:
            sm.wait(s, val)
        for fc in range(nchunks):
            s_d.inc(s.dma_start(out=dst_sb[:, fc * tcols:(fc + 1) * tcols],
                                in_=src_dram[:, fc * 128:(fc + 1) * 128], transpose=True), 16)
    b.on("sync", fs)



def proj_fm_out(b, wsrc_dram, wcol0, ncols, dst_dram, drow0, s_g, s_t, s_v, s_d,
                xnfm, prj, stg, wslots, wload_snap, scale_cols=None, bias_pp=None):
    raise NotImplementedError



def build(cfg: CFG, upto="D"):
    b = B(cfg)
    nc = b.nc
    import ml_dtypes
    D, T, NT, DC, NG, NH = cfg.D, cfg.T, cfg.NT, cfg.DC, cfg.NG, cfg.NH
    FF, FC, LR = cfg.FF, cfg.FC, cfg.LR

    x_in = b.dr("x", [T, D], F32, kind="ExternalInput")
    msk_in = b.dr("masks", [4, 4, 128, 512], BF16, kind="ExternalInput")
    wnames = [
        ("L0_attn_qkv", [D, 3 * D]), ("L0_attn_o_proj", [D, D]),
        ("L0_attn_gamma", [D]), ("L0_attn_beta", [D]),
        ("L0_ffn_simple_bias", [D]), ("L0_ffn_w_down", [LR, D]), ("L0_ffn_w_up", [D, LR]),
        ("L0_ffn_w_router", [3, D]), ("L0_ffn_ln1_scale", [D]), ("L0_ffn_ln1_bias", [D]),
        ("L0_ffn_wq", [D, D]), ("L0_ffn_wk", [D, D]), ("L0_ffn_wv", [D, D]),
        ("L0_ffn_wo", [D, D]), ("L0_ffn_ln2_scale", [D]), ("L0_ffn_ln2_bias", [D]),
        ("L0_ffn_w_ff1", [FF, D]), ("L0_ffn_w_ff2", [D, FF]),
        ("L1_attn_w_qkv", [D, 3 * D]), ("L1_attn_b_qkv", [3 * D]),
        ("L1_attn_w_out", [D, D]), ("L1_attn_b_out", [D]),
        ("L1_ffn_w_gate", [D, FF]), ("L1_ffn_w_up", [D, FF]), ("L1_ffn_w_down", [FF, D]),
    ]
    w = {}
    for name, shape in wnames:
        w[name] = b.dr(name, shape, F32, kind="ExternalInput")
    y_out = b.dr("y", [T, D], F32, kind="ExternalOutput")

    X = b.dr("X", [T, D], F32)
    XNBF = b.dr("XNBF", [T, D], BF16)
    QF = b.dr("QF", [D, T], BF16)
    KLOC = b.dr("KLOC", [D, T], BF16)
    VLOC = b.dr("VLOC", [T, D], BF16)
    KALL = b.dr("KALL", [4 * D, T], BF16)
    VALL = b.dr("VALL", [4 * T, D], BF16)
    CTXBF = b.dr("CTXBF", [T, D], BF16)
    KN2L = b.dr("KN2L", [1, 128], F32)
    KN2G = b.dr("KN2G", [1, 128], F32)
    HS = b.dr("HS", [T, D], F32)
    PART1 = b.dr("PART1", [T, D], F32)
    XPRIME = b.dr("XPRIME", [T, D], F32)
    XN2BF = b.dr("XN2BF", [T, D], BF16)
    HBF = b.dr("HBF", [T, FF], BF16)
    WQB = b.dr("WQB", [D, D], BF16)
    WKB = b.dr("WKB", [D, D], BF16)
    WVB = b.dr("WVB", [D, D], BF16)
    WOB = b.dr("WOB", [D, D], BF16)
    F1B = b.dr("F1B", [FF, D], BF16)
    F2B = b.dr("F2B", [D, FF], BF16)
    WDB = b.dr("WDB", [LR, D], BF16)
    WUB = b.dr("WUB", [D, LR], BF16)

    idbf = nc.inline_tensor(np.eye(128, dtype=ml_dtypes.bfloat16), "c_idbf").ap()
    idf32 = nc.inline_tensor(np.eye(128, dtype=np.float32), "c_idf32").ap()
    b.sb(b.es, "ID_BF", [128, 128], BF16)
    b.sb(b.es, "ID_F32", [128, 128], F32)
    b.sb(b.es, "ONES_col_bf", [128, 1], BF16)
    b.sb(b.es, "ONES_row_f32", [1, 128], F32)
    b.sb(b.es, "MSK", [128, 16 * 512], BF16)
    b.sb(b.es, "ANMAX", [1, 2], F32)
    b.sb(b.es, "BNMAX", [1, 2 * NH], F32)
    b.sb(b.es, "CNMAX", [1, 2 * NH], F32)
    b.sb(b.es, "MAGIC", [128, 1], mybir.dt.uint32)
    b.sb(b.es, "RM0", [128, NT], F32)
    b.sb(b.es, "RM1", [128, NT], F32)
    b.sb(b.es, "RM2", [128, NT], F32)

    s_g = b.sem("s_g"); s_d = b.sem("s_d"); s_e = b.sem("s_e")
    s_t = b.sem("s_t"); s_v = b.sem("s_v"); s_sc = b.sem("s_sc")
    s_p = b.sem("s_p"); s_cc = b.sem("s_cc"); s_o = b.sem("s_o")
    s_pc = b.sem("B_pc")
    s_y = b.sem("s_y")
    S = (s_g, s_d, s_e, s_t, s_v, s_sc, s_p, s_o)

    def fin_copy(xsem_names):
        seg = contextlib.ExitStack()
        emit_seg_fence(b)
        fin = b.sb(seg, "FIN", [128, D], F32)
        def f(s):
            for nm in xsem_names:
                if nm in b.sems:
                    b.sems[nm].wait(s)
            for i in range(NT):
                s_d.inc(s.dma_start(out=fin[:, :], in_=X[i * 128:(i + 1) * 128, :]), 16)
                s_d.wait(s)
                s_y.inc(s.dma_start(out=y_out[i * 128:(i + 1) * 128, :], in_=fin[:, :]), 16)
                s_y.wait(s)
        b.on("sync", f)
        seg.close()

    with b.es:
        blk0 = b.es.enter_context(nc.Block("main"))
        b.blk = blk0
        D512 = D // 512

        seg = contextlib.ExitStack()
        def f(s):
            s_d.inc(s.dma_start(out=b.t["ID_BF"][:, :], in_=idbf[:, :]), 16)
            s_d.inc(s.dma_start(out=b.t["ID_F32"][:, :], in_=idf32[:, :]), 16)
            s_d.inc(s.dma_start(
                out=b.t["MSK"][:, :].rearrange("p (s n) -> p s n", n=512),
                in_=msk_in.rearrange("d r p n -> p (d r) n")), 16)
        b.on("sync", f)
        def f(g):
            g.memset(b.t["ONES_col_bf"][:, :], 1.0)
            g.memset(b.t["MAGIC"][:, :], 0x5f3759df)
            s_p.inc(g.memset(b.t["ONES_row_f32"][:, :], 1.0), 1)
        b.on("gpsimd", f)
        emit_attnA(b, seg, x_in, w, X, XNBF, QF, KLOC, VLOC, KN2L, s_list=S)
        if upto != "A":
            def f(g):
                for src_n, dst in (("L0_ffn_wq", WQB), ("L0_ffn_wk", WKB),
                                   ("L0_ffn_wv", WVB), ("L0_ffn_wo", WOB),
                                   ("L0_ffn_w_ff1", F1B), ("L0_ffn_w_ff2", F2B),
                                   ("L0_ffn_w_down", WDB), ("L0_ffn_w_up", WUB)):
                    s_pc.inc(g.dma_start(out=dst[:, :], in_=b.dram[src_n][:, :]), 16)
            b.on("gpsimd", f)
        seg.close()

        seg = contextlib.ExitStack()
        emit_seg_fence(b)
        sub = contextlib.ExitStack()
        emit_flash_L0(b, sub, QF, KALL, VALL, KN2G, CTXBF, s_list=S)
        sub.close()
        emit_oproj_resid(b, seg, CTXBF, w["L0_attn_o_proj"], X, s_list=S,
                         tag="A5", prev_x_sem="A_xst", src_store_sem="fl_cts",
                         dst=X, dst_sem_name="op_xw")
        seg.close()

        if upto == "A":
            fin_copy(["op_xw"])
            return b

        seg = contextlib.ExitStack()
        emit_seg_fence(b)
        emit_B1(b, seg, X, w, HS, PART1, XNBF, s_list=S)
        seg.close()

        seg = contextlib.ExitStack()
        emit_seg_fence(b)
        NP3 = 3 * D512
        order_kvq = list(range(D512, 3 * D512)) + list(range(D512))
        def after_kv_B():
            emit_norms(b, seg, QF, KLOC, KN2L, s_list=S, per_head=True, tag="B",
                       which="k")
            emit_gather3(b, KLOC, VLOC, KN2L, KALL, VALL, KN2G,
                         wait_names=["qk_stg", "qk_stgv", "knst"])
        emit_qkv_proj(b, seg, XNBF, None, None, QF, KLOC, VLOC, s_list=S,
                      interleaved=False, tag="B", nb_sem="B1_nb",
                      wsrcT={"q": WQB, "k": WKB, "v": WVB},
                      order=order_kvq, after_kv=(2 * D512, after_kv_B))
        emit_norms(b, seg, QF, KLOC, KN2L, s_list=S, per_head=True, tag="B",
                   which="q")
        seg.close()

        seg = contextlib.ExitStack()
        emit_seg_fence(b)
        ctxfm = b.sb(seg, "BCTXFM", [128, NH * T], BF16)
        sub = contextlib.ExitStack()
        emit_flash_MHA(b, sub, QF, KALL, VALL, KN2G, ctxfm, "BNMAX", s_list=S, tag="BF")
        sub.close()
        emit_oproj_ln2(b, seg, ctxfm, WOB, HS, XPRIME, XN2BF, w, s_list=S)
        seg.close()

        seg = contextlib.ExitStack()
        emit_seg_fence(b)
        sub = contextlib.ExitStack()
        b.sems["B6_nbsrc"] = b.sems["B5_nb"]
        def w1load_B(s, ip, dst, sem, which):
            if ip == 0:
                b.sems["B_pc"].wait(s)
            for fc in range(DC):
                sem.inc(s.dma_start(out=dst[:, fc * 512:(fc + 1) * 512],
                                    in_=F1B[ip * 512:(ip + 1) * 512, fc * 128:(fc + 1) * 128],
                                    transpose=True), 16)
        emit_ffn_up(b, sub, XN2BF, HBF, s_list=S, tag="B6", mode="gelu",
                    wload_up=w1load_B)
        sub.close()
        emit_local_fence(b)
        xpb = [b.sb(seg, "B6XPa", [128, 512], F32), b.sb(seg, "B6XPb", [128, 512], F32)]
        xcb = [b.sb(seg, "B6XCa", [128, 512], F32), b.sb(seg, "B6XCb", [128, 512], F32)]
        ptb = [b.sb(seg, "B6PTa", [128, 512], F32), b.sb(seg, "B6PTb", [128, 512], F32)]
        tb1 = b.sb(seg, "B6T1", [128, 512], F32)
        tb2 = b.sb(seg, "B6T2", [128, 512], F32)
        stb = b.sb(seg, "B6ST", [128, 512], F32)
        s_x3 = [b.sem("B6_x3a"), b.sem("B6_x3b")]
        s_xw3 = b.sem("B6_xw")
        ecnt = [0]
        def evacB(tt, oc, psum):
            par = ecnt[0] % 2
            ecnt[0] += 1
            def f(s, tt=tt, oc=oc, par=par):
                s_v.wait(s)
                if tt == 0 and oc == 0:
                    b.sems["B5_xp"].wait(s)
                    b.sems["B1_p1"].wait(s)
                s_x3[par].inc(s.dma_start(
                    out=xpb[par][:, :],
                    in_=XPRIME[tt * 128:(tt + 1) * 128, oc * 512:(oc + 1) * 512]), 16)
                s_x3[par].inc(s.dma_start(
                    out=xcb[par][:, :],
                    in_=X[tt * 128:(tt + 1) * 128, oc * 512:(oc + 1) * 512]), 16)
                s_x3[par].inc(s.dma_start(
                    out=ptb[par][:, :],
                    in_=PART1[tt * 128:(tt + 1) * 128, oc * 512:(oc + 1) * 512]), 16)
            b.on("sync", f)
            xl = s_x3[par].n
            def f(v, tt=tt, oc=oc, par=par, psum=psum, xl=xl):
                s_t.wait(v)
                s_v.wait(v)
                s_x3[par].wait(v, xl)
                s_xw3.wait(v)
                s_v.inc(v.tensor_tensor(out=tb1[:, :], in0=psum[:, :],
                                        in1=xpb[par][:, :], op=ALU.add), 1)
                s_v.inc(v.tensor_tensor(out=tb2[:, :], in0=xcb[par][:, :],
                                        in1=ptb[par][:, :], op=ALU.add), 1)
                s_v.wait(v)
                s_v.inc(v.scalar_tensor_tensor(
                    out=stb[:, :], in0=tb1[:, :], scalar=b.t["RM2"][:, tt:tt + 1],
                    in1=tb2[:, :], op0=ALU.mult, op1=ALU.add), 1)
            b.on("vector", f)
            def f(s, tt=tt, oc=oc):
                s_v.wait(s)
                s_xw3.inc(s.dma_start(
                    out=X[tt * 128:(tt + 1) * 128, oc * 512:(oc + 1) * 512],
                    in_=stb[:, :]), 16)
            b.on("sync", f)
        def w2load_B2(s, oc, w2p, sem):
            for ic in range(FC):
                sem.inc(s.dma_start(out=w2p[:, ic * 512:(ic + 1) * 512],
                                    in_=F2B[oc * 512:(oc + 1) * 512, ic * 128:(ic + 1) * 128],
                                    transpose=True), 16)
        emit_ffn_down(b, seg, HBF, w2load_B2, evacB, s_list=S, tag="B6d",
                      loader_eng="sync", h_store_sem="B6_hst")
        seg.close()

        if upto == "B":
            fin_copy(["B6_xw"])
            return b

        seg = contextlib.ExitStack()
        emit_seg_fence(b)
        emit_ln_simple(b, seg, X, XNBF, s_list=S, tag="C1", prev_x_sem="B6_xw")
        vb = b.sb(seg, "CVBC", [128, D], F32)
        vrow = b.sb(seg, "CVROW", [1, D], F32)
        s_vb = b.sem("C_vb")
        with nc.psum_tensor("CVBPS", [128, 512], F32) as bcps:
            def f(s):
                for h in range(NH):
                    s_vb.inc(s.dma_start(out=vrow[0:1, h * 128:(h + 1) * 128],
                                         in_=w["L1_attn_b_qkv"][None, 384 * h + 256:384 * h + 384]), 16)
            b.on("sync", f)
            def f(t):
                s_vb.wait(t)
            b.on("tensor", f)
            bcast_rows(b, vrow[0:1, :], vb, D, s_t, s_v, bcps)
        emit_qkv_proj(b, seg, XNBF, w["L1_attn_w_qkv"], w["L1_attn_b_qkv"],
                      QF, KLOC, VLOC, s_list=S, interleaved=True, tag="C",
                      nb_sem="C1_nb", vbias_bc=vb)
        emit_norms(b, seg, QF, KLOC, KN2L, s_list=S, per_head=True, tag="C",
                   which="k")
        emit_gather3(b, KLOC, VLOC, KN2L, KALL, VALL, KN2G,
                     wait_names=["qk_stg", "qk_stgv", "knst"])
        emit_norms(b, seg, QF, KLOC, KN2L, s_list=S, per_head=True, tag="C",
                   which="q")
        seg.close()

        seg = contextlib.ExitStack()
        emit_seg_fence(b)
        ctxfm2 = b.sb(seg, "CCTXFM", [128, NH * T], BF16)
        sub = contextlib.ExitStack()
        emit_flash_MHA(b, sub, QF, KALL, VALL, KN2G, ctxfm2, "CNMAX", s_list=S, tag="CF")
        sub.close()
        emit_cproj(b, seg, ctxfm2, w["L1_attn_w_out"], w["L1_attn_b_out"], X, s_list=S)
        seg.close()

        if upto == "C":
            fin_copy(["C5_xw"])
            return b

        seg = contextlib.ExitStack()
        emit_seg_fence(b)
        emit_ln_simple(b, seg, X, XNBF, s_list=S, tag="D1", prev_x_sem="C5_xw")
        sub = contextlib.ExitStack()
        b.sems["D2_nbsrc"] = b.sems["D1_nb"]
        def wgu_load(g, ip, dst, sem, which):
            wsrc = w["L1_ffn_w_gate"] if which == "gate" else w["L1_ffn_w_up"]
            sem.inc(g.dma_start(
                out=dst[:, :].rearrange("p (c n) -> p c n", n=512),
                in_=wsrc[:, ip * 512:(ip + 1) * 512].rearrange("(c p) n -> p c n", p=128)), 16)
        emit_ffn_up(b, sub, XNBF, HBF, s_list=S, tag="D2", mode="gateup24",
                    wload_up=wgu_load)
        sub.close()
        emit_local_fence(b)
        xcd = [b.sb(seg, "DXCa", [128, 512], F32), b.sb(seg, "DXCb", [128, 512], F32)]
        yod = b.sb(seg, "DYO", [128, 512], F32)
        s_x4 = [b.sem("D_x4a"), b.sem("D_x4b")]
        ecnt2 = [0]
        def w2load_D(g, oc, w2p, sem):
            sem.inc(g.dma_start(
                out=w2p[:, :].rearrange("p (c n) -> p c n", n=512),
                in_=w["L1_ffn_w_down"][:, oc * 512:(oc + 1) * 512]
                    .rearrange("(c p) n -> p c n", p=128)), 16)
        def evacD(tt, oc, psum):
            par = ecnt2[0] % 2
            ecnt2[0] += 1
            def f(s, tt=tt, oc=oc, par=par):
                s_v.wait(s)
                s_x4[par].inc(s.dma_start(
                    out=xcd[par][:, :],
                    in_=X[tt * 128:(tt + 1) * 128, oc * 512:(oc + 1) * 512]), 16)
            b.on("sync", f)
            xl = s_x4[par].n
            def f(v, tt=tt, oc=oc, par=par, psum=psum, xl=xl):
                s_t.wait(v)
                s_x4[par].wait(v, xl)
                s_y.wait(v)
                s_v.inc(v.tensor_tensor(out=yod[:, :], in0=psum[:, :],
                                        in1=xcd[par][:, :], op=ALU.add), 1)
            b.on("vector", f)
            def f(s, tt=tt, oc=oc):
                s_v.wait(s)
                s_y.inc(s.dma_start(
                    out=y_out[tt * 128:(tt + 1) * 128, oc * 512:(oc + 1) * 512],
                    in_=yod[:, :]), 16)
            b.on("sync", f)
        emit_ffn_down(b, seg, HBF, w2load_D, evacD, s_list=S, tag="D3",
                      loader_eng="gpsimd", h_store_sem="D2_hst")
        def f(s):
            s_y.wait(s)
        b.on("sync", f)
        seg.close()
    return b


def emit_local_fence(b, engs=("sync", "gpsimd", "scalar", "vector", "tensor")):
    snap = [(s.h, s.n) for s in b.sems.values() if s.n > 0]
    for eng in engs:
        def f(e, snap=snap):
            for h, n in snap:
                e.wait_ge(h, n)
        b.on(eng, f)


def emit_seg_fence(b):
    snap = [(s.h, s.n) for s in b.sems.values() if s.n > 0]
    for eng in ("sync", "gpsimd", "scalar", "vector", "tensor"):
        def f(e, snap=snap):
            for h, n in snap:
                e.wait_ge(h, n)
        b.on(eng, f)


def emit_ln_simple(b, seg, X, XNBF, s_list, tag, prev_x_sem):
    (s_g, s_d, s_e, s_t, s_v, s_sc, s_p, s_o) = s_list
    nc, cfg = b.nc, b.cfg
    D, NT = b.cfg.D, b.cfg.NT
    xt = b.sb(seg, tag + "XT", [128, D], F32)
    scr = b.sb(seg, tag + "SCR", [128, D], F32)
    stat = b.sb(seg, tag + "STAT", [128, 8], F32)
    xnb = b.sb(seg, tag + "XNB", [128, D], BF16)
    s_xt = b.sem("ln_xt")
    s_nb = b.sem(tag + "_nb")
    for i in range(NT):
        def f(s, i=i):
            s_v.wait(s); s_sc.wait(s)
            if i == 0 and prev_x_sem in b.sems:
                b.sems[prev_x_sem].wait(s)
            s_xt.inc(s.dma_start(out=xt[:, :], in_=X[i * 128:(i + 1) * 128, :]), 16)
        b.on("sync", f)
        def f(v):
            s_xt.wait(v)
        b.on("vector", f)
        def f(sc):
            s_xt.wait(sc)
        b.on("scalar", f)
        ln_stats(b, xt, D, stat, scr, s_sc, s_v)
        ln_apply(b, xt, scr, D, stat, s_sc, s_v)
        def f(v):
            s_v.wait(v)
            s_nb.wait(v)
            s_v.inc(v.tensor_copy(out=xnb[:, :], in_=scr[:, :]), 1)
        b.on("vector", f)
        def f(s, i=i):
            s_v.wait(s)
            s_nb.inc(s.dma_start(out=XNBF[i * 128:(i + 1) * 128, :], in_=xnb[:, :]), 16)
        b.on("sync", f)


def emit_cproj(b, seg, ctxfm, w_out, b_out, X, s_list):
    (s_g, s_d, s_e, s_t, s_v, s_sc, s_p, s_o) = s_list
    nc, cfg = b.nc, b.cfg
    D, T, NT, DC, NH = cfg.D, cfg.T, cfg.NT, cfg.DC, cfg.NH

    emit_local_fence(b)
    bo = b.sb(seg, "CBOBC", [128, D], F32)
    rowt = b.sb(seg, "C5ROW", [1, D], F32)
    s_br = b.sem("C5_br")
    with nc.psum_tensor("C5BCPS", [128, 512], F32) as bcps:
        def f(s):
            s_br.inc(s.dma_start(out=rowt[0:1, :], in_=b_out[None, :]), 16)
        b.on("sync", f)
        def f(t):
            s_br.wait(t)
        b.on("tensor", f)
        bcast_rows(b, rowt[0:1, :], bo, D, s_t, s_v, bcps)

    NPAN = D // 512
    owp = b.sb(seg, "C5WP", [128, NPAN * DC * 512], BF16)
    prj = [b.ps(seg, "C5P0", [128, 512]), b.ps(seg, "C5P1", [128, 512])]
    xrt = b.sb(seg, "C5XR", [128, D], F32)
    xwt = b.sb(seg, "C5XW", [128, D], F32)
    s_ow = b.sem("C5_ow")
    s_xr = b.sem("C5_xr")
    s_xw = b.sem("C5_xw")
    for p in range(NPAN):
        def f(g, p=p):
            s_ow.inc(g.dma_start(
                out=owp[:, p * DC * 512:(p + 1) * DC * 512]
                    .rearrange("p (c n) -> p c n", n=512),
                in_=w_out[:, p * 512:(p + 1) * 512].rearrange("(c p) n -> p c n", p=128)), 16)
        b.on("gpsimd", f)
    owdone = s_ow.n
    prjfree = [0, 0]
    for tt in range(NT):
        def f(s, tt=tt):
            s_v.wait(s)
            if tt == 0:
                b.sems["B6_xw"].wait(s)
            s_xr.inc(s.dma_start(out=xrt[:, :], in_=X[tt * 128:(tt + 1) * 128, :]), 16)
        b.on("sync", f)
        xload = s_xr.n
        for p in range(NPAN):
            bank = p % 2
            def f(t, tt=tt, p=p, bank=bank):
                if tt == 0 and p == 0:
                    s_ow.wait(t, owdone)
                    s_v.wait(t)
                s_v.wait(t, prjfree[bank])
                for h in range(NH):
                    s_t.inc(t.matmul(
                        prj[bank][:, :],
                        ctxfm[:, h * T + tt * 128: h * T + (tt + 1) * 128],
                        owp[:, p * DC * 512 + h * 512: p * DC * 512 + (h + 1) * 512],
                        start=(h == 0), stop=(h == NH - 1)), 1)
            b.on("tensor", f)
            mmdone = s_t.n
            def f(v, p=p, bank=bank, xload=xload, tt=tt):
                s_t.wait(v, mmdone)
                s_xr.wait(v, xload)
                if p == 0:
                    s_xw.wait(v)
                s_v.inc(v.tensor_tensor(out=xwt[:, p * 512:(p + 1) * 512],
                                        in0=prj[bank][:, :],
                                        in1=bo[:, p * 512:(p + 1) * 512], op=ALU.add), 1)
                s_v.wait(v)
                s_v.inc(v.tensor_tensor(out=xwt[:, p * 512:(p + 1) * 512],
                                        in0=xwt[:, p * 512:(p + 1) * 512],
                                        in1=xrt[:, p * 512:(p + 1) * 512], op=ALU.add), 1)
            b.on("vector", f)
            prjfree[bank] = s_v.n
        def f(s, tt=tt):
            s_v.wait(s)
            s_xw.inc(s.dma_start(out=X[tt * 128:(tt + 1) * 128, :], in_=xwt[:, :]), 16)
        b.on("sync", f)


GROUPS = [[0, 1, 2, 3], [4, 5, 6, 7]]


def emit_attnA(b, seg, x_in, w, X, XNBF, QF, KLOC, VLOC, KN2L, s_list):
    (s_g, s_d, s_e, s_t, s_v, s_sc, s_p, s_o) = s_list
    nc, cfg = b.nc, b.cfg
    D, T, NT, DC = b.cfg.D, b.cfg.T, b.cfg.NT, b.cfg.DC

    rowa = b.sb(seg, "ROWA", [1, D], F32)
    rowb = b.sb(seg, "ROWB", [1, D], F32)
    g0 = b.sb(seg, "G0BC", [128, D], F32)
    b0 = b.sb(seg, "B0BC", [128, D], F32)
    bcps = b.ps(seg, "BCPS", [128, 512])
    def f(s):
        s_d.inc(s.dma_start(out=rowa[0:1, :], in_=w["L0_attn_gamma"][None, :]), 16)
        s_d.inc(s.dma_start(out=rowb[0:1, :], in_=w["L0_attn_beta"][None, :]), 16)
    b.on("sync", f)
    def f(t):
        s_d.wait(t); s_p.wait(t)
    b.on("tensor", f)
    bcast_rows(b, rowa[0:1, :], g0, D, s_t, s_v, bcps)
    bcast_rows(b, rowb[0:1, :], b0, D, s_t, s_v, bcps)

    s_xt = b.sem("A_xt")
    s_xst = b.sem("A_xst")
    s_nbst = b.sem("A_nbst")
    xt = b.sb(seg, "XT", [128, D], F32)
    scr = b.sb(seg, "SCR", [128, D], F32)
    stat = b.sb(seg, "STAT", [128, 8], F32)
    xn1 = b.sb(seg, "XN1", [128, D], F32)
    xnb = b.sb(seg, "XNB", [128, D], BF16)
    for i in range(NT):
        def f(s, i=i):
            s_v.wait(s)
            s_sc.wait(s)
            s_xt.inc(s.dma_start(out=xt[:, :], in_=x_in[i * 128:(i + 1) * 128, :]), 16)
        b.on("sync", f)
        def f(v):
            s_xt.wait(v)
        b.on("vector", f)
        def f(sc):
            s_xt.wait(sc)
        b.on("scalar", f)
        ln_stats(b, xt, D, stat, scr, s_sc, s_v)
        ln_apply(b, xt, xn1, D, stat, s_sc, s_v)
        def f(v):
            s_v.wait(v)
            s_v.inc(v.tensor_tensor(out=scr[:, :], in0=xt[:, :], in1=xn1[:, :],
                                    op=ALU.add), 1)
        b.on("vector", f)
        def f(s, i=i):
            s_v.wait(s)
            s_xst.inc(s.dma_start(out=X[i * 128:(i + 1) * 128, :], in_=scr[:, :]), 16)
        b.on("sync", f)
        def f(sc):
            s_xst.wait(sc)
        b.on("scalar", f)
        ln_stats(b, xn1, D, stat, scr, s_sc, s_v)
        ln_apply(b, xn1, xn1, D, stat, s_sc, s_v, gbc=b.t["G0BC"], bbc=b.t["B0BC"])
        def f(v):
            s_v.wait(v)
            s_nbst.wait(v)
            s_v.inc(v.tensor_copy(out=xnb[:, :], in_=xn1[:, :]), 1)
        b.on("vector", f)
        def f(s, i=i):
            s_v.wait(s)
            s_nbst.inc(s.dma_start(out=XNBF[i * 128:(i + 1) * 128, :], in_=xnb[:, :]), 16)
        b.on("sync", f)

    cfgD512 = D // 512
    AKALL = b.dram["KALL"]; AVALL = b.dram["VALL"]; AKN2G = b.dram["KN2G"]
    def after_kv_A():
        emit_norms(b, seg, QF, KLOC, KN2L, s_list, per_head=False, tag="A",
                   which="k")
        emit_gather3(b, KLOC, VLOC, KN2L, AKALL, AVALL, AKN2G,
                     wait_names=["qk_stg", "qk_stgv", "knst"])
    order_kvq = list(range(cfgD512, 3 * cfgD512)) + list(range(cfgD512))
    emit_qkv_proj(b, seg, XNBF, w["L0_attn_qkv"], None, QF, KLOC, VLOC, s_list,
                  interleaved=False, tag="A", nb_sem="A_nbst",
                  order=order_kvq, after_kv=(2 * cfgD512, after_kv_A))
    emit_norms(b, seg, QF, KLOC, KN2L, s_list, per_head=False, tag="A", which="q")


def emit_qkv_proj(b, seg, XNBF, wqkv, bqkv, QF, KLOC, VLOC, s_list, interleaved,
                  tag="A", nb_sem="A_nbst", wsrcT=None, vbias_bc=None,
                  order=None, after_kv=None):
    (s_g, s_d, s_e, s_t, s_v, s_sc, s_p, s_o) = s_list
    nc, cfg = b.nc, b.cfg
    D, T, NT, DC, NH = cfg.D, cfg.T, cfg.NT, cfg.DC, cfg.NH

    s_fm = b.sem("qk_fm")
    if wsrcT is None:
        s_wp = [b.sem("qk_wp0"), b.sem("qk_wp1")]
    else:
        s_wp = [b.sem("qk_wpT0"), b.sem("qk_wpT1")]
    s_stg = b.sem("qk_stg")
    s_stgv = b.sem("qk_stgv")
    xnfm = b.sb(seg, tag + "XN_FM", [128, DC * T], BF16)
    nbst = b.sems.get(nb_sem)
    waits = [(nbst, nbst.n)] if nbst is not None else []
    load_fm(b, xnfm, XNBF, DC, T, s_fm, wait_sems=waits)
    xnfm_ready_fm = s_fm.n

    wp = [b.sb(seg, tag + "WP0", [128, DC * 512], BF16),
          b.sb(seg, tag + "WP1", [128, DC * 512], BF16)]
    prj = [b.ps(seg, tag + "PRJ0", [128, 512]), b.ps(seg, tag + "PRJ1", [128, 512])]
    stg = b.sb(seg, tag + "STG", [128, T], BF16)
    stgv = b.sb(seg, tag + "STGV", [128, 512], BF16)
    bq = None
    if bqkv is not None:
        bq = b.sb(seg, tag + "BQPP", [128, 3 * D // 128], F32)
        emit_bias_cols(b, seg, bqkv, bq, s_list, tag)

    NPAN = 3 * D // 512
    panload = [None] * NPAN
    panfree = [0, 0]
    prjfree = [0, 0]
    stg_rd = [0]
    xnfm_ready = xnfm_ready_fm

    if order is None:
        order = list(range(NPAN))
    first_mm = [True]
    for oi, p in enumerate(order):
        if after_kv is not None and oi == after_kv[0]:
            after_kv[1]()
        slot = oi % 2
        if wsrcT is None:
            def f(g, p=p, slot=slot):
                s_t.wait(g, panfree[slot])
                s_wp[slot].inc(g.dma_start(
                    out=wp[slot][:, :].rearrange("p (c n) -> p c n", n=512),
                    in_=wqkv[:, p * 512:(p + 1) * 512].rearrange("(c p) n -> p c n", p=128)), 16)
            b.on("gpsimd", f)
        else:
            c0 = p * 512
            role = "q" if c0 < D else ("k" if c0 < 2 * D else "v")
            r0 = c0 - (0 if role == "q" else (D if role == "k" else 2 * D))
            def f(s, p=p, slot=slot, role=role, r0=r0):
                s_t.wait(s, panfree[slot])
                if p == 0:
                    b.sems["B_pc"].wait(s)
                for fc in range(DC):
                    s_wp[slot].inc(s.dma_start(
                        out=wp[slot][:, fc * 512:(fc + 1) * 512],
                        in_=wsrcT[role][r0:r0 + 512, fc * 128:(fc + 1) * 128],
                        transpose=True), 16)
            b.on("sync", f)
        panload[p] = s_wp[slot].n

        for cc in range(4):
            col0 = p * 512 + cc * 128
            if not interleaved:
                if col0 < D:
                    role, drow = "q", col0
                elif col0 < 2 * D:
                    role, drow = "k", col0 - D
                else:
                    role, drow = "v", col0 - 2 * D
            else:
                h, off = divmod(col0, 384)
                if off < 128:
                    role, drow = "q", h * 128 + off
                elif off < 256:
                    role, drow = "k", h * 128 + (off - 128)
                else:
                    role, drow = "v", h * 128 + (off - 256)
            if role in ("q", "k"):
                dst = QF if role == "q" else KLOC
                for tg in range(T // 512):
                    bank = tg % 2
                    def f(t, p=p, cc=cc, tg=tg, bank=bank, slot=slot, fst=first_mm[0]):
                        s_wp[slot].wait(t, panload[p])
                        if fst:
                            s_fm.wait(t, xnfm_ready)
                            s_v.wait(t)
                        s_v.wait(t, prjfree[bank])
                        for fc in range(DC):
                            s_t.inc(t.matmul(
                                prj[bank][:, :],
                                wp[slot][:, fc * 512 + cc * 128: fc * 512 + (cc + 1) * 128],
                                xnfm[:, fc * T + tg * 512: fc * T + (tg + 1) * 512],
                                start=(fc == 0), stop=(fc == DC - 1)), 1)
                    b.on("tensor", f)
                    mmdone = s_t.n
                    def f(v, tg=tg, bank=bank, drow=drow):
                        s_t.wait(v, mmdone)
                        if tg == 0:
                            s_stg.wait(v, stg_rd[0])
                        if bq is not None:
                            s_v.inc(v.tensor_scalar(
                                out=stg[:, tg * 512:(tg + 1) * 512], in0=prj[bank][:, :],
                                scalar1=bq[:, drow // 128: drow // 128 + 1], scalar2=None,
                                op0=ALU.add), 1)
                        else:
                            s_v.inc(v.tensor_copy(out=stg[:, tg * 512:(tg + 1) * 512],
                                                  in_=prj[bank][:, :]), 1)
                    b.on("vector", f)
                    prjfree[bank] = s_v.n
                def f(s, dst=dst, drow=drow):
                    s_v.wait(s)
                    s_stg.inc(s.dma_start(out=dst[drow:drow + 128, :], in_=stg[:, :]), 16)
                b.on("sync", f)
                stg_rd[0] = s_stg.n
            else:
                pass
        if (not interleaved) and p * 512 >= 2 * D:
            col0 = p * 512 - 2 * D
            for tt in range(NT):
                bank = tt % 2
                def f(t, p=p, tt=tt, bank=bank, slot=slot, fst=first_mm[0]):
                    s_wp[slot].wait(t, panload[p])
                    if fst and tt == 0:
                        s_fm.wait(t, xnfm_ready)
                        s_v.wait(t)
                    s_v.wait(t, prjfree[bank])
                    for fc in range(DC):
                        s_t.inc(t.matmul(
                            prj[bank][:, :],
                            xnfm[:, fc * T + tt * 128: fc * T + (tt + 1) * 128],
                            wp[slot][:, fc * 512:(fc + 1) * 512],
                            start=(fc == 0), stop=(fc == DC - 1)), 1)
                b.on("tensor", f)
                mmdone = s_t.n
                def f(v, bank=bank, col0=col0):
                    s_t.wait(v, mmdone)
                    s_stgv.wait(v)
                    if vbias_bc is not None:
                        s_v.inc(v.tensor_tensor(out=stgv[:, :], in0=prj[bank][:, :],
                                                in1=vbias_bc[:, col0:col0 + 512],
                                                op=ALU.add), 1)
                    else:
                        s_v.inc(v.tensor_copy(out=stgv[:, :], in_=prj[bank][:, :]), 1)
                b.on("vector", f)
                prjfree[bank] = s_v.n
                def f(s, tt=tt, col0=col0):
                    s_v.wait(s)
                    s_stgv.inc(s.dma_start(out=VLOC[tt * 128:(tt + 1) * 128, col0:col0 + 512],
                                           in_=stgv[:, :]), 16)
                b.on("scalar", f)
        if interleaved:
            for cc in range(4):
                col0 = p * 512 + cc * 128
                h, off = divmod(col0, 384)
                if off < 256:
                    continue
                vcol = h * 128 + (off - 256)
                for tt in range(NT):
                    bank = tt % 2
                    def f(t, p=p, cc=cc, tt=tt, bank=bank, slot=slot):
                        s_wp[slot].wait(t, panload[p])
                        s_v.wait(t, prjfree[bank])
                        for fc in range(DC):
                            s_t.inc(t.matmul(
                                prj[bank][:, 0:128],
                                xnfm[:, fc * T + tt * 128: fc * T + (tt + 1) * 128],
                                wp[slot][:, fc * 512 + cc * 128: fc * 512 + (cc + 1) * 128],
                                start=(fc == 0), stop=(fc == DC - 1)), 1)
                    b.on("tensor", f)
                    mmdone = s_t.n
                    def f(v, bank=bank, vcol=vcol):
                        s_t.wait(v, mmdone)
                        s_stgv.wait(v)
                        if vbias_bc is not None:
                            s_v.inc(v.tensor_tensor(out=stgv[:, 0:128],
                                                    in0=prj[bank][:, 0:128],
                                                    in1=vbias_bc[:, vcol:vcol + 128],
                                                    op=ALU.add), 1)
                        else:
                            s_v.inc(v.tensor_copy(out=stgv[:, 0:128],
                                                  in_=prj[bank][:, 0:128]), 1)
                    b.on("vector", f)
                    prjfree[bank] = s_v.n
                    def f(s, tt=tt, vcol=vcol):
                        s_v.wait(s)
                        s_stgv.inc(s.dma_start(out=VLOC[tt * 128:(tt + 1) * 128, vcol:vcol + 128],
                                               in_=stgv[:, 0:128]), 16)
                    b.on("scalar", f)
        panfree[slot] = s_t.n
        first_mm[0] = False


def emit_bias_cols(b, seg, bqkv, bq, s_list, tag=""):
    (s_g, s_d, s_e, s_t, s_v, s_sc, s_p, s_o) = s_list
    nc, cfg = b.nc, b.cfg
    nch = 3 * cfg.D // 128
    s_bq = b.sem("bqld")
    def f(g):
        s_bq.inc(g.dma_start(out=bq[:, :],
                             in_=bqkv.rearrange("(i p) -> p i", p=128),
                             allow_slow_non_contiguous=True), 16)
    b.on("gpsimd", f)
    def f(t):
        s_bq.wait(t)
    b.on("tensor", f)
    def f(v):
        s_bq.wait(v)
    b.on("vector", f)


def emit_norms(b, seg, QF, KLOC, KN2L, s_list, per_head, tag="", which="qk"):
    (s_g, s_d, s_e, s_t, s_v, s_sc, s_p, s_o) = s_list
    nc, cfg = b.nc, b.cfg
    D, T, DC = cfg.D, cfg.T, cfg.DC
    ntg = T // 512

    s_nr = b.sem("nrm_ld")
    nrm = b.sb(seg, tag + which + "NRMT", [128, T], BF16)
    nsq = b.sb(seg, tag + which + "NSQ", [128, T], BF16)
    npsctx = nc.psum_tensor(tag + which + "NPS", [1, (T // 512) * 512], F32)
    nps = npsctx.__enter__()
    nacc = b.sb(seg, tag + which + "NACC", [1, 2 * DC * ntg], F32)
    nmax = b.t[tag + "NMAX"]
    krow = b.sb(seg, tag + which + "KROWS", [1, 128], F32)

    stg_done = b.sems.get("qk_stg")
    stgv_done = b.sems.get("qk_stgv")
    first_load = [True]
    pairs = {"qk": ((0, QF), (1, KLOC)), "q": ((0, QF),), "k": ((1, KLOC),)}[which]
    for wi, src in pairs:
        for fc in range(DC):
            def f(s, fc=fc, src=src):
                if first_load[0]:
                    if stg_done is not None:
                        stg_done.wait(s)
                    if stgv_done is not None:
                        stgv_done.wait(s)
                    first_load[0] = False
                s_sc.wait(s)
                s_nr.inc(s.dma_start(out=nrm[:, :], in_=src[fc * 128:(fc + 1) * 128, :]), 16)
            b.on("sync", f)
            def f(sc):
                s_nr.wait(sc)
                s_t.wait(sc)
                s_sc.inc(sc.activation(out=nsq[:, :], in_=nrm[:, :], func=AF.Square), 1)
            b.on("scalar", f)
            for tg in range(ntg):
                first = (fc == 0) or per_head
                last = (fc == DC - 1) or per_head
                def f(t, fc=fc, tg=tg, first=first, last=last):
                    s_sc.wait(t)
                    if first:
                        s_v.wait(t)
                    s_t.inc(t.matmul(nps[:, tg * 512:(tg + 1) * 512],
                                     b.t["ONES_col_bf"][:, :],
                                     nsq[:, tg * 512:(tg + 1) * 512],
                                     start=first, stop=last), 1)
                b.on("tensor", f)
                if last:
                    idx = (wi * DC * ntg) + fc * ntg + tg if per_head else \
                          wi * DC * ntg + tg
                    def f(v, idx=idx, tg=tg):
                        s_t.wait(v)
                        s_v.inc(v.reduce_max(out=nacc[0:1, idx:idx + 1],
                                             in_=nps[0:1, tg * 512:(tg + 1) * 512],
                                             axis=AX.X), 1)
                    b.on("vector", f)
    def f(v):
        s_v.wait(v)
        nh = cfg.NH if per_head else 1
        for wi, _ in pairs:
            if per_head:
                for h in range(cfg.NH):
                    s_v.inc(v.reduce_max(
                        out=nmax[0:1, wi * nh + h: wi * nh + h + 1],
                        in_=nacc[0:1, (wi * cfg.DC + h) * ntg:(wi * cfg.DC + h + 1) * ntg],
                        axis=AX.X), 1)
            else:
                s_v.inc(v.reduce_max(out=nmax[0:1, wi:wi + 1],
                                     in_=nacc[0:1, wi * cfg.DC * ntg: wi * cfg.DC * ntg + ntg],
                                     axis=AX.X), 1)
    b.on("vector", f)
    npsctx.__exit__(None, None, None)
    if which in ("qk", "k"):
        emit_norms_small(b, nmax, krow, KN2L, per_head, s_list, tag)


def emit_flash_L0(b, seg, QF, KALL, VALL, KN2G, CTXBF, s_list, nmax_name="ANMAX"):
    (s_g, s_d, s_e, s_t, s_v, s_sc, s_p, s_o) = s_list
    nc, cfg = b.nc, b.cfg
    D, T, DC, NG = cfg.D, cfg.T, cfg.DC, cfg.NG
    c_soft = 1.0 / float(np.sqrt(np.float32(D)))

    negc = b.sb(seg, "NEGC", [128, 1], F32)
    kng = b.sb(seg, "KNGS", [1, 1], F32)
    nsc = b.sb(seg, "NSC", [1, 4], F32)
    with nc.psum_tensor("NEGCPS", [128, 1], F32) as negcps:
        s_kng = b.sem("fl_kng")
        def f(s):
            b.sems["s_cc"].wait(s)
            s_kng.inc(s.dma_start(out=kng[0:1, :], in_=KN2G[0:1, 0:1]), 16)
        b.on("sync", f)
        def f(v):
            s_kng.wait(v)
            s_v.inc(v.tensor_tensor(out=nsc[0:1, 0:1], in0=b.t[nmax_name][0:1, 0:1],
                                    in1=kng[0:1, 0:1], op=ALU.mult), 1)
        b.on("vector", f)
        def f(sc):
            s_v.wait(sc)
            s_sc.inc(sc.activation(out=nsc[0:1, 1:2], in_=nsc[0:1, 0:1], func=AF.Ln), 1)
            s_sc.wait(sc)
            s_sc.inc(sc.activation(out=nsc[0:1, 2:3], in_=nsc[0:1, 1:2], func=AF.Exp,
                                   scale=0.5), 1)
        b.on("scalar", f)
        def f(v):
            s_sc.wait(v)
            s_v.inc(v.tensor_scalar(out=nsc[0:1, 3:4], in0=nsc[0:1, 2:3],
                                    scalar1=-c_soft, scalar2=None, op0=ALU.mult), 1)
        b.on("vector", f)
        def f(t):
            s_v.wait(t)
            s_t.inc(t.matmul(negcps[:, :], b.t["ONES_row_f32"][:, :], nsc[0:1, 3:4],
                             start=True, stop=True), 1)
        b.on("tensor", f)
        def f(v):
            s_t.wait(v)
            s_v.inc(v.tensor_copy(out=negc[:, :], in_=negcps[:, :]), 1)
        b.on("vector", f)

    NR = 4
    kch = [b.sb(seg, f"KCH{i}", [128, DC * 128], BF16) for i in range(NR)]
    vch = [b.sb(seg, f"VCH{i}", [128, D], BF16) for i in range(NR)]
    qg = b.sb(seg, "QG", [128, DC * 512], BF16)
    ptar = b.sb(seg, "PTAR", [128, cfg.maxslots * 512], BF16)
    ctstg = b.sb(seg, "CTSTG", [128, D], BF16)
    lic = b.sb(seg, "LIC", [128, 4], F32)
    scps = [b.ps(seg, "SCPS0", [128, 512]), b.ps(seg, "SCPS1", [128, 512])]
    lcps = b.ps(seg, "LCPS", [128, 4])
    ctxps = b.ps(seg, "CTXPS", [128, D])
    s_kc = [b.sem(f"fl_kc{i}") for i in range(NR)]
    s_vc = [b.sem(f"fl_vc{i}") for i in range(NR)]
    s_qg = b.sem("fl_qg")
    s_cts = b.sem("fl_cts")
    kfree = [0] * NR
    scfree = [0, 0]
    vfree = [0] * NR

    for g in range(NG):
        chunks = [(r, c) for r in range(4) for c in range(4 * g + 4)]
        nch = len(chunks)
        for ci, (r, c) in enumerate(chunks):
            par = ci % NR
            def f(s, g=g, r=r, c=c, par=par, ci=ci):
                s_t.wait(s, kfree[par])
                s_kc[par].inc(s.dma_start(
                    out=kch[par][:, :].rearrange("p (f k) -> p f k", k=128),
                    in_=KALL[r * D:(r + 1) * D, c * 128:(c + 1) * 128]
                        .rearrange("(f p) k -> p f k", p=128)), 16)
            b.on("sync", f)
            kload = s_kc[par].n
            if ci == 0:
                def f(s, g=g):
                    s_t.wait(s)
                    for fc in range(DC):
                        s_qg.inc(s.dma_start(
                            out=qg[:, fc * 512:(fc + 1) * 512],
                            in_=QF[fc * 128:(fc + 1) * 128, g * 512:(g + 1) * 512]), 16)
                b.on("scalar", f)
            qload = s_qg.n
            spar = ci % 2
            def f(t, ci=ci, par=par, spar=spar, kload=kload, qload=qload, g=g):
                if ci == 0 and g == 0:
                    s_v.wait(t)
                s_kc[par].wait(t, kload)
                if ci == 0:
                    s_qg.wait(t, qload)
                s_sc.wait(t, scfree[spar])
                for fc in range(DC):
                    s_t.inc(t.matmul(scps[spar][:, :],
                                     kch[par][:, fc * 128:(fc + 1) * 128],
                                     qg[:, fc * 512:(fc + 1) * 512],
                                     start=(fc == 0), stop=(fc == DC - 1)), 1)
            b.on("tensor", f)
            scdone = s_t.n
            kfree[par] = scdone
            def f(sc, spar=spar, slot=ci, scdone=scdone, ci=ci):
                s_t.wait(sc, scdone)
                if ci == 0:
                    s_v.wait(sc)
                s_sc.inc(sc.activation(out=ptar[:, slot * 512:(slot + 1) * 512],
                                       in_=scps[spar][:, :], func=AF.Exp,
                                       bias=negc[:, 0:1], scale=c_soft), 1)
            b.on("scalar", f)
            scfree[spar] = s_sc.n
            d = c - 4 * g
            if d >= 0:
                def f(v, slot=ci, d=d, r=r):
                    s_sc.wait(v)
                    s_v.inc(v.tensor_tensor(
                        out=ptar[:, slot * 512:(slot + 1) * 512],
                        in0=ptar[:, slot * 512:(slot + 1) * 512],
                        in1=b.t["MSK"][:, (d * 4 + r) * 512:(d * 4 + r + 1) * 512],
                        op=ALU.mult), 1)
                b.on("vector", f)
        def f(t, chunks=chunks, nch=nch):
            s_sc.wait(t)
            s_v.wait(t)
            for ci in range(nch):
                for m in range(4):
                    s_t.inc(t.matmul(lcps[:, m:m + 1],
                                     ptar[:, ci * 512 + m * 128: ci * 512 + (m + 1) * 128],
                                     b.t["ONES_col_bf"][:, :],
                                     start=(ci == 0 and m == 0),
                                     stop=(ci == nch - 1 and m == 3)), 1)
        b.on("tensor", f)
        ldone = s_t.n
        def f(v, ldone=ldone):
            s_t.wait(v, ldone)
            s_v.inc(v.reciprocal(out=lic[:, :], in_=lcps[:, :]), 1)
        b.on("vector", f)
        for m in range(4):
            sub = [(r, c) for (r, c) in chunks if c <= 4 * g + m]
            for si, (r, c) in enumerate(sub):
                par = si % NR
                def f(s, r=r, c=c, par=par):
                    s_t.wait(s, vfree[par])
                    s_vc[par].inc(s.dma_start(out=vch[par][:, :],
                                              in_=VALL[r * T + c * 128: r * T + (c + 1) * 128, :]), 16)
                b.on("sync", f)
                vload = s_vc[par].n
                slot = chunks.index((r, c))
                def f(t, si=si, par=par, slot=slot, m=m, vload=vload, nsub=len(sub)):
                    s_vc[par].wait(t, vload)
                    if si == 0:
                        s_v.wait(t)
                    for dv in range(D // 512):
                        s_t.inc(t.matmul(
                            ctxps[:, dv * 512:(dv + 1) * 512],
                            ptar[:, slot * 512 + m * 128: slot * 512 + (m + 1) * 128],
                            vch[par][:, dv * 512:(dv + 1) * 512],
                            start=(si == 0), stop=(si == nsub - 1)), 1)
                b.on("tensor", f)
                vfree[par] = s_t.n
            def f(v, m=m, g=g):
                s_t.wait(v)
                s_cts.wait(v)
                s_v.inc(v.tensor_scalar(out=ctstg[:, :], in0=ctxps[:, :],
                                        scalar1=lic[:, m:m + 1], scalar2=None,
                                        op0=ALU.mult), 1)
            b.on("vector", f)
            def f(s, m=m, g=g):
                s_v.wait(s)
                s_cts.inc(s.dma_start(
                    out=CTXBF[g * 512 + m * 128: g * 512 + (m + 1) * 128, :],
                    in_=ctstg[:, :]), 16)
            b.on("sync", f)


def emit_oproj_resid(b, seg, CTXBF, wo, X, s_list, tag="A5", prev_x_sem="A_xst",
                     src_store_sem="fl_cts", dst=None, dst_sem_name="op_xw"):
    (s_g, s_d, s_e, s_t, s_v, s_sc, s_p, s_o) = s_list
    nc, cfg = b.nc, b.cfg
    D, T, NT, DC = cfg.D, cfg.T, cfg.NT, cfg.DC
    if dst is None:
        dst = X

    s_fm2 = b.sem("op_fm")
    s_ow = b.sem("op_w2")
    s_xr = b.sem("op_xr")
    s_xw = b.sem(dst_sem_name)
    cts = b.sems.get(src_store_sem)
    cfm = b.sb(seg, tag + "CFM", [128, DC * T], BF16)
    emit_local_fence(b)
    load_fm(b, cfm, CTXBF, DC, T, s_fm2, wait_sems=[(cts, cts.n)])
    cfm_ready = s_fm2.n
    NPAN = D // 512
    owp = b.sb(seg, tag + "OWP", [128, NPAN * DC * 512], BF16)
    prj = [b.ps(seg, tag + "OPRJ0", [128, 512]), b.ps(seg, tag + "OPRJ1", [128, 512])]
    xrt = b.sb(seg, tag + "XRT", [128, D], F32)
    xwt = b.sb(seg, tag + "XWT", [128, D], F32)

    panload = [None] * NPAN
    prjfree = [0, 0]
    for p in range(NPAN):
        def f(g, p=p):
            s_ow.inc(g.dma_start(
                out=owp[:, p * DC * 512:(p + 1) * DC * 512]
                    .rearrange("p (c n) -> p c n", n=512),
                in_=wo[:, p * 512:(p + 1) * 512].rearrange("(c p) n -> p c n", p=128)), 16)
        b.on("gpsimd", f)
        panload[p] = s_ow.n
    for tt in range(NT):
        def f(s, tt=tt):
            s_v.wait(s)
            if tt == 0 and prev_x_sem in b.sems:
                b.sems[prev_x_sem].wait(s)
            s_xr.inc(s.dma_start(out=xrt[:, :], in_=X[tt * 128:(tt + 1) * 128, :]), 16)
        b.on("sync", f)
        xload = s_xr.n
        for p in range(NPAN):
            bank = p % 2
            def f(t, tt=tt, p=p, bank=bank):
                if tt == 0:
                    s_ow.wait(t, panload[NPAN - 1])
                if tt == 0 and p == 0:
                    s_fm2.wait(t, cfm_ready)
                s_v.wait(t, prjfree[bank])
                for dc in range(DC):
                    s_t.inc(t.matmul(
                        prj[bank][:, :],
                        cfm[:, dc * T + tt * 128: dc * T + (tt + 1) * 128],
                        owp[:, p * DC * 512 + dc * 512: p * DC * 512 + (dc + 1) * 512],
                        start=(dc == 0), stop=(dc == DC - 1)), 1)
            b.on("tensor", f)
            mmdone = s_t.n
            def f(v, p=p, bank=bank, xload=xload):
                s_t.wait(v, mmdone)
                s_xr.wait(v, xload)
                if p == 0:
                    s_xw.wait(v)
                s_v.inc(v.tensor_tensor(out=xwt[:, p * 512:(p + 1) * 512],
                                        in0=prj[bank][:, :],
                                        in1=xrt[:, p * 512:(p + 1) * 512], op=ALU.add), 1)
            b.on("vector", f)
            prjfree[bank] = s_v.n
        def f(s, tt=tt):
            s_v.wait(s)
            s_xw.inc(s.dma_start(out=X[tt * 128:(tt + 1) * 128, :], in_=xwt[:, :]), 16)
        b.on("sync", f)


def emit_norms_small(b, nmax, krow, KN2L, per_head, s_list, tag=""):
    (s_g, s_d, s_e, s_t, s_v, s_sc, s_p, s_o) = s_list
    cfg = b.cfg
    nh = cfg.NH if per_head else 1
    def f(g):
        s_d.wait(g)
        s_p.inc(g.memset(krow[0:1, :], 0.0), 1)
    b.on("gpsimd", f)
    def f(v):
        s_p.wait(v)
        s_v.wait(v)
        s_v.inc(v.tensor_copy(out=krow[0:1, 0:nh], in_=nmax[0:1, nh:2 * nh]), 1)
    b.on("vector", f)
    s_kn = b.sem("knst")
    def f(s):
        s_v.wait(s)
        s_kn.inc(s.dma_start(out=KN2L[0:1, :], in_=krow[0:1, :]), 16)
    b.on("sync", f)


def emit_negc_mha(b, seg, KN2G, nmax_name, negch, c_soft, s_list, tag):
    (s_g, s_d, s_e, s_t, s_v, s_sc, s_p, s_o) = s_list
    nc, cfg = b.nc, b.cfg
    NH = cfg.NH
    kng = b.sb(seg, tag + "KNGH", [1, NH], F32)
    nrow = b.sb(seg, tag + "NROW", [1, NH], F32)
    s_kg = b.sem("F_kng")
    with nc.psum_tensor(tag + "NGPS", [128, NH], F32) as ngps:
        def f(s):
            b.sems["s_cc"].wait(s)
            s_kg.inc(s.dma_start(out=kng[0:1, :], in_=KN2G[0:1, 0:NH]), 16)
        b.on("sync", f)
        def f(v):
            s_kg.wait(v)
            s_v.inc(v.tensor_tensor(out=nrow[0:1, :], in0=b.t[nmax_name][0:1, 0:NH],
                                    in1=kng[0:1, :], op=ALU.mult), 1)
        b.on("vector", f)
        def f(sc):
            s_v.wait(sc)
            s_sc.inc(sc.activation(out=nrow[0:1, :], in_=nrow[0:1, :], func=AF.Ln), 1)
            s_sc.wait(sc)
            s_sc.inc(sc.activation(out=nrow[0:1, :], in_=nrow[0:1, :], func=AF.Exp,
                                   scale=0.5), 1)
        b.on("scalar", f)
        def f(v):
            s_sc.wait(v)
            s_v.inc(v.tensor_scalar(out=nrow[0:1, :], in0=nrow[0:1, :],
                                    scalar1=-c_soft, scalar2=None, op0=ALU.mult), 1)
        b.on("vector", f)
        def f(t):
            s_v.wait(t)
            for h in range(NH):
                s_t.inc(t.matmul(ngps[:, h:h + 1], b.t["ONES_row_f32"][:, :],
                                 nrow[0:1, h:h + 1], start=(h == 0), stop=(h == NH - 1)), 1)
        b.on("tensor", f)
        def f(v):
            s_t.wait(v)
            s_v.inc(v.tensor_copy(out=negch[:, :], in_=ngps[:, :]), 1)
        b.on("vector", f)


def emit_flash_MHA(b, seg, QF, KALL, VALL, KN2G, ctxfm, nmax_name, s_list, tag):
    (s_g, s_d, s_e, s_t, s_v, s_sc, s_p, s_o) = s_list
    nc, cfg = b.nc, b.cfg
    D, T, DC, NG, NH = cfg.D, cfg.T, cfg.DC, cfg.NG, cfg.NH
    c_soft = 1.0 / float(np.sqrt(np.float32(cfg.DH)))
    EB = 3
    NSLOT = 12

    negch = b.sb(seg, tag + "NEGCH", [128, NH], F32)
    emit_negc_mha(b, seg, KN2G, nmax_name, negch, c_soft, s_list, tag)

    kh = b.sb(seg, tag + "KH", [128, 4 * T], BF16)
    vh = b.sb(seg, tag + "VH", [128, 4 * T], BF16)
    qh = b.sb(seg, tag + "QH", [128, T], BF16)
    ptar = b.sb(seg, tag + "PTA", [128, NSLOT * 512], BF16)
    linv = b.sb(seg, tag + "LINV", [1, 512], F32)
    lbcs = b.sb(seg, tag + "LBCS", [128, 512], F32)
    scps = [b.ps(seg, tag + "SC0", [128, EB * 512]), b.ps(seg, tag + "SC1", [128, EB * 512])]
    lrow = b.ps(seg, tag + "LROW", [128, 512])
    cfps = b.ps(seg, tag + "CFPS", [128, 512])
    s_kh = b.sem("F_kh")
    s_vh = b.sem("F_vh")
    s_qh = b.sem("F_qh")
    scfree = [0, 0]
    slot_rd = {}

    for h in range(NH):
        def f(s, h=h):
            s_t.wait(s)
            for r in range(4):
                s_kh.inc(s.dma_start(
                    out=kh[:, r * T:(r + 1) * T],
                    in_=KALL[r * D + h * 128: r * D + (h + 1) * 128, :]), 16)
            s_vh.inc(s.dma_start(
                out=vh[:, :].rearrange("p (n d) -> p n d", d=128),
                in_=VALL[:, h * 128:(h + 1) * 128].rearrange("(n p) d -> p n d", p=128)), 16)
            s_qh.inc(s.dma_start(out=qh[:, :], in_=QF[h * 128:(h + 1) * 128, :]), 16)
        b.on("sync", f)
        khload, vhload, qhload = s_kh.n, s_vh.n, s_qh.n
        bcounter = [0]
        for g in range(NG):
            chunks = [(r, c) for r in range(4) for c in range(4 * g + 4)]
            nch = len(chunks)
            blocks = [chunks[i:i + EB] for i in range(0, nch, EB)]
            nb = len(blocks)
            expdone = [0] * nb
            maskdone = [0] * nb

            def emit_pv(t, bi, blocks=blocks, nch=nch, g=g):
                blk = blocks[bi]
                ci0 = bi * EB
                for j, (r, c) in enumerate(blk):
                    ci = ci0 + j
                    slot = ci % NSLOT
                    n = r * (T // 128) + c
                    s_t.inc(t.matmul(cfps[:, :],
                                     vh[:, n * 128:(n + 1) * 128],
                                     ptar[:, slot * 512:(slot + 1) * 512],
                                     start=(ci == 0), stop=(ci == nch - 1)), 1)
                    s_t.inc(t.matmul(lrow[0:1, :], b.t["ONES_col_bf"][:, :],
                                     ptar[:, slot * 512:(slot + 1) * 512],
                                     start=(ci == 0), stop=(ci == nch - 1)), 1)

            sc_cell = {}
            for bi, blk in enumerate(blocks):
                par = bi % 2
                ci0 = bi * EB
                def f(t, h=h, g=g, bi=bi, par=par, blk=blk, ci0=ci0,
                      khload=khload, vhload=vhload, qhload=qhload):
                    if bi == 0 and g == 0:
                        s_kh.wait(t, khload)
                        s_qh.wait(t, qhload)
                        s_vh.wait(t, vhload)
                        s_v.wait(t)
                    if bi == 0:
                        s_v.wait(t)
                    s_sc.wait(t, scfree[par])
                    for j, (r, c) in enumerate(blk):
                        s_t.inc(t.matmul(scps[par][:, j * 512:(j + 1) * 512],
                                         kh[:, r * T + c * 128: r * T + (c + 1) * 128],
                                         qh[:, g * 512:(g + 1) * 512],
                                         start=True, stop=True), 1)
                    sc_cell[bi] = s_t.n
                    if bi >= 1:
                        s_sc.wait(t, expdone[bi - 1])
                        if maskdone[bi - 1]:
                            s_v.wait(t, maskdone[bi - 1])
                        emit_pv(t, bi - 1)
                b.on("tensor", f)
                scdone = sc_cell[bi]
                def f(sc, h=h, par=par, blk=blk, ci0=ci0, scdone=scdone, bi=bi, g=g):
                    s_t.wait(sc, scdone)
                    gb = bcounter[0]
                    if gb >= NSLOT // EB:
                        pass
                    slot0 = ci0 % NSLOT
                    s_sc.inc(sc.activation(
                        out=ptar[:, slot0 * 512:(slot0 + len(blk)) * 512],
                        in_=scps[par][:, 0:len(blk) * 512], func=AF.Exp,
                        bias=negch[:, h:h + 1], scale=c_soft), 1)
                b.on("scalar", f)
                expdone[bi] = s_sc.n
                scfree[par] = s_sc.n
                dm = [(j, r, c) for j, (r, c) in enumerate(blk) if c - 4 * g >= 0]
                if dm:
                    def f(v, dm=dm, ci0=ci0, g=g, bi=bi):
                        s_sc.wait(v, expdone[bi])
                        for j, r, c in dm:
                            d = c - 4 * g
                            slot = (ci0 + j) % NSLOT
                            s_v.inc(v.tensor_tensor(
                                out=ptar[:, slot * 512:(slot + 1) * 512],
                                in0=ptar[:, slot * 512:(slot + 1) * 512],
                                in1=b.t["MSK"][:, (d * 4 + r) * 512:(d * 4 + r + 1) * 512],
                                op=ALU.mult), 1)
                    b.on("vector", f)
                    maskdone[bi] = s_v.n
                bcounter[0] += 1
            def f(t, nb=nb):
                s_sc.wait(t, expdone[nb - 1])
                if maskdone[nb - 1]:
                    s_v.wait(t, maskdone[nb - 1])
                emit_pv(t, nb - 1)
            b.on("tensor", f)
            pvdone = s_t.n
            def f(v, pvdone=pvdone):
                s_t.wait(v, pvdone)
                s_v.inc(v.reciprocal(out=linv[0:1, :], in_=lrow[0:1, :]), 1)
            b.on("vector", f)
            def f(t):
                s_v.wait(t)
                s_t.inc(t.matmul(lrow[:, :], b.t["ONES_row_f32"][:, :],
                                 linv[0:1, :], start=True, stop=True), 1)
            b.on("tensor", f)
            def f(v, h=h, g=g):
                s_t.wait(v)
                s_v.inc(v.tensor_copy(out=lbcs[:, :], in_=lrow[:, :]), 1)
                s_v.wait(v)
                s_v.inc(v.tensor_tensor(out=ctxfm[:, h * T + g * 512: h * T + (g + 1) * 512],
                                        in0=cfps[:, :], in1=lbcs[:, :], op=ALU.mult), 1)
            b.on("vector", f)


def emit_ffn_down(b, seg, HBF, w2_load, evac, s_list, tag, loader_eng="gpsimd",
                  h_store_sem=None):
    (s_g, s_d, s_e, s_t, s_v, s_sc, s_p, s_o) = s_list
    nc, cfg = b.nc, b.cfg
    D, T, FF, FC = cfg.D, cfg.T, cfg.FF, cfg.FC

    hfmg = b.sb(seg, tag + "HFMG", [128, FC * 512], BF16)
    w2p = b.sb(seg, tag + "W2P", [128, FC * 512], BF16)
    dps = [b.ps(seg, tag + "DPS0", [128, 512]), b.ps(seg, tag + "DPS1", [128, 512])]
    s_hf = b.sem("ffn_hf")
    s_w2 = b.sem("ffn_w2" + loader_eng)
    dpsfree = [0, 0]
    hs_sem = b.sems.get("ffn_hst") if h_store_sem else None
    hs_n = hs_sem.n if hs_sem else 0
    for tg in range(T // 512):
        def f(s, tg=tg):
            if tg == 0 and hs_sem is not None:
                hs_sem.wait(s, hs_n)
            s_t.wait(s)
            for ic in range(FC):
                s_hf.inc(s.dma_start(
                    out=hfmg[:, ic * 512:(ic + 1) * 512],
                    in_=HBF[tg * 512:(tg + 1) * 512, ic * 128:(ic + 1) * 128],
                    transpose=True), 16)
        b.on("sync", f)
        hload = s_hf.n
        for oc in range(D // 512):
            def f(g, tg=tg, oc=oc):
                s_t.wait(g)
                w2_load(g, oc, w2p, s_w2)
            b.on(loader_eng, f)
            w2load = s_w2.n
            for tt in range(4):
                bank = tt % 2
                def f(t, tg=tg, oc=oc, tt=tt, bank=bank, hload=hload, w2load=w2load):
                    s_hf.wait(t, hload)
                    s_w2.wait(t, w2load)
                    s_v.wait(t, dpsfree[bank])
                    for ic in range(FC):
                        s_t.inc(t.matmul(
                            dps[bank][:, :],
                            hfmg[:, ic * 512 + tt * 128: ic * 512 + (tt + 1) * 128],
                            w2p[:, ic * 512:(ic + 1) * 512],
                            start=(ic == 0), stop=(ic == FC - 1)), 1)
                b.on("tensor", f)
                evac(tg * 4 + tt, oc, dps[bank])
                dpsfree[bank] = s_v.n


def emit_gather3(b, KLOC, VLOC, KN2L, KALL, VALL, KN2G, wait_names):
    s_cc = b.sems["s_cc"]
    waits = [(b.sems[nm].h, b.sems[nm].n) for nm in wait_names if nm in b.sems]
    def f(g, waits=waits):
        for h, n in waits:
            g.wait_ge(h, n)
        s_cc.inc(g.collective_compute(
            "AllGather", ALU.bypass, replica_groups=GROUPS,
            ins=[KLOC.opt()], outs=[KALL.opt()]), 1)
        s_cc.inc(g.collective_compute(
            "AllGather", ALU.bypass, replica_groups=GROUPS,
            ins=[VLOC.opt()], outs=[VALL.opt()]), 1)
        s_cc.inc(g.collective_compute(
            "AllReduce", ALU.max, replica_groups=GROUPS,
            ins=[KN2L.opt()], outs=[KN2G.opt()]), 1)
    b.on("gpsimd", f)


def emit_B1(b, seg, X, w, HS, PART1, XNBF, s_list):
    (s_g, s_d, s_e, s_t, s_v, s_sc, s_p, s_o) = s_list
    nc, cfg = b.nc, b.cfg
    D, T, NT, DC, LR = cfg.D, cfg.T, cfg.NT, cfg.DC, cfg.LR
    import ml_dtypes

    g1 = b.sb(seg, "G1BC", [128, D], F32)
    b1 = b.sb(seg, "B1BC", [128, D], F32)
    bs = b.sb(seg, "BSBC", [128, D], F32)
    rowt = b.sb(seg, "B1ROW", [1, D], F32)
    s_br = b.sem("B1_br")
    with nc.psum_tensor("B1BCPS", [128, 512], F32) as bcps:
        for wname, dstbc in (("L0_ffn_ln1_scale", g1), ("L0_ffn_ln1_bias", b1),
                             ("L0_ffn_simple_bias", bs)):
            def f(s, wname=wname):
                s_v.wait(s)
                s_br.inc(s.dma_start(out=rowt[0:1, :], in_=b.dram[wname][None, :]), 16)
            b.on("sync", f)
            def f(t):
                s_br.wait(t)
            b.on("tensor", f)
            bcast_rows(b, rowt[0:1, :], dstbc, D, s_t, s_v, bcps)

    wrt = b.sb(seg, "WRT", [128, DC * 3], F32)
    wrow = b.sb(seg, "WRROW", [3, D], F32)
    id3 = nc.inline_tensor(np.eye(3, dtype=np.float32), "c_id3f").ap()
    id3s = b.sb(seg, "ID3F", [3, 3], F32)
    s_wr = b.sem("B1_wr")
    with nc.psum_tensor("B1WRPS", [128, 3], F32) as wrps:
        def f(s):
            s_wr.inc(s.dma_start(out=wrow[0:3, :], in_=b.dram["L0_ffn_w_router"][:, :]), 16)
            s_wr.inc(s.dma_start(out=id3s[0:3, :], in_=id3[:, :]), 16)
        b.on("sync", f)
        for fc in range(DC):
            def f(t, fc=fc):
                s_wr.wait(t)
                s_v.wait(t)
                s_t.inc(t.transpose(wrps[:, :], wrow[0:3, fc * 128:(fc + 1) * 128],
                                    id3s[0:3, 0:3]), 1)
            b.on("tensor", f)
            def f(v, fc=fc):
                s_t.wait(v)
                s_v.inc(v.tensor_copy(out=wrt[:, fc * 3:(fc + 1) * 3], in_=wrps[:, :]), 1)
            b.on("vector", f)

    wdt = b.sb(seg, "WDT", [128, DC * LR], BF16)
    wut = b.sb(seg, "WUT", [32, D], BF16)
    s_lw = b.sem("B1_lw")
    def f(s):
        b.sems["B_pc"].wait(s)
        for fc in range(DC):
            s_lw.inc(s.dma_start(out=wdt[:, fc * LR:(fc + 1) * LR],
                                 in_=b.dram["WDB"][:, fc * 128:(fc + 1) * 128],
                                 transpose=True), 16)
    b.on("sync", f)
    wub_sb = b.sb(seg, "WUBSB", [128, LR], BF16)
    with nc.psum_tensor("B1WUPS", [32, 128], BF16) as wups:
        for fc in range(DC):
            def f(s, fc=fc):
                s_t.wait(s)
                s_lw.inc(s.dma_start(out=wub_sb[:, :],
                                     in_=b.dram["WUB"][fc * 128:(fc + 1) * 128, :]), 16)
            b.on("sync", f)
            lw = s_lw.n
            def f(t, fc=fc, lw=lw):
                s_lw.wait(t, lw)
                s_v.wait(t)
                s_t.inc(t.transpose(wups[:, :], wub_sb[:, 0:LR], b.t["ID_BF"][:, :]), 1)
            b.on("tensor", f)
            def f(v, fc=fc):
                s_t.wait(v)
                s_v.inc(v.tensor_copy(out=wut[0:32, fc * 128:(fc + 1) * 128],
                                      in_=wups[0:32, :]), 1)
            b.on("vector", f)

    trps = [b.ps(seg, "B1TR0", [128, 128]), b.ps(seg, "B1TR1", [128, 128])]
    psml = b.ps(seg, "B1SML", [128, 512])
    y1t_ps = b.ps(seg, "B1Y1T", [32, 128], BF16)
    delps = b.ps(seg, "B1DEL", [128, 2048] if D >= 2048 else [128, D])

    xt = b.sb(seg, "BXT", [128, D], F32)
    hst = b.sb(seg, "BHST", [128, D], F32)
    scr = b.sb(seg, "BSCR", [128, D], F32)
    stat = b.sb(seg, "BSTAT", [128, 8], F32)
    pa = b.sb(seg, "BPA", [128, D], F32)
    pb = b.sb(seg, "BPB", [128, D], F32)
    xnb = b.sb(seg, "BXNB", [128, D], BF16)
    hfm32 = b.sb(seg, "HSFM32", [128, DC * 128], F32)
    hfmbf = b.sb(seg, "HSFMBF", [128, DC * 128], BF16)
    y1b = b.sb(seg, "BY1B", [128, LR], BF16)
    y1t = b.sb(seg, "BY1T", [32, 128], BF16)
    s_xt = b.sem("B1_xt")
    s_hs = b.sem("B1_hs")
    s_p1 = b.sem("B1_p1")
    s_nb = b.sem("B1_nb")

    for i in range(NT):
        def f(s, i=i):
            s_v.wait(s); s_sc.wait(s)
            if i == 0:
                b.sems["op_xw"].wait(s)
            s_xt.inc(s.dma_start(out=xt[:, :], in_=X[i * 128:(i + 1) * 128, :]), 16)
        b.on("sync", f)
        def f(v):
            s_xt.wait(v)
        b.on("vector", f)
        def f(sc):
            s_xt.wait(sc)
        b.on("scalar", f)
        ln_stats(b, xt, D, stat, scr, s_sc, s_v)
        def f(v):
            s_hs.wait(v)
        b.on("vector", f)
        ln_apply(b, xt, hst, D, stat, s_sc, s_v)
        def f(s, i=i):
            s_v.wait(s)
            s_hs.inc(s.dma_start(out=HS[i * 128:(i + 1) * 128, :], in_=hst[:, :]), 16)
        b.on("sync", f)
        for fc in range(DC):
            par = fc % 2
            def f(t, fc=fc, par=par):
                s_v.wait(t)
                s_t.inc(t.transpose(trps[par][:, :], xt_dummy_src(b, hst, fc),
                                    b.t["ID_F32"][:, :]), 1)
            b.on("tensor", f)
            def f(v, fc=fc, par=par):
                s_t.wait(v)
                s_v.inc(v.tensor_copy(out=hfm32[:, fc * 128:(fc + 1) * 128],
                                      in_=trps[par][:, :]), 1)
            b.on("vector", f)
        def f(v):
            s_v.wait(v)
            s_v.inc(v.tensor_copy(out=hfmbf[:, :], in_=hfm32[:, :]), 1)
        b.on("vector", f)
        hfm_done = s_v.n
        def f(t, hfm_done=hfm_done):
            s_v.wait(t, hfm_done)
            for fc in range(DC):
                s_t.inc(t.matmul(psml[:, 0:3], hfm32[:, fc * 128:(fc + 1) * 128],
                                 wrt[:, fc * 3:(fc + 1) * 3],
                                 start=(fc == 0), stop=(fc == DC - 1)), 1)
        b.on("tensor", f)
        rt_done = s_t.n
        def f(v, i=i, rt_done=rt_done):
            s_t.wait(v, rt_done)
            s_v.inc(v.tensor_copy(out=stat[:, 5:8], in_=psml[:, 0:3]), 1)
            s_v.wait(v)
            l0, l1, l2 = stat[:, 5:6], stat[:, 6:7], stat[:, 7:8]
            s_v.inc(v.tensor_tensor(out=stat[:, 0:1], in0=l2, in1=l0, op=ALU.is_gt), 1)
            s_v.inc(v.tensor_tensor(out=stat[:, 1:2], in0=l2, in1=l1, op=ALU.is_gt), 1)
            s_v.inc(v.tensor_tensor(out=stat[:, 2:3], in0=l1, in1=l0, op=ALU.is_gt), 1)
            s_v.inc(v.tensor_tensor(out=stat[:, 3:4], in0=l1, in1=l2, op=ALU.is_ge), 1)
            s_v.wait(v)
            s_v.inc(v.tensor_tensor(out=b.t["RM2"][:, i:i + 1], in0=stat[:, 0:1],
                                    in1=stat[:, 1:2], op=ALU.mult), 1)
            s_v.inc(v.tensor_tensor(out=stat[:, 4:5], in0=stat[:, 2:3],
                                    in1=stat[:, 3:4], op=ALU.mult), 1)
            s_v.wait(v)
            s_v.inc(v.tensor_scalar(out=stat[:, 0:1], in0=b.t["RM2"][:, i:i + 1],
                                    scalar1=-1.0, scalar2=1.0, op0=ALU.mult,
                                    op1=ALU.add), 1)
            s_v.wait(v)
            s_v.inc(v.tensor_tensor(out=b.t["RM1"][:, i:i + 1], in0=stat[:, 4:5],
                                    in1=stat[:, 0:1], op=ALU.mult), 1)
            s_v.wait(v)
            s_v.inc(v.tensor_tensor(out=b.t["RM0"][:, i:i + 1], in0=stat[:, 0:1],
                                    in1=b.t["RM1"][:, i:i + 1], op=ALU.subtract), 1)
        b.on("vector", f)
        masks_done = s_v.n
        def f(t, masks_done=masks_done):
            s_v.wait(t, masks_done)
            for fc in range(DC):
                s_t.inc(t.matmul(psml[:, 32:32 + LR], hfmbf[:, fc * 128:(fc + 1) * 128],
                                 wdt[:, fc * LR:(fc + 1) * LR],
                                 start=(fc == 0), stop=(fc == DC - 1)), 1)
        b.on("tensor", f)
        def f(v):
            s_t.wait(v)
            s_v.inc(v.tensor_copy(out=y1b[:, :], in_=psml[:, 32:32 + LR]), 1)
        b.on("vector", f)
        def f(t):
            s_v.wait(t)
            s_t.inc(t.transpose(y1t_ps[:, :], y1b[:, 0:LR], b.t["ID_BF"][:, :]), 1)
        b.on("tensor", f)
        def f(v):
            s_t.wait(v)
            s_v.inc(v.tensor_copy(out=y1t[0:32, :], in_=y1t_ps[0:32, :]), 1)
        b.on("vector", f)
        y1t_done = s_v.n
        def f(t, y1t_done=y1t_done):
            s_v.wait(t, y1t_done)
            for oc in range(D // 512):
                s_t.inc(t.matmul(delps[:, oc * 512:(oc + 1) * 512], y1t[0:32, :],
                                 wut[0:32, oc * 512:(oc + 1) * 512],
                                 start=True, stop=True), 1)
        b.on("tensor", f)
        del_done = s_t.n
        def f(v, i=i, del_done=del_done):
            s_t.wait(v, del_done)
            s_p1.wait(v)
            s_v.inc(v.tensor_scalar(out=pa[:, :], in0=hst[:, :],
                                    scalar1=stat[:, 0:1], scalar2=None,
                                    op0=ALU.mult), 1)
            s_v.wait(v)
            s_v.inc(v.scalar_tensor_tensor(out=pb[:, :], in0=bs[:, :],
                                           scalar=b.t["RM0"][:, i:i + 1], in1=pa[:, :],
                                           op0=ALU.mult, op1=ALU.add), 1)
            s_v.wait(v)
            for oc in range(D // 512):
                s_v.inc(v.scalar_tensor_tensor(
                    out=pa[:, oc * 512:(oc + 1) * 512],
                    in0=delps[:, oc * 512:(oc + 1) * 512],
                    scalar=b.t["RM1"][:, i:i + 1], in1=pb[:, oc * 512:(oc + 1) * 512],
                    op0=ALU.mult, op1=ALU.add), 1)
        b.on("vector", f)
        def f(s, i=i):
            s_v.wait(s)
            s_p1.inc(s.dma_start(out=PART1[i * 128:(i + 1) * 128, :], in_=pa[:, :]), 16)
        b.on("sync", f)
        ln_stats(b, hst, D, stat, scr, s_sc, s_v)
        ln_apply(b, hst, pb, D, stat, s_sc, s_v, gbc=g1, bbc=b1)
        def f(v):
            s_v.wait(v)
            s_nb.wait(v)
            s_v.inc(v.tensor_copy(out=xnb[:, :], in_=pb[:, :]), 1)
        b.on("vector", f)
        def f(s, i=i):
            s_v.wait(s)
            s_nb.inc(s.dma_start(out=XNBF[i * 128:(i + 1) * 128, :], in_=xnb[:, :]), 16)
        b.on("sync", f)


def xt_dummy_src(b, hst, fc):
    return hst[:, fc * 128:(fc + 1) * 128]


def emit_oproj_ln2(b, seg, ctxfm, WOB, HS, XPRIME, XN2BF, w, s_list):
    (s_g, s_d, s_e, s_t, s_v, s_sc, s_p, s_o) = s_list
    nc, cfg = b.nc, b.cfg
    D, T, NT, DC, NH = cfg.D, cfg.T, cfg.NT, cfg.DC, cfg.NH

    emit_local_fence(b)
    g2 = b.sb(seg, "G2BC", [128, D], F32)
    b2 = b.sb(seg, "B2BC", [128, D], F32)
    rowt = b.sb(seg, "B5ROW", [1, D], F32)
    s_br = b.sem("B5_br")
    with nc.psum_tensor("B5BCPS", [128, 512], F32) as bcps:
        for wname, dstbc in (("L0_ffn_ln2_scale", g2), ("L0_ffn_ln2_bias", b2)):
            def f(s, wname=wname):
                s_v.wait(s)
                s_br.inc(s.dma_start(out=rowt[0:1, :], in_=b.dram[wname][None, :]), 16)
            b.on("sync", f)
            def f(t):
                s_br.wait(t)
            b.on("tensor", f)
            bcast_rows(b, rowt[0:1, :], dstbc, D, s_t, s_v, bcps)

    wot = b.sb(seg, "WOT", [128, DC * D], BF16)
    s_wo = b.sem("B5_wo")
    def f(s):
        b.sems["B_pc"].wait(s)
        for hc in range(DC):
            s_wo.inc(s.dma_start(out=wot[:, hc * D:(hc + 1) * D],
                                 in_=WOB[:, hc * 128:(hc + 1) * 128], transpose=True), 16)
    b.on("sync", f)
    wo_done = s_wo.n

    ops = [b.ps(seg, "B5OP0", [128, 512]), b.ps(seg, "B5OP1", [128, 512])]
    hst = b.sb(seg, "B5HS", [128, D], F32)
    xp = b.sb(seg, "B5XP", [128, D], F32)
    scr = b.sb(seg, "B5SCR", [128, D], F32)
    stat = b.sb(seg, "B5STAT", [128, 8], F32)
    xnb = b.sb(seg, "B5XNB", [128, D], BF16)
    s_hs2 = b.sem("B5_hs")
    s_xp = b.sem("B5_xp")
    s_nb2 = b.sem("B5_nb")
    opfree = [0, 0]
    for tt in range(NT):
        def f(s, tt=tt):
            s_v.wait(s)
            b.sems["B1_hs"].wait(s)
            s_hs2.inc(s.dma_start(out=hst[:, :], in_=HS[tt * 128:(tt + 1) * 128, :]), 16)
        b.on("sync", f)
        hsl = s_hs2.n
        for oc in range(D // 512):
            bank = oc % 2
            def f(t, tt=tt, oc=oc, bank=bank, wo_done=wo_done):
                if tt == 0 and oc == 0:
                    s_wo.wait(t, wo_done)
                    s_v.wait(t)
                s_v.wait(t, opfree[bank])
                for h in range(NH):
                    s_t.inc(t.matmul(
                        ops[bank][:, :],
                        ctxfm[:, h * T + tt * 128: h * T + (tt + 1) * 128],
                        wot[:, h * D + oc * 512: h * D + (oc + 1) * 512],
                        start=(h == 0), stop=(h == NH - 1)), 1)
            b.on("tensor", f)
            mmdone = s_t.n
            def f(v, oc=oc, bank=bank, hsl=hsl, mmdone=mmdone):
                s_t.wait(v, mmdone)
                s_hs2.wait(v, hsl)
                if oc == 0:
                    s_xp.wait(v)
                    s_sc.wait(v)
                s_v.inc(v.tensor_tensor(out=xp[:, oc * 512:(oc + 1) * 512],
                                        in0=ops[bank][:, :],
                                        in1=hst[:, oc * 512:(oc + 1) * 512], op=ALU.add), 1)
            b.on("vector", f)
            opfree[bank] = s_v.n
        def f(s, tt=tt):
            s_v.wait(s)
            s_xp.inc(s.dma_start(out=XPRIME[tt * 128:(tt + 1) * 128, :], in_=xp[:, :]), 16)
        b.on("sync", f)
        def f(sc):
            s_xp.wait(sc)
        b.on("scalar", f)
        ln_stats(b, xp, D, stat, scr, s_sc, s_v)
        ln_apply(b, xp, scr, D, stat, s_sc, s_v, gbc=g2, bbc=b2)
        def f(v):
            s_v.wait(v)
            s_nb2.wait(v)
            s_v.inc(v.tensor_copy(out=xnb[:, :], in_=scr[:, :]), 1)
        b.on("vector", f)
        def f(s, tt=tt):
            s_v.wait(s)
            s_nb2.inc(s.dma_start(out=XN2BF[tt * 128:(tt + 1) * 128, :], in_=xnb[:, :]), 16)
        b.on("sync", f)


def emit_ffn_up(b, seg, XNSRC, HBF, s_list, tag, mode, wload_up):
    (s_g, s_d, s_e, s_t, s_v, s_sc, s_p, s_o) = s_list
    nc, cfg = b.nc, b.cfg
    D, T, NT, DC, FF = cfg.D, cfg.T, cfg.NT, cfg.DC, cfg.FF

    s_fmu = b.sem("ffn_fmu")
    xnfm = b.sb(seg, tag + "XNFM", [128, DC * T], BF16)
    nbs = b.sems.get(tag + "_nbsrc")
    load_fm(b, xnfm, XNSRC, DC, T, s_fmu,
            wait_sems=[(nbs, nbs.n)] if nbs else [])
    xn_ready = s_fmu.n

    nsl = 2 if mode == "gelu" else 4
    wps = [b.sb(seg, tag + f"W1P{i}", [128, DC * 512], BF16) for i in range(nsl)]
    pfx = "ffn_w1T" if mode == "gelu" else "ffn_w1g"
    s_w1 = [b.sem(f"{pfx}{i}") for i in range(nsl)]
    fps = [b.ps(seg, tag + "FPS0", [128, 512]), b.ps(seg, tag + "FPS1", [128, 512])]
    gpsfree = [0, 0]
    if mode == "gateup24":
        gps = [b.ps(seg, tag + "GPS0", [128, 512]), b.ps(seg, tag + "GPS1", [128, 512])]
        sg = b.sb(seg, tag + "SG", [128, 512], F32)
        hh = b.sb(seg, tag + "HH", [128, 512], F32)
        aa = b.sb(seg, tag + "AA", [128, 512], F32)
        nn = b.sb(seg, tag + "NN", [128, 512], F32)
        mk_ = b.sb(seg, tag + "MK", [128, 512], F32)
        hsp = b.sb(seg, tag + "HSP", [128, 512], F32)
    stg = b.sb(seg, tag + "FSTG", [128, 512], BF16)
    s_hst = b.sem("ffn_hst")
    panfree = {}
    fpsfree = [0, 0]
    NP = FF // 512
    for ip in range(NP):
        slot = ip % 2
        if mode == "gelu":
            def f(s, ip=ip, slot=slot):
                s_t.wait(s, panfree.get(("w", slot), 0))
                wload_up(s, ip, wps[slot], s_w1[slot], "w")
            b.on("sync", f)
            wl = s_w1[slot].n
        else:
            def f(g, ip=ip, slot=slot):
                s_t.wait(g, panfree.get(("g", slot), 0))
                wload_up(g, ip, wps[slot], s_w1[slot], "gate")
            b.on("gpsimd", f)
            def f(g, ip=ip, slot=slot):
                s_t.wait(g, panfree.get(("u", slot), 0))
                wload_up(g, ip, wps[2 + slot], s_w1[2 + slot], "up")
            b.on("gpsimd", f)
            wl = s_w1[slot].n
            wl2 = s_w1[2 + slot].n
        for tt in range(NT):
            bank = tt % 2
            def f(t, ip=ip, tt=tt, bank=bank, slot=slot, wl=wl):
                s_w1[slot].wait(t, wl)
                if ip == 0 and tt == 0:
                    s_fmu.wait(t, xn_ready)
                if mode == "gelu":
                    s_sc.wait(t, fpsfree[bank])
                else:
                    s_sc.wait(t, fpsfree[bank])
                for fc in range(DC):
                    s_t.inc(t.matmul(
                        fps[bank][:, :],
                        xnfm[:, fc * T + tt * 128: fc * T + (tt + 1) * 128],
                        wps[slot][:, fc * 512:(fc + 1) * 512],
                        start=(fc == 0), stop=(fc == DC - 1)), 1)
            b.on("tensor", f)
            mmdone = s_t.n
            if mode == "gelu":
                def f(sc, bank=bank, mmdone=mmdone):
                    s_t.wait(sc, mmdone)
                    s_hst.wait(sc)
                    s_sc.inc(sc.activation(out=stg[:, :], in_=fps[bank][:, :],
                                           func=AF.Gelu_apprx_tanh), 1)
                b.on("scalar", f)
                fpsfree[bank] = s_sc.n
                def f(s, ip=ip, tt=tt):
                    s_sc.wait(s)
                    s_hst.inc(s.dma_start(
                        out=HBF[tt * 128:(tt + 1) * 128, ip * 512:(ip + 1) * 512],
                        in_=stg[:, :]), 16)
                b.on("sync", f)
            else:
                def f(t, ip=ip, tt=tt, bank=bank, slot=slot, wl2=wl2):
                    s_w1[2 + slot].wait(t, wl2)
                    s_v.wait(t, gpsfree[bank])
                    for fc in range(DC):
                        s_t.inc(t.matmul(
                            gps[bank][:, :],
                            xnfm[:, fc * T + tt * 128: fc * T + (tt + 1) * 128],
                            wps[2 + slot][:, fc * 512:(fc + 1) * 512],
                            start=(fc == 0), stop=(fc == DC - 1)), 1)
                b.on("tensor", f)
                mm2 = s_t.n
                def f(sc, bank=bank, mmdone=mmdone):
                    s_t.wait(sc, mmdone)
                    s_v.wait(sc)
                    s_sc.inc(sc.activation(out=sg[:, :], in_=fps[bank][:, :],
                                           func=AF.Silu), 1)
                b.on("scalar", f)
                fpsfree[bank] = s_sc.n
                def f(v, bank=bank, mm2=mm2):
                    s_t.wait(v, mm2)
                    s_sc.wait(v)
                    s_v.wait(v)
                    s_v.inc(v.tensor_tensor(out=hh[:, :], in0=sg[:, :],
                                            in1=gps[bank][:, :], op=ALU.mult), 1)
                b.on("vector", f)
                gpsfree[bank] = s_v.n
                def f(sc):
                    s_v.wait(sc)
                    s_sc.inc(sc.activation(out=aa[:, :], in_=hh[:, :], func=AF.Abs), 1)
                b.on("scalar", f)
                def f(v):
                    s_sc.wait(v)
                    a = [aa[:, k::4] for k in range(4)]
                    g01, g02, g03 = nn[:, 0::4], nn[:, 1::4], nn[:, 2::4]
                    g12 = nn[:, 3::4]
                    s_v.inc(v.tensor_tensor(out=g01, in0=a[1], in1=a[0], op=ALU.is_gt), 1)
                    s_v.inc(v.tensor_tensor(out=g02, in0=a[2], in1=a[0], op=ALU.is_gt), 1)
                    s_v.inc(v.tensor_tensor(out=g03, in0=a[3], in1=a[0], op=ALU.is_gt), 1)
                    s_v.inc(v.tensor_tensor(out=g12, in0=a[2], in1=a[1], op=ALU.is_gt), 1)
                    g13 = mk_[:, 0::4]; g23 = mk_[:, 1::4]
                    s_v.inc(v.tensor_tensor(out=g13, in0=a[3], in1=a[1], op=ALU.is_gt), 1)
                    s_v.inc(v.tensor_tensor(out=g23, in0=a[3], in1=a[2], op=ALU.is_gt), 1)
                    s_v.wait(v)
                    n0 = aa[:, 0::4]; n1 = aa[:, 1::4]; n2 = aa[:, 2::4]; n3 = aa[:, 3::4]
                    s_v.inc(v.tensor_tensor(out=n0, in0=g01, in1=g02, op=ALU.add), 1)
                    s_v.inc(v.tensor_tensor(out=n1, in0=g12, in1=g13, op=ALU.add), 1)
                    s_v.inc(v.tensor_tensor(out=n2, in0=g23, in1=g02, op=ALU.subtract), 1)
                    s_v.inc(v.tensor_tensor(out=n3, in0=g03, in1=g13, op=ALU.add), 1)
                    s_v.wait(v)
                    s_v.inc(v.tensor_tensor(out=n0, in0=n0, in1=g03, op=ALU.add), 1)
                    s_v.inc(v.tensor_tensor(out=n1, in0=n1, in1=g01, op=ALU.subtract), 1)
                    s_v.inc(v.tensor_tensor(out=n2, in0=n2, in1=g12, op=ALU.subtract), 1)
                    s_v.inc(v.tensor_tensor(out=n3, in0=n3, in1=g23, op=ALU.add), 1)
                    s_v.wait(v)
                    s_v.inc(v.tensor_scalar(out=n1, in0=n1, scalar1=1.0, scalar2=None,
                                            op0=ALU.add), 1)
                    s_v.inc(v.tensor_scalar(out=n2, in0=n2, scalar1=2.0, scalar2=None,
                                            op0=ALU.add), 1)
                    s_v.inc(v.tensor_scalar(out=n3, in0=n3, scalar1=-1.0, scalar2=3.0,
                                            op0=ALU.mult, op1=ALU.add), 1)
                    s_v.wait(v)
                    for k in range(4):
                        s_v.inc(v.tensor_scalar(out=mk_[:, k::4], in0=aa[:, k::4],
                                                scalar1=1.5, scalar2=None,
                                                op0=ALU.is_le), 1)
                    s_v.wait(v)
                    s_v.inc(v.tensor_tensor(out=hsp[:, :], in0=hh[:, :], in1=mk_[:, :],
                                            op=ALU.mult), 1)
                    s_v.wait(v)
                    s_hst.wait(v)
                    s_v.inc(v.tensor_copy(out=stg[:, :], in_=hsp[:, :]), 1)
                b.on("vector", f)
                def f(s, ip=ip, tt=tt):
                    s_v.wait(s)
                    s_hst.inc(s.dma_start(
                        out=HBF[tt * 128:(tt + 1) * 128, ip * 512:(ip + 1) * 512],
                        in_=stg[:, :]), 16)
                b.on("sync", f)
        if mode == "gelu":
            panfree[("w", slot)] = s_t.n
        else:
            panfree[("g", slot)] = s_t.n
            panfree[("u", slot)] = s_t.n



_W_NAMES = [
    "L0_attn_qkv", "L0_attn_o_proj", "L0_attn_gamma", "L0_attn_beta",
    "L0_ffn_simple_bias", "L0_ffn_w_down", "L0_ffn_w_up", "L0_ffn_w_router",
    "L0_ffn_ln1_scale", "L0_ffn_ln1_bias", "L0_ffn_wq", "L0_ffn_wk",
    "L0_ffn_wv", "L0_ffn_wo", "L0_ffn_ln2_scale", "L0_ffn_ln2_bias",
    "L0_ffn_w_ff1", "L0_ffn_w_ff2", "L1_attn_w_qkv", "L1_attn_b_qkv",
    "L1_attn_w_out", "L1_attn_b_out", "L1_ffn_w_gate", "L1_ffn_w_up",
    "L1_ffn_w_down",
]


def kernel(**inputs):
    from concourse.bass_utils import run_bass_kernel_spmd
    cfg = CFG(D=2048, T=2048, FF=8192, LR=32)
    x = np.ascontiguousarray(inputs["L0_attn_x"], dtype=np.float32)
    shards = shard_x(x)
    w = {n: np.ascontiguousarray(inputs[n], dtype=np.float32) for n in _W_NAMES}
    b = build(cfg, upto="D")
    in_maps = []
    for c in range(8):
        m = {"x": shards[c], "masks": host_masks(c % 4)}
        m.update(w)
        in_maps.append(m)
    res = run_bass_kernel_spmd(b.nc, in_maps, core_ids=list(range(8)))
    ys = [res.results[c]["y"] for c in range(8)]
    return unshard_y(cfg, ys)


# version 06
# speedup vs baseline: 1.0854x; 1.0371x over previous
import contextlib
import numpy as np
import concourse.bass as bass
import concourse.mybir as mybir

F32 = mybir.dt.float32
BF16 = mybir.dt.bfloat16
AF = mybir.ActivationFunctionType
ALU = mybir.AluOpType
AX = mybir.AxisListType


class CFG:
    def __init__(self, D=2048, T=2048, FF=8192, LR=32):
        self.D = D; self.T = T; self.FF = FF; self.LR = LR
        self.R = 4; self.DH = 128
        self.NH = D // 128
        self.NT = T // 128
        self.NG = T // 512
        self.DC = D // 128
        self.FC = FF // 128
        self.S = 4 * T
        self.EPS = 1e-5
        self.maxslots = 4 * (4 * (self.NG - 1) + 4)


def cdiv(a, b):
    return (a + b - 1) // b


class Sem:
    def __init__(self, nc, es, name):
        self.h = es.enter_context(nc.semaphore(name)); self.n = 0
    def inc(self, inst, k=1):
        inst.then_inc(self.h, k); self.n += k
    def wait(self, eng, val=None):
        eng.wait_ge(self.h, self.n if val is None else val)


class B:
    def __init__(self, cfg: CFG):
        self.cfg = cfg
        self.nc = bass.Bass("TRN2", target_bir_lowering=False, num_devices=8)
        self.es = contextlib.ExitStack()
        self.t = {}
        self.dram = {}
        self.sems = {}
        self.blk = None

    def sem(self, name):
        if name not in self.sems:
            self.sems[name] = Sem(self.nc, self.es, name)
        return self.sems[name]

    def sb(self, es, name, shape, dtype):
        self.t[name] = es.enter_context(self.nc.sbuf_tensor(name, shape, dtype))
        return self.t[name]

    def ps(self, es, name, shape, dtype=F32):
        self.t[name] = es.enter_context(self.nc.psum_tensor(name, shape, dtype))
        return self.t[name]

    def dr(self, name, shape, dtype, kind="Internal"):
        self.dram[name] = self.nc.dram_tensor(name, shape, dtype, kind=kind).ap()
        return self.dram[name]

    def on(self, engname, f):
        getattr(self.blk, engname)(f)



def host_masks(j: int):
    import ml_dtypes
    out = np.zeros((4, 4, 128, 512), dtype=np.float32)
    k = np.arange(128)[:, None]; q = np.arange(128)[None, :]
    for d in range(4):
        for r in range(4):
            tri = (k < q) | ((k == q) & (r <= j))
            for m in range(4):
                if d < m:
                    out[d, r, :, 128 * m:128 * (m + 1)] = 1.0
                elif d == m:
                    out[d, r, :, 128 * m:128 * (m + 1)] = tri
    return np.ascontiguousarray(out.astype(ml_dtypes.bfloat16))


def shard_x(x):
    return [np.ascontiguousarray(x[c // 4, (c % 4)::4, :]) for c in range(8)]


def unshard_y(cfg, ys):
    out = np.empty((2, cfg.S, cfg.D), dtype=np.float32)
    for c in range(8):
        out[c // 4, (c % 4)::4, :] = ys[c]
    return out



def ln_stats(b, src, n, stat, scratch, s_sc, s_v):
    cfg = b.cfg
    def fsc(sc):
        s_v.wait(sc)
        s_sc.inc(sc.activation(out=scratch[:, 0:n], in_=src[:, 0:n], func=AF.Square,
                               accum_out=stat[:, 0:1]), 1)
    b.on("scalar", fsc)
    def fv(v):
        s_v.wait(v)
        s_v.inc(v.reduce_sum(out=stat[:, 1:2], in_=src[:, 0:n], axis=AX.X), 1)
        s_v.wait(v)
        s_v.inc(v.tensor_scalar(out=stat[:, 2:3], in0=stat[:, 1:2], scalar1=1.0 / n,
                                scalar2=None, op0=ALU.mult), 1)
        s_sc.wait(v)
        s_v.inc(v.tensor_scalar(out=stat[:, 3:4], in0=stat[:, 0:1], scalar1=1.0 / n,
                                scalar2=None, op0=ALU.mult), 1)
        s_v.wait(v)
        s_v.inc(v.tensor_tensor(out=stat[:, 4:5], in0=stat[:, 2:3], in1=stat[:, 2:3],
                                op=ALU.mult), 1)
        s_v.wait(v)
        s_v.inc(v.tensor_tensor(out=stat[:, 5:6], in0=stat[:, 3:4], in1=stat[:, 4:5],
                                op=ALU.subtract), 1)
        s_v.wait(v)
        s_v.inc(v.tensor_scalar(out=stat[:, 5:6], in0=stat[:, 5:6], scalar1=float(cfg.EPS),
                                scalar2=None, op0=ALU.add), 1)
        v_ = stat[:, 5:6]
        y = stat[:, 7:8]
        t_ = stat[:, 6:7]
        s_v.wait(v)
        s_v.inc(v.tensor_scalar(out=y.bitcast(mybir.dt.uint32), in0=v_.bitcast(mybir.dt.uint32),
                                scalar1=1, scalar2=None, op0=ALU.logical_shift_right), 1)
        s_v.wait(v)
        s_v.inc(v.tensor_tensor(out=y.bitcast(mybir.dt.uint32), in0=b.t["MAGIC"][:, :],
                                in1=y.bitcast(mybir.dt.uint32), op=ALU.subtract), 1)
        for _ in range(3):
            s_v.wait(v)
            s_v.inc(v.tensor_tensor(out=t_, in0=y, in1=y, op=ALU.mult), 1)
            s_v.wait(v)
            s_v.inc(v.tensor_tensor(out=t_, in0=t_, in1=v_, op=ALU.mult), 1)
            s_v.wait(v)
            s_v.inc(v.tensor_scalar(out=t_, in0=t_, scalar1=-0.5, scalar2=1.5,
                                    op0=ALU.mult, op1=ALU.add), 1)
            s_v.wait(v)
            s_v.inc(v.tensor_tensor(out=y, in0=y, in1=t_, op=ALU.mult), 1)
    b.on("vector", fv)


def ln_apply(b, src, dst, n, stat, s_sc, s_v, gbc=None, bbc=None):
    def fv(v):
        s_sc.wait(v)
        s_v.wait(v)
        s_v.inc(v.tensor_scalar(out=dst[:, 0:n], in0=src[:, 0:n],
                                scalar1=stat[:, 2:3], scalar2=stat[:, 7:8],
                                op0=ALU.subtract, op1=ALU.mult), 1)
        if gbc is not None:
            s_v.wait(v)
            s_v.inc(v.tensor_tensor(out=dst[:, 0:n], in0=dst[:, 0:n], in1=gbc[:, 0:n],
                                    op=ALU.mult), 1)
            s_v.wait(v)
            s_v.inc(v.tensor_tensor(out=dst[:, 0:n], in0=dst[:, 0:n], in1=bbc[:, 0:n],
                                    op=ALU.add), 1)
    b.on("vector", fv)


def bcast_rows(b, row_sb, out_sb, n, s_t, s_v, bcps):
    for i in range(cdiv(n, 512)):
        c0, c1 = i * 512, min(n, (i + 1) * 512)
        def ft(t, c0=c0, c1=c1):
            s_v.wait(t)
            s_t.inc(t.matmul(bcps[:, 0:c1 - c0], b.t["ONES_row_f32"][:, :],
                             row_sb[:, c0:c1], start=True, stop=True), 1)
        b.on("tensor", ft)
        def fv(v, c0=c0, c1=c1):
            s_t.wait(v)
            s_v.inc(v.tensor_copy(out=out_sb[:, c0:c1], in_=bcps[:, 0:c1 - c0]), 1)
        b.on("vector", fv)


def load_fm(b, dst_sb, src_dram, nchunks, tcols, s_d, wait_sems=()):
    def fs(s):
        for sm, val in wait_sems:
            sm.wait(s, val)
        for fc in range(nchunks):
            s_d.inc(s.dma_start(out=dst_sb[:, fc * tcols:(fc + 1) * tcols],
                                in_=src_dram[:, fc * 128:(fc + 1) * 128], transpose=True), 16)
    b.on("sync", fs)



def proj_fm_out(b, wsrc_dram, wcol0, ncols, dst_dram, drow0, s_g, s_t, s_v, s_d,
                xnfm, prj, stg, wslots, wload_snap, scale_cols=None, bias_pp=None):
    raise NotImplementedError



def build(cfg: CFG, upto="D"):
    b = B(cfg)
    nc = b.nc
    import ml_dtypes
    D, T, NT, DC, NG, NH = cfg.D, cfg.T, cfg.NT, cfg.DC, cfg.NG, cfg.NH
    FF, FC, LR = cfg.FF, cfg.FC, cfg.LR

    x_in = b.dr("x", [T, D], F32, kind="ExternalInput")
    msk_in = b.dr("masks", [4, 4, 128, 512], BF16, kind="ExternalInput")
    wnames = [
        ("L0_attn_qkv", [D, 3 * D]), ("L0_attn_o_proj", [D, D]),
        ("L0_attn_gamma", [D]), ("L0_attn_beta", [D]),
        ("L0_ffn_simple_bias", [D]), ("L0_ffn_w_down", [LR, D]), ("L0_ffn_w_up", [D, LR]),
        ("L0_ffn_w_router", [3, D]), ("L0_ffn_ln1_scale", [D]), ("L0_ffn_ln1_bias", [D]),
        ("L0_ffn_wq", [D, D]), ("L0_ffn_wk", [D, D]), ("L0_ffn_wv", [D, D]),
        ("L0_ffn_wo", [D, D]), ("L0_ffn_ln2_scale", [D]), ("L0_ffn_ln2_bias", [D]),
        ("L0_ffn_w_ff1", [FF, D]), ("L0_ffn_w_ff2", [D, FF]),
        ("L1_attn_w_qkv", [D, 3 * D]), ("L1_attn_b_qkv", [3 * D]),
        ("L1_attn_w_out", [D, D]), ("L1_attn_b_out", [D]),
        ("L1_ffn_w_gate", [D, FF]), ("L1_ffn_w_up", [D, FF]), ("L1_ffn_w_down", [FF, D]),
    ]
    w = {}
    for name, shape in wnames:
        w[name] = b.dr(name, shape, F32, kind="ExternalInput")
    y_out = b.dr("y", [T, D], F32, kind="ExternalOutput")

    X = b.dr("X", [T, D], F32)
    XNBF = b.dr("XNBF", [T, D], BF16)
    QF = b.dr("QF", [D, T], BF16)
    KLOC = b.dr("KLOC", [D, T], BF16)
    VLOC = b.dr("VLOC", [T, D], BF16)
    KALL = b.dr("KALL", [4 * D, T], BF16)
    VALL = b.dr("VALL", [4 * T, D], BF16)
    CTXBF = b.dr("CTXBF", [T, D], BF16)
    KN2L = b.dr("KN2L", [1, 128], F32)
    KN2G = b.dr("KN2G", [1, 128], F32)
    HS = b.dr("HS", [T, D], F32)
    PART1 = b.dr("PART1", [T, D], F32)
    XPRIME = b.dr("XPRIME", [T, D], F32)
    XN2BF = b.dr("XN2BF", [T, D], BF16)
    HBF = b.dr("HBF", [T, FF], BF16)
    WQB = b.dr("WQB", [D, D], BF16)
    WKB = b.dr("WKB", [D, D], BF16)
    WVB = b.dr("WVB", [D, D], BF16)
    WOB = b.dr("WOB", [D, D], BF16)
    F1B = b.dr("F1B", [FF, D], BF16)
    F2B = b.dr("F2B", [D, FF], BF16)
    WDB = b.dr("WDB", [LR, D], BF16)
    WUB = b.dr("WUB", [D, LR], BF16)

    idbf = nc.inline_tensor(np.eye(128, dtype=ml_dtypes.bfloat16), "c_idbf").ap()
    idf32 = nc.inline_tensor(np.eye(128, dtype=np.float32), "c_idf32").ap()
    b.sb(b.es, "ID_BF", [128, 128], BF16)
    b.sb(b.es, "ID_F32", [128, 128], F32)
    b.sb(b.es, "ONES_col_bf", [128, 1], BF16)
    b.sb(b.es, "ONES_row_f32", [1, 128], F32)
    b.sb(b.es, "MSK", [128, 16 * 512], BF16)
    b.sb(b.es, "ANMAX", [1, 2], F32)
    b.sb(b.es, "BNMAX", [1, 2 * NH], F32)
    b.sb(b.es, "CNMAX", [1, 2 * NH], F32)
    b.sb(b.es, "MAGIC", [128, 1], mybir.dt.uint32)
    b.sb(b.es, "RM0", [128, NT], F32)
    b.sb(b.es, "RM1", [128, NT], F32)
    b.sb(b.es, "RM2", [128, NT], F32)

    s_g = b.sem("s_g"); s_d = b.sem("s_d"); s_e = b.sem("s_e")
    s_t = b.sem("s_t"); s_v = b.sem("s_v"); s_sc = b.sem("s_sc")
    s_p = b.sem("s_p"); s_cc = b.sem("s_cc"); s_o = b.sem("s_o")
    s_pc = b.sem("B_pc")
    s_y = b.sem("s_y")
    S = (s_g, s_d, s_e, s_t, s_v, s_sc, s_p, s_o)

    def fin_copy(xsem_names):
        seg = contextlib.ExitStack()
        emit_seg_fence(b)
        fin = b.sb(seg, "FIN", [128, D], F32)
        def f(s):
            for nm in xsem_names:
                if nm in b.sems:
                    b.sems[nm].wait(s)
            for i in range(NT):
                s_d.inc(s.dma_start(out=fin[:, :], in_=X[i * 128:(i + 1) * 128, :]), 16)
                s_d.wait(s)
                s_y.inc(s.dma_start(out=y_out[i * 128:(i + 1) * 128, :], in_=fin[:, :]), 16)
                s_y.wait(s)
        b.on("sync", f)
        seg.close()

    with b.es:
        blk0 = b.es.enter_context(nc.Block("main"))
        b.blk = blk0
        D512 = D // 512

        seg = contextlib.ExitStack()
        def f(s):
            s_d.inc(s.dma_start(out=b.t["ID_BF"][:, :], in_=idbf[:, :]), 16)
            s_d.inc(s.dma_start(out=b.t["ID_F32"][:, :], in_=idf32[:, :]), 16)
            s_d.inc(s.dma_start(
                out=b.t["MSK"][:, :].rearrange("p (s n) -> p s n", n=512),
                in_=msk_in.rearrange("d r p n -> p (d r) n")), 16)
        b.on("sync", f)
        def f(g):
            g.memset(b.t["ONES_col_bf"][:, :], 1.0)
            g.memset(b.t["MAGIC"][:, :], 0x5f3759df)
            s_p.inc(g.memset(b.t["ONES_row_f32"][:, :], 1.0), 1)
        b.on("gpsimd", f)
        emit_attnA(b, seg, x_in, w, X, XNBF, QF, KLOC, VLOC, KN2L, s_list=S)
        if upto != "A":
            def f(g):
                for src_n, dst in (("L0_ffn_wq", WQB), ("L0_ffn_wk", WKB),
                                   ("L0_ffn_wv", WVB), ("L0_ffn_wo", WOB),
                                   ("L0_ffn_w_ff1", F1B), ("L0_ffn_w_ff2", F2B),
                                   ("L0_ffn_w_down", WDB), ("L0_ffn_w_up", WUB)):
                    s_pc.inc(g.dma_start(out=dst[:, :], in_=b.dram[src_n][:, :]), 16)
            b.on("gpsimd", f)
        seg.close()

        seg = contextlib.ExitStack()
        emit_seg_fence(b)
        sub = contextlib.ExitStack()
        emit_flash_L0(b, sub, QF, KALL, VALL, KN2G, CTXBF, s_list=S)
        sub.close()
        emit_oproj_resid(b, seg, CTXBF, w["L0_attn_o_proj"], X, s_list=S,
                         tag="A5", prev_x_sem="A_xst", src_store_sem="fl_cts",
                         dst=X, dst_sem_name="op_xw")
        seg.close()

        if upto == "A":
            fin_copy(["op_xw"])
            return b

        seg = contextlib.ExitStack()
        emit_seg_fence(b)
        emit_B1(b, seg, X, w, HS, PART1, XNBF, s_list=S)
        seg.close()

        seg = contextlib.ExitStack()
        emit_seg_fence(b)
        NP3 = 3 * D512
        order_kvq = list(range(D512, 3 * D512)) + list(range(D512))
        def after_kv_B():
            emit_norms(b, seg, QF, KLOC, KN2L, s_list=S, per_head=True, tag="B",
                       which="k")
            emit_gather3(b, KLOC, VLOC, KN2L, KALL, VALL, KN2G,
                         wait_names=["qk_stg", "qk_stgv", "knst"])
        emit_qkv_proj(b, seg, XNBF, None, None, QF, KLOC, VLOC, s_list=S,
                      interleaved=False, tag="B", nb_sem="B1_nb",
                      wsrcT={"q": WQB, "k": WKB, "v": WVB},
                      order=order_kvq, after_kv=(2 * D512, after_kv_B))
        emit_norms(b, seg, QF, KLOC, KN2L, s_list=S, per_head=True, tag="B",
                   which="q")
        seg.close()

        seg = contextlib.ExitStack()
        emit_seg_fence(b)
        ctxfm = b.sb(seg, "BCTXFM", [128, NH * T], BF16)
        sub = contextlib.ExitStack()
        emit_flash_MHA(b, sub, QF, KALL, VALL, KN2G, ctxfm, "BNMAX", s_list=S, tag="BF")
        sub.close()
        emit_oproj_ln2(b, seg, ctxfm, WOB, HS, XPRIME, XN2BF, w, s_list=S)
        seg.close()

        seg = contextlib.ExitStack()
        emit_seg_fence(b)
        sub = contextlib.ExitStack()
        b.sems["B6_nbsrc"] = b.sems["B5_nb"]
        def w1load_B(s, ip, dst, sem, which):
            if ip == 0:
                b.sems["B_pc"].wait(s)
            for fc in range(DC):
                sem.inc(s.dma_start(out=dst[:, fc * 512:(fc + 1) * 512],
                                    in_=F1B[ip * 512:(ip + 1) * 512, fc * 128:(fc + 1) * 128],
                                    transpose=True), 16)
        emit_ffn_up(b, sub, XN2BF, HBF, s_list=S, tag="B6", mode="gelu",
                    wload_up=w1load_B)
        sub.close()
        emit_local_fence(b)
        xpb = [b.sb(seg, "B6XPa", [128, 512], F32), b.sb(seg, "B6XPb", [128, 512], F32)]
        xcb = [b.sb(seg, "B6XCa", [128, 512], F32), b.sb(seg, "B6XCb", [128, 512], F32)]
        ptb = [b.sb(seg, "B6PTa", [128, 512], F32), b.sb(seg, "B6PTb", [128, 512], F32)]
        tb1 = b.sb(seg, "B6T1", [128, 512], F32)
        tb2 = b.sb(seg, "B6T2", [128, 512], F32)
        stb = b.sb(seg, "B6ST", [128, 512], F32)
        s_x3 = [b.sem("B6_x3a"), b.sem("B6_x3b")]
        s_xw3 = b.sem("B6_xw")
        ecnt = [0]
        def evacB(tt, oc, psum):
            par = ecnt[0] % 2
            ecnt[0] += 1
            def f(s, tt=tt, oc=oc, par=par):
                s_v.wait(s)
                if tt == 0 and oc == 0:
                    b.sems["B5_xp"].wait(s)
                    b.sems["B1_p1"].wait(s)
                s_x3[par].inc(s.dma_start(
                    out=xpb[par][:, :],
                    in_=XPRIME[tt * 128:(tt + 1) * 128, oc * 512:(oc + 1) * 512]), 16)
                s_x3[par].inc(s.dma_start(
                    out=xcb[par][:, :],
                    in_=X[tt * 128:(tt + 1) * 128, oc * 512:(oc + 1) * 512]), 16)
                s_x3[par].inc(s.dma_start(
                    out=ptb[par][:, :],
                    in_=PART1[tt * 128:(tt + 1) * 128, oc * 512:(oc + 1) * 512]), 16)
            b.on("sync", f)
            xl = s_x3[par].n
            def f(v, tt=tt, oc=oc, par=par, psum=psum, xl=xl):
                s_t.wait(v)
                s_v.wait(v)
                s_x3[par].wait(v, xl)
                s_xw3.wait(v)
                s_v.inc(v.tensor_tensor(out=tb1[:, :], in0=psum[:, :],
                                        in1=xpb[par][:, :], op=ALU.add), 1)
                s_v.inc(v.tensor_tensor(out=tb2[:, :], in0=xcb[par][:, :],
                                        in1=ptb[par][:, :], op=ALU.add), 1)
                s_v.wait(v)
                s_v.inc(v.scalar_tensor_tensor(
                    out=stb[:, :], in0=tb1[:, :], scalar=b.t["RM2"][:, tt:tt + 1],
                    in1=tb2[:, :], op0=ALU.mult, op1=ALU.add), 1)
            b.on("vector", f)
            def f(s, tt=tt, oc=oc):
                s_v.wait(s)
                s_xw3.inc(s.dma_start(
                    out=X[tt * 128:(tt + 1) * 128, oc * 512:(oc + 1) * 512],
                    in_=stb[:, :]), 16)
            b.on("sync", f)
        def w2load_B2(s, oc, w2p, sem):
            for ic in range(FC):
                sem.inc(s.dma_start(out=w2p[:, ic * 512:(ic + 1) * 512],
                                    in_=F2B[oc * 512:(oc + 1) * 512, ic * 128:(ic + 1) * 128],
                                    transpose=True), 16)
        emit_ffn_down(b, seg, HBF, w2load_B2, evacB, s_list=S, tag="B6d",
                      loader_eng="sync", h_store_sem="B6_hst")
        seg.close()

        if upto == "B":
            fin_copy(["B6_xw"])
            return b

        seg = contextlib.ExitStack()
        emit_seg_fence(b)
        emit_ln_simple(b, seg, X, XNBF, s_list=S, tag="C1", prev_x_sem="B6_xw")
        vb = b.sb(seg, "CVBC", [128, D], F32)
        vrow = b.sb(seg, "CVROW", [1, D], F32)
        s_vb = b.sem("C_vb")
        with nc.psum_tensor("CVBPS", [128, 512], F32) as bcps:
            def f(s):
                for h in range(NH):
                    s_vb.inc(s.dma_start(out=vrow[0:1, h * 128:(h + 1) * 128],
                                         in_=w["L1_attn_b_qkv"][None, 384 * h + 256:384 * h + 384]), 16)
            b.on("sync", f)
            def f(t):
                s_vb.wait(t)
            b.on("tensor", f)
            bcast_rows(b, vrow[0:1, :], vb, D, s_t, s_v, bcps)
        emit_qkv_proj(b, seg, XNBF, w["L1_attn_w_qkv"], w["L1_attn_b_qkv"],
                      QF, KLOC, VLOC, s_list=S, interleaved=True, tag="C",
                      nb_sem="C1_nb", vbias_bc=vb)
        emit_norms(b, seg, QF, KLOC, KN2L, s_list=S, per_head=True, tag="C",
                   which="k")
        emit_gather3(b, KLOC, VLOC, KN2L, KALL, VALL, KN2G,
                     wait_names=["qk_stg", "qk_stgv", "knst"])
        emit_norms(b, seg, QF, KLOC, KN2L, s_list=S, per_head=True, tag="C",
                   which="q")
        seg.close()

        seg = contextlib.ExitStack()
        emit_seg_fence(b)
        ctxfm2 = b.sb(seg, "CCTXFM", [128, NH * T], BF16)
        sub = contextlib.ExitStack()
        emit_flash_MHA(b, sub, QF, KALL, VALL, KN2G, ctxfm2, "CNMAX", s_list=S, tag="CF")
        sub.close()
        emit_cproj(b, seg, ctxfm2, w["L1_attn_w_out"], w["L1_attn_b_out"], X, s_list=S)
        seg.close()

        if upto == "C":
            fin_copy(["C5_xw"])
            return b

        seg = contextlib.ExitStack()
        emit_seg_fence(b)
        emit_ln_simple(b, seg, X, XNBF, s_list=S, tag="D1", prev_x_sem="C5_xw")
        sub = contextlib.ExitStack()
        b.sems["D2_nbsrc"] = b.sems["D1_nb"]
        def wgu_load(g, ip, dst, sem, which):
            wsrc = w["L1_ffn_w_gate"] if which == "gate" else w["L1_ffn_w_up"]
            sem.inc(g.dma_start(
                out=dst[:, :].rearrange("p (c n) -> p c n", n=512),
                in_=wsrc[:, ip * 512:(ip + 1) * 512].rearrange("(c p) n -> p c n", p=128)), 16)
        emit_ffn_up(b, sub, XNBF, HBF, s_list=S, tag="D2", mode="gateup24",
                    wload_up=wgu_load)
        sub.close()
        emit_local_fence(b)
        xcd = [b.sb(seg, "DXCa", [128, 512], F32), b.sb(seg, "DXCb", [128, 512], F32)]
        yod = b.sb(seg, "DYO", [128, 512], F32)
        s_x4 = [b.sem("D_x4a"), b.sem("D_x4b")]
        ecnt2 = [0]
        def w2load_D(g, oc, w2p, sem):
            sem.inc(g.dma_start(
                out=w2p[:, :].rearrange("p (c n) -> p c n", n=512),
                in_=w["L1_ffn_w_down"][:, oc * 512:(oc + 1) * 512]
                    .rearrange("(c p) n -> p c n", p=128)), 16)
        def evacD(tt, oc, psum):
            par = ecnt2[0] % 2
            ecnt2[0] += 1
            def f(s, tt=tt, oc=oc, par=par):
                s_v.wait(s)
                s_x4[par].inc(s.dma_start(
                    out=xcd[par][:, :],
                    in_=X[tt * 128:(tt + 1) * 128, oc * 512:(oc + 1) * 512]), 16)
            b.on("sync", f)
            xl = s_x4[par].n
            def f(v, tt=tt, oc=oc, par=par, psum=psum, xl=xl):
                s_t.wait(v)
                s_x4[par].wait(v, xl)
                s_y.wait(v)
                s_v.inc(v.tensor_tensor(out=yod[:, :], in0=psum[:, :],
                                        in1=xcd[par][:, :], op=ALU.add), 1)
            b.on("vector", f)
            def f(s, tt=tt, oc=oc):
                s_v.wait(s)
                s_y.inc(s.dma_start(
                    out=y_out[tt * 128:(tt + 1) * 128, oc * 512:(oc + 1) * 512],
                    in_=yod[:, :]), 16)
            b.on("sync", f)
        emit_ffn_down(b, seg, HBF, w2load_D, evacD, s_list=S, tag="D3",
                      loader_eng="gpsimd", h_store_sem="D2_hst")
        def f(s):
            s_y.wait(s)
        b.on("sync", f)
        seg.close()
    return b


def emit_local_fence(b, engs=("sync", "gpsimd", "scalar", "vector", "tensor")):
    snap = [(s.h, s.n) for s in b.sems.values() if s.n > 0]
    for eng in engs:
        def f(e, snap=snap):
            for h, n in snap:
                e.wait_ge(h, n)
        b.on(eng, f)


def emit_seg_fence(b):
    snap = [(s.h, s.n) for s in b.sems.values() if s.n > 0]
    for eng in ("sync", "gpsimd", "scalar", "vector", "tensor"):
        def f(e, snap=snap):
            for h, n in snap:
                e.wait_ge(h, n)
        b.on(eng, f)


def emit_ln_simple(b, seg, X, XNBF, s_list, tag, prev_x_sem):
    (s_g, s_d, s_e, s_t, s_v, s_sc, s_p, s_o) = s_list
    nc, cfg = b.nc, b.cfg
    D, NT = b.cfg.D, b.cfg.NT
    xt = b.sb(seg, tag + "XT", [128, D], F32)
    scr = b.sb(seg, tag + "SCR", [128, D], F32)
    stat = b.sb(seg, tag + "STAT", [128, 8], F32)
    xnb = b.sb(seg, tag + "XNB", [128, D], BF16)
    s_xt = b.sem("ln_xt")
    s_nb = b.sem(tag + "_nb")
    for i in range(NT):
        def f(s, i=i):
            s_v.wait(s); s_sc.wait(s)
            if i == 0 and prev_x_sem in b.sems:
                b.sems[prev_x_sem].wait(s)
            s_xt.inc(s.dma_start(out=xt[:, :], in_=X[i * 128:(i + 1) * 128, :]), 16)
        b.on("sync", f)
        def f(v):
            s_xt.wait(v)
        b.on("vector", f)
        def f(sc):
            s_xt.wait(sc)
        b.on("scalar", f)
        ln_stats(b, xt, D, stat, scr, s_sc, s_v)
        ln_apply(b, xt, scr, D, stat, s_sc, s_v)
        def f(v):
            s_v.wait(v)
            s_nb.wait(v)
            s_v.inc(v.tensor_copy(out=xnb[:, :], in_=scr[:, :]), 1)
        b.on("vector", f)
        def f(s, i=i):
            s_v.wait(s)
            s_nb.inc(s.dma_start(out=XNBF[i * 128:(i + 1) * 128, :], in_=xnb[:, :]), 16)
        b.on("sync", f)


def emit_cproj(b, seg, ctxfm, w_out, b_out, X, s_list):
    (s_g, s_d, s_e, s_t, s_v, s_sc, s_p, s_o) = s_list
    nc, cfg = b.nc, b.cfg
    D, T, NT, DC, NH = cfg.D, cfg.T, cfg.NT, cfg.DC, cfg.NH

    emit_local_fence(b)
    bo = b.sb(seg, "CBOBC", [128, D], F32)
    rowt = b.sb(seg, "C5ROW", [1, D], F32)
    s_br = b.sem("C5_br")
    with nc.psum_tensor("C5BCPS", [128, 512], F32) as bcps:
        def f(s):
            s_br.inc(s.dma_start(out=rowt[0:1, :], in_=b_out[None, :]), 16)
        b.on("sync", f)
        def f(t):
            s_br.wait(t)
        b.on("tensor", f)
        bcast_rows(b, rowt[0:1, :], bo, D, s_t, s_v, bcps)

    NPAN = D // 512
    owp = b.sb(seg, "C5WP", [128, NPAN * DC * 512], BF16)
    prj = [b.ps(seg, "C5P0", [128, 512]), b.ps(seg, "C5P1", [128, 512])]
    xrt = b.sb(seg, "C5XR", [128, D], F32)
    xwt = b.sb(seg, "C5XW", [128, D], F32)
    s_ow = b.sem("C5_ow")
    s_xr = b.sem("C5_xr")
    s_xw = b.sem("C5_xw")
    for p in range(NPAN):
        def f(g, p=p):
            s_ow.inc(g.dma_start(
                out=owp[:, p * DC * 512:(p + 1) * DC * 512]
                    .rearrange("p (c n) -> p c n", n=512),
                in_=w_out[:, p * 512:(p + 1) * 512].rearrange("(c p) n -> p c n", p=128)), 16)
        b.on("gpsimd", f)
    owdone = s_ow.n
    prjfree = [0, 0]
    for tt in range(NT):
        def f(s, tt=tt):
            s_v.wait(s)
            if tt == 0:
                b.sems["B6_xw"].wait(s)
            s_xr.inc(s.dma_start(out=xrt[:, :], in_=X[tt * 128:(tt + 1) * 128, :]), 16)
        b.on("sync", f)
        xload = s_xr.n
        for p in range(NPAN):
            bank = p % 2
            def f(t, tt=tt, p=p, bank=bank):
                if tt == 0 and p == 0:
                    s_ow.wait(t, owdone)
                    s_v.wait(t)
                s_v.wait(t, prjfree[bank])
                for h in range(NH):
                    s_t.inc(t.matmul(
                        prj[bank][:, :],
                        ctxfm[:, h * T + tt * 128: h * T + (tt + 1) * 128],
                        owp[:, p * DC * 512 + h * 512: p * DC * 512 + (h + 1) * 512],
                        start=(h == 0), stop=(h == NH - 1)), 1)
            b.on("tensor", f)
            mmdone = s_t.n
            def f(v, p=p, bank=bank, xload=xload, tt=tt):
                s_t.wait(v, mmdone)
                s_xr.wait(v, xload)
                if p == 0:
                    s_xw.wait(v)
                s_v.inc(v.tensor_tensor(out=xwt[:, p * 512:(p + 1) * 512],
                                        in0=prj[bank][:, :],
                                        in1=bo[:, p * 512:(p + 1) * 512], op=ALU.add), 1)
                s_v.wait(v)
                s_v.inc(v.tensor_tensor(out=xwt[:, p * 512:(p + 1) * 512],
                                        in0=xwt[:, p * 512:(p + 1) * 512],
                                        in1=xrt[:, p * 512:(p + 1) * 512], op=ALU.add), 1)
            b.on("vector", f)
            prjfree[bank] = s_v.n
        def f(s, tt=tt):
            s_v.wait(s)
            s_xw.inc(s.dma_start(out=X[tt * 128:(tt + 1) * 128, :], in_=xwt[:, :]), 16)
        b.on("sync", f)


GROUPS = [[0, 1, 2, 3], [4, 5, 6, 7]]


def emit_attnA(b, seg, x_in, w, X, XNBF, QF, KLOC, VLOC, KN2L, s_list):
    (s_g, s_d, s_e, s_t, s_v, s_sc, s_p, s_o) = s_list
    nc, cfg = b.nc, b.cfg
    D, T, NT, DC = b.cfg.D, b.cfg.T, b.cfg.NT, b.cfg.DC

    rowa = b.sb(seg, "ROWA", [1, D], F32)
    rowb = b.sb(seg, "ROWB", [1, D], F32)
    g0 = b.sb(seg, "G0BC", [128, D], F32)
    b0 = b.sb(seg, "B0BC", [128, D], F32)
    bcps = b.ps(seg, "BCPS", [128, 512])
    def f(s):
        s_d.inc(s.dma_start(out=rowa[0:1, :], in_=w["L0_attn_gamma"][None, :]), 16)
        s_d.inc(s.dma_start(out=rowb[0:1, :], in_=w["L0_attn_beta"][None, :]), 16)
    b.on("sync", f)
    def f(t):
        s_d.wait(t); s_p.wait(t)
    b.on("tensor", f)
    bcast_rows(b, rowa[0:1, :], g0, D, s_t, s_v, bcps)
    bcast_rows(b, rowb[0:1, :], b0, D, s_t, s_v, bcps)

    s_xt = b.sem("A_xt")
    s_xst = b.sem("A_xst")
    s_nbst = b.sem("A_nbst")
    xt = b.sb(seg, "XT", [128, D], F32)
    scr = b.sb(seg, "SCR", [128, D], F32)
    stat = b.sb(seg, "STAT", [128, 8], F32)
    xn1 = b.sb(seg, "XN1", [128, D], F32)
    xnb = b.sb(seg, "XNB", [128, D], BF16)
    for i in range(NT):
        def f(s, i=i):
            s_v.wait(s)
            s_sc.wait(s)
            s_xt.inc(s.dma_start(out=xt[:, :], in_=x_in[i * 128:(i + 1) * 128, :]), 16)
        b.on("sync", f)
        def f(v):
            s_xt.wait(v)
        b.on("vector", f)
        def f(sc):
            s_xt.wait(sc)
        b.on("scalar", f)
        ln_stats(b, xt, D, stat, scr, s_sc, s_v)
        ln_apply(b, xt, xn1, D, stat, s_sc, s_v)
        def f(v):
            s_v.wait(v)
            s_v.inc(v.tensor_tensor(out=scr[:, :], in0=xt[:, :], in1=xn1[:, :],
                                    op=ALU.add), 1)
        b.on("vector", f)
        def f(s, i=i):
            s_v.wait(s)
            s_xst.inc(s.dma_start(out=X[i * 128:(i + 1) * 128, :], in_=scr[:, :]), 16)
        b.on("sync", f)
        def f(sc):
            s_xst.wait(sc)
        b.on("scalar", f)
        ln_stats(b, xn1, D, stat, scr, s_sc, s_v)
        ln_apply(b, xn1, xn1, D, stat, s_sc, s_v, gbc=b.t["G0BC"], bbc=b.t["B0BC"])
        def f(v):
            s_v.wait(v)
            s_nbst.wait(v)
            s_v.inc(v.tensor_copy(out=xnb[:, :], in_=xn1[:, :]), 1)
        b.on("vector", f)
        def f(s, i=i):
            s_v.wait(s)
            s_nbst.inc(s.dma_start(out=XNBF[i * 128:(i + 1) * 128, :], in_=xnb[:, :]), 16)
        b.on("sync", f)

    cfgD512 = D // 512
    AKALL = b.dram["KALL"]; AVALL = b.dram["VALL"]; AKN2G = b.dram["KN2G"]
    def after_kv_A():
        emit_norms(b, seg, QF, KLOC, KN2L, s_list, per_head=False, tag="A",
                   which="k")
        emit_gather3(b, KLOC, VLOC, KN2L, AKALL, AVALL, AKN2G,
                     wait_names=["qk_stg", "qk_stgv", "knst"])
    order_kvq = list(range(cfgD512, 3 * cfgD512)) + list(range(cfgD512))
    emit_qkv_proj(b, seg, XNBF, w["L0_attn_qkv"], None, QF, KLOC, VLOC, s_list,
                  interleaved=False, tag="A", nb_sem="A_nbst",
                  order=order_kvq, after_kv=(2 * cfgD512, after_kv_A))
    emit_norms(b, seg, QF, KLOC, KN2L, s_list, per_head=False, tag="A", which="q")


def emit_qkv_proj(b, seg, XNBF, wqkv, bqkv, QF, KLOC, VLOC, s_list, interleaved,
                  tag="A", nb_sem="A_nbst", wsrcT=None, vbias_bc=None,
                  order=None, after_kv=None):
    (s_g, s_d, s_e, s_t, s_v, s_sc, s_p, s_o) = s_list
    nc, cfg = b.nc, b.cfg
    D, T, NT, DC, NH = cfg.D, cfg.T, cfg.NT, cfg.DC, cfg.NH

    s_fm = b.sem("qk_fm")
    if wsrcT is None:
        s_wp = [b.sem("qk_wp0"), b.sem("qk_wp1")]
    else:
        s_wp = [b.sem("qk_wpT0"), b.sem("qk_wpT1")]
    s_stg = b.sem("qk_stg")
    s_stgv = b.sem("qk_stgv")
    xnfm = b.sb(seg, tag + "XN_FM", [128, DC * T], BF16)
    nbst = b.sems.get(nb_sem)
    waits = [(nbst, nbst.n)] if nbst is not None else []
    load_fm(b, xnfm, XNBF, DC, T, s_fm, wait_sems=waits)
    xnfm_ready_fm = s_fm.n

    wp = [b.sb(seg, tag + "WP0", [128, DC * 512], BF16),
          b.sb(seg, tag + "WP1", [128, DC * 512], BF16)]
    prj = [b.ps(seg, tag + "PRJ0", [128, 512]), b.ps(seg, tag + "PRJ1", [128, 512])]
    stg = b.sb(seg, tag + "STG", [128, T], BF16)
    stgv = b.sb(seg, tag + "STGV", [128, 512], BF16)
    bq = None
    if bqkv is not None:
        bq = b.sb(seg, tag + "BQPP", [128, 3 * D // 128], F32)
        emit_bias_cols(b, seg, bqkv, bq, s_list, tag)

    NPAN = 3 * D // 512
    panload = [None] * NPAN
    panfree = [0, 0]
    prjfree = [0, 0]
    stg_rd = [0]
    xnfm_ready = xnfm_ready_fm

    if order is None:
        order = list(range(NPAN))
    first_mm = [True]
    for oi, p in enumerate(order):
        if after_kv is not None and oi == after_kv[0]:
            after_kv[1]()
        slot = oi % 2
        if wsrcT is None:
            def f(g, p=p, slot=slot):
                s_t.wait(g, panfree[slot])
                s_wp[slot].inc(g.dma_start(
                    out=wp[slot][:, :].rearrange("p (c n) -> p c n", n=512),
                    in_=wqkv[:, p * 512:(p + 1) * 512].rearrange("(c p) n -> p c n", p=128)), 16)
            b.on("gpsimd", f)
        else:
            c0 = p * 512
            role = "q" if c0 < D else ("k" if c0 < 2 * D else "v")
            r0 = c0 - (0 if role == "q" else (D if role == "k" else 2 * D))
            def f(s, p=p, slot=slot, role=role, r0=r0):
                s_t.wait(s, panfree[slot])
                if p == 0:
                    b.sems["B_pc"].wait(s)
                for fc in range(DC):
                    s_wp[slot].inc(s.dma_start(
                        out=wp[slot][:, fc * 512:(fc + 1) * 512],
                        in_=wsrcT[role][r0:r0 + 512, fc * 128:(fc + 1) * 128],
                        transpose=True), 16)
            b.on("sync", f)
        panload[p] = s_wp[slot].n

        for cc in range(4):
            col0 = p * 512 + cc * 128
            if not interleaved:
                if col0 < D:
                    role, drow = "q", col0
                elif col0 < 2 * D:
                    role, drow = "k", col0 - D
                else:
                    role, drow = "v", col0 - 2 * D
            else:
                h, off = divmod(col0, 384)
                if off < 128:
                    role, drow = "q", h * 128 + off
                elif off < 256:
                    role, drow = "k", h * 128 + (off - 128)
                else:
                    role, drow = "v", h * 128 + (off - 256)
            if role in ("q", "k"):
                dst = QF if role == "q" else KLOC
                for tg in range(T // 512):
                    bank = tg % 2
                    def f(t, p=p, cc=cc, tg=tg, bank=bank, slot=slot, fst=first_mm[0]):
                        s_wp[slot].wait(t, panload[p])
                        if fst:
                            s_fm.wait(t, xnfm_ready)
                            s_v.wait(t)
                        s_v.wait(t, prjfree[bank])
                        for fc in range(DC):
                            s_t.inc(t.matmul(
                                prj[bank][:, :],
                                wp[slot][:, fc * 512 + cc * 128: fc * 512 + (cc + 1) * 128],
                                xnfm[:, fc * T + tg * 512: fc * T + (tg + 1) * 512],
                                start=(fc == 0), stop=(fc == DC - 1)), 1)
                    b.on("tensor", f)
                    mmdone = s_t.n
                    def f(v, tg=tg, bank=bank, drow=drow):
                        s_t.wait(v, mmdone)
                        if tg == 0:
                            s_stg.wait(v, stg_rd[0])
                        if bq is not None:
                            s_v.inc(v.tensor_scalar(
                                out=stg[:, tg * 512:(tg + 1) * 512], in0=prj[bank][:, :],
                                scalar1=bq[:, drow // 128: drow // 128 + 1], scalar2=None,
                                op0=ALU.add), 1)
                        else:
                            s_v.inc(v.tensor_copy(out=stg[:, tg * 512:(tg + 1) * 512],
                                                  in_=prj[bank][:, :]), 1)
                    b.on("vector", f)
                    prjfree[bank] = s_v.n
                def f(s, dst=dst, drow=drow):
                    s_v.wait(s)
                    s_stg.inc(s.dma_start(out=dst[drow:drow + 128, :], in_=stg[:, :]), 16)
                b.on("sync", f)
                stg_rd[0] = s_stg.n
            else:
                pass
        if (not interleaved) and p * 512 >= 2 * D:
            col0 = p * 512 - 2 * D
            for tt in range(NT):
                bank = tt % 2
                def f(t, p=p, tt=tt, bank=bank, slot=slot, fst=first_mm[0]):
                    s_wp[slot].wait(t, panload[p])
                    if fst and tt == 0:
                        s_fm.wait(t, xnfm_ready)
                        s_v.wait(t)
                    s_v.wait(t, prjfree[bank])
                    for fc in range(DC):
                        s_t.inc(t.matmul(
                            prj[bank][:, :],
                            xnfm[:, fc * T + tt * 128: fc * T + (tt + 1) * 128],
                            wp[slot][:, fc * 512:(fc + 1) * 512],
                            start=(fc == 0), stop=(fc == DC - 1)), 1)
                b.on("tensor", f)
                mmdone = s_t.n
                def f(v, bank=bank, col0=col0):
                    s_t.wait(v, mmdone)
                    s_stgv.wait(v)
                    if vbias_bc is not None:
                        s_v.inc(v.tensor_tensor(out=stgv[:, :], in0=prj[bank][:, :],
                                                in1=vbias_bc[:, col0:col0 + 512],
                                                op=ALU.add), 1)
                    else:
                        s_v.inc(v.tensor_copy(out=stgv[:, :], in_=prj[bank][:, :]), 1)
                b.on("vector", f)
                prjfree[bank] = s_v.n
                def f(s, tt=tt, col0=col0):
                    s_v.wait(s)
                    s_stgv.inc(s.dma_start(out=VLOC[tt * 128:(tt + 1) * 128, col0:col0 + 512],
                                           in_=stgv[:, :]), 16)
                b.on("scalar", f)
        if interleaved:
            for cc in range(4):
                col0 = p * 512 + cc * 128
                h, off = divmod(col0, 384)
                if off < 256:
                    continue
                vcol = h * 128 + (off - 256)
                for tt in range(NT):
                    bank = tt % 2
                    def f(t, p=p, cc=cc, tt=tt, bank=bank, slot=slot):
                        s_wp[slot].wait(t, panload[p])
                        s_v.wait(t, prjfree[bank])
                        for fc in range(DC):
                            s_t.inc(t.matmul(
                                prj[bank][:, 0:128],
                                xnfm[:, fc * T + tt * 128: fc * T + (tt + 1) * 128],
                                wp[slot][:, fc * 512 + cc * 128: fc * 512 + (cc + 1) * 128],
                                start=(fc == 0), stop=(fc == DC - 1)), 1)
                    b.on("tensor", f)
                    mmdone = s_t.n
                    def f(v, bank=bank, vcol=vcol):
                        s_t.wait(v, mmdone)
                        s_stgv.wait(v)
                        if vbias_bc is not None:
                            s_v.inc(v.tensor_tensor(out=stgv[:, 0:128],
                                                    in0=prj[bank][:, 0:128],
                                                    in1=vbias_bc[:, vcol:vcol + 128],
                                                    op=ALU.add), 1)
                        else:
                            s_v.inc(v.tensor_copy(out=stgv[:, 0:128],
                                                  in_=prj[bank][:, 0:128]), 1)
                    b.on("vector", f)
                    prjfree[bank] = s_v.n
                    def f(s, tt=tt, vcol=vcol):
                        s_v.wait(s)
                        s_stgv.inc(s.dma_start(out=VLOC[tt * 128:(tt + 1) * 128, vcol:vcol + 128],
                                               in_=stgv[:, 0:128]), 16)
                    b.on("scalar", f)
        panfree[slot] = s_t.n
        first_mm[0] = False


def emit_bias_cols(b, seg, bqkv, bq, s_list, tag=""):
    (s_g, s_d, s_e, s_t, s_v, s_sc, s_p, s_o) = s_list
    nc, cfg = b.nc, b.cfg
    nch = 3 * cfg.D // 128
    s_bq = b.sem("bqld")
    def f(g):
        s_bq.inc(g.dma_start(out=bq[:, :],
                             in_=bqkv.rearrange("(i p) -> p i", p=128),
                             allow_slow_non_contiguous=True), 16)
    b.on("gpsimd", f)
    def f(t):
        s_bq.wait(t)
    b.on("tensor", f)
    def f(v):
        s_bq.wait(v)
    b.on("vector", f)


def emit_norms(b, seg, QF, KLOC, KN2L, s_list, per_head, tag="", which="qk"):
    (s_g, s_d, s_e, s_t, s_v, s_sc, s_p, s_o) = s_list
    nc, cfg = b.nc, b.cfg
    D, T, DC = cfg.D, cfg.T, cfg.DC
    ntg = T // 512

    s_nr = b.sem("nrm_ld")
    nrm = b.sb(seg, tag + which + "NRMT", [128, T], BF16)
    nsq = b.sb(seg, tag + which + "NSQ", [128, T], BF16)
    npsctx = nc.psum_tensor(tag + which + "NPS", [1, (T // 512) * 512], F32)
    nps = npsctx.__enter__()
    nacc = b.sb(seg, tag + which + "NACC", [1, 2 * DC * ntg], F32)
    nmax = b.t[tag + "NMAX"]
    krow = b.sb(seg, tag + which + "KROWS", [1, 128], F32)

    stg_done = b.sems.get("qk_stg")
    stgv_done = b.sems.get("qk_stgv")
    first_load = [True]
    pairs = {"qk": ((0, QF), (1, KLOC)), "q": ((0, QF),), "k": ((1, KLOC),)}[which]
    for wi, src in pairs:
        for fc in range(DC):
            def f(s, fc=fc, src=src):
                if first_load[0]:
                    if stg_done is not None:
                        stg_done.wait(s)
                    if stgv_done is not None:
                        stgv_done.wait(s)
                    first_load[0] = False
                s_sc.wait(s)
                s_nr.inc(s.dma_start(out=nrm[:, :], in_=src[fc * 128:(fc + 1) * 128, :]), 16)
            b.on("sync", f)
            def f(sc):
                s_nr.wait(sc)
                s_t.wait(sc)
                s_sc.inc(sc.activation(out=nsq[:, :], in_=nrm[:, :], func=AF.Square), 1)
            b.on("scalar", f)
            for tg in range(ntg):
                first = (fc == 0) or per_head
                last = (fc == DC - 1) or per_head
                def f(t, fc=fc, tg=tg, first=first, last=last):
                    s_sc.wait(t)
                    if first:
                        s_v.wait(t)
                    s_t.inc(t.matmul(nps[:, tg * 512:(tg + 1) * 512],
                                     b.t["ONES_col_bf"][:, :],
                                     nsq[:, tg * 512:(tg + 1) * 512],
                                     start=first, stop=last), 1)
                b.on("tensor", f)
                if last:
                    idx = (wi * DC * ntg) + fc * ntg + tg if per_head else \
                          wi * DC * ntg + tg
                    def f(v, idx=idx, tg=tg):
                        s_t.wait(v)
                        s_v.inc(v.reduce_max(out=nacc[0:1, idx:idx + 1],
                                             in_=nps[0:1, tg * 512:(tg + 1) * 512],
                                             axis=AX.X), 1)
                    b.on("vector", f)
    def f(v):
        s_v.wait(v)
        nh = cfg.NH if per_head else 1
        for wi, _ in pairs:
            if per_head:
                for h in range(cfg.NH):
                    s_v.inc(v.reduce_max(
                        out=nmax[0:1, wi * nh + h: wi * nh + h + 1],
                        in_=nacc[0:1, (wi * cfg.DC + h) * ntg:(wi * cfg.DC + h + 1) * ntg],
                        axis=AX.X), 1)
            else:
                s_v.inc(v.reduce_max(out=nmax[0:1, wi:wi + 1],
                                     in_=nacc[0:1, wi * cfg.DC * ntg: wi * cfg.DC * ntg + ntg],
                                     axis=AX.X), 1)
    b.on("vector", f)
    npsctx.__exit__(None, None, None)
    if which in ("qk", "k"):
        emit_norms_small(b, nmax, krow, KN2L, per_head, s_list, tag)


def emit_flash_L0(b, seg, QF, KALL, VALL, KN2G, CTXBF, s_list, nmax_name="ANMAX"):
    (s_g, s_d, s_e, s_t, s_v, s_sc, s_p, s_o) = s_list
    nc, cfg = b.nc, b.cfg
    D, T, DC, NG = cfg.D, cfg.T, cfg.DC, cfg.NG
    c_soft = 1.0 / float(np.sqrt(np.float32(D)))

    negc = b.sb(seg, "NEGC", [128, 1], F32)
    kng = b.sb(seg, "KNGS", [1, 1], F32)
    nsc = b.sb(seg, "NSC", [1, 4], F32)
    with nc.psum_tensor("NEGCPS", [128, 1], F32) as negcps:
        s_kng = b.sem("fl_kng")
        def f(s):
            b.sems["s_cc"].wait(s)
            s_kng.inc(s.dma_start(out=kng[0:1, :], in_=KN2G[0:1, 0:1]), 16)
        b.on("sync", f)
        def f(v):
            s_kng.wait(v)
            s_v.inc(v.tensor_tensor(out=nsc[0:1, 0:1], in0=b.t[nmax_name][0:1, 0:1],
                                    in1=kng[0:1, 0:1], op=ALU.mult), 1)
        b.on("vector", f)
        def f(sc):
            s_v.wait(sc)
            s_sc.inc(sc.activation(out=nsc[0:1, 1:2], in_=nsc[0:1, 0:1], func=AF.Ln), 1)
            s_sc.wait(sc)
            s_sc.inc(sc.activation(out=nsc[0:1, 2:3], in_=nsc[0:1, 1:2], func=AF.Exp,
                                   scale=0.5), 1)
        b.on("scalar", f)
        def f(v):
            s_sc.wait(v)
            s_v.inc(v.tensor_scalar(out=nsc[0:1, 3:4], in0=nsc[0:1, 2:3],
                                    scalar1=-c_soft, scalar2=None, op0=ALU.mult), 1)
        b.on("vector", f)
        def f(t):
            s_v.wait(t)
            s_t.inc(t.matmul(negcps[:, :], b.t["ONES_row_f32"][:, :], nsc[0:1, 3:4],
                             start=True, stop=True), 1)
        b.on("tensor", f)
        def f(v):
            s_t.wait(v)
            s_v.inc(v.tensor_copy(out=negc[:, :], in_=negcps[:, :]), 1)
        b.on("vector", f)

    NR = 4
    kch = [b.sb(seg, f"KCH{i}", [128, DC * 128], BF16) for i in range(NR)]
    vch = [b.sb(seg, f"VCH{i}", [128, D], BF16) for i in range(NR)]
    qg = b.sb(seg, "QG", [128, DC * 512], BF16)
    ptar = b.sb(seg, "PTAR", [128, cfg.maxslots * 512], BF16)
    ctstg = b.sb(seg, "CTSTG", [128, D], BF16)
    lic = b.sb(seg, "LIC", [128, 4], F32)
    scps = [b.ps(seg, "SCPS0", [128, 512]), b.ps(seg, "SCPS1", [128, 512])]
    lcps = b.ps(seg, "LCPS", [128, 4])
    ctxps = b.ps(seg, "CTXPS", [128, D])
    s_kc = [b.sem(f"fl_kc{i}") for i in range(NR)]
    s_vc = [b.sem(f"fl_vc{i}") for i in range(NR)]
    s_qg = b.sem("fl_qg")
    s_cts = b.sem("fl_cts")
    kfree = [0] * NR
    scfree = [0, 0]
    vfree = [0] * NR

    for g in range(NG):
        chunks = [(r, c) for r in range(4) for c in range(4 * g + 4)]
        nch = len(chunks)
        sc0 = s_sc.n
        exp_done = [sc0 + ci + 1 for ci in range(nch)]
        scdone_list = []
        for ci, (r, c) in enumerate(chunks):
            par = ci % NR
            def f(s, g=g, r=r, c=c, par=par, ci=ci):
                s_t.wait(s, kfree[par])
                s_kc[par].inc(s.dma_start(
                    out=kch[par][:, :].rearrange("p (f k) -> p f k", k=128),
                    in_=KALL[r * D:(r + 1) * D, c * 128:(c + 1) * 128]
                        .rearrange("(f p) k -> p f k", p=128)), 16)
            b.on("sync", f)
            kload = s_kc[par].n
            if ci == 0:
                def f(s, g=g):
                    s_t.wait(s)
                    for fc in range(DC):
                        s_qg.inc(s.dma_start(
                            out=qg[:, fc * 512:(fc + 1) * 512],
                            in_=QF[fc * 128:(fc + 1) * 128, g * 512:(g + 1) * 512]), 16)
                b.on("scalar", f)
            qload = s_qg.n
            spar = ci % 2
            def f(t, ci=ci, par=par, spar=spar, kload=kload, qload=qload, g=g,
                  sc0=sc0):
                if ci == 0 and g == 0:
                    s_v.wait(t)
                s_kc[par].wait(t, kload)
                if ci == 0:
                    s_qg.wait(t, qload)
                if ci >= 2:
                    s_sc.wait(t, sc0 + ci - 1)
                elif scfree[spar]:
                    s_sc.wait(t, scfree[spar])
                for fc in range(DC):
                    s_t.inc(t.matmul(scps[spar][:, :],
                                     kch[par][:, fc * 128:(fc + 1) * 128],
                                     qg[:, fc * 512:(fc + 1) * 512],
                                     start=(fc == 0), stop=(fc == DC - 1)), 1)
            b.on("tensor", f)
            scdone_list.append(s_t.n)
            kfree[par] = s_t.n
        def f(sc, chunks=chunks, scdone_list=scdone_list, g=g):
            for ci in range(len(chunks)):
                s_t.wait(sc, scdone_list[ci])
                if ci == 0:
                    s_v.wait(sc)
                s_sc.inc(sc.activation(out=ptar[:, ci * 512:(ci + 1) * 512],
                                       in_=scps[ci % 2][:, :], func=AF.Exp,
                                       bias=negc[:, 0:1], scale=c_soft), 1)
        b.on("scalar", f)
        assert s_sc.n == exp_done[-1]
        scfree[0] = exp_done[-1] if (nch - 1) % 2 == 0 else exp_done[-2]
        scfree[1] = exp_done[-1] if (nch - 1) % 2 == 1 else exp_done[-2]
        def f(v, chunks=chunks, g=g, exp_done=exp_done):
            for ci, (r, c) in enumerate(chunks):
                d = c - 4 * g
                if d < 0:
                    continue
                s_sc.wait(v, exp_done[ci])
                s_v.inc(v.tensor_tensor(
                    out=ptar[:, ci * 512:(ci + 1) * 512],
                    in0=ptar[:, ci * 512:(ci + 1) * 512],
                    in1=b.t["MSK"][:, (d * 4 + r) * 512:(d * 4 + r + 1) * 512],
                    op=ALU.mult), 1)
        b.on("vector", f)
        def f(t, chunks=chunks, nch=nch):
            s_sc.wait(t)
            s_v.wait(t)
            for ci in range(nch):
                for m in range(4):
                    s_t.inc(t.matmul(lcps[:, m:m + 1],
                                     ptar[:, ci * 512 + m * 128: ci * 512 + (m + 1) * 128],
                                     b.t["ONES_col_bf"][:, :],
                                     start=(ci == 0 and m == 0),
                                     stop=(ci == nch - 1 and m == 3)), 1)
        b.on("tensor", f)
        ldone = s_t.n
        def f(v, ldone=ldone):
            s_t.wait(v, ldone)
            s_v.inc(v.reciprocal(out=lic[:, :], in_=lcps[:, :]), 1)
        b.on("vector", f)
        for m in range(4):
            sub = [(r, c) for (r, c) in chunks if c <= 4 * g + m]
            for si, (r, c) in enumerate(sub):
                par = si % NR
                def f(s, r=r, c=c, par=par):
                    s_t.wait(s, vfree[par])
                    s_vc[par].inc(s.dma_start(out=vch[par][:, :],
                                              in_=VALL[r * T + c * 128: r * T + (c + 1) * 128, :]), 16)
                b.on("sync", f)
                vload = s_vc[par].n
                slot = chunks.index((r, c))
                def f(t, si=si, par=par, slot=slot, m=m, vload=vload, nsub=len(sub)):
                    s_vc[par].wait(t, vload)
                    if si == 0:
                        s_v.wait(t)
                    for dv in range(D // 512):
                        s_t.inc(t.matmul(
                            ctxps[:, dv * 512:(dv + 1) * 512],
                            ptar[:, slot * 512 + m * 128: slot * 512 + (m + 1) * 128],
                            vch[par][:, dv * 512:(dv + 1) * 512],
                            start=(si == 0), stop=(si == nsub - 1)), 1)
                b.on("tensor", f)
                vfree[par] = s_t.n
            def f(v, m=m, g=g):
                s_t.wait(v)
                s_cts.wait(v)
                s_v.inc(v.tensor_scalar(out=ctstg[:, :], in0=ctxps[:, :],
                                        scalar1=lic[:, m:m + 1], scalar2=None,
                                        op0=ALU.mult), 1)
            b.on("vector", f)
            def f(s, m=m, g=g):
                s_v.wait(s)
                s_cts.inc(s.dma_start(
                    out=CTXBF[g * 512 + m * 128: g * 512 + (m + 1) * 128, :],
                    in_=ctstg[:, :]), 16)
            b.on("sync", f)


def emit_oproj_resid(b, seg, CTXBF, wo, X, s_list, tag="A5", prev_x_sem="A_xst",
                     src_store_sem="fl_cts", dst=None, dst_sem_name="op_xw"):
    (s_g, s_d, s_e, s_t, s_v, s_sc, s_p, s_o) = s_list
    nc, cfg = b.nc, b.cfg
    D, T, NT, DC = cfg.D, cfg.T, cfg.NT, cfg.DC
    if dst is None:
        dst = X

    s_fm2 = b.sem("op_fm")
    s_ow = b.sem("op_w2")
    s_xr = b.sem("op_xr")
    s_xw = b.sem(dst_sem_name)
    cts = b.sems.get(src_store_sem)
    cfm = b.sb(seg, tag + "CFM", [128, DC * T], BF16)
    emit_local_fence(b)
    load_fm(b, cfm, CTXBF, DC, T, s_fm2, wait_sems=[(cts, cts.n)])
    cfm_ready = s_fm2.n
    NPAN = D // 512
    owp = b.sb(seg, tag + "OWP", [128, NPAN * DC * 512], BF16)
    prj = [b.ps(seg, tag + "OPRJ0", [128, 512]), b.ps(seg, tag + "OPRJ1", [128, 512])]
    xrt = b.sb(seg, tag + "XRT", [128, D], F32)
    xwt = b.sb(seg, tag + "XWT", [128, D], F32)

    panload = [None] * NPAN
    prjfree = [0, 0]
    for p in range(NPAN):
        def f(g, p=p):
            s_ow.inc(g.dma_start(
                out=owp[:, p * DC * 512:(p + 1) * DC * 512]
                    .rearrange("p (c n) -> p c n", n=512),
                in_=wo[:, p * 512:(p + 1) * 512].rearrange("(c p) n -> p c n", p=128)), 16)
        b.on("gpsimd", f)
        panload[p] = s_ow.n
    for tt in range(NT):
        def f(s, tt=tt):
            s_v.wait(s)
            if tt == 0 and prev_x_sem in b.sems:
                b.sems[prev_x_sem].wait(s)
            s_xr.inc(s.dma_start(out=xrt[:, :], in_=X[tt * 128:(tt + 1) * 128, :]), 16)
        b.on("sync", f)
        xload = s_xr.n
        for p in range(NPAN):
            bank = p % 2
            def f(t, tt=tt, p=p, bank=bank):
                if tt == 0:
                    s_ow.wait(t, panload[NPAN - 1])
                if tt == 0 and p == 0:
                    s_fm2.wait(t, cfm_ready)
                s_v.wait(t, prjfree[bank])
                for dc in range(DC):
                    s_t.inc(t.matmul(
                        prj[bank][:, :],
                        cfm[:, dc * T + tt * 128: dc * T + (tt + 1) * 128],
                        owp[:, p * DC * 512 + dc * 512: p * DC * 512 + (dc + 1) * 512],
                        start=(dc == 0), stop=(dc == DC - 1)), 1)
            b.on("tensor", f)
            mmdone = s_t.n
            def f(v, p=p, bank=bank, xload=xload):
                s_t.wait(v, mmdone)
                s_xr.wait(v, xload)
                if p == 0:
                    s_xw.wait(v)
                s_v.inc(v.tensor_tensor(out=xwt[:, p * 512:(p + 1) * 512],
                                        in0=prj[bank][:, :],
                                        in1=xrt[:, p * 512:(p + 1) * 512], op=ALU.add), 1)
            b.on("vector", f)
            prjfree[bank] = s_v.n
        def f(s, tt=tt):
            s_v.wait(s)
            s_xw.inc(s.dma_start(out=X[tt * 128:(tt + 1) * 128, :], in_=xwt[:, :]), 16)
        b.on("sync", f)


def emit_norms_small(b, nmax, krow, KN2L, per_head, s_list, tag=""):
    (s_g, s_d, s_e, s_t, s_v, s_sc, s_p, s_o) = s_list
    cfg = b.cfg
    nh = cfg.NH if per_head else 1
    def f(g):
        s_d.wait(g)
        s_p.inc(g.memset(krow[0:1, :], 0.0), 1)
    b.on("gpsimd", f)
    def f(v):
        s_p.wait(v)
        s_v.wait(v)
        s_v.inc(v.tensor_copy(out=krow[0:1, 0:nh], in_=nmax[0:1, nh:2 * nh]), 1)
    b.on("vector", f)
    s_kn = b.sem("knst")
    def f(s):
        s_v.wait(s)
        s_kn.inc(s.dma_start(out=KN2L[0:1, :], in_=krow[0:1, :]), 16)
    b.on("sync", f)


def emit_negc_mha(b, seg, KN2G, nmax_name, negch, c_soft, s_list, tag):
    (s_g, s_d, s_e, s_t, s_v, s_sc, s_p, s_o) = s_list
    nc, cfg = b.nc, b.cfg
    NH = cfg.NH
    kng = b.sb(seg, tag + "KNGH", [1, NH], F32)
    nrow = b.sb(seg, tag + "NROW", [1, NH], F32)
    s_kg = b.sem("F_kng")
    with nc.psum_tensor(tag + "NGPS", [128, NH], F32) as ngps:
        def f(s):
            b.sems["s_cc"].wait(s)
            s_kg.inc(s.dma_start(out=kng[0:1, :], in_=KN2G[0:1, 0:NH]), 16)
        b.on("sync", f)
        def f(v):
            s_kg.wait(v)
            s_v.inc(v.tensor_tensor(out=nrow[0:1, :], in0=b.t[nmax_name][0:1, 0:NH],
                                    in1=kng[0:1, :], op=ALU.mult), 1)
        b.on("vector", f)
        def f(sc):
            s_v.wait(sc)
            s_sc.inc(sc.activation(out=nrow[0:1, :], in_=nrow[0:1, :], func=AF.Ln), 1)
            s_sc.wait(sc)
            s_sc.inc(sc.activation(out=nrow[0:1, :], in_=nrow[0:1, :], func=AF.Exp,
                                   scale=0.5), 1)
        b.on("scalar", f)
        def f(v):
            s_sc.wait(v)
            s_v.inc(v.tensor_scalar(out=nrow[0:1, :], in0=nrow[0:1, :],
                                    scalar1=-c_soft, scalar2=None, op0=ALU.mult), 1)
        b.on("vector", f)
        def f(t):
            s_v.wait(t)
            for h in range(NH):
                s_t.inc(t.matmul(ngps[:, h:h + 1], b.t["ONES_row_f32"][:, :],
                                 nrow[0:1, h:h + 1], start=(h == 0), stop=(h == NH - 1)), 1)
        b.on("tensor", f)
        def f(v):
            s_t.wait(v)
            s_v.inc(v.tensor_copy(out=negch[:, :], in_=ngps[:, :]), 1)
        b.on("vector", f)


def emit_flash_MHA(b, seg, QF, KALL, VALL, KN2G, ctxfm, nmax_name, s_list, tag):
    (s_g, s_d, s_e, s_t, s_v, s_sc, s_p, s_o) = s_list
    nc, cfg = b.nc, b.cfg
    D, T, DC, NG, NH = cfg.D, cfg.T, cfg.DC, cfg.NG, cfg.NH
    c_soft = 1.0 / float(np.sqrt(np.float32(cfg.DH)))
    EB = 3
    NSLOT = 12

    negch = b.sb(seg, tag + "NEGCH", [128, NH], F32)
    emit_negc_mha(b, seg, KN2G, nmax_name, negch, c_soft, s_list, tag)

    kh = b.sb(seg, tag + "KH", [128, 4 * T], BF16)
    vh = b.sb(seg, tag + "VH", [128, 4 * T], BF16)
    qh = b.sb(seg, tag + "QH", [128, T], BF16)
    ptar = b.sb(seg, tag + "PTA", [128, NSLOT * 512], BF16)
    linv = b.sb(seg, tag + "LINV", [1, 512], F32)
    lbcs = b.sb(seg, tag + "LBCS", [128, 512], F32)
    scps = [b.ps(seg, tag + "SC0", [128, EB * 512]), b.ps(seg, tag + "SC1", [128, EB * 512])]
    lrow = b.ps(seg, tag + "LROW", [128, 512])
    cfps = b.ps(seg, tag + "CFPS", [128, 512])
    s_kh = b.sem("F_kh")
    s_vh = b.sem("F_vh")
    s_qh = b.sem("F_qh")
    scfree = [0, 0]

    for h in range(NH):
        def f(s, h=h):
            s_t.wait(s)
            for r in range(4):
                s_kh.inc(s.dma_start(
                    out=kh[:, r * T:(r + 1) * T],
                    in_=KALL[r * D + h * 128: r * D + (h + 1) * 128, :]), 16)
            s_vh.inc(s.dma_start(
                out=vh[:, :].rearrange("p (n d) -> p n d", d=128),
                in_=VALL[:, h * 128:(h + 1) * 128].rearrange("(n p) d -> p n d", p=128)), 16)
            s_qh.inc(s.dma_start(out=qh[:, :], in_=QF[h * 128:(h + 1) * 128, :]), 16)
        b.on("sync", f)
        khload, vhload, qhload = s_kh.n, s_vh.n, s_qh.n
        for g in range(NG):
            chunks = [(r, c) for r in range(4) for c in range(4 * g + 4)]
            nch = len(chunks)
            blocks = [chunks[i:i + EB] for i in range(0, nch, EB)]
            nb = len(blocks)
            t0, sc0, v0 = s_t.n, s_sc.n, s_v.n
            sc_done = []
            acc = t0
            for bi in range(nb):
                acc += len(blocks[bi])
                sc_done.append(acc)
                if bi >= 1:
                    acc += 2 * len(blocks[bi - 1])
            exp_done = [sc0 + bi + 1 for bi in range(nb)]
            mask_done = []
            mcnt = v0
            for bi in range(nb):
                mcnt += sum(1 for (r, c) in blocks[bi] if c - 4 * g >= 0)
                mask_done.append(mcnt)

            def f(t, h=h, g=g, blocks=blocks, nb=nb, khload=khload, vhload=vhload,
                  qhload=qhload, exp_done=exp_done, mask_done=mask_done, v0=v0):
                if g == 0:
                    s_kh.wait(t, khload)
                    s_qh.wait(t, qhload)
                    s_vh.wait(t, vhload)
                s_v.wait(t, v0)
                for bi, blk in enumerate(blocks):
                    par = bi % 2
                    s_sc.wait(t, scfree[par] if bi < 2 else exp_done[bi - 2])
                    for j, (r, c) in enumerate(blk):
                        s_t.inc(t.matmul(scps[par][:, j * 512:(j + 1) * 512],
                                         kh[:, r * T + c * 128: r * T + (c + 1) * 128],
                                         qh[:, g * 512:(g + 1) * 512],
                                         start=True, stop=True), 1)
                    if bi >= 1:
                        s_sc.wait(t, exp_done[bi - 1])
                        if mask_done[bi - 1] > (mask_done[bi - 2] if bi >= 2 else v0):
                            s_v.wait(t, mask_done[bi - 1])
                        ci0 = (bi - 1) * EB
                        for j, (r, c) in enumerate(blocks[bi - 1]):
                            ci = ci0 + j
                            slot = ci % NSLOT
                            n = r * (T // 128) + c
                            s_t.inc(t.matmul(cfps[:, :],
                                             vh[:, n * 128:(n + 1) * 128],
                                             ptar[:, slot * 512:(slot + 1) * 512],
                                             start=(ci == 0), stop=False), 1)
                            s_t.inc(t.matmul(lrow[0:1, :], b.t["ONES_col_bf"][:, :],
                                             ptar[:, slot * 512:(slot + 1) * 512],
                                             start=(ci == 0), stop=False), 1)
            b.on("tensor", f)
            assert s_t.n == sc_done[-1] + (2 * len(blocks[-2]) if nb >= 2 else 0), (s_t.n, sc_done)

            def f(sc, h=h, blocks=blocks, sc_done=sc_done):
                for bi, blk in enumerate(blocks):
                    par = bi % 2
                    ci0 = bi * EB
                    s_t.wait(sc, sc_done[bi])
                    slot0 = ci0 % NSLOT
                    s_sc.inc(sc.activation(
                        out=ptar[:, slot0 * 512:(slot0 + len(blk)) * 512],
                        in_=scps[par][:, 0:len(blk) * 512], func=AF.Exp,
                        bias=negch[:, h:h + 1], scale=c_soft), 1)
            b.on("scalar", f)
            assert s_sc.n == exp_done[-1]
            scfree[0] = exp_done[-1] if (nb - 1) % 2 == 0 else exp_done[-2] if nb >= 2 else scfree[0]
            scfree[1] = exp_done[-1] if (nb - 1) % 2 == 1 else exp_done[-2] if nb >= 2 else scfree[1]

            def f(v, blocks=blocks, g=g, exp_done=exp_done):
                for bi, blk in enumerate(blocks):
                    dm = [(j, r, c) for j, (r, c) in enumerate(blk) if c - 4 * g >= 0]
                    if not dm:
                        continue
                    s_sc.wait(v, exp_done[bi])
                    for j, r, c in dm:
                        d = c - 4 * g
                        slot = (bi * EB + j) % NSLOT
                        s_v.inc(v.tensor_tensor(
                            out=ptar[:, slot * 512:(slot + 1) * 512],
                            in0=ptar[:, slot * 512:(slot + 1) * 512],
                            in1=b.t["MSK"][:, (d * 4 + r) * 512:(d * 4 + r + 1) * 512],
                            op=ALU.mult), 1)
            b.on("vector", f)
            assert s_v.n == mask_done[-1]

            def f(t, blocks=blocks, nb=nb, nch=nch, exp_done=exp_done,
                  mask_done=mask_done, v0=v0):
                s_sc.wait(t, exp_done[nb - 1])
                if mask_done[nb - 1] > (mask_done[nb - 2] if nb >= 2 else v0):
                    s_v.wait(t, mask_done[nb - 1])
                ci0 = (nb - 1) * EB
                for j, (r, c) in enumerate(blocks[nb - 1]):
                    ci = ci0 + j
                    slot = ci % NSLOT
                    n = r * (T // 128) + c
                    last = (ci == nch - 1)
                    s_t.inc(t.matmul(cfps[:, :],
                                     vh[:, n * 128:(n + 1) * 128],
                                     ptar[:, slot * 512:(slot + 1) * 512],
                                     start=(ci == 0), stop=last), 1)
                    s_t.inc(t.matmul(lrow[0:1, :], b.t["ONES_col_bf"][:, :],
                                     ptar[:, slot * 512:(slot + 1) * 512],
                                     start=(ci == 0), stop=last), 1)
            b.on("tensor", f)
            pvdone = s_t.n
            def f(v, pvdone=pvdone):
                s_t.wait(v, pvdone)
                s_v.inc(v.reciprocal(out=linv[0:1, :], in_=lrow[0:1, :]), 1)
            b.on("vector", f)
            def f(t):
                s_v.wait(t)
                s_t.inc(t.matmul(lrow[:, :], b.t["ONES_row_f32"][:, :],
                                 linv[0:1, :], start=True, stop=True), 1)
            b.on("tensor", f)
            def f(v, h=h, g=g):
                s_t.wait(v)
                s_v.inc(v.tensor_copy(out=lbcs[:, :], in_=lrow[:, :]), 1)
                s_v.wait(v)
                s_v.inc(v.tensor_tensor(out=ctxfm[:, h * T + g * 512: h * T + (g + 1) * 512],
                                        in0=cfps[:, :], in1=lbcs[:, :], op=ALU.mult), 1)
            b.on("vector", f)


def emit_ffn_down(b, seg, HBF, w2_load, evac, s_list, tag, loader_eng="gpsimd",
                  h_store_sem=None):
    (s_g, s_d, s_e, s_t, s_v, s_sc, s_p, s_o) = s_list
    nc, cfg = b.nc, b.cfg
    D, T, FF, FC = cfg.D, cfg.T, cfg.FF, cfg.FC

    hfmg = b.sb(seg, tag + "HFMG", [128, FC * 512], BF16)
    w2p = b.sb(seg, tag + "W2P", [128, FC * 512], BF16)
    dps = [b.ps(seg, tag + "DPS0", [128, 512]), b.ps(seg, tag + "DPS1", [128, 512])]
    s_hf = b.sem("ffn_hf")
    s_w2 = b.sem("ffn_w2" + loader_eng)
    dpsfree = [0, 0]
    hs_sem = b.sems.get("ffn_hst") if h_store_sem else None
    hs_n = hs_sem.n if hs_sem else 0
    for tg in range(T // 512):
        def f(s, tg=tg):
            if tg == 0 and hs_sem is not None:
                hs_sem.wait(s, hs_n)
            s_t.wait(s)
            for ic in range(FC):
                s_hf.inc(s.dma_start(
                    out=hfmg[:, ic * 512:(ic + 1) * 512],
                    in_=HBF[tg * 512:(tg + 1) * 512, ic * 128:(ic + 1) * 128],
                    transpose=True), 16)
        b.on("sync", f)
        hload = s_hf.n
        for oc in range(D // 512):
            def f(g, tg=tg, oc=oc):
                s_t.wait(g)
                w2_load(g, oc, w2p, s_w2)
            b.on(loader_eng, f)
            w2load = s_w2.n
            for tt in range(4):
                bank = tt % 2
                def f(t, tg=tg, oc=oc, tt=tt, bank=bank, hload=hload, w2load=w2load):
                    s_hf.wait(t, hload)
                    s_w2.wait(t, w2load)
                    s_v.wait(t, dpsfree[bank])
                    for ic in range(FC):
                        s_t.inc(t.matmul(
                            dps[bank][:, :],
                            hfmg[:, ic * 512 + tt * 128: ic * 512 + (tt + 1) * 128],
                            w2p[:, ic * 512:(ic + 1) * 512],
                            start=(ic == 0), stop=(ic == FC - 1)), 1)
                b.on("tensor", f)
                evac(tg * 4 + tt, oc, dps[bank])
                dpsfree[bank] = s_v.n


def emit_gather3(b, KLOC, VLOC, KN2L, KALL, VALL, KN2G, wait_names):
    s_cc = b.sems["s_cc"]
    waits = [(b.sems[nm].h, b.sems[nm].n) for nm in wait_names if nm in b.sems]
    def f(g, waits=waits):
        for h, n in waits:
            g.wait_ge(h, n)
        s_cc.inc(g.collective_compute(
            "AllGather", ALU.bypass, replica_groups=GROUPS,
            ins=[KLOC.opt()], outs=[KALL.opt()]), 1)
        s_cc.inc(g.collective_compute(
            "AllGather", ALU.bypass, replica_groups=GROUPS,
            ins=[VLOC.opt()], outs=[VALL.opt()]), 1)
        s_cc.inc(g.collective_compute(
            "AllReduce", ALU.max, replica_groups=GROUPS,
            ins=[KN2L.opt()], outs=[KN2G.opt()]), 1)
    b.on("gpsimd", f)


def emit_B1(b, seg, X, w, HS, PART1, XNBF, s_list):
    (s_g, s_d, s_e, s_t, s_v, s_sc, s_p, s_o) = s_list
    nc, cfg = b.nc, b.cfg
    D, T, NT, DC, LR = cfg.D, cfg.T, cfg.NT, cfg.DC, cfg.LR
    import ml_dtypes

    g1 = b.sb(seg, "G1BC", [128, D], F32)
    b1 = b.sb(seg, "B1BC", [128, D], F32)
    bs = b.sb(seg, "BSBC", [128, D], F32)
    rowt = b.sb(seg, "B1ROW", [1, D], F32)
    s_br = b.sem("B1_br")
    with nc.psum_tensor("B1BCPS", [128, 512], F32) as bcps:
        for wname, dstbc in (("L0_ffn_ln1_scale", g1), ("L0_ffn_ln1_bias", b1),
                             ("L0_ffn_simple_bias", bs)):
            def f(s, wname=wname):
                s_v.wait(s)
                s_br.inc(s.dma_start(out=rowt[0:1, :], in_=b.dram[wname][None, :]), 16)
            b.on("sync", f)
            def f(t):
                s_br.wait(t)
            b.on("tensor", f)
            bcast_rows(b, rowt[0:1, :], dstbc, D, s_t, s_v, bcps)

    wrt = b.sb(seg, "WRT", [128, DC * 3], F32)
    wrow = b.sb(seg, "WRROW", [3, D], F32)
    id3 = nc.inline_tensor(np.eye(3, dtype=np.float32), "c_id3f").ap()
    id3s = b.sb(seg, "ID3F", [3, 3], F32)
    s_wr = b.sem("B1_wr")
    with nc.psum_tensor("B1WRPS", [128, 3], F32) as wrps:
        def f(s):
            s_wr.inc(s.dma_start(out=wrow[0:3, :], in_=b.dram["L0_ffn_w_router"][:, :]), 16)
            s_wr.inc(s.dma_start(out=id3s[0:3, :], in_=id3[:, :]), 16)
        b.on("sync", f)
        for fc in range(DC):
            def f(t, fc=fc):
                s_wr.wait(t)
                s_v.wait(t)
                s_t.inc(t.transpose(wrps[:, :], wrow[0:3, fc * 128:(fc + 1) * 128],
                                    id3s[0:3, 0:3]), 1)
            b.on("tensor", f)
            def f(v, fc=fc):
                s_t.wait(v)
                s_v.inc(v.tensor_copy(out=wrt[:, fc * 3:(fc + 1) * 3], in_=wrps[:, :]), 1)
            b.on("vector", f)

    wdt = b.sb(seg, "WDT", [128, DC * LR], BF16)
    wut = b.sb(seg, "WUT", [32, D], BF16)
    s_lw = b.sem("B1_lw")
    def f(s):
        b.sems["B_pc"].wait(s)
        for fc in range(DC):
            s_lw.inc(s.dma_start(out=wdt[:, fc * LR:(fc + 1) * LR],
                                 in_=b.dram["WDB"][:, fc * 128:(fc + 1) * 128],
                                 transpose=True), 16)
    b.on("sync", f)
    wub_sb = b.sb(seg, "WUBSB", [128, LR], BF16)
    with nc.psum_tensor("B1WUPS", [32, 128], BF16) as wups:
        for fc in range(DC):
            def f(s, fc=fc):
                s_t.wait(s)
                s_lw.inc(s.dma_start(out=wub_sb[:, :],
                                     in_=b.dram["WUB"][fc * 128:(fc + 1) * 128, :]), 16)
            b.on("sync", f)
            lw = s_lw.n
            def f(t, fc=fc, lw=lw):
                s_lw.wait(t, lw)
                s_v.wait(t)
                s_t.inc(t.transpose(wups[:, :], wub_sb[:, 0:LR], b.t["ID_BF"][:, :]), 1)
            b.on("tensor", f)
            def f(v, fc=fc):
                s_t.wait(v)
                s_v.inc(v.tensor_copy(out=wut[0:32, fc * 128:(fc + 1) * 128],
                                      in_=wups[0:32, :]), 1)
            b.on("vector", f)

    trps = [b.ps(seg, "B1TR0", [128, 128]), b.ps(seg, "B1TR1", [128, 128])]
    psml = b.ps(seg, "B1SML", [128, 512])
    y1t_ps = b.ps(seg, "B1Y1T", [32, 128], BF16)
    delps = b.ps(seg, "B1DEL", [128, 2048] if D >= 2048 else [128, D])

    xt = b.sb(seg, "BXT", [128, D], F32)
    hst = b.sb(seg, "BHST", [128, D], F32)
    scr = b.sb(seg, "BSCR", [128, D], F32)
    stat = b.sb(seg, "BSTAT", [128, 8], F32)
    pa = b.sb(seg, "BPA", [128, D], F32)
    pb = b.sb(seg, "BPB", [128, D], F32)
    xnb = b.sb(seg, "BXNB", [128, D], BF16)
    hfm32 = b.sb(seg, "HSFM32", [128, DC * 128], F32)
    hfmbf = b.sb(seg, "HSFMBF", [128, DC * 128], BF16)
    y1b = b.sb(seg, "BY1B", [128, LR], BF16)
    y1t = b.sb(seg, "BY1T", [32, 128], BF16)
    s_xt = b.sem("B1_xt")
    s_hs = b.sem("B1_hs")
    s_p1 = b.sem("B1_p1")
    s_nb = b.sem("B1_nb")

    for i in range(NT):
        def f(s, i=i):
            s_v.wait(s); s_sc.wait(s)
            if i == 0:
                b.sems["op_xw"].wait(s)
            s_xt.inc(s.dma_start(out=xt[:, :], in_=X[i * 128:(i + 1) * 128, :]), 16)
        b.on("sync", f)
        def f(v):
            s_xt.wait(v)
        b.on("vector", f)
        def f(sc):
            s_xt.wait(sc)
        b.on("scalar", f)
        ln_stats(b, xt, D, stat, scr, s_sc, s_v)
        def f(v):
            s_hs.wait(v)
        b.on("vector", f)
        ln_apply(b, xt, hst, D, stat, s_sc, s_v)
        def f(s, i=i):
            s_v.wait(s)
            s_hs.inc(s.dma_start(out=HS[i * 128:(i + 1) * 128, :], in_=hst[:, :]), 16)
        b.on("sync", f)
        for fc in range(DC):
            par = fc % 2
            def f(t, fc=fc, par=par):
                s_v.wait(t)
                s_t.inc(t.transpose(trps[par][:, :], xt_dummy_src(b, hst, fc),
                                    b.t["ID_F32"][:, :]), 1)
            b.on("tensor", f)
            def f(v, fc=fc, par=par):
                s_t.wait(v)
                s_v.inc(v.tensor_copy(out=hfm32[:, fc * 128:(fc + 1) * 128],
                                      in_=trps[par][:, :]), 1)
            b.on("vector", f)
        def f(v):
            s_v.wait(v)
            s_v.inc(v.tensor_copy(out=hfmbf[:, :], in_=hfm32[:, :]), 1)
        b.on("vector", f)
        hfm_done = s_v.n
        def f(t, hfm_done=hfm_done):
            s_v.wait(t, hfm_done)
            for fc in range(DC):
                s_t.inc(t.matmul(psml[:, 0:3], hfm32[:, fc * 128:(fc + 1) * 128],
                                 wrt[:, fc * 3:(fc + 1) * 3],
                                 start=(fc == 0), stop=(fc == DC - 1)), 1)
        b.on("tensor", f)
        rt_done = s_t.n
        def f(v, i=i, rt_done=rt_done):
            s_t.wait(v, rt_done)
            s_v.inc(v.tensor_copy(out=stat[:, 5:8], in_=psml[:, 0:3]), 1)
            s_v.wait(v)
            l0, l1, l2 = stat[:, 5:6], stat[:, 6:7], stat[:, 7:8]
            s_v.inc(v.tensor_tensor(out=stat[:, 0:1], in0=l2, in1=l0, op=ALU.is_gt), 1)
            s_v.inc(v.tensor_tensor(out=stat[:, 1:2], in0=l2, in1=l1, op=ALU.is_gt), 1)
            s_v.inc(v.tensor_tensor(out=stat[:, 2:3], in0=l1, in1=l0, op=ALU.is_gt), 1)
            s_v.inc(v.tensor_tensor(out=stat[:, 3:4], in0=l1, in1=l2, op=ALU.is_ge), 1)
            s_v.wait(v)
            s_v.inc(v.tensor_tensor(out=b.t["RM2"][:, i:i + 1], in0=stat[:, 0:1],
                                    in1=stat[:, 1:2], op=ALU.mult), 1)
            s_v.inc(v.tensor_tensor(out=stat[:, 4:5], in0=stat[:, 2:3],
                                    in1=stat[:, 3:4], op=ALU.mult), 1)
            s_v.wait(v)
            s_v.inc(v.tensor_scalar(out=stat[:, 0:1], in0=b.t["RM2"][:, i:i + 1],
                                    scalar1=-1.0, scalar2=1.0, op0=ALU.mult,
                                    op1=ALU.add), 1)
            s_v.wait(v)
            s_v.inc(v.tensor_tensor(out=b.t["RM1"][:, i:i + 1], in0=stat[:, 4:5],
                                    in1=stat[:, 0:1], op=ALU.mult), 1)
            s_v.wait(v)
            s_v.inc(v.tensor_tensor(out=b.t["RM0"][:, i:i + 1], in0=stat[:, 0:1],
                                    in1=b.t["RM1"][:, i:i + 1], op=ALU.subtract), 1)
        b.on("vector", f)
        masks_done = s_v.n
        def f(t, masks_done=masks_done):
            s_v.wait(t, masks_done)
            for fc in range(DC):
                s_t.inc(t.matmul(psml[:, 32:32 + LR], hfmbf[:, fc * 128:(fc + 1) * 128],
                                 wdt[:, fc * LR:(fc + 1) * LR],
                                 start=(fc == 0), stop=(fc == DC - 1)), 1)
        b.on("tensor", f)
        def f(v):
            s_t.wait(v)
            s_v.inc(v.tensor_copy(out=y1b[:, :], in_=psml[:, 32:32 + LR]), 1)
        b.on("vector", f)
        def f(t):
            s_v.wait(t)
            s_t.inc(t.transpose(y1t_ps[:, :], y1b[:, 0:LR], b.t["ID_BF"][:, :]), 1)
        b.on("tensor", f)
        def f(v):
            s_t.wait(v)
            s_v.inc(v.tensor_copy(out=y1t[0:32, :], in_=y1t_ps[0:32, :]), 1)
        b.on("vector", f)
        y1t_done = s_v.n
        def f(t, y1t_done=y1t_done):
            s_v.wait(t, y1t_done)
            for oc in range(D // 512):
                s_t.inc(t.matmul(delps[:, oc * 512:(oc + 1) * 512], y1t[0:32, :],
                                 wut[0:32, oc * 512:(oc + 1) * 512],
                                 start=True, stop=True), 1)
        b.on("tensor", f)
        del_done = s_t.n
        def f(v, i=i, del_done=del_done):
            s_t.wait(v, del_done)
            s_p1.wait(v)
            s_v.inc(v.tensor_scalar(out=pa[:, :], in0=hst[:, :],
                                    scalar1=stat[:, 0:1], scalar2=None,
                                    op0=ALU.mult), 1)
            s_v.wait(v)
            s_v.inc(v.scalar_tensor_tensor(out=pb[:, :], in0=bs[:, :],
                                           scalar=b.t["RM0"][:, i:i + 1], in1=pa[:, :],
                                           op0=ALU.mult, op1=ALU.add), 1)
            s_v.wait(v)
            for oc in range(D // 512):
                s_v.inc(v.scalar_tensor_tensor(
                    out=pa[:, oc * 512:(oc + 1) * 512],
                    in0=delps[:, oc * 512:(oc + 1) * 512],
                    scalar=b.t["RM1"][:, i:i + 1], in1=pb[:, oc * 512:(oc + 1) * 512],
                    op0=ALU.mult, op1=ALU.add), 1)
        b.on("vector", f)
        def f(s, i=i):
            s_v.wait(s)
            s_p1.inc(s.dma_start(out=PART1[i * 128:(i + 1) * 128, :], in_=pa[:, :]), 16)
        b.on("sync", f)
        ln_stats(b, hst, D, stat, scr, s_sc, s_v)
        ln_apply(b, hst, pb, D, stat, s_sc, s_v, gbc=g1, bbc=b1)
        def f(v):
            s_v.wait(v)
            s_nb.wait(v)
            s_v.inc(v.tensor_copy(out=xnb[:, :], in_=pb[:, :]), 1)
        b.on("vector", f)
        def f(s, i=i):
            s_v.wait(s)
            s_nb.inc(s.dma_start(out=XNBF[i * 128:(i + 1) * 128, :], in_=xnb[:, :]), 16)
        b.on("sync", f)


def xt_dummy_src(b, hst, fc):
    return hst[:, fc * 128:(fc + 1) * 128]


def emit_oproj_ln2(b, seg, ctxfm, WOB, HS, XPRIME, XN2BF, w, s_list):
    (s_g, s_d, s_e, s_t, s_v, s_sc, s_p, s_o) = s_list
    nc, cfg = b.nc, b.cfg
    D, T, NT, DC, NH = cfg.D, cfg.T, cfg.NT, cfg.DC, cfg.NH

    emit_local_fence(b)
    g2 = b.sb(seg, "G2BC", [128, D], F32)
    b2 = b.sb(seg, "B2BC", [128, D], F32)
    rowt = b.sb(seg, "B5ROW", [1, D], F32)
    s_br = b.sem("B5_br")
    with nc.psum_tensor("B5BCPS", [128, 512], F32) as bcps:
        for wname, dstbc in (("L0_ffn_ln2_scale", g2), ("L0_ffn_ln2_bias", b2)):
            def f(s, wname=wname):
                s_v.wait(s)
                s_br.inc(s.dma_start(out=rowt[0:1, :], in_=b.dram[wname][None, :]), 16)
            b.on("sync", f)
            def f(t):
                s_br.wait(t)
            b.on("tensor", f)
            bcast_rows(b, rowt[0:1, :], dstbc, D, s_t, s_v, bcps)

    wot = b.sb(seg, "WOT", [128, DC * D], BF16)
    s_wo = b.sem("B5_wo")
    def f(s):
        b.sems["B_pc"].wait(s)
        for hc in range(DC):
            s_wo.inc(s.dma_start(out=wot[:, hc * D:(hc + 1) * D],
                                 in_=WOB[:, hc * 128:(hc + 1) * 128], transpose=True), 16)
    b.on("sync", f)
    wo_done = s_wo.n

    ops = [b.ps(seg, "B5OP0", [128, 512]), b.ps(seg, "B5OP1", [128, 512])]
    hst = b.sb(seg, "B5HS", [128, D], F32)
    xp = b.sb(seg, "B5XP", [128, D], F32)
    scr = b.sb(seg, "B5SCR", [128, D], F32)
    stat = b.sb(seg, "B5STAT", [128, 8], F32)
    xnb = b.sb(seg, "B5XNB", [128, D], BF16)
    s_hs2 = b.sem("B5_hs")
    s_xp = b.sem("B5_xp")
    s_nb2 = b.sem("B5_nb")
    opfree = [0, 0]
    for tt in range(NT):
        def f(s, tt=tt):
            s_v.wait(s)
            b.sems["B1_hs"].wait(s)
            s_hs2.inc(s.dma_start(out=hst[:, :], in_=HS[tt * 128:(tt + 1) * 128, :]), 16)
        b.on("sync", f)
        hsl = s_hs2.n
        for oc in range(D // 512):
            bank = oc % 2
            def f(t, tt=tt, oc=oc, bank=bank, wo_done=wo_done):
                if tt == 0 and oc == 0:
                    s_wo.wait(t, wo_done)
                    s_v.wait(t)
                s_v.wait(t, opfree[bank])
                for h in range(NH):
                    s_t.inc(t.matmul(
                        ops[bank][:, :],
                        ctxfm[:, h * T + tt * 128: h * T + (tt + 1) * 128],
                        wot[:, h * D + oc * 512: h * D + (oc + 1) * 512],
                        start=(h == 0), stop=(h == NH - 1)), 1)
            b.on("tensor", f)
            mmdone = s_t.n
            def f(v, oc=oc, bank=bank, hsl=hsl, mmdone=mmdone):
                s_t.wait(v, mmdone)
                s_hs2.wait(v, hsl)
                if oc == 0:
                    s_xp.wait(v)
                    s_sc.wait(v)
                s_v.inc(v.tensor_tensor(out=xp[:, oc * 512:(oc + 1) * 512],
                                        in0=ops[bank][:, :],
                                        in1=hst[:, oc * 512:(oc + 1) * 512], op=ALU.add), 1)
            b.on("vector", f)
            opfree[bank] = s_v.n
        def f(s, tt=tt):
            s_v.wait(s)
            s_xp.inc(s.dma_start(out=XPRIME[tt * 128:(tt + 1) * 128, :], in_=xp[:, :]), 16)
        b.on("sync", f)
        def f(sc):
            s_xp.wait(sc)
        b.on("scalar", f)
        ln_stats(b, xp, D, stat, scr, s_sc, s_v)
        ln_apply(b, xp, scr, D, stat, s_sc, s_v, gbc=g2, bbc=b2)
        def f(v):
            s_v.wait(v)
            s_nb2.wait(v)
            s_v.inc(v.tensor_copy(out=xnb[:, :], in_=scr[:, :]), 1)
        b.on("vector", f)
        def f(s, tt=tt):
            s_v.wait(s)
            s_nb2.inc(s.dma_start(out=XN2BF[tt * 128:(tt + 1) * 128, :], in_=xnb[:, :]), 16)
        b.on("sync", f)


def emit_ffn_up(b, seg, XNSRC, HBF, s_list, tag, mode, wload_up):
    (s_g, s_d, s_e, s_t, s_v, s_sc, s_p, s_o) = s_list
    nc, cfg = b.nc, b.cfg
    D, T, NT, DC, FF = cfg.D, cfg.T, cfg.NT, cfg.DC, cfg.FF

    s_fmu = b.sem("ffn_fmu")
    xnfm = b.sb(seg, tag + "XNFM", [128, DC * T], BF16)
    nbs = b.sems.get(tag + "_nbsrc")
    load_fm(b, xnfm, XNSRC, DC, T, s_fmu,
            wait_sems=[(nbs, nbs.n)] if nbs else [])
    xn_ready = s_fmu.n

    nsl = 2 if mode == "gelu" else 4
    wps = [b.sb(seg, tag + f"W1P{i}", [128, DC * 512], BF16) for i in range(nsl)]
    pfx = "ffn_w1T" if mode == "gelu" else "ffn_w1g"
    s_w1 = [b.sem(f"{pfx}{i}") for i in range(nsl)]
    fps = [b.ps(seg, tag + "FPS0", [128, 512]), b.ps(seg, tag + "FPS1", [128, 512])]
    gpsfree = [0, 0]
    if mode == "gateup24":
        gps = [b.ps(seg, tag + "GPS0", [128, 512]), b.ps(seg, tag + "GPS1", [128, 512])]
        sg = b.sb(seg, tag + "SG", [128, 512], F32)
        hh = b.sb(seg, tag + "HH", [128, 512], F32)
        aa = b.sb(seg, tag + "AA", [128, 512], F32)
        nn = b.sb(seg, tag + "NN", [128, 512], F32)
        mk_ = b.sb(seg, tag + "MK", [128, 512], F32)
        hsp = b.sb(seg, tag + "HSP", [128, 512], F32)
    stg = b.sb(seg, tag + "FSTG", [128, 512], BF16)
    s_hst = b.sem("ffn_hst")
    panfree = {}
    fpsfree = [0, 0]
    NP = FF // 512
    for ip in range(NP):
        slot = ip % 2
        if mode == "gelu":
            def f(s, ip=ip, slot=slot):
                s_t.wait(s, panfree.get(("w", slot), 0))
                wload_up(s, ip, wps[slot], s_w1[slot], "w")
            b.on("sync", f)
            wl = s_w1[slot].n
        else:
            def f(g, ip=ip, slot=slot):
                s_t.wait(g, panfree.get(("g", slot), 0))
                wload_up(g, ip, wps[slot], s_w1[slot], "gate")
            b.on("gpsimd", f)
            def f(g, ip=ip, slot=slot):
                s_t.wait(g, panfree.get(("u", slot), 0))
                wload_up(g, ip, wps[2 + slot], s_w1[2 + slot], "up")
            b.on("gpsimd", f)
            wl = s_w1[slot].n
            wl2 = s_w1[2 + slot].n
        for tt in range(NT):
            bank = tt % 2
            def f(t, ip=ip, tt=tt, bank=bank, slot=slot, wl=wl):
                s_w1[slot].wait(t, wl)
                if ip == 0 and tt == 0:
                    s_fmu.wait(t, xn_ready)
                if mode == "gelu":
                    s_sc.wait(t, fpsfree[bank])
                else:
                    s_sc.wait(t, fpsfree[bank])
                for fc in range(DC):
                    s_t.inc(t.matmul(
                        fps[bank][:, :],
                        xnfm[:, fc * T + tt * 128: fc * T + (tt + 1) * 128],
                        wps[slot][:, fc * 512:(fc + 1) * 512],
                        start=(fc == 0), stop=(fc == DC - 1)), 1)
            b.on("tensor", f)
            mmdone = s_t.n
            if mode == "gelu":
                def f(sc, bank=bank, mmdone=mmdone):
                    s_t.wait(sc, mmdone)
                    s_hst.wait(sc)
                    s_sc.inc(sc.activation(out=stg[:, :], in_=fps[bank][:, :],
                                           func=AF.Gelu_apprx_tanh), 1)
                b.on("scalar", f)
                fpsfree[bank] = s_sc.n
                def f(s, ip=ip, tt=tt):
                    s_sc.wait(s)
                    s_hst.inc(s.dma_start(
                        out=HBF[tt * 128:(tt + 1) * 128, ip * 512:(ip + 1) * 512],
                        in_=stg[:, :]), 16)
                b.on("sync", f)
            else:
                def f(t, ip=ip, tt=tt, bank=bank, slot=slot, wl2=wl2):
                    s_w1[2 + slot].wait(t, wl2)
                    s_v.wait(t, gpsfree[bank])
                    for fc in range(DC):
                        s_t.inc(t.matmul(
                            gps[bank][:, :],
                            xnfm[:, fc * T + tt * 128: fc * T + (tt + 1) * 128],
                            wps[2 + slot][:, fc * 512:(fc + 1) * 512],
                            start=(fc == 0), stop=(fc == DC - 1)), 1)
                b.on("tensor", f)
                mm2 = s_t.n
                def f(sc, bank=bank, mmdone=mmdone):
                    s_t.wait(sc, mmdone)
                    s_v.wait(sc)
                    s_sc.inc(sc.activation(out=sg[:, :], in_=fps[bank][:, :],
                                           func=AF.Silu), 1)
                b.on("scalar", f)
                fpsfree[bank] = s_sc.n
                def f(v, bank=bank, mm2=mm2):
                    s_t.wait(v, mm2)
                    s_sc.wait(v)
                    s_v.wait(v)
                    s_v.inc(v.tensor_tensor(out=hh[:, :], in0=sg[:, :],
                                            in1=gps[bank][:, :], op=ALU.mult), 1)
                b.on("vector", f)
                gpsfree[bank] = s_v.n
                def f(sc):
                    s_v.wait(sc)
                    s_sc.inc(sc.activation(out=aa[:, :], in_=hh[:, :], func=AF.Abs), 1)
                b.on("scalar", f)
                def f(v):
                    s_sc.wait(v)
                    a = [aa[:, k::4] for k in range(4)]
                    g01, g02, g03 = nn[:, 0::4], nn[:, 1::4], nn[:, 2::4]
                    g12 = nn[:, 3::4]
                    s_v.inc(v.tensor_tensor(out=g01, in0=a[1], in1=a[0], op=ALU.is_gt), 1)
                    s_v.inc(v.tensor_tensor(out=g02, in0=a[2], in1=a[0], op=ALU.is_gt), 1)
                    s_v.inc(v.tensor_tensor(out=g03, in0=a[3], in1=a[0], op=ALU.is_gt), 1)
                    s_v.inc(v.tensor_tensor(out=g12, in0=a[2], in1=a[1], op=ALU.is_gt), 1)
                    g13 = mk_[:, 0::4]; g23 = mk_[:, 1::4]
                    s_v.inc(v.tensor_tensor(out=g13, in0=a[3], in1=a[1], op=ALU.is_gt), 1)
                    s_v.inc(v.tensor_tensor(out=g23, in0=a[3], in1=a[2], op=ALU.is_gt), 1)
                    s_v.wait(v)
                    n0 = aa[:, 0::4]; n1 = aa[:, 1::4]; n2 = aa[:, 2::4]; n3 = aa[:, 3::4]
                    s_v.inc(v.tensor_tensor(out=n0, in0=g01, in1=g02, op=ALU.add), 1)
                    s_v.inc(v.tensor_tensor(out=n1, in0=g12, in1=g13, op=ALU.add), 1)
                    s_v.inc(v.tensor_tensor(out=n2, in0=g23, in1=g02, op=ALU.subtract), 1)
                    s_v.inc(v.tensor_tensor(out=n3, in0=g03, in1=g13, op=ALU.add), 1)
                    s_v.wait(v)
                    s_v.inc(v.tensor_tensor(out=n0, in0=n0, in1=g03, op=ALU.add), 1)
                    s_v.inc(v.tensor_tensor(out=n1, in0=n1, in1=g01, op=ALU.subtract), 1)
                    s_v.inc(v.tensor_tensor(out=n2, in0=n2, in1=g12, op=ALU.subtract), 1)
                    s_v.inc(v.tensor_tensor(out=n3, in0=n3, in1=g23, op=ALU.add), 1)
                    s_v.wait(v)
                    s_v.inc(v.tensor_scalar(out=n1, in0=n1, scalar1=1.0, scalar2=None,
                                            op0=ALU.add), 1)
                    s_v.inc(v.tensor_scalar(out=n2, in0=n2, scalar1=2.0, scalar2=None,
                                            op0=ALU.add), 1)
                    s_v.inc(v.tensor_scalar(out=n3, in0=n3, scalar1=-1.0, scalar2=3.0,
                                            op0=ALU.mult, op1=ALU.add), 1)
                    s_v.wait(v)
                    for k in range(4):
                        s_v.inc(v.tensor_scalar(out=mk_[:, k::4], in0=aa[:, k::4],
                                                scalar1=1.5, scalar2=None,
                                                op0=ALU.is_le), 1)
                    s_v.wait(v)
                    s_v.inc(v.tensor_tensor(out=hsp[:, :], in0=hh[:, :], in1=mk_[:, :],
                                            op=ALU.mult), 1)
                    s_v.wait(v)
                    s_hst.wait(v)
                    s_v.inc(v.tensor_copy(out=stg[:, :], in_=hsp[:, :]), 1)
                b.on("vector", f)
                def f(s, ip=ip, tt=tt):
                    s_v.wait(s)
                    s_hst.inc(s.dma_start(
                        out=HBF[tt * 128:(tt + 1) * 128, ip * 512:(ip + 1) * 512],
                        in_=stg[:, :]), 16)
                b.on("sync", f)
        if mode == "gelu":
            panfree[("w", slot)] = s_t.n
        else:
            panfree[("g", slot)] = s_t.n
            panfree[("u", slot)] = s_t.n



_W_NAMES = [
    "L0_attn_qkv", "L0_attn_o_proj", "L0_attn_gamma", "L0_attn_beta",
    "L0_ffn_simple_bias", "L0_ffn_w_down", "L0_ffn_w_up", "L0_ffn_w_router",
    "L0_ffn_ln1_scale", "L0_ffn_ln1_bias", "L0_ffn_wq", "L0_ffn_wk",
    "L0_ffn_wv", "L0_ffn_wo", "L0_ffn_ln2_scale", "L0_ffn_ln2_bias",
    "L0_ffn_w_ff1", "L0_ffn_w_ff2", "L1_attn_w_qkv", "L1_attn_b_qkv",
    "L1_attn_w_out", "L1_attn_b_out", "L1_ffn_w_gate", "L1_ffn_w_up",
    "L1_ffn_w_down",
]


def kernel(**inputs):
    from concourse.bass_utils import run_bass_kernel_spmd
    cfg = CFG(D=2048, T=2048, FF=8192, LR=32)
    x = np.ascontiguousarray(inputs["L0_attn_x"], dtype=np.float32)
    shards = shard_x(x)
    w = {n: np.ascontiguousarray(inputs[n], dtype=np.float32) for n in _W_NAMES}
    b = build(cfg, upto="D")
    in_maps = []
    for c in range(8):
        m = {"x": shards[c], "masks": host_masks(c % 4)}
        m.update(w)
        in_maps.append(m)
    res = run_bass_kernel_spmd(b.nc, in_maps, core_ids=list(range(8)))
    ys = [res.results[c]["y"] for c in range(8)]
    return unshard_y(cfg, ys)
